# version 01
# baseline (speedup 1.0000x reference)
import numpy as np
import ml_dtypes
from contextlib import ExitStack
import concourse.bass as bass
import concourse.mybir as mybir
from concourse.bass_utils import run_bass_kernel_spmd

F32 = mybir.dt.float32
BF16 = mybir.dt.bfloat16
AF = mybir.ActivationFunctionType
ALU = mybir.AluOpType
AX = mybir.AxisListType

C = 1024
NCG = 8
KT = 8
NCK = 512
EPS_WKV = 1e-8
EPS_LN = 1e-5


class Prog:
    def __init__(self, name="prog"):
        self.nc = bass.Bass("TRN2", target_bir_lowering=False)
        self.es = ExitStack()
        self.name = name
        self._semn = 0

    def sbuf(self, name, shape, dt=F32):
        return self.es.enter_context(self.nc.sbuf_tensor(name, list(shape), dt))

    def psum(self, name, shape, dt=F32):
        return self.es.enter_context(self.nc.psum_tensor(name, list(shape), dt))

    def sem(self, name=None):
        self._semn += 1
        es = getattr(self, "sem_es", None) or self.es
        return es.enter_context(self.nc.semaphore(name or f"s{self._semn}"))


class Script:
    ENGINES = ("sync", "gpsimd", "vector", "scalar", "tensor")

    def __init__(self, P):
        self.P = P
        self.steps = []
        self.count = {}

    def sem(self, name=None):
        h = self.P.sem(name)
        self.count[id(h)] = [h, 0]
        return h

    def n(self, h):
        return self.count[id(h)][1]

    def step(self, engine, emit, waits=(), incs=()):
        assert engine in self.ENGINES
        rw = []
        for w in waits:
            if isinstance(w, tuple):
                h, v = w
                v = self.n(h) if v is None else v
            else:
                h, v = w, self.n(w)
            assert v <= self.n(h), f"wait target {v} not yet emitted (sem at {self.n(h)})"
            if v > 0:
                rw.append((h, v))
        assert len(incs) <= 1, "hardware: one semaphore update per instruction"
        for h, amt in incs:
            self.count[id(h)][1] += amt
        self.steps.append((engine, rw, emit, list(incs)))

    def replay(self, block):
        for eng_name in self.ENGINES:
            mysteps = [s for s in self.steps if s[0] == eng_name]
            if not mysteps:
                continue
            sec = getattr(block, eng_name)

            def _body(eng, mysteps=mysteps):
                for _, waits, emit, incs in mysteps:
                    for h, v in waits:
                        eng.wait_ge(h, v)
                    instr = emit(eng)
                    for h, amt in incs:
                        assert instr is not None, "inc on step with no instruction"
                        instr.then_inc(h, amt)
            sec(_body)



def _colmajor_vec(v):
    return np.ascontiguousarray(np.asarray(v, dtype=np.float32).reshape(8, 128).T)

def _tile_colblocks(wT):
    cin, m = wT.shape
    ncg = m // 128
    a = np.asarray(wT).reshape(cin // 128, 128, ncg, 128)
    return np.ascontiguousarray(a.transpose(2, 1, 0, 3))

def prep_host(inputs):
    g = {}
    x = np.asarray(inputs["L0_attn_x"])
    B, T, _ = x.shape
    g["B"], g["T"] = B, T
    g["xT"] = [np.ascontiguousarray(x[b].T) for b in range(B)]
    g["wkt"] = _tile_colblocks(np.ascontiguousarray(np.asarray(inputs["L0_attn_Wk_t"]).T))
    g["wvt"] = _tile_colblocks(np.ascontiguousarray(np.asarray(inputs["L0_attn_Wv_t"]).T))
    g["wrt"] = _tile_colblocks(np.ascontiguousarray(np.asarray(inputs["L0_attn_Wr_t"]).T))
    pv = np.concatenate([
        _colmajor_vec(inputs["L0_attn_ln1_g"]),
        _colmajor_vec(inputs["L0_attn_ln1_b"]),
        _colmajor_vec(inputs["L0_attn_mix_k"]),
        _colmajor_vec(inputs["L0_attn_mix_v"]),
        _colmajor_vec(inputs["L0_attn_mix_r"]),
        _colmajor_vec(inputs["L0_attn_time_first"]),
        _colmajor_vec(inputs["L0_attn_ln2_g"]),
        _colmajor_vec(inputs["L0_attn_ln2_b"]),
        _colmajor_vec(inputs["L0_attn_cmix_k"]),
        _colmajor_vec(inputs["L0_attn_cmix_r"]),
    ], axis=1)
    g["pv0"] = np.ascontiguousarray(pv.astype(np.float32))
    return g


def emit_a1(P, S, io, gl, T):
    NT = T // NCK
    HALF = T // 2
    NHC = HALF // NCK

    epsc, onecf = gl["epsc"], gl["onecf"]

    xh = P.sbuf("a1_xh", [128, NCG * (T + 1)], BF16)

    def xg(i):
        return i * (T + 1) + 1
    pvt = P.sbuf("a1_pv", [128, 80], F32)
    pvd = P.sbuf("a1_pvd", [128, 40], F32)
    onesc = P.sbuf("a1_ones1", [128, 1], BF16)
    onesr = P.sbuf("a1_onesr", [1, 128], F32)
    sqt = P.sbuf("a1_sqt", [128, 2 * NCK], BF16)
    mrow = P.sbuf("a1_mrow", [1, T], F32)
    rrow = P.sbuf("a1_rrow", [1, T], F32)
    hb1 = P.sbuf("a1_hb1", [128, NCK], F32)
    hb2 = P.sbuf("a1_hb2", [128, NCK], F32)
    wring = P.sbuf("a1_wring", [128, 2 * 3 * C], BF16)
    stage = P.sbuf("a1_stage", [128, 6 * KT * 128], BF16)
    kH = P.sbuf("a1_kH", [128, HALF], F32)
    vH = P.sbuf("a1_vH", [128, HALF], F32)
    zH = P.sbuf("a1_zH", [128, HALF], F32)
    aH = P.sbuf("a1_aH", [128, HALF], F32)
    u1 = P.sbuf("a1_u1", [128, HALF], F32)
    u2 = P.sbuf("a1_u2", [128, HALF], F32)
    wH = P.sbuf("a1_wH", [128, HALF], F32)
    xr = P.sbuf("a1_xr", [128, HALF], F32)
    x1s = P.sbuf("a1_x1s", [128, HALF], F32)
    rg1 = P.sbuf("a1_rg1", [128, NCK], F32)
    rg2 = P.sbuf("a1_rg2", [128, NCK], F32)
    scr = P.sbuf("a1_scr", [128, 16], F32)

    CM, CNM, CMP, CDM, CEDM, CZC, CAC, CZR = range(8)

    pstat = [P.psum(f"a1_ps{i}", [128, NCK]) for i in range(4)]
    pbc = [P.psum(f"a1_pb{i}", [128, NCK]) for i in range(4)]
    pkb = pstat[0:2]
    pvb = pstat[2:4]
    prb = pbc[0]

    sLD0 = S.sem(); sX = S.sem(); sDRV = S.sem(); sSQ = S.sem(); sSQU = S.sem()
    sSTC = S.sem(); sRW = S.sem(); sBC = S.sem(); sBCC = S.sem()
    sWL = [S.sem(), S.sem()]; sPR = S.sem(); sKV = S.sem(); sKVE = S.sem()
    sVS = S.sem(); sSV = S.sem(); sRC = S.sem(); sRCC = S.sem()
    sXR = S.sem(); sXRC = S.sem(); sCMB = S.sem(); sST = S.sem(); sAA = S.sem()
    sVV = S.sem(); sSS = S.sem()

    xTd = io["xT"]; x1Td = io["x1T"]
    wktd = io["wkt"]; wvtd = io["wvt"]; wrtd = io["wrt"]; pv0d = io["pv0"]

    S.step("sync", lambda e: e.dma_start(out=pvt[:, 0:80], in_=pv0d), incs=[(sLD0, 16)])
    S.step("vector", lambda e: e.memset(onesc[:], 1.0))
    for _g in range(NCG):
        S.step("vector", lambda e, _g=_g: e.memset(xh[:, _g * (T + 1):_g * (T + 1) + 1], 0.0))
    S.step("vector", lambda e: e.memset(onesr[:], 1.0))
    S.step("vector", lambda e: e.tensor_scalar(pvd[:, 0:8], pvt[:, 16:24], -1.0, 1.0, ALU.mult, ALU.add),
           waits=[sLD0])
    S.step("vector", lambda e: e.tensor_scalar(pvd[:, 8:16], pvt[:, 24:32], -1.0, 1.0, ALU.mult, ALU.add))
    S.step("vector", lambda e: e.tensor_scalar(pvd[:, 16:24], pvt[:, 32:40], -1.0, 1.0, ALU.mult, ALU.add),
           incs=[(sDRV, 1)])
    S.step("scalar", lambda e: e.activation(pvd[:, 24:32], pvt[:, 40:48], AF.Exp), waits=[sLD0],
           incs=[(sDRV, 1)])

    for cg in range(NCG):
        S.step("gpsimd",
               lambda e, cg=cg: e.dma_start(out=xh[:, xg(cg):xg(cg) + T],
                                            in_=xTd[cg * 128:(cg + 1) * 128, :]),
               incs=[(sX, 16)])

    for ck in range(NT):
        bs = ck % 2
        for cg in range(NCG):
            i = ck * NCG + cg
            sl = (i % 2) * NCK
            S.step("scalar",
                   lambda e, cg=cg, ck=ck, sl=sl: e.activation(
                       sqt[:, sl:sl + NCK], xh[:, xg(cg) + ck * NCK: xg(cg) + (ck + 1) * NCK], AF.Square),
                   waits=[(sX, 16 * NCG)] + ([(sSQU, i - 1)] if i >= 2 else []),
                   incs=[(sSQ, 1)])
            S.step("tensor",
                   lambda e, cg=cg, ck=ck, bs=bs: e.matmul(
                       pstat[bs][0:1, :], onesc[:], xh[:, xg(cg) + ck * NCK: xg(cg) + (ck + 1) * NCK],
                       start=(cg == 0), stop=(cg == NCG - 1)),
                   waits=([(sX, 16 * NCG), (sDRV, 2)] if (ck == 0 and cg == 0) else []) +
                         ([(sSTC, 2 * (ck - 1))] if (cg == 0 and ck >= 2) else []))
            S.step("tensor",
                   lambda e, bs=bs, sl=sl, cg=cg: e.matmul(
                       pstat[2 + bs][0:1, :], onesc[:], sqt[:, sl:sl + NCK],
                       start=(cg == 0), stop=(cg == NCG - 1)),
                   waits=[(sSQ, i + 1)],
                   incs=[(sSQU, 1)])
        S.step("vector",
               lambda e, ck=ck, bs=bs: e.tensor_copy(mrow[0:1, ck * NCK:(ck + 1) * NCK], pstat[bs][0:1, :]),
               waits=[(sSQU, NCG * (ck + 1))])
        S.step("vector",
               lambda e, ck=ck, bs=bs: e.tensor_copy(rrow[0:1, ck * NCK:(ck + 1) * NCK], pstat[2 + bs][0:1, :]),
               incs=[(sSTC, 2)])

    nrw = 0
    for pi in range(2):
        ro = slice(pi * HALF, (pi + 1) * HALF)
        S.step("vector", lambda e, ro=ro: e.tensor_scalar(mrow[0:1, ro], mrow[0:1, ro], 1.0 / C, None, ALU.mult),
               waits=([(sSTC, 2 * NT)] if pi == 0 else []))
        S.step("vector", lambda e, ro=ro: e.tensor_scalar(rrow[0:1, ro], rrow[0:1, ro], 1.0 / C, None, ALU.mult),
               incs=[(sRW, 1)]); nrw += 1
        S.step("scalar", lambda e, ro=ro: e.activation(x1s[0:1, 0:HALF], mrow[0:1, ro], AF.Square),
               waits=[(sRW, nrw)], incs=[(sRW, 1)]); nrw += 1
        S.step("vector", lambda e, ro=ro: e.tensor_sub(rrow[0:1, ro], rrow[0:1, ro], x1s[0:1, 0:HALF]),
               waits=[(sRW, nrw)], incs=[(sRW, 1)]); nrw += 1
        S.step("scalar", lambda e, ro=ro: e.activation(x1s[0:1, 0:HALF], rrow[0:1, ro], AF.Ln, bias=epsc[0:1, 0:1]),
               waits=[(sRW, nrw)], incs=[(sAA, 1)])
        S.step("scalar", lambda e, ro=ro: e.activation(rrow[0:1, ro], x1s[0:1, 0:HALF], AF.Exp, scale=-0.5),
               waits=[sAA], incs=[(sRW, 1)]); nrw += 1

    for ck in range(NT):
        bs = ck % 2
        S.step("tensor",
               lambda e, ck=ck, bs=bs: e.matmul(pbc[bs][:, :], onesr[:], mrow[0:1, ck * NCK:(ck + 1) * NCK],
                                                start=True, stop=True),
               waits=[(sRW, 8)] + ([(sBCC, ck - 1)] if ck >= 2 else []))
        S.step("tensor",
               lambda e, ck=ck, bs=bs: e.matmul(pbc[2 + bs][:, :], onesr[:], rrow[0:1, ck * NCK:(ck + 1) * NCK],
                                                start=True, stop=True),
               incs=[(sBC, 1)])
        for cg in range(NCG):
            lo = xg(cg) + ck * NCK
            S.step("vector",
                   lambda e, lo=lo, bs=bs: e.tensor_sub(hb1[:], xh[:, lo:lo + NCK], pbc[bs][:, :]),
                   waits=([(sBC, ck + 1)] if cg == 0 else []), incs=[(sVV, 1)])
            S.step("vector",
                   lambda e, bs=bs: e.tensor_mul(hb2[:], hb1[:], pbc[2 + bs][:, :]),
                   waits=[(sVV, None)], incs=[(sVV, 1)])
            S.step("vector",
                   lambda e, lo=lo, cg=cg: e.tensor_scalar(xh[:, lo:lo + NCK], hb2[:], pvt[:, cg:cg + 1],
                                                           pvt[:, 8 + cg:8 + cg + 1], ALU.mult, ALU.add),
                   waits=[(sVV, None)],
                   incs=([(sBCC, 1)] if cg == NCG - 1 else []))

    QK1, QK2, QV1, QV2, QR1, QR2 = range(6)
    mixcol = {QK1: (pvt, 16), QK2: (pvd, 0), QV1: (pvt, 24), QV2: (pvd, 8), QR1: (pvt, 32), QR2: (pvd, 16)}
    wofs = {QK1: 0, QK2: 0, QV1: C, QV2: C, QR1: 2 * C, QR2: 2 * C}

    def stg(q, kt):
        o = (q * KT + kt) * 128
        return stage[:, o:o + 128]

    for cg in range(NCG):
        wslot = (cg % 2) * 3 * C
        for wi, wd in enumerate((wktd, wvtd, wrtd)):
            S.step("gpsimd",
                   lambda e, wi=wi, wd=wd, cg=cg, wslot=wslot: e.dma_start(
                       out=wring[:, wslot + wi * C: wslot + (wi + 1) * C],
                       in_=wd[cg].rearrange("p kt j -> p (kt j)")),
                   waits=([(sPR, cg - 1)] if cg >= 2 else []),
                   incs=[(sWL[cg % 2], 16)])
        for q in range(6):
            for kt in range(KT):
                col_t, col_o = mixcol[q]
                S.step("vector",
                       lambda e, q=q, kt=kt, wslot=wslot, col_t=col_t, col_o=col_o: e.tensor_scalar(
                           stg(q, kt), wring[:, wslot + wofs[q] + kt * 128: wslot + wofs[q] + (kt + 1) * 128],
                           col_t[:, col_o + kt:col_o + kt + 1], None, ALU.mult),
                       waits=([(sWL[cg % 2], 48 * (cg // 2 + 1)), (sDRV, 2), (sBCC, NT)] if (q == 0 and kt == 0) else []),
                       incs=([(sPR, 1)] if (q == 5 and kt == KT - 1) else []))
        for hf in range(2):
            hbase = hf * HALF
            first = (hf == 0)
            for hc in range(NHC):
                ck = hf * NHC + hc
                gck = cg * NT + ck
                t0 = hbase + hc * NCK
                kb = pkb[ck % 2]; vb = pvb[ck % 2]
                for q, bank, q2 in ((QK1, kb, QK2), (QV1, vb, QV2)):
                    for kt in range(KT):
                        S.step("tensor",
                               lambda e, q=q, kt=kt, bank=bank, t0=t0: e.matmul(
                                   bank[:, :], stg(q, kt), xh[:, xg(kt) + t0: xg(kt) + t0 + NCK],
                                   start=(kt == 0), stop=False),
                               waits=([(sPR, cg + 1)] + ([(sKVE, gck - 1)] if gck >= 2 else [])
                                      if (q == QK1 and kt == 0) else []))
                    for kt in range(KT):
                        S.step("tensor",
                               lambda e, q2=q2, kt=kt, bank=bank, t0=t0: e.matmul(
                                   bank[:, :], stg(q2, kt), xh[:, xg(kt) + t0 - 1: xg(kt) + t0 + NCK - 1],
                                   start=False, stop=(kt == KT - 1)),
                               incs=([(sKV, 1)] if kt == KT - 1 else []))
                S.step("vector",
                       lambda e, hc=hc, kb=kb: e.tensor_copy(kH[:, hc * NCK:(hc + 1) * NCK], kb[:, :]),
                       waits=[(sKV, 2 * gck + 2)])
                S.step("vector",
                       lambda e, hc=hc, vb=vb: e.tensor_copy(vH[:, hc * NCK:(hc + 1) * NCK], vb[:, :]),
                       incs=[(sKVE, 1)])

            zs = zH[:, 0:HALF - 1]
            last_v = [None]
            last_s = [None]

            def vs(emit, waits=(), inc=False):
                w = list(waits) + ([last_v[0]] if last_v[0] else [])
                s = sVS if inc else sVV
                S.step("vector", emit, waits=w, incs=[(s, 1)])
                last_v[0] = (s, S.n(s))

            def ss(emit, waits=(), inc=False):
                w = list(waits) + ([last_s[0]] if last_s[0] else [])
                s = sSV if inc else sSS
                S.step("scalar", emit, waits=w, incs=[(s, 1)])
                last_s[0] = (s, S.n(s))

            vs(lambda e: e.reduce_max(scr[:, CM:CM + 1], kH[:], axis=AX.X),
               waits=[(sKVE, cg * NT + hf * NHC + NHC)])
            vs(lambda e: e.memset(u2[:], 1.0))
            if not first:
                vs(lambda e: e.tensor_sub(scr[:, CDM:CDM + 1], scr[:, CMP:CMP + 1], scr[:, CM:CM + 1]))
            vs(lambda e: e.tensor_scalar(scr[:, CNM:CNM + 1], scr[:, CM:CM + 1], -1.0, None, ALU.mult),
               inc=True)
            if not first:
                ss(lambda e: e.activation(scr[:, CEDM:CEDM + 1], scr[:, CDM:CDM + 1], AF.Exp),
                   waits=[(sVS, None)])
            ss(lambda e: e.activation(wH[:], kH[:], AF.Exp, bias=scr[:, CNM:CNM + 1]),
               waits=[(sVS, None)], inc=True)
            if not first:
                vs(lambda e: e.tensor_mul(scr[:, CZC:CZC + 1], scr[:, CZR:CZR + 1], scr[:, CEDM:CEDM + 1]),
                   waits=[(sSV, None)])
            init = 0.0 if first else scr[:, CZC:CZC + 1]
            vs(lambda e, init=init: e.tensor_tensor_scan(zH[:], u2[:], wH[:], init, ALU.mult, ALU.add),
               waits=[(sSV, None)])
            if first:
                vs(lambda e: e.memset(u1[:, 0:1], 0.0))
            else:
                vs(lambda e: e.tensor_tensor(u1[:, 0:1], scr[:, CZC:CZC + 1], wH[:, 0:1], ALU.is_ge))
            vs(lambda e: e.tensor_tensor(u1[:, 1:HALF], zs, wH[:, 1:HALF], ALU.is_ge))
            vs(lambda e: e.tensor_scalar(u2[:], u1[:], EPS_WKV, 1.0, ALU.mult, ALU.add))
            vs(lambda e: e.tensor_scalar(u1[:], u1[:], -EPS_WKV, 1.0 + EPS_WKV, ALU.mult, ALU.add))
            vs(lambda e: e.tensor_mul(u1[:], u1[:], wH[:]))
            vs(lambda e, init=init: e.tensor_tensor_scan(zH[:], u2[:], u1[:], init, ALU.mult, ALU.add))
            if first:
                vs(lambda e: e.tensor_copy(u2[:, 0:1], wH[:, 0:1]))
            else:
                vs(lambda e: e.tensor_tensor(u2[:, 0:1], scr[:, CZC:CZC + 1], wH[:, 0:1], ALU.max))
            vs(lambda e: e.tensor_tensor(u2[:, 1:HALF], zs, wH[:, 1:HALF], ALU.max), inc=True)
            ss(lambda e: e.activation(u1[:], u2[:], AF.Ln), waits=[(sVS, None)])
            ss(lambda e: e.activation(u2[:], u1[:], AF.Exp, scale=-1.0), inc=True)
            if first:
                vs(lambda e: e.memset(u1[:, 0:1], 0.0), waits=[(sSV, None)])
            else:
                vs(lambda e: e.tensor_tensor(u1[:, 0:1], scr[:, CZC:CZC + 1], wH[:, 0:1], ALU.is_ge),
                   waits=[(sSV, None)])
            vs(lambda e: e.tensor_tensor(u1[:, 1:HALF], zs, wH[:, 1:HALF], ALU.is_ge))
            if first:
                vs(lambda e: e.memset(kH[:, 0:1], 0.0))
            else:
                vs(lambda e: e.tensor_mul(kH[:, 0:1], scr[:, CZC:CZC + 1], u2[:, 0:1]))
            vs(lambda e: e.tensor_mul(kH[:, 1:HALF], zs, u2[:, 1:HALF]))
            vs(lambda e: e.tensor_scalar(kH[:], kH[:], 1.0, None, ALU.min))
            vs(lambda e: e.tensor_tensor(kH[:], kH[:], u1[:], ALU.max))
            vs(lambda e: e.tensor_mul(u2[:], u2[:], wH[:]))
            vs(lambda e: e.tensor_scalar(u2[:], u2[:], 1.0, None, ALU.min))
            vs(lambda e: e.tensor_mul(u2[:], u2[:], vH[:]))
            ainit = 0.0 if first else scr[:, CAC:CAC + 1]
            vs(lambda e, ainit=ainit: e.tensor_tensor_scan(aH[:], kH[:], u2[:], ainit, ALU.mult, ALU.add))
            vs(lambda e, cg=cg: e.tensor_scalar(u2[:], wH[:], pvd[:, 24 + cg:24 + cg + 1], None, ALU.mult))
            if first:
                vs(lambda e: e.tensor_copy(wH[:, 0:1], u2[:, 0:1]))
            else:
                vs(lambda e: e.tensor_tensor(wH[:, 0:1], scr[:, CZC:CZC + 1], u2[:, 0:1], ALU.max))
            vs(lambda e: e.tensor_tensor(wH[:, 1:HALF], zs, u2[:, 1:HALF], ALU.max))
            vs(lambda e: e.scalar_tensor_tensor(wH[:], wH[:], EPS_WKV, u2[:], ALU.mult, ALU.add))
            vs(lambda e: e.tensor_add(wH[:, 1:HALF], wH[:, 1:HALF], zs))
            if not first:
                vs(lambda e: e.tensor_add(wH[:, 0:1], wH[:, 0:1], scr[:, CZC:CZC + 1]), inc=True)
            else:
                vs(lambda e: e.tensor_copy(scr[:, CZR:CZR + 1], zH[:, HALF - 1:HALF]), inc=True)
            ss(lambda e: e.activation(u1[:], wH[:], AF.Ln), waits=[(sVS, None)])
            ss(lambda e: e.activation(wH[:], u1[:], AF.Exp, scale=-1.0), inc=True)
            if first:
                vs(lambda e: e.memset(u1[:, 0:1], 0.0), waits=[(sSV, None)])
            else:
                vs(lambda e: e.tensor_mul(u1[:, 0:1], scr[:, CZC:CZC + 1], scr[:, CAC:CAC + 1]),
                   waits=[(sSV, None)])
            vs(lambda e: e.tensor_mul(u1[:, 1:HALF], zs, aH[:, 0:HALF - 1]))
            vs(lambda e: e.tensor_mul(u2[:], u2[:], vH[:]))
            vs(lambda e: e.tensor_add(u1[:], u1[:], u2[:]))
            vs(lambda e: e.tensor_mul(u1[:], u1[:], wH[:]))
            if hf == 0:
                vs(lambda e: e.tensor_copy(scr[:, CAC:CAC + 1], aH[:, HALF - 1:HALF]))
                vs(lambda e: e.tensor_copy(scr[:, CMP:CMP + 1], scr[:, CM:CM + 1]))
            wkv_tok = last_v[0]

            ih = 2 * cg + hf
            S.step("sync",
                   lambda e, cg=cg, hbase=hbase: e.dma_start(
                       out=xr[:], in_=xTd[cg * 128:(cg + 1) * 128, hbase:hbase + HALF]),
                   waits=([(sXRC, ih)] if ih >= 1 else []),
                   incs=[(sXR, 16)])

            for hc in range(NHC):
                t0 = hbase + hc * NCK
                j = cg * NT + hf * NHC + hc
                for kt in range(KT):
                    S.step("tensor",
                           lambda e, kt=kt, t0=t0: e.matmul(prb[:, :], stg(QR1, kt),
                                                            xh[:, xg(kt) + t0: xg(kt) + t0 + NCK],
                                                            start=(kt == 0), stop=False),
                           waits=([(sRCC, j)] if (kt == 0 and j >= 1) else []))
                for kt in range(KT):
                    S.step("tensor",
                           lambda e, kt=kt, t0=t0: e.matmul(prb[:, :], stg(QR2, kt),
                                                            xh[:, xg(kt) + t0 - 1: xg(kt) + t0 + NCK - 1],
                                                            start=False, stop=(kt == KT - 1)),
                           incs=([(sRC, 1)] if kt == KT - 1 else []))
                S.step("scalar", lambda e: e.activation(rg1[:], prb[:, :], AF.Exp, scale=-1.0),
                       waits=[sRC] + ([(sCMB, j)] if j >= 1 else []), incs=[(sAA, 1)])
                S.step("scalar", lambda e: e.activation(rg2[:], rg1[:], AF.Ln, bias=onecf[:, 0:1]),
                       waits=[sAA], incs=[(sAA, 1)])
                S.step("scalar", lambda e: e.activation(rg1[:], rg2[:], AF.Exp, scale=-1.0),
                       waits=[sAA], incs=[(sRCC, 1)])
                sl = slice(hc * NCK, (hc + 1) * NCK)
                S.step("vector", lambda e, sl=sl: e.tensor_mul(hb1[:], rg1[:], u1[:, sl]),
                       waits=[sRCC, (sXR, 16 * (ih + 1)), wkv_tok] +
                             ([(sST, 16 * ih)] if (hc == 0 and ih >= 1) else []) +
                             ([prev_add_tok] if hc >= 1 else []),
                       incs=[(sCMB, 1)])
                S.step("vector", lambda e, sl=sl: e.tensor_add(x1s[:, sl], hb1[:], xr[:, sl]),
                       waits=[(sCMB, None)],
                       incs=([(sXRC, 1)] if hc == NHC - 1 else [(sVV, 1)]))
                prev_add_tok = (sVV, S.n(sVV))
            S.step("sync",
                   lambda e, cg=cg, hbase=hbase: e.dma_start(
                       out=x1Td[cg * 128:(cg + 1) * 128, hbase:hbase + HALF], in_=x1s[:]),
                   waits=[(sXRC, ih + 1)],
                   incs=[(sST, 16)])
    S.step("sync", lambda e: None, waits=[(sST, 16 * NCG * 2)])


def build_a1(T):
    P = Prog("a1")
    nc = P.nc
    io = {
        "xT": nc.dram_tensor("xT", [C, T], F32, kind="ExternalInput").ap(),
        "x1T": nc.dram_tensor("x1T", [C, T], F32, kind="ExternalOutput").ap(),
        "wkt": nc.dram_tensor("wkt", [8, 128, 8, 128], F32, kind="ExternalInput").ap(),
        "wvt": nc.dram_tensor("wvt", [8, 128, 8, 128], F32, kind="ExternalInput").ap(),
        "wrt": nc.dram_tensor("wrt", [8, 128, 8, 128], F32, kind="ExternalInput").ap(),
        "pv0": nc.dram_tensor("pv0", [128, 80], F32, kind="ExternalInput").ap(),
    }
    with P.es:
        gl = {
            "epsc": P.sbuf("g_eps", [1, 1], F32),
            "onecf": P.sbuf("g_one", [128, 1], F32),
        }
        S = Script(P)
        S.step("vector", lambda e: e.memset(gl["epsc"][:], EPS_LN))
        S.step("vector", lambda e: e.memset(gl["onecf"][:], 1.0))
        block = P.es.enter_context(nc.Block())
        emit_a1(P, S, io, gl, T)
        S.replay(block)
    return nc


def _prep_a2(inputs):
    return {
        "wkc": _tile_colblocks(np.ascontiguousarray(np.asarray(inputs["L0_attn_Wk_c"]).T)),
        "wvc": _tile_colblocks(np.ascontiguousarray(np.asarray(inputs["L0_attn_Wv_c"]).T)),
        "wrc": _tile_colblocks(np.ascontiguousarray(np.asarray(inputs["L0_attn_Wr_c"]).T)),
    }


def kernel(**inputs):
    T = int(np.asarray(inputs["L0_attn_x"]).shape[1])
    B = int(np.asarray(inputs["L0_attn_x"]).shape[0])
    cores = list(range(B))
    g1 = prep_host(inputs)
    ga2 = _prep_a2(inputs)
    gb1 = prep_b1(inputs, T)
    gb2 = prep_b2(inputs, T)
    gc = prep_c(inputs, T)
    gd = prep_d(inputs, T)

    nc_a1 = build_a1(T)
    maps = [dict(xT=g1["xT"][b], wkt=g1["wkt"], wvt=g1["wvt"], wrt=g1["wrt"], pv0=g1["pv0"])
            for b in range(B)]
    r1 = run_bass_kernel_spmd(nc_a1, maps, core_ids=cores)
    x1 = [r1.results[b]["x1T"] for b in range(B)]

    nc_a2 = build_a2(T)
    maps = [dict(xT=g1["xT"][b], x1T=x1[b], wkc=ga2["wkc"], wvc=ga2["wvc"], wrc=ga2["wrc"],
                 pv0=g1["pv0"]) for b in range(B)]
    r2 = run_bass_kernel_spmd(nc_a2, maps, core_ids=cores)
    y = [r2.results[b]["yT"] for b in range(B)]

    nc_b1 = build_b1(T)
    maps = [dict(yT=y[b], wqkv_t=gb1["wqkv_t"], wo_t=gb1["wo_t"], lnw=gb1["lnw"],
                 cosb_d=gb1["cosb_d"], sinb_d=gb1["sinb_d"], masks_d=gb1["masks_d"],
                 ident_d=gb1["ident_d"]) for b in range(B)]
    r3 = run_bass_kernel_spmd(nc_b1, maps, core_ids=cores)
    x2 = [r3.results[b]["x2T"] for b in range(B)]

    nc_b2 = build_b2(T)
    maps = [dict(yT=y[b], x2T=x2[b], gw_t=gb2["gw_t"], w13_t=gb2["w13_t"], w2_t=gb2["w2_t"],
                 ln2w=gb2["ln2w"], identf_d=gb2["identf_d"], oh8_d=gb2["oh8_d"]) for b in range(B)]
    r4 = run_bass_kernel_spmd(nc_b2, maps, core_ids=cores)
    xB = [r4.results[b]["xBT"] for b in range(B)]

    nc_c = build_c(T)
    maps = [dict(xBT=xB[b], key_t=gc["key_t"], vp_t=gc["vp_t"]) for b in range(B)]
    r5 = run_bass_kernel_spmd(nc_c, maps, core_ids=cores)
    xC = [r5.results[b]["xCT"] for b in range(B)]

    nc_d = build_d(T)
    maps = [dict(xCT=xC[b], l1w_t=gd["l1w_t"], alpha_d=gd["alpha_d"], bias_col=gd["bias_col"])
            for b in range(B)]
    r6 = run_bass_kernel_spmd(nc_d, maps, core_ids=cores)
    out = np.stack([np.ascontiguousarray(r6.results[b]["outT"].T) for b in range(B)], axis=0)
    return out.astype(np.float32)



def emit_rows(P, S, pref, src, T, rms, rowbufs, waits0=()):
    NT = T // NCK
    mrow, rrow, tmprow = rowbufs
    xall = P.sbuf(pref + "_xall", [128, NCG * T], BF16)
    sqt = P.sbuf(pref + "_sqt", [128, 2 * NCK], BF16)
    onesc = P.sbuf(pref + "_ones1", [128, 1], BF16)
    pst = [P.psum(pref + f"_ps{i}", [128, NCK]) for i in range(4)]
    sX = S.sem(); sSQ = S.sem(); sSQU = S.sem(); sSTC = S.sem(); sRW = S.sem(); sAA = S.sem()
    S.step("vector", lambda e: e.memset(onesc[:], 1.0), incs=[(sRW, 1)])
    for cg in range(NCG):
        S.step("gpsimd",
               lambda e, cg=cg: e.dma_start(out=xall[:, cg * T:(cg + 1) * T],
                                            in_=src[cg * 128:(cg + 1) * 128, :]),
               waits=list(waits0), incs=[(sX, 16)])
    for ck in range(NT):
        bs = ck % 2
        for cg in range(NCG):
            i = ck * NCG + cg
            sl = (i % 2) * NCK
            S.step("scalar",
                   lambda e, cg=cg, ck=ck, sl=sl: e.activation(
                       sqt[:, sl:sl + NCK], xall[:, cg * T + ck * NCK: cg * T + (ck + 1) * NCK], AF.Square),
                   waits=[(sX, 16 * NCG)] + ([(sSQU, i - 1)] if i >= 2 else []),
                   incs=[(sSQ, 1)])
            if not rms:
                S.step("tensor",
                       lambda e, cg=cg, ck=ck, bs=bs: e.matmul(
                           pst[bs][0:1, :], onesc[:], xall[:, cg * T + ck * NCK: cg * T + (ck + 1) * NCK],
                           start=(cg == 0), stop=(cg == NCG - 1)),
                       waits=([(sX, 16 * NCG), (sRW, 1)] if (ck == 0 and cg == 0) else []) +
                             ([(sSTC, 2 * (ck - 1))] if (cg == 0 and ck >= 2) else []))
            S.step("tensor",
                   lambda e, bs=bs, sl=sl, cg=cg: e.matmul(
                       pst[2 + bs][0:1, :], onesc[:], sqt[:, sl:sl + NCK],
                       start=(cg == 0), stop=(cg == NCG - 1)),
                   waits=[(sSQ, i + 1)] +
                         ([(sRW, 1)] if (rms and ck == 0 and cg == 0) else []) +
                         ([(sSTC, 2 * (ck - 1))] if (rms and cg == 0 and ck >= 2) else []),
                   incs=[(sSQU, 1)])
        if not rms:
            S.step("vector",
                   lambda e, ck=ck, bs=bs: e.tensor_copy(mrow[0:1, ck * NCK:(ck + 1) * NCK], pst[bs][0:1, :]),
                   waits=[(sSQU, NCG * (ck + 1))])
            S.step("vector",
                   lambda e, ck=ck, bs=bs: e.tensor_copy(rrow[0:1, ck * NCK:(ck + 1) * NCK], pst[2 + bs][0:1, :]),
                   incs=[(sSTC, 2)])
        else:
            S.step("vector",
                   lambda e, ck=ck, bs=bs: e.tensor_copy(rrow[0:1, ck * NCK:(ck + 1) * NCK], pst[2 + bs][0:1, :]),
                   waits=[(sSQU, NCG * (ck + 1))], incs=[(sSTC, 2)])
    nrw = 1
    HALF = T // 2
    epsc = P.gl["epsc"]
    for pi in range(2):
        ro = slice(pi * HALF, (pi + 1) * HALF)
        if not rms:
            S.step("vector", lambda e, ro=ro: e.tensor_scalar(mrow[0:1, ro], mrow[0:1, ro], 1.0 / C, None, ALU.mult),
                   waits=([(sSTC, 2 * NT)] if pi == 0 else []))
        S.step("vector", lambda e, ro=ro: e.tensor_scalar(rrow[0:1, ro], rrow[0:1, ro], 1.0 / C, None, ALU.mult),
               waits=([(sSTC, 2 * NT)] if (pi == 0 and rms) else []), incs=[(sRW, 1)]); nrw += 1
        if not rms:
            S.step("scalar", lambda e, ro=ro: e.activation(tmprow[0:1, 0:HALF], mrow[0:1, ro], AF.Square),
                   waits=[(sRW, nrw)], incs=[(sRW, 1)]); nrw += 1
            S.step("vector", lambda e, ro=ro: e.tensor_sub(rrow[0:1, ro], rrow[0:1, ro], tmprow[0:1, 0:HALF]),
                   waits=[(sRW, nrw)], incs=[(sRW, 1)]); nrw += 1
        S.step("scalar", lambda e, ro=ro: e.activation(tmprow[0:1, 0:HALF], rrow[0:1, ro], AF.Ln, bias=epsc[0:1, 0:1]),
               waits=[(sRW, nrw)], incs=[(sAA, 1)])
        S.step("scalar", lambda e, ro=ro: e.activation(rrow[0:1, ro], tmprow[0:1, 0:HALF], AF.Exp, scale=-0.5),
               waits=[sAA], incs=[(sRW, 1)]); nrw += 1
    return sRW, nrw


def emit_normalize(P, S, pref, src, dst, T, rowbufs, rowsem, rms=False, extra_src_waits=()):
    HALF = T // 2
    NHC = HALF // NCK
    mrow, rrow, _ = rowbufs
    semv, semn = rowsem
    onesr = P.sbuf(pref + "_onesr", [1, 128], F32)
    xr = P.sbuf(pref + "_xr", [128, HALF], F32)
    ys = P.sbuf(pref + "_ys", [128, HALF], F32)
    pbc = [P.psum(pref + f"_pb{i}", [128, NCK]) for i in range(4)]
    sXR = S.sem(); sBC = S.sem(); sBCC = S.sem(); sST = S.sem(); sVV = S.sem()
    S.step("vector", lambda e: e.memset(onesr[:], 1.0), incs=[(sVV, 1)])
    nb = 0
    nx = 0
    for cg in range(NCG):
        for hf in range(2):
            ih = 2 * cg + hf
            hbase = hf * HALF
            S.step("sync",
                   lambda e, cg=cg, hbase=hbase: e.dma_start(
                       out=xr[:], in_=src[cg * 128:(cg + 1) * 128, hbase:hbase + HALF]),
                   waits=([(sBCC, NHC * ih)] if ih >= 1 else []) + list(extra_src_waits),
                   incs=[(sXR, 16)])
            nx += 16
            for hc in range(NHC):
                ck = hf * NHC + hc
                bs = nb % 2
                if not rms:
                    S.step("tensor",
                           lambda e, ck=ck, bs=bs: e.matmul(pbc[bs][:, :], onesr[:],
                                                            mrow[0:1, ck * NCK:(ck + 1) * NCK],
                                                            start=True, stop=True),
                           waits=([(semv, semn), (sVV, 1)] if nb == 0 else []) +
                                 ([(sBCC, nb - 1)] if nb >= 2 else []))
                S.step("tensor",
                       lambda e, ck=ck, bs=bs: e.matmul(pbc[2 + bs][:, :], onesr[:],
                                                        rrow[0:1, ck * NCK:(ck + 1) * NCK],
                                                        start=True, stop=True),
                       waits=([(semv, semn), (sVV, 1)] if (rms and nb == 0) else []) +
                             ([(sBCC, nb - 1)] if (rms and nb >= 2) else []),
                       incs=[(sBC, 1)])
                nb += 1
                sl = slice(hc * NCK, (hc + 1) * NCK)
                if not rms:
                    S.step("vector",
                           lambda e, sl=sl, bs=bs: e.tensor_sub(ys[:, sl], xr[:, sl], pbc[bs][:, :]),
                           waits=[(sBC, nb), (sXR, nx)] + ([(sST, 16 * ih)] if (hc == 0 and ih >= 1) else []),
                           incs=[(sVV, 1)])
                    S.step("vector",
                           lambda e, sl=sl, bs=bs: e.tensor_mul(ys[:, sl], ys[:, sl], pbc[2 + bs][:, :]),
                           waits=[(sVV, None)], incs=[(sBCC, 1)])
                else:
                    S.step("vector",
                           lambda e, sl=sl, bs=bs: e.tensor_mul(ys[:, sl], xr[:, sl], pbc[2 + bs][:, :]),
                           waits=[(sBC, nb), (sXR, nx)] + ([(sST, 16 * ih)] if (hc == 0 and ih >= 1) else []),
                           incs=[(sBCC, 1)])
            S.step("sync",
                   lambda e, cg=cg, hbase=hbase: e.dma_start(
                       out=dst[cg * 128:(cg + 1) * 128, hbase:hbase + HALF], in_=ys[:]),
                   waits=[(sBCC, NHC * (ih + 1))],
                   incs=[(sST, 16)])
    S.step("sync", lambda e: None, waits=[(sST, 16 * NCG * 2)])
    return sST, 16 * NCG * 2



def emit_a2_tsbuild(P, S, pref, x1d, tsbuf, mixc_off, rowbufs, rowsem, pvt, sST_prev=None):
    T = P.T
    mrow, rrow, _ = rowbufs
    semv, semn = rowsem
    onesr = P.gl["onesr"]
    x1b = P.sbuf(pref + "_x1b", [128, P.T], BF16)
    h2t = P.sbuf(pref + "_h2t", [128, NCK], BF16)
    hb1 = P.sbuf(pref + "_hb1", [128, NCK], F32)
    hb2 = P.sbuf(pref + "_hb2", [128, NCK], F32)
    pbc = [P.psum(pref + f"_pb{i}", [128, NCK]) for i in range(4)]
    NT = T // NCK
    carry = P.sbuf(pref + "_carry", [128, 1], F32)
    sX1 = [S.sem() for _ in range(2)]
    sBC = S.sem(); sBCC = S.sem(); sVV = S.sem(); sCGD = S.sem()

    def xg(i):
        return i * (T + 1) + 1
    for _g in range(NCG):
        S.step("vector", lambda e, _g=_g: e.memset(tsbuf[:, _g * (T + 1):_g * (T + 1) + 1], 0.0),
               incs=[(sVV, 1)])
    nb = 0
    for cg in range(NCG):
        S.step("gpsimd",
               lambda e, cg=cg: e.dma_start(out=x1b[:, :], in_=x1d[cg * 128:(cg + 1) * 128, :]),
               waits=([(sCGD, cg)] if cg >= 1 else []) + ([] if sST_prev is None or cg > 0 else [sST_prev]),
               incs=[(sX1[cg % 2], 16)])
        for ck in range(NT):
            bs = nb % 2
            S.step("tensor",
                   lambda e, ck=ck, bs=bs: e.matmul(pbc[bs][:, :], onesr[:], mrow[0:1, ck * NCK:(ck + 1) * NCK],
                                                    start=True, stop=True),
                   waits=([(semv, semn)] if nb == 0 else []) + ([(sBCC, nb - 1)] if nb >= 2 else []))
            S.step("tensor",
                   lambda e, ck=ck, bs=bs: e.matmul(pbc[2 + bs][:, :], onesr[:], rrow[0:1, ck * NCK:(ck + 1) * NCK],
                                                    start=True, stop=True),
                   incs=[(sBC, 1)])
            nb += 1
            S.step("vector",
                   lambda e, ck=ck, bs=bs: e.tensor_sub(hb1[:], x1b[:, ck * NCK:(ck + 1) * NCK], pbc[bs][:, :]),
                   waits=[(sX1[cg % 2], 16 * (cg // 2 + 1)), (sBC, nb), (sVV, None)], incs=[(sVV, 1)])
            S.step("vector",
                   lambda e, bs=bs: e.tensor_mul(hb2[:], hb1[:], pbc[2 + bs][:, :]),
                   waits=[(sVV, None)], incs=[(sVV, 1)])
            S.step("vector",
                   lambda e, cg=cg: e.tensor_scalar(h2t[:], hb2[:], pvt[:, 48 + cg:48 + cg + 1],
                                                    pvt[:, 56 + cg:56 + cg + 1], ALU.mult, ALU.add),
                   waits=[(sVV, None)], incs=[(sBCC, 1)])
            S.step("vector",
                   lambda e, cg=cg, ck=ck: e.tensor_scalar(
                       hb1[:, 0:NCK], h2t[:], pvt[:, mixc_off + cg:mixc_off + cg + 1], None, ALU.mult),
                   waits=[(sBCC, None)], incs=[(sVV, 1)])
            S.step("vector",
                   lambda e, cg=cg, ck=ck: e.scalar_tensor_tensor(
                       tsbuf[:, xg(cg) + ck * NCK + 1: xg(cg) + (ck + 1) * NCK],
                       h2t[:, 0:NCK - 1], pvd2_g(P, mixc_off, cg), hb1[:, 1:NCK], ALU.mult, ALU.add),
                   waits=[(sVV, None)], incs=[(sVV, 1)])
            if ck == 0:
                S.step("vector",
                       lambda e, cg=cg, ck=ck: e.tensor_copy(
                           tsbuf[:, xg(cg) + ck * NCK: xg(cg) + ck * NCK + 1], hb1[:, 0:1]),
                       waits=[(sVV, None)], incs=[(sVV, 1)])
            else:
                S.step("vector",
                       lambda e, cg=cg, ck=ck: e.scalar_tensor_tensor(
                           tsbuf[:, xg(cg) + ck * NCK: xg(cg) + ck * NCK + 1],
                           carry[:, 0:1], pvd2_g(P, mixc_off, cg), hb1[:, 0:1], ALU.mult, ALU.add),
                       waits=[(sVV, None)], incs=[(sVV, 1)])
            S.step("vector",
                   lambda e: e.tensor_copy(carry[:, 0:1], h2t[:, NCK - 1:NCK]),
                   waits=[(sVV, None)],
                   incs=([(sCGD, 1)] if ck == NT - 1 else [(sVV, 1)]))
    return sCGD


def pvd2_g(P, mixc_off, cg):
    base = 0 if mixc_off == 64 else 8
    return P.gl["pvd2"][:, base + cg:base + cg + 1]


def emit_a2_core(P, S, io, gl, T, waits0=()):
    NT = T // NCK
    HALF = T // 2
    NHC = HALF // NCK

    def xg(i):
        return i * (T + 1) + 1

    tsck = io["tsck"]
    wkcd = io["wkc"]; wvcd = io["wvc"]
    vvT = io["vvT"]

    sqC = P.sbuf("a2_sqC", [128, 32 * NCK], BF16)
    rel1 = P.sbuf("a2_rel1", [128, 4 * NCK], BF16)
    wkcr = P.sbuf("a2_wkcr", [128, 4 * KT * 128], BF16)
    wvcr = P.sbuf("a2_wvcr", [128, 2 * 32 * 128], BF16)
    vvs = P.sbuf("a2_vvs", [128, 2 * NCK], F32)
    pk = [P.psum(f"a2_pk{i}", [128, NCK]) for i in range(2)]
    pv_ = [P.psum(f"a2_pv{i}", [128, NCK]) for i in range(2)]
    sWK = [S.sem() for _ in range(4)]
    sWV = [S.sem() for _ in range(2)]
    sKC = S.sem(); sSQ = S.sem(); sVV_ = S.sem(); sVVE = S.sem(); sVST = S.sem()
    sAA = S.sem()

    nwk = [0] * 4
    nwv = [0] * 2
    for tc in range(NT):
        for fg in range(32):
            j = tc * 32 + fg
            slot = j % 4
            S.step("gpsimd",
                   lambda e, fg=fg, slot=slot: e.dma_start(
                       out=wkcr[:, slot * KT * 128:(slot + 1) * KT * 128],
                       in_=wkcd[fg].rearrange("p kt j -> p (kt j)")),
                   waits=([(sKC, j - 3)] if j >= 4 else []) + (list(waits0) if j == 0 else []),
                   incs=[(sWK[slot], 16)])
            nwk[slot] += 16
            bk = pk[j % 2]
            for kt in range(KT):
                S.step("tensor",
                       lambda e, kt=kt, fg=fg, tc=tc, bk=bk, slot=slot: e.matmul(
                           bk[:, :], wkcr[:, (slot * KT + kt) * 128:(slot * KT + kt + 1) * 128],
                           tsck[:, xg(kt) + tc * NCK: xg(kt) + (tc + 1) * NCK],
                           start=(kt == 0), stop=(kt == KT - 1)),
                       waits=([(sWK[slot], nwk[slot])] if kt == 0 else []) +
                             ([(sSQ, j - 1)] if (kt == 0 and j >= 2) else []),
                       incs=([(sKC, 1)] if kt == KT - 1 else []))
            S.step("scalar",
                   lambda e, j=j, bk=bk: e.activation(rel1[:, (j % 4) * NCK:(j % 4 + 1) * NCK], bk[:, :], AF.Relu),
                   waits=[(sKC, j + 1)] + ([(sSQ, j - 3)] if j >= 4 else []),
                   incs=[(sAA, 1)])
            S.step("vector",
                   lambda e, j=j, fg=fg: e.tensor_mul(
                       sqC[:, fg * NCK:(fg + 1) * NCK],
                       rel1[:, (j % 4) * NCK:(j % 4 + 1) * NCK], rel1[:, (j % 4) * NCK:(j % 4 + 1) * NCK]),
                   waits=[(sAA, None)] + ([(sVV_, tc * NCG)] if (fg == 0 and tc >= 1) else []),
                   incs=[(sSQ, 1)])
        for cg in range(NCG):
            jj = tc * NCG + cg
            slot = jj % 2
            S.step("gpsimd",
                   lambda e, cg=cg, slot=slot: e.dma_start(
                       out=wvcr[:, slot * 32 * 128:(slot + 1) * 32 * 128],
                       in_=wvcd[cg].rearrange("p ft j -> p (ft j)")),
                   waits=([(sVV_, jj - 1)] if jj >= 2 else []),
                   incs=[(sWV[slot], 16)])
            nwv[slot] += 16
            bv = pv_[jj % 2]
            for ft in range(32):
                S.step("tensor",
                       lambda e, ft=ft, bv=bv, slot=slot: e.matmul(
                           bv[:, :], wvcr[:, (slot * 32 + ft) * 128:(slot * 32 + ft + 1) * 128],
                           sqC[:, ft * NCK:(ft + 1) * NCK],
                           start=(ft == 0), stop=(ft == 31)),
                       waits=([(sWV[slot], nwv[slot]), (sSQ, (tc + 1) * 32)] if ft == 0 else []) +
                             ([(sVVE, jj - 1)] if (ft == 0 and jj >= 2) else []),
                       incs=([(sVV_, 1)] if ft == 31 else []))
            S.step("vector",
                   lambda e, jj=jj, bv=bv: e.tensor_copy(vvs[:, (jj % 2) * NCK:(jj % 2 + 1) * NCK], bv[:, :]),
                   waits=[(sVV_, jj + 1)] + ([(sVST, 16 * jj)] if jj >= 2 else []),
                   incs=[(sVVE, 1)])
            S.step("sync",
                   lambda e, tc=tc, cg=cg, jj=jj: e.dma_start(
                       out=vvT[cg * 128:(cg + 1) * 128, tc * NCK:(tc + 1) * NCK],
                       in_=vvs[:, (jj % 2) * NCK:(jj % 2 + 1) * NCK]),
                   waits=[(sVVE, jj + 1)],
                   incs=[(sVST, 16)])
    S.step("sync", lambda e: None, waits=[(sVST, 16 * NT * NCG)])
    return sVST, 16 * NT * NCG


def emit_a2_combine(P, S, io, gl, T, waits0=()):
    NT = T // NCK
    HALF = T // 2
    NHC = HALF // NCK

    def xg(i):
        return i * (T + 1) + 1
    tscr = io["tscr"]; wrcd = io["wrc"]
    vvT = io["vvT"]; xTd = io["xT"]; x1Td = io["x1T"]; s2Td = io["s2T"]
    onecf = gl["onecf"]

    wrr = P.sbuf("a2c_wrr", [128, 2 * KT * 128], BF16)
    rg1 = P.sbuf("a2c_rg1", [128, NCK], F32)
    rg2 = P.sbuf("a2c_rg2", [128, NCK], F32)
    xr0 = P.sbuf("a2c_xr0", [128, HALF], F32)
    xr1 = P.sbuf("a2c_xr1", [128, HALF], F32)
    vvr = P.sbuf("a2c_vvr", [128, HALF], F32)
    s2s = P.sbuf("a2c_s2s", [128, HALF], F32)
    pr = [P.psum(f"a2c_pr{i}", [128, NCK]) for i in range(2)]
    sWR = [S.sem() for _ in range(2)]
    sRC = S.sem(); sRCC = S.sem(); sXR = S.sem(); sST = S.sem(); sAA = S.sem(); sVV = S.sem()
    nwr = [0] * 2
    nxr = 0
    for cg in range(NCG):
        slot = cg % 2
        S.step("gpsimd",
               lambda e, cg=cg, slot=slot: e.dma_start(
                   out=wrr[:, slot * KT * 128:(slot + 1) * KT * 128],
                   in_=wrcd[cg].rearrange("p kt j -> p (kt j)")),
               waits=([(sRCC, (cg - 1) * NT)] if cg >= 2 else []) + (list(waits0) if cg == 0 else []),
               incs=[(sWR[slot], 16)])
        nwr[slot] += 16
        for hf in range(2):
            ih = 2 * cg + hf
            hbase = hf * HALF
            S.step("sync",
                   lambda e, cg=cg, hbase=hbase: e.dma_start(
                       out=xr0[:], in_=xTd[cg * 128:(cg + 1) * 128, hbase:hbase + HALF]),
                   waits=([(sVV, 3 * NHC * ih)] if ih >= 1 else []) + (list(waits0) if ih == 0 else []),
                   incs=[(sXR, 16)])
            S.step("sync",
                   lambda e, cg=cg, hbase=hbase: e.dma_start(
                       out=xr1[:], in_=x1Td[cg * 128:(cg + 1) * 128, hbase:hbase + HALF]),
                   incs=[(sXR, 16)])
            S.step("sync",
                   lambda e, cg=cg, hbase=hbase: e.dma_start(
                       out=vvr[:], in_=vvT[cg * 128:(cg + 1) * 128, hbase:hbase + HALF]),
                   incs=[(sXR, 16)])
            nxr += 48
            for hc in range(NHC):
                tc = hf * NHC + hc
                j = cg * NT + tc
                bk = pr[j % 2]
                for kt in range(KT):
                    S.step("tensor",
                           lambda e, kt=kt, tc=tc, bk=bk, slot=slot: e.matmul(
                               bk[:, :], wrr[:, (slot * KT + kt) * 128:(slot * KT + kt + 1) * 128],
                               tscr[:, xg(kt) + tc * NCK: xg(kt) + (tc + 1) * NCK],
                               start=(kt == 0), stop=(kt == KT - 1)),
                           waits=([(sWR[slot], nwr[slot])] if kt == 0 else []) +
                                 ([(sRCC, j - 1)] if (kt == 0 and j >= 2) else []),
                           incs=([(sRC, 1)] if kt == KT - 1 else []))
                S.step("scalar", lambda e, bk=bk, j=j: e.activation(rg1[:], bk[:, :], AF.Exp, scale=-1.0),
                       waits=[(sRC, j + 1)] + ([(sVV, 3 * j - 2)] if j >= 1 else []), incs=[(sAA, 1)])
                S.step("scalar", lambda e: e.activation(rg2[:], rg1[:], AF.Ln, bias=onecf[:, 0:1]),
                       waits=[(sAA, None)], incs=[(sAA, 1)])
                S.step("scalar", lambda e: e.activation(rg1[:], rg2[:], AF.Exp, scale=-1.0),
                       waits=[(sAA, None)], incs=[(sRCC, 1)])
                sl = slice(hc * NCK, (hc + 1) * NCK)
                S.step("vector", lambda e, sl=sl: e.tensor_mul(s2s[:, sl], rg1[:], vvr[:, sl]),
                       waits=[(sRCC, None), (sXR, nxr)] + ([(sST, 16 * ih)] if (hc == 0 and ih >= 1) else []),
                       incs=[(sVV, 1)])
                S.step("vector", lambda e, sl=sl: e.tensor_add(s2s[:, sl], s2s[:, sl], xr0[:, sl]),
                       waits=[(sVV, None)], incs=[(sVV, 1)])
                S.step("vector", lambda e, sl=sl: e.tensor_add(s2s[:, sl], s2s[:, sl], xr1[:, sl]),
                       waits=[(sVV, None)], incs=[(sVV, 1)])
            S.step("sync",
                   lambda e, cg=cg, hbase=hbase: e.dma_start(
                       out=s2Td[cg * 128:(cg + 1) * 128, hbase:hbase + HALF], in_=s2s[:]),
                   waits=[(sVV, None)],
                   incs=[(sST, 16)])
    S.step("sync", lambda e: None, waits=[(sST, 16 * NCG * 2)])
    return sST, 16 * NCG * 2


def build_a2(T):
    P = Prog("a2")
    P.T = T
    nc = P.nc
    io = {
        "xT": nc.dram_tensor("xT", [C, T], F32, kind="ExternalInput").ap(),
        "x1T": nc.dram_tensor("x1T", [C, T], F32, kind="ExternalInput").ap(),
        "yT": nc.dram_tensor("yT", [C, T], F32, kind="ExternalOutput").ap(),
        "wkc": nc.dram_tensor("wkc", [32, 128, 8, 128], F32, kind="ExternalInput").ap(),
        "wvc": nc.dram_tensor("wvc", [8, 128, 32, 128], F32, kind="ExternalInput").ap(),
        "wrc": nc.dram_tensor("wrc", [8, 128, 8, 128], F32, kind="ExternalInput").ap(),
        "pv0": nc.dram_tensor("pv0", [128, 80], F32, kind="ExternalInput").ap(),
        "vvT": nc.dram_tensor("vvT", [C, T], F32, kind="Internal").ap(),
        "s2T": nc.dram_tensor("s2T", [C, T], F32, kind="Internal").ap(),
    }
    with P.es:
        P.sem_es = P.es
        gl = {
            "epsc": P.sbuf("g_eps", [1, 1], F32),
            "onecf": P.sbuf("g_one", [128, 1], F32),
            "onesr": P.sbuf("g_onesr", [1, 128], F32),
            "pvd2": P.sbuf("g_pvd2", [128, 16], F32),
        }
        P.gl = gl
        pvt = P.sbuf("g_pvt", [128, 80], F32)
        rowbufs = (P.sbuf("g_mrow", [1, T], F32), P.sbuf("g_rrow", [1, T], F32), P.sbuf("g_trow", [1, T // 2], F32))
        S = Script(P)
        sLD0 = S.sem()
        S.step("vector", lambda e: e.memset(gl["epsc"][:], EPS_LN))
        S.step("vector", lambda e: e.memset(gl["onecf"][:], 1.0))
        S.step("vector", lambda e: e.memset(gl["onesr"][:], 1.0))
        S.step("sync", lambda e: e.dma_start(out=pvt[:, 0:80], in_=io["pv0"]), incs=[(sLD0, 16)])
        S.step("vector", lambda e: e.tensor_scalar(gl["pvd2"][:, 0:8], pvt[:, 64:72], -1.0, 1.0, ALU.mult, ALU.add),
               waits=[sLD0])
        S.step("vector", lambda e: e.tensor_scalar(gl["pvd2"][:, 8:16], pvt[:, 72:80], -1.0, 1.0, ALU.mult, ALU.add))
        block = P.es.enter_context(nc.Block())
        with ExitStack() as sc0:
            P_es_old = P.es; P.es = sc0
            rowsem = emit_rows(P, S, "a2r", io["x1T"], T, rms=False, rowbufs=rowbufs)
            P.es = P_es_old
        io2 = dict(io)
        with ExitStack() as sc1:
            P_es_old = P.es; P.es = sc1
            tsck = P.sbuf("a2_tsck", [128, NCG * (T + 1)], BF16)
            sCk = emit_a2_tsbuild(P, S, "a2tk", io["x1T"], tsck, 64, rowbufs, rowsem, pvt,
                                  sST_prev=rowsem)
            io2["tsck"] = tsck
            svst, nvst = emit_a2_core(P, S, io2, gl, T, waits0=[(sCk, NCG)])
            P.es = P_es_old
        with ExitStack() as sc2:
            P_es_old = P.es; P.es = sc2
            tscr = P.sbuf("a2_tscr", [128, NCG * (T + 1)], BF16)
            sCr = emit_a2_tsbuild(P, S, "a2tr", io["x1T"], tscr, 72, rowbufs, rowsem, pvt,
                                  sST_prev=(svst, nvst))
            io2["tscr"] = tscr
            sstc, nstc = emit_a2_combine(P, S, io2, gl, T, waits0=[(sCr, NCG)])
            P.es = P_es_old
        with ExitStack() as sc4:
            P_es_old = P.es; P.es = sc4
            rowsem2 = emit_rows(P, S, "a2p", io["s2T"], T, rms=False, rowbufs=rowbufs,
                                waits0=[(sstc, nstc)])
            emit_normalize(P, S, "a2n", io["s2T"], io["yT"], T, rowbufs, rowsem2,
                           extra_src_waits=[(sstc, nstc)])
            P.es = P_es_old
        S.replay(block)
    return nc



def emit_b1_h1(P, S, io, gl, T, rowbufs, rowsem, lnwsem=None):
    NT = T // NCK
    yTd = io["yT"]; h1Td = io["h1T"]; lw = io["lnw_col"]
    onesr = gl["onesr"]
    mrow, rrow, _ = rowbufs
    semv, semn = rowsem
    yb = P.sbuf("b1h_yb", [128, 2 * P.T], BF16)
    h1t = P.sbuf("b1h_h1", [128, P.T], BF16)
    pbc = [P.psum(f"b1h_pb{i}", [128, NCK]) for i in range(2)]
    sY = [S.sem(), S.sem()]; sBC = S.sem(); sBCC = S.sem(); sVV = S.sem(); sST = S.sem()
    nb = 0
    for cg in range(NCG):
        S.step("gpsimd",
               lambda e, cg=cg: e.dma_start(out=yb[:, (cg % 2) * T:(cg % 2 + 1) * T],
                                            in_=yTd[cg * 128:(cg + 1) * 128, :]),
               waits=([rowsem] if cg == 0 else []) + ([(sST, 16 * (cg - 1))] if cg >= 2 else []),
               incs=[(sY[cg % 2], 16)])
        for ck in range(NT):
            bs = nb % 2
            S.step("tensor",
                   lambda e, ck=ck, bs=bs: e.matmul(pbc[bs][:, :], onesr[:], rrow[0:1, ck * NCK:(ck + 1) * NCK],
                                                    start=True, stop=True),
                   waits=([(semv, semn)] if nb == 0 else []) + ([(sBCC, nb - 1)] if nb >= 2 else []),
                   incs=[(sBC, 1)])
            nb += 1
            S.step("vector",
                   lambda e, ck=ck, bs=bs, cg=cg: e.tensor_mul(h1t[:, ck * NCK:(ck + 1) * NCK],
                                                               yb[:, (cg % 2) * T + ck * NCK:(cg % 2) * T + (ck + 1) * NCK],
                                                               pbc[bs][:, :]),
                   waits=[(sY[cg % 2], 16 * (cg // 2 + 1)), (sBC, nb)] +
                         ([(sST, 16 * cg)] if (ck == 0 and cg >= 1) else []),
                   incs=[(sVV, 1)])
            S.step("vector",
                   lambda e, ck=ck, cg=cg: e.tensor_scalar(h1t[:, ck * NCK:(ck + 1) * NCK],
                                                           h1t[:, ck * NCK:(ck + 1) * NCK],
                                                           lw[:, cg:cg + 1], None, ALU.mult),
                   waits=[(sVV, None)] + ([lnwsem] if (lnwsem and cg == 0 and ck == 0) else []),
                   incs=[(sBCC, 1)])
        S.step("sync",
               lambda e, cg=cg: e.dma_start(out=h1Td[cg * 128:(cg + 1) * 128, :], in_=h1t[:, :]),
               waits=[(sBCC, nb)],
               incs=[(sST, 16)])
    S.step("sync", lambda e: None, waits=[(sST, 16 * NCG)])
    return sST, 16 * NCG


def emit_b1_qkv(P, S, io, gl, T, waits0=()):
    NT = T // NCK
    h1Td = io["h1T"]; wq = io["wqkv_t"]
    Qr, Kr, Vb, VT = io["Qr"], io["Kr"], io["Vb"], io["VT"]
    cosb, sinb = io["cosb"], io["sinb"]
    ident = io["identb"]
    hr = P.sbuf("b1q_hr", [128, KT * NCK], BF16)
    wqs = P.sbuf("b1q_wqs", [128, 12 * KT * 128], BF16)
    wrs = P.sbuf("b1q_wrs", [128, 10 * KT * 128], BF16)
    t1 = P.sbuf("b1q_t1", [128, NCK], F32)
    t2 = P.sbuf("b1q_t2", [128, NCK], F32)
    pA = [P.psum(f"b1q_pA{i}", [128, NCK]) for i in range(2)]
    pB = [P.psum(f"b1q_pB{i}", [128, NCK]) for i in range(2)]
    pT = P.psum("b1q_pT", [128, 64], BF16)
    sWQ = S.sem(); sH = [S.sem(), S.sem()]; sMM = S.sem(); sRP = S.sem(); sVV = S.sem()
    sTR = S.sem(); sTRE = S.sem(); sAA = S.sem()
    S.step("gpsimd", lambda e: e.dma_start(
               out=wqs[:].rearrange("p (hq kt j) -> p hq kt j", kt=KT, j=128),
               in_=wq.rearrange("hq p kt j -> p hq kt j")),
           waits=list(waits0), incs=[(sWQ, 16)])
    S.step("vector", lambda e: e.tensor_scalar(wqs[:, 0:8 * KT * 128], wqs[:, 0:8 * KT * 128], 0.125, None, ALU.mult),
           waits=[(sWQ, 16)], incs=[(sVV, 1)])
    for hq in range(10):
        for kt in range(KT):
            o = (hq * KT + kt) * 128
            S.step("vector", lambda e, o=o: e.tensor_scalar(wrs[:, o:o + 32], wqs[:, o + 32:o + 64], -1.0, None, ALU.mult),
                   waits=[(sVV, None)])
            S.step("vector", lambda e, o=o: e.tensor_copy(wrs[:, o + 32:o + 64], wqs[:, o:o + 32]))
            S.step("vector", lambda e, o=o: e.tensor_scalar(wrs[:, o + 64:o + 96], wqs[:, o + 96:o + 128], -1.0, None, ALU.mult))
            if not (hq == 9 and kt == KT - 1):
                S.step("vector", lambda e, o=o: e.tensor_copy(wrs[:, o + 96:o + 128], wqs[:, o + 64:o + 96]))
    o9 = (9 * KT + (KT - 1)) * 128
    S.step("vector", lambda e, o9=o9: e.tensor_copy(wrs[:, o9 + 96:o9 + 128], wqs[:, o9 + 64:o9 + 96]),
           incs=[(sRP, 1)])
    nmm = 0
    for tc in range(NT):
        hslot = 0
        S.step("sync",
               lambda e, tc=tc: e.dma_start(
                   out=hr[:, 0:KT * NCK].rearrange("p (kt n) -> p kt n", n=NCK),
                   in_=h1Td[:, tc * NCK:(tc + 1) * NCK].rearrange("(kt p) n -> p kt n", p=128)),
               waits=([(sMM, tc * 12)] if tc >= 1 else []),
               incs=[(sH[0], 16)])
        for hq in range(12):
            ia = (tc * 12 + hq)
            bA = pA[ia % 2]; bB = pB[ia % 2]
            rope = hq < 10
            for kt in range(KT):
                S.step("tensor",
                       lambda e, hq=hq, kt=kt, bA=bA, hslot=hslot: e.matmul(
                           bA[:, :], wqs[:, (hq * KT + kt) * 128:(hq * KT + kt + 1) * 128],
                           hr[:, hslot * KT * NCK + kt * NCK: hslot * KT * NCK + (kt + 1) * NCK],
                           start=(kt == 0), stop=(kt == KT - 1)),
                       waits=([(sH[0], 16 * (tc + 1)), (sRP, 1)] if kt == 0 else []) +
                             ([(sVV, 1 + 3 * (ia - 1))] if (kt == 0 and ia >= 2) else []))
            if rope:
                for kt in range(KT):
                    S.step("tensor",
                           lambda e, hq=hq, kt=kt, bB=bB, hslot=hslot: e.matmul(
                               bB[:, :], wrs[:, (hq * KT + kt) * 128:(hq * KT + kt + 1) * 128],
                               hr[:, hslot * KT * NCK + kt * NCK: hslot * KT * NCK + (kt + 1) * NCK],
                               start=(kt == 0), stop=(kt == KT - 1)),
                           incs=([(sMM, 1)] if kt == KT - 1 else []))
            else:
                S.step("tensor", lambda e, bB=bB: e.matmul(bB[:, 0:1], wqs[:, 0:128], wqs[:, 0:1],
                                                           start=True, stop=True),
                       incs=[(sMM, 1)])
            sl = slice(tc * NCK, (tc + 1) * NCK)
            if rope:
                dst = Qr if hq < 8 else Kr
                do = hq if hq < 8 else (hq - 8)
                S.step("vector",
                       lambda e, bA=bA, sl=sl: e.tensor_mul(t1[:], bA[:, :], cosb[:, sl]),
                       waits=[(sMM, ia + 1), (sVV, None)], incs=[(sVV, 1)])
                S.step("vector",
                       lambda e, bB=bB, sl=sl: e.tensor_mul(t2[:], bB[:, :], sinb[:, sl]),
                       waits=[(sVV, None)], incs=[(sVV, 1)])
                S.step("vector",
                       lambda e, dst=dst, do=do, tc=tc: e.tensor_add(
                           dst[:, do * T + tc * NCK: do * T + (tc + 1) * NCK], t1[:], t2[:]),
                       waits=[(sVV, None)], incs=[(sVV, 1)])
            else:
                do = hq - 10
                S.step("vector",
                       lambda e, bA=bA, do=do, tc=tc: e.tensor_copy(
                           Vb[:, do * T + tc * NCK: do * T + (tc + 1) * NCK], bA[:, :]),
                       waits=[(sMM, ia + 1)], incs=[(sVV, 3)])
            nmm += 1
    S.step("vector", lambda e: e.memset(VT[:], 1.0), waits=[(sVV, None)], incs=[(sVV, 1)])
    ntr = 0
    for kvh in range(4):
        vt = kvh // 2
        pbase = (kvh % 2) * 64
        for kt in range(T // 128):
            S.step("tensor",
                   lambda e, vt=vt, pbase=pbase, kt=kt: e.transpose(
                       pT[:, :], Vb[pbase:pbase + 64, vt * T + kt * 128: vt * T + (kt + 1) * 128],
                       ident[pbase:pbase + 64, 0:64]),
                   waits=[(sVV, None)] + ([(sTRE, ntr)] if ntr >= 1 else []),
                   incs=[(sTR, 1)])
            S.step("vector",
                   lambda e, kvh=kvh, kt=kt: e.tensor_copy(
                       VT[:, (kvh * (T // 128) + kt) * 65:(kvh * (T // 128) + kt) * 65 + 64], pT[:, :]),
                   waits=[(sTR, ntr + 1)], incs=[(sTRE, 1)])
            ntr += 1
    S.step("vector", lambda e: e.tensor_copy(t1[:, 0:1], t1[:, 0:1]), waits=[(sTRE, ntr)], incs=[(sRP, 1)])
    return sRP, 2


def emit_b1_attn(P, S, io, gl, T, qkvsem):
    NT = T // NCK
    NKT = T // 128
    Qr, Kr, VT = io["Qr"], io["Kr"], io["VT"]
    attTd = io["attT"]
    ATTs = P.sbuf("b1a_ATTs", [128, P.T], BF16)
    sATS = S.sem()
    masks = io["masksb"]
    onesr = gl["onesr"]
    onesbc = io["onesbc"]
    onesf = io["onesf"]
    sqq = P.sbuf("b1a_sqq", [128, P.T], BF16)
    hsc = P.sbuf("b1a_hsc", [1, 64], F32)
    nmc = P.sbuf("b1a_nmc", [128, 16], F32)
    Pt = P.sbuf("b1a_pt", [128, 2 * NCK], BF16)
    dnr = P.sbuf("b1a_dnr", [128, NCK], F32)
    bcr = P.sbuf("b1a_bcr", [128, NCK], F32)
    pS = [P.psum(f"b1a_pS{i}", [128, NCK]) for i in range(2)]
    pO = [P.psum(f"b1a_pO{i}", [65, NCK]) for i in range(2)]
    pR = P.psum("b1a_pR", [128, NCK])
    pN = P.psum("b1a_pN", [1, NCK])
    sNB = S.sem(); sNBE = S.sem(); sSB = S.sem(); sSE = S.sem(); sPVk = S.sem()
    sDN = S.sem(); sAA = S.sem(); sVV = S.sem(); sOE = S.sem(); sBC2 = S.sem(); sBCE = S.sem()

    nq = 0
    for src_, nh, base in ((Qr, 16, 0), (Kr, 4, 16)):
        npairs = nh // 2
        for hp in range(npairs):
            S.step("vector",
                   lambda e, src_=src_, hp=hp: e.tensor_mul(sqq[:], src_[:, hp * T:(hp + 1) * T],
                                                            src_[:, hp * T:(hp + 1) * T]),
                   waits=([qkvsem] if (hp == 0 and base == 0) else []) + [(sVV, None)], incs=[(sVV, 1)])
            for sub in range(2):
                h = 2 * hp + sub
                for ck in range(NT):
                    S.step("tensor",
                           lambda e, sub=sub, ck=ck: e.matmul(
                               pN[0:1, :], onesbc[sub * 64:sub * 64 + 64, 0:1],
                               sqq[sub * 64:sub * 64 + 64, ck * NCK:(ck + 1) * NCK],
                               start=True, stop=True),
                           waits=[(sVV, None)] + ([(sNBE, nq)] if nq >= 1 else []),
                           incs=[(sNB, 1)])
                    nq += 1
                    S.step("vector",
                           lambda e, h=h, ck=ck, base=base: e.tensor_reduce(
                               hsc[0:1, (base + h):(base + h + 1)] if ck == 0 else hsc[0:1, 48:49],
                               pN[0:1, :], axis=AX.X, op=ALU.max),
                           waits=[(sNB, nq)], incs=[(sNBE, 1)])
                    if ck > 0:
                        S.step("vector",
                               lambda e, h=h, base=base: e.tensor_tensor(
                                   hsc[0:1, (base + h):(base + h + 1)], hsc[0:1, (base + h):(base + h + 1)],
                                   hsc[0:1, 48:49], ALU.max),
                               waits=[(sNBE, None)], incs=[(sVV, 1)])
    S.step("vector", lambda e: e.tensor_copy(hsc[0:1, 48:49], hsc[0:1, 48:49]),
           waits=[(sNBE, nq), (sVV, None)], incs=[(sVV, 1)])
    S.step("scalar", lambda e: e.activation(hsc[0:1, 20:40], hsc[0:1, 0:20], AF.Ln),
           waits=[(sVV, None)], incs=[(sAA, 1)])
    S.step("scalar", lambda e: e.activation(hsc[0:1, 0:20], hsc[0:1, 20:40], AF.Exp, scale=0.5),
           waits=[sAA], incs=[(sDN, 1)])
    for h in range(16):
        S.step("vector",
               lambda e, h=h: e.tensor_mul(hsc[0:1, 32 + h:33 + h], hsc[0:1, h:h + 1],
                                           hsc[0:1, 16 + h // 4:17 + h // 4]),
               waits=[(sDN, 1)] if h == 0 else [], incs=([(sVV, 1)] if h == 15 else []))
    S.step("vector", lambda e: e.tensor_scalar(hsc[0:1, 32:48], hsc[0:1, 32:48], -1.0, None, ALU.mult),
           waits=[(sVV, None)], incs=[(sVV, 1)])
    for h in range(16):
        S.step("tensor",
               lambda e, h=h: e.matmul(pR[:, 0:1], onesr[:], hsc[0:1, 32 + h:33 + h], start=True, stop=True),
               waits=[(sVV, None)] + ([(sBCE, h)] if h >= 1 else []),
               incs=[(sNB, 1)])
        S.step("vector", lambda e, h=h: e.tensor_copy(nmc[:, h:h + 1], pR[:, 0:1]),
               waits=[(sNB, None)], incs=[(sBCE, 1)])

    nse = 0
    npvk = 0
    nob = 0
    for qh in range(16):
        kvh = qh // 4
        vt = kvh // 2
        kbase = (kvh % 2) * 64
        qt = (qh % 4) + 4 * (qh // 8)
        qbase = ((qh // 4) % 2) * 64
        for tc in range(NT):
            nkt = 4 * (tc + 1)
            iob = qh * NT + tc
            ob = pO[iob % 2]
            for kt in range(nkt):
                sb = pS[nse % 2]
                S.step("tensor",
                       lambda e, kt=kt, tc=tc, sb=sb, qt=qt, qbase=qbase, vt=vt, kbase=kbase: e.matmul(
                           sb[:, :], Kr[kbase:kbase + 64, vt * T + kt * 128: vt * T + (kt + 1) * 128],
                           Qr[qbase:qbase + 64, qt * T + tc * NCK: qt * T + (tc + 1) * NCK],
                           start=True, stop=True),
                       waits=([(sBCE, 16)] if nse == 0 else []) + ([(sSE, nse - 1)] if nse >= 2 else []),
                       incs=[(sSB, 1)])
                if kt >= 4 * tc:
                    S.step("vector",
                           lambda e, sb=sb, kt=kt, tc=tc: e.tensor_add(
                               sb[:, :], sb[:, :], masks[:, (kt - 4 * tc) * NCK:(kt - 4 * tc + 1) * NCK]),
                           waits=[(sSB, nse + 1)], incs=[(sVV, 1)])
                    ewaits = [(sVV, None)]
                else:
                    ewaits = [(sSB, nse + 1)]
                S.step("scalar",
                       lambda e, sb=sb, qh=qh, i=nse: e.activation(
                           Pt[:, (i % 2) * NCK:(i % 2 + 1) * NCK], sb[:, :], AF.Exp, bias=nmc[:, qh:qh + 1]),
                       waits=ewaits + ([(sPVk, npvk - 1)] if npvk >= 2 else []),
                       incs=[(sSE, 1)])
                S.step("tensor",
                       lambda e, ob=ob, kvh=kvh, kt=kt, i=nse, nkt=nkt: e.matmul(
                           ob[:, :], VT[:, (kvh * NKT + kt) * 65:(kvh * NKT + kt + 1) * 65],
                           Pt[:, (i % 2) * NCK:(i % 2 + 1) * NCK],
                           start=(kt == 0), stop=(kt == nkt - 1)),
                       waits=[(sSE, nse + 1)] + ([(sOE, iob - 1)] if (kt == 0 and iob >= 2) else []),
                       incs=[(sPVk, 1)])
                nse += 1
                npvk += 1
            S.step("scalar", lambda e, ob=ob: e.activation(dnr[64:65, :], ob[64:65, :], AF.Ln),
                   waits=[(sPVk, npvk)], incs=[(sAA, 1)])
            S.step("scalar", lambda e: e.activation(bcr[64:65, :], dnr[64:65, :], AF.Exp, scale=-1.0),
                   waits=[sAA], incs=[(sDN, 1)])
            S.step("tensor",
                   lambda e: e.matmul(pR[0:64, :], onesf[64:65, 0:64], bcr[64:65, :], start=True, stop=True),
                   waits=[(sDN, None)] + ([(sBC2, iob)] if iob >= 1 else []),
                   incs=[(sNB, 1)])
            S.step("vector", lambda e: e.tensor_copy(bcr[0:64, :], pR[0:64, :]),
                   waits=[(sNB, None)], incs=[(sBC2, 1)])
            S.step("vector",
                   lambda e, ob=ob, tc=tc: e.tensor_mul(
                       ATTs[0:64, tc * NCK:(tc + 1) * NCK], ob[0:64, :], bcr[0:64, :]),
                   waits=[(sBC2, None)] + ([(sATS, 16 * qh)] if (tc == 0 and qh >= 1) else []),
                   incs=[(sOE, 1)])
            nob += 1
        S.step("sync",
               lambda e, qt=qt, qbase=qbase: e.dma_start(
                   out=attTd[qt * 128 + qbase: qt * 128 + qbase + 64, :], in_=ATTs[0:64, :]),
               waits=[(sOE, nob)], incs=[(sATS, 16)])
    S.step("sync", lambda e: None, waits=[(sATS, 16 * 16)])
    return sATS, 16 * 16


def emit_b1_wo(P, S, io, gl, T, attsem):
    NT = T // NCK
    attTd = io["attT"]; wod = io["wo_t"]; yTd = io["yT"]; x2Td = io["x2T"]
    wos = P.sbuf("b1w_wos", [128, KT * KT * 128], BF16)
    atr = P.sbuf("b1w_atr", [128, KT * NCK], BF16)
    yr = P.sbuf("b1w_yr", [128, KT * NCK], F32)
    x2s = P.sbuf("b1w_x2s", [128, KT * NCK], F32)
    pW = [P.psum(f"b1w_p{i}", [128, NCK]) for i in range(2)]
    sWL = S.sem(); sAT = S.sem(); sY = S.sem(); sMM = S.sem(); sVV = S.sem(); sST = S.sem()
    S.step("gpsimd", lambda e: e.dma_start(
               out=wos[:].rearrange("p (cg kt j) -> p cg kt j", kt=KT, j=128),
               in_=wod.rearrange("cg p kt j -> p cg kt j")),
           waits=[attsem], incs=[(sWL, 16)])
    for tc in range(NT):
        S.step("sync",
               lambda e, tc=tc: e.dma_start(
                   out=atr[:].rearrange("p (kt n) -> p kt n", n=NCK),
                   in_=attTd[:, tc * NCK:(tc + 1) * NCK].rearrange("(kt p) n -> p kt n", p=128)),
               waits=([(sMM, tc * NCG)] if tc >= 1 else []) + ([attsem] if tc == 0 else []),
               incs=[(sAT, 16)])
        S.step("sync",
               lambda e, tc=tc: e.dma_start(
                   out=yr[:].rearrange("p (cg n) -> p cg n", n=NCK),
                   in_=yTd[:, tc * NCK:(tc + 1) * NCK].rearrange("(cg p) n -> p cg n", p=128)),
               waits=([(sVV, tc * NCG)] if tc >= 1 else []),
               incs=[(sY, 16)])
        for cg in range(NCG):
            j = tc * NCG + cg
            b = pW[j % 2]
            for kt in range(KT):
                S.step("tensor",
                       lambda e, cg=cg, kt=kt, b=b: e.matmul(
                           b[:, :], wos[:, (cg * KT + kt) * 128:(cg * KT + kt + 1) * 128],
                           atr[:, kt * NCK:(kt + 1) * NCK],
                           start=(kt == 0), stop=(kt == KT - 1)),
                       waits=([(sWL, 16), (sAT, 16 * (tc + 1))] if kt == 0 and cg == 0 else []) +
                             ([(sVV, j - 1)] if (kt == 0 and j >= 2) else []),
                       incs=([(sMM, 1)] if kt == KT - 1 else []))
            S.step("vector", lambda e, cg=cg, b=b: e.tensor_add(
                       x2s[:, cg * NCK:(cg + 1) * NCK], b[:, :], yr[:, cg * NCK:(cg + 1) * NCK]),
                   waits=[(sMM, j + 1), (sY, 16 * (tc + 1))] +
                         ([(sST, 16 * tc)] if (cg == 0 and tc >= 1) else []),
                   incs=[(sVV, 1)])
        S.step("sync",
               lambda e, tc=tc: e.dma_start(
                   out=x2Td[:, tc * NCK:(tc + 1) * NCK].rearrange("(cg p) n -> p cg n", p=128),
                   in_=x2s[:].rearrange("p (cg n) -> p cg n", n=NCK)),
               waits=[(sVV, (tc + 1) * NCG)],
               incs=[(sST, 16)])
    S.step("sync", lambda e: None, waits=[(sST, 16 * NT)])
    return sST, 16 * NT


def build_b1(T):
    P = Prog("b1")
    P.T = T
    nc = P.nc
    NKT = T // 128
    io = {
        "yT": nc.dram_tensor("yT", [C, T], F32, kind="ExternalInput").ap(),
        "x2T": nc.dram_tensor("x2T", [C, T], F32, kind="ExternalOutput").ap(),
        "wqkv_t": nc.dram_tensor("wqkv_t", [12, 128, 8, 128], F32, kind="ExternalInput").ap(),
        "wo_t": nc.dram_tensor("wo_t", [8, 128, 8, 128], F32, kind="ExternalInput").ap(),
        "lnw": nc.dram_tensor("lnw", [128, 8], F32, kind="ExternalInput").ap(),
        "cosb_d": nc.dram_tensor("cosb_d", [128, T], F32, kind="ExternalInput").ap(),
        "sinb_d": nc.dram_tensor("sinb_d", [128, T], F32, kind="ExternalInput").ap(),
        "masks_d": nc.dram_tensor("masks_d", [128, 4 * NCK], F32, kind="ExternalInput").ap(),
        "ident_d": nc.dram_tensor("ident_d", [128, 128], F32, kind="ExternalInput").ap(),
        "h1T": nc.dram_tensor("h1T", [C, T], BF16, kind="Internal").ap(),
        "attT": nc.dram_tensor("attT", [C, T], BF16, kind="Internal").ap(),
    }
    with P.es:
        P.sem_es = P.es
        gl = {
            "epsc": P.sbuf("g_eps", [1, 1], F32),
            "onecf": P.sbuf("g_one", [128, 1], F32),
            "onesr": P.sbuf("g_onesr", [1, 128], F32),
        }
        P.gl = gl
        mrow_d = P.sbuf("g_mrow", [1, 1], F32)
        rrow_g = P.sbuf("g_rrow", [1, T], F32)
        lnwc = P.sbuf("g_lnw", [128, 8], F32)
        S = Script(P)
        sLD0 = S.sem()
        S.step("vector", lambda e: e.memset(gl["epsc"][:], EPS_LN))
        S.step("vector", lambda e: e.memset(gl["onecf"][:], 1.0))
        S.step("vector", lambda e: e.memset(gl["onesr"][:], 1.0))
        S.step("sync", lambda e: e.dma_start(out=lnwc[:], in_=io["lnw"]), incs=[(sLD0, 16)])
        block = P.es.enter_context(nc.Block())
        with ExitStack() as sc0:
            P_es_old = P.es; P.es = sc0
            rowbufs = (mrow_d, rrow_g, P.sbuf("g_trow", [1, T // 2], F32))
            rowsem = emit_rows(P, S, "b1r", io["yT"], T, rms=True, rowbufs=rowbufs)
            P.es = P_es_old
        io2 = dict(io)
        io2["lnw_col"] = lnwc
        with ExitStack() as sc1:
            P_es_old = P.es; P.es = sc1
            sh1, nh1 = emit_b1_h1(P, S, io2, gl, T, rowbufs, rowsem, lnwsem=(sLD0, 16))
            P.es = P_es_old
        with ExitStack() as scQKA:
            P_es_qka = P.es; P.es = scQKA
            Qr = P.sbuf("b1_Qr", [128, 8 * T], BF16)
            Kr = P.sbuf("b1_Kr", [128, 2 * T], BF16)
            io2["Qr"] = Qr; io2["Kr"] = Kr
            VT = P.sbuf("b1_VT", [128, 4 * NKT * 65], BF16)
            io2["VT"] = VT
            with ExitStack() as sc2:
                P_es_old = P.es; P.es = sc2
                cosb = P.sbuf("b1_cosb", [128, T], BF16)
                sinb = P.sbuf("b1_sinb", [128, T], BF16)
                identb = P.sbuf("b1_identb", [128, 128], BF16)
                io2["cosb"] = cosb; io2["sinb"] = sinb; io2["identb"] = identb
                sC = S.sem()
                S.step("gpsimd", lambda e: e.dma_start(out=cosb[:], in_=io["cosb_d"]),
                       waits=[(sh1, nh1)], incs=[(sC, 16)])
                S.step("gpsimd", lambda e: e.dma_start(out=sinb[:], in_=io["sinb_d"]), incs=[(sC, 16)])
                S.step("gpsimd", lambda e: e.dma_start(out=identb[:], in_=io["ident_d"]), incs=[(sC, 16)])
                Vb = P.sbuf("b1_Vb", [128, 2 * T], BF16)
                io2["Vb"] = Vb
                qkvsem = emit_b1_qkv(P, S, io2, gl, T, waits0=[(sh1, nh1), (sC, 48)])
                P.es = P_es_old
            with ExitStack() as sc3:
                P_es_old = P.es; P.es = sc3
                masksb = P.sbuf("b1_masks", [128, 4 * NCK], F32)
                onesbc = P.sbuf("b1_onesbc", [128, 1], BF16)
                onesf = P.sbuf("b1_onesf", [128, 64], F32)
                io2["masksb"] = masksb
                io2["onesbc"] = onesbc
                io2["onesf"] = onesf
                sM = S.sem()
                S.step("sync", lambda e: e.dma_start(out=masksb[:], in_=io["masks_d"]), waits=[qkvsem],
                       incs=[(sM, 16)])
                S.step("vector", lambda e: e.memset(onesbc[:], 1.0), waits=[(sM, 16), qkvsem])
                S.step("vector", lambda e: e.memset(onesf[:], 1.0))
                attsem = emit_b1_attn(P, S, io2, gl, T, qkvsem)
                P.es = P_es_old
            P.es = P_es_qka
        with ExitStack() as sc4:
            P_es_old = P.es; P.es = sc4
            emit_b1_wo(P, S, io2, gl, T, attsem)
            P.es = P_es_old
        S.replay(block)
    return nc


_HEADLIST = [0, 4, 1, 5, 2, 6, 3, 7, 8, 12, 9, 13, 10, 14, 11, 15]

def prep_b1(inputs, T):
    g = {}
    wqkv = np.asarray(inputs["L0_ffn_wqkv"])
    wo = np.asarray(inputs["L0_ffn_wo"])
    qcols = np.concatenate([np.arange(h * 64, (h + 1) * 64) for h in _HEADLIST] +
                           [np.arange(1024, 1536)])
    wqkvT = np.ascontiguousarray(wqkv.T[:, qcols])
    g["wqkv_t"] = _tile_colblocks(wqkvT)
    orows = np.concatenate([np.arange(h * 64, (h + 1) * 64) for h in _HEADLIST])
    woT = np.ascontiguousarray(wo.T[orows, :])
    g["wo_t"] = _tile_colblocks(woT)
    g["lnw"] = _colmajor_vec(inputs["L0_ffn_ln1_w"])
    hd = 64
    inv = 1.0 / (10000.0 ** (np.arange(0, hd, 2) / hd))
    t = np.arange(T)[:, None]
    fr = t * inv[None, :]
    emb = np.concatenate([fr, fr], axis=1)
    cos = np.cos(emb).T.astype(np.float32)
    sin = np.sin(emb).T.astype(np.float32)
    g["cosb_d"] = np.ascontiguousarray(np.vstack([cos, cos]))
    g["sinb_d"] = np.ascontiguousarray(np.vstack([sin, sin]))
    i_ = np.arange(128)[:, None]
    j_ = np.arange(NCK)[None, :]
    masks = np.concatenate([np.where(128 * o + i_ <= j_, 0.0, -1e30) for o in range(4)],
                           axis=1).astype(np.float32)
    g["masks_d"] = np.ascontiguousarray(masks)
    g["ident_d"] = np.ascontiguousarray(
        np.hstack([np.tile(np.eye(64, dtype=np.float32), (2, 1)), np.zeros((128, 64), np.float32)]))
    return g



def emit_b2_gates(P, S, io, gl, T, waits0=()):
    NT = T // NCK
    h2Td = io["h2T"]; gwd = io["gw_t"]; identf = io["identf"]
    g8all = io["g8all"]
    onesr = gl["onesr"]
    gws = P.sbuf("b2g_gws", [128, 64], BF16)
    hr = P.sbuf("b2g_hr", [128, KT * NCK], BF16)
    lg8 = P.sbuf("b2g_lg8", [8, NCK], F32)
    ltt = P.sbuf("b2g_ltt", [128, 8], F32)
    gtt = P.sbuf("b2g_gtt", [128, 8], F32)
    tf1 = P.sbuf("b2g_tf1", [128, 8], F32)
    c0 = P.sbuf("b2g_c0", [128, 4], F32)
    pL = P.psum("b2g_pL", [8, NCK])
    pT1 = P.psum("b2g_pT1", [128, 8])
    pT2 = P.psum("b2g_pT2", [8, 128])
    sW = S.sem(); sH = S.sem(); sMM = S.sem(); sLE = S.sem(); sT1 = S.sem(); sT1E = S.sem()
    sT2 = S.sem(); sT2E = S.sem(); sVV = S.sem(); sAA = S.sem()
    S.step("gpsimd", lambda e: e.dma_start(out=gws[:], in_=gwd), waits=list(waits0), incs=[(sW, 16)])
    nt1 = 0
    for tc in range(NT):
        S.step("sync",
               lambda e, tc=tc: e.dma_start(
                   out=hr[:].rearrange("p (kt n) -> p kt n", n=NCK),
                   in_=h2Td[:, tc * NCK:(tc + 1) * NCK].rearrange("(kt p) n -> p kt n", p=128)),
               waits=([(sMM, tc)] if tc >= 1 else []) + (list(waits0) if tc == 0 else []),
               incs=[(sH, 16)])
        for kt in range(KT):
            S.step("tensor",
                   lambda e, kt=kt: e.matmul(pL[:, :], gws[:, kt * 8:(kt + 1) * 8],
                                             hr[:, kt * NCK:(kt + 1) * NCK],
                                             start=(kt == 0), stop=(kt == KT - 1)),
                   waits=([(sW, 16), (sH, 16 * (tc + 1))] if kt == 0 else []) +
                         ([(sLE, tc)] if (kt == 0 and tc >= 1) else []),
                   incs=([(sMM, 1)] if kt == KT - 1 else []))
        S.step("vector", lambda e: e.tensor_copy(lg8[:, :], pL[:, :]),
               waits=[(sMM, tc + 1), (sVV, None)], incs=[(sLE, 1)])
        for tt in range(NCK // 128):
            gt = tc * (NCK // 128) + tt
            S.step("tensor",
                   lambda e, tt=tt: e.transpose(pT1[:, :], lg8[:, tt * 128:(tt + 1) * 128], identf[0:8, 0:8]),
                   waits=[(sLE, tc + 1)] + ([(sT1E, nt1)] if nt1 >= 1 else []),
                   incs=[(sT1, 1)])
            nt1 += 1
            S.step("vector", lambda e: e.tensor_copy(ltt[:, :], pT1[:, :]),
                   waits=[(sT1, nt1)], incs=[(sT1E, 1)])
            S.step("vector", lambda e: e.tensor_reduce(c0[:, 0:1], ltt[:, :], axis=AX.X, op=ALU.max),
                   waits=[(sT1E, None)], incs=[(sVV, 1)])
            S.step("vector", lambda e: e.tensor_scalar(tf1[:, :], ltt[:, :], c0[:, 0:1], None, ALU.is_ge),
                   waits=[(sVV, None)], incs=[(sVV, 1)])
            S.step("vector", lambda e: e.scalar_tensor_tensor(tf1[:, :], tf1[:, :], -1e30, ltt[:, :],
                                                              ALU.mult, ALU.add),
                   waits=[(sVV, None)], incs=[(sVV, 1)])
            S.step("vector", lambda e: e.tensor_reduce(c0[:, 1:2], tf1[:, :], axis=AX.X, op=ALU.max),
                   waits=[(sVV, None)], incs=[(sVV, 1)])
            S.step("vector", lambda e: e.tensor_scalar(c0[:, 2:3], c0[:, 0:1], -1.0, None, ALU.mult),
                   waits=[(sVV, None)], incs=[(sVV, 1)])
            S.step("scalar", lambda e: e.activation(gtt[:, :], ltt[:, :], AF.Exp, bias=c0[:, 2:3]),
                   waits=[(sVV, None)], incs=[(sAA, 1)])
            S.step("vector", lambda e: e.tensor_scalar(tf1[:, :], ltt[:, :], c0[:, 1:2], None, ALU.is_ge),
                   waits=[(sAA, None)], incs=[(sVV, 1)])
            S.step("vector", lambda e: e.tensor_mul(gtt[:, :], gtt[:, :], tf1[:, :]),
                   waits=[(sVV, None)], incs=[(sVV, 1)])
            S.step("vector", lambda e: e.tensor_reduce(c0[:, 3:4], gtt[:, :], axis=AX.X, op=ALU.add),
                   waits=[(sVV, None)], incs=[(sVV, 1)])
            S.step("vector", lambda e: e.reciprocal(c0[:, 2:3], c0[:, 3:4]),
                   waits=[(sVV, None)], incs=[(sVV, 1)])
            S.step("vector", lambda e: e.tensor_scalar(gtt[:, :], gtt[:, :], c0[:, 2:3], None, ALU.mult),
                   waits=[(sVV, None)], incs=[(sVV, 1)])
            S.step("tensor",
                   lambda e: e.transpose(pT2[:, :], gtt[:, :], identf[:, :]),
                   waits=[(sVV, None)] + ([(sT2E, S.n(sT2E))] if S.n(sT2) >= 1 else []),
                   incs=[(sT2, 1)])
            S.step("vector", lambda e, gt=gt: e.tensor_copy(g8all[0:8, gt * 128:(gt + 1) * 128], pT2[:, :]),
                   waits=[(sT2, None)], incs=[(sT2E, 1)])
    S.step("vector", lambda e: e.tensor_copy(c0[:, 0:1], c0[:, 0:1]), waits=[(sT2E, None)],
           incs=[(sVV, 1)])
    return sVV, S.n(sVV)


def emit_b2_experts(P, S, io, gl, T, gatesem, waits0=()):
    NT = T // NCK
    h2Td = io["h2T"]; w13d = io["w13_t"]; w2d = io["w2_t"]
    yTd = io["yT"]; x2Td = io["x2T"]; accTd = io["accT"]
    g8all = io["g8all"]; oh8 = io["oh8"]
    onesr = gl["onesr"]
    w13s = P.sbuf("b2e_w13", [128, KT * 32 * 128], BF16)
    w2s = P.sbuf("b2e_w2", [128, 16 * KT * 128], BF16)
    hr = P.sbuf("b2e_hr", [128, KT * NCK], BF16)
    gg = P.sbuf("b2e_gg", [128, 16 * NCK], BF16)
    sgt = P.sbuf("b2e_sgt", [128, NCK], F32)
    grs = P.sbuf("b2e_grs", [1, NCK], F32)
    gbs = P.sbuf("b2e_gbs", [128, NCK], F32)
    accs = P.sbuf("b2e_accs", [128, KT * NCK], F32)
    xtr = P.sbuf("b2e_xtr", [128, KT * NCK], F32)
    pG = [P.psum(f"b2e_pG{i}", [128, NCK]) for i in range(4)]
    pD = [P.psum(f"b2e_pD{i}", [128, NCK]) for i in range(2)]
    pE = P.psum("b2e_pE", [1, NCK])
    pB = P.psum("b2e_pB", [128, NCK])
    sW13 = S.sem(); sW2 = S.sem(); sH = S.sem(); sGE = S.sem(); sGB = S.sem(); sGBE = S.sem()
    sG1 = S.sem(); sSI = S.sem(); sGG = S.sem(); sDN = S.sem(); sAC = S.sem(); sACC = S.sem()
    sST = S.sem(); sAA = S.sem(); sVV = S.sem(); sPDE = S.sem()
    nh = 0; ngg = 0; ndn = 0; nac = 0; nst = 0
    for e in range(NCG):
        S.step("gpsimd",
               lambda e_, e=e: e_.dma_start(
                   out=w13s[:].rearrange("p (kt ft j) -> p kt ft j", ft=32, j=128),
                   in_=w13d[e].rearrange("p kt ft j -> p kt ft j")),
               waits=(list(waits0) if e == 0 else []) + ([(sDN, NCG * NT * e)] if e >= 1 else []),
               incs=[(sW13, 16)])
        S.step("gpsimd",
               lambda e_, e=e: e_.dma_start(
                   out=w2s[:].rearrange("p (ft cg j) -> p ft cg j", cg=KT, j=128),
                   in_=w2d[e].rearrange("p ft cg j -> p ft cg j")),
               waits=([(sDN, NCG * NT * e)] if e >= 1 else []),
               incs=[(sW2, 16)])
        for tc in range(NT):
            it = e * NT + tc
            S.step("sync",
                   lambda e_, tc=tc: e_.dma_start(
                       out=hr[:].rearrange("p (kt n) -> p kt n", n=NCK),
                       in_=h2Td[:, tc * NCK:(tc + 1) * NCK].rearrange("(kt p) n -> p kt n", p=128)),
                   waits=([(sG1, 32 * it)] if it >= 1 else []) + ([gatesem] if it == 0 else []),
                   incs=[(sH, 16)])
            nh += 16
            S.step("tensor",
                   lambda e_, e=e, tc=tc: e_.matmul(pE[0:1, :], oh8[0:8, e:e + 1],
                                                    g8all[0:8, tc * NCK:(tc + 1) * NCK],
                                                    start=True, stop=True),
                   waits=[gatesem] + ([(sGE, it)] if it >= 1 else []),
                   incs=[(sGB, 1)])
            S.step("vector", lambda e_: e_.tensor_copy(grs[0:1, :], pE[0:1, :]),
                   waits=[(sGB, None)], incs=[(sGE, 1)])
            S.step("tensor",
                   lambda e_: e_.matmul(pB[:, :], onesr[:], grs[0:1, :], start=True, stop=True),
                   waits=[(sGE, None)] + ([(sGBE, it)] if it >= 1 else []),
                   incs=[(sGB, 1)])
            S.step("vector", lambda e_: e_.tensor_copy(gbs[:, :], pB[:, :]),
                   waits=[(sGB, None)] + ([(sACC, NCG * it)] if it >= 1 else []), incs=[(sGBE, 1)])
            for ft in range(16):
                b1 = pG[(2 * ft) % 4]; b3 = pG[(2 * ft + 1) % 4]
                for kt in range(KT):
                    S.step("tensor",
                           lambda e_, kt=kt, ft=ft, b1=b1: e_.matmul(
                               b1[:, :], w13s[:, (kt * 32 + ft) * 128:(kt * 32 + ft + 1) * 128],
                               hr[:, kt * NCK:(kt + 1) * NCK],
                               start=(kt == 0), stop=(kt == KT - 1)),
                           waits=([(sW13, 16 * (e + 1)), (sH, nh)] if (ft == 0 and kt == 0) else []) +
                                 ([(sSI, 2 * (it * 16 + ft) - 2)] if (kt == 0 and (it * 16 + ft) >= 2) else []),
                           incs=([(sG1, 1)] if kt == KT - 1 else []))
                for kt in range(KT):
                    S.step("tensor",
                           lambda e_, kt=kt, ft=ft, b3=b3: e_.matmul(
                               b3[:, :], w13s[:, (kt * 32 + ft + 16) * 128:(kt * 32 + ft + 16 + 1) * 128],
                               hr[:, kt * NCK:(kt + 1) * NCK],
                               start=(kt == 0), stop=(kt == KT - 1)),
                           incs=([(sG1, 1)] if kt == KT - 1 else []))
                S.step("scalar",
                       lambda e_, b1=b1: e_.activation(sgt[:, :], b1[:, :], AF.Silu),
                       waits=[(sG1, None), (sSI, None)] +
                             ([(sDN, NCG * it), (sACC, NCG * it)] if (ft == 0 and it >= 1) else []),
                       incs=[(sSI, 1)])
                S.step("vector",
                       lambda e_, ft=ft, b3=b3: e_.tensor_mul(gg[:, ft * NCK:(ft + 1) * NCK], sgt[:, :], b3[:, :]),
                       waits=[(sSI, None)], incs=[(sSI, 1)])
                ngg += 1
            S.step("sync",
                   lambda e_, tc=tc, e=e: e_.dma_start(
                       out=xtr[:].rearrange("p (cg n) -> p cg n", n=NCK),
                       in_=(yTd if e == 0 else accTd)[:, tc * NCK:(tc + 1) * NCK].rearrange(
                           "(cg p) n -> p cg n", p=128)),
                   waits=([(sACC, NCG * it)] if it >= 1 else []),
                   incs=[(sAC, 16)])
            nac += 16
            if e == 0:
                S.step("sync",
                       lambda e_, tc=tc: e_.dma_start(
                           out=accs[:].rearrange("p (cg n) -> p cg n", n=NCK),
                           in_=x2Td[:, tc * NCK:(tc + 1) * NCK].rearrange("(cg p) n -> p cg n", p=128)),
                       waits=([(sST, 16 * it)] if it >= 1 else []),
                       incs=[(sAC, 16)])
                nac += 16
            for cg in range(NCG):
                bd = pD[(it * NCG + cg) % 2]
                for ft in range(16):
                    S.step("tensor",
                           lambda e_, ft=ft, cg=cg, bd=bd: e_.matmul(
                               bd[:, :], w2s[:, (ft * KT + cg) * 128:(ft * KT + cg + 1) * 128],
                               gg[:, ft * NCK:(ft + 1) * NCK],
                               start=(ft == 0), stop=(ft == 15)),
                           waits=([(sW2, 16 * (e + 1)), (sSI, 32 * (it + 1))] if (cg == 0 and ft == 0) else []) +
                                 ([(sPDE, it * NCG + cg - 1)] if (ft == 0 and (it * NCG + cg) >= 2) else []),
                           incs=([(sDN, 1)] if ft == 15 else []))
                sl = slice(cg * NCK, (cg + 1) * NCK)
                S.step("vector", lambda e_, sl=sl, bd=bd: e_.tensor_mul(sgt[:, :], bd[:, :], gbs[:, :]),
                       waits=[(sDN, None), (sGBE, None), (sAC, nac), (sACC, None)] +
                             ([(sST, 16 * it)] if (cg == 0 and it >= 1) else []),
                       incs=[(sPDE, 1)])
                if e == 0:
                    S.step("vector", lambda e_, sl=sl: e_.tensor_add(accs[:, sl], accs[:, sl], xtr[:, sl]),
                           waits=[(sPDE, None)], incs=[(sVV, 1)])
                    S.step("vector", lambda e_, sl=sl: e_.tensor_add(accs[:, sl], accs[:, sl], sgt[:, :]),
                           waits=[(sVV, None)], incs=[(sACC, 1)])
                else:
                    S.step("vector", lambda e_, sl=sl: e_.tensor_add(accs[:, sl], xtr[:, sl], sgt[:, :]),
                           waits=[(sPDE, None)], incs=[(sACC, 1)])
            S.step("sync",
                   lambda e_, tc=tc: e_.dma_start(
                       out=accTd[:, tc * NCK:(tc + 1) * NCK].rearrange("(cg p) n -> p cg n", p=128),
                       in_=accs[:].rearrange("p (cg n) -> p cg n", n=NCK)),
                   waits=[(sACC, NCG * (it + 1))],
                   incs=[(sST, 16)])
            nst += 16
    S.step("sync", lambda e: None, waits=[(sST, nst)])
    return sST, nst


def prep_b2(inputs, T):
    g = {}
    gw = np.asarray(inputs["L0_ffn_gate_w"])
    gwT = gw.T.reshape(8, 128, 8).transpose(1, 0, 2).reshape(128, 64)
    g["gw_t"] = np.ascontiguousarray(gwT.astype(np.float32))
    w13 = np.asarray(inputs["L0_ffn_w13"])
    w13T = np.ascontiguousarray(np.transpose(w13, (0, 2, 1)))
    g["w13_t"] = np.ascontiguousarray(
        w13T.reshape(8, 8, 128, 32, 128).transpose(0, 2, 1, 3, 4))
    w2 = np.asarray(inputs["L0_ffn_w2"])
    w2T = np.ascontiguousarray(np.transpose(w2, (0, 2, 1)))
    g["w2_t"] = np.ascontiguousarray(
        w2T.reshape(8, 16, 128, 8, 128).transpose(0, 2, 1, 3, 4))
    g["ln2w"] = _colmajor_vec(inputs["L0_ffn_ln2_w"])
    g["identf_d"] = np.eye(128, dtype=np.float32)
    g["oh8_d"] = np.ascontiguousarray(np.eye(8, dtype=np.float32)[:, :])
    return g


def build_b2(T):
    P = Prog("b2")
    P.T = T
    nc = P.nc
    io = {
        "yT": nc.dram_tensor("yT", [C, T], F32, kind="ExternalInput").ap(),
        "x2T": nc.dram_tensor("x2T", [C, T], F32, kind="ExternalInput").ap(),
        "xBT": nc.dram_tensor("xBT", [C, T], F32, kind="ExternalOutput").ap(),
        "gw_t": nc.dram_tensor("gw_t", [128, 64], F32, kind="ExternalInput").ap(),
        "w13_t": nc.dram_tensor("w13_t", [8, 128, 8, 32, 128], F32, kind="ExternalInput").ap(),
        "w2_t": nc.dram_tensor("w2_t", [8, 128, 16, 8, 128], F32, kind="ExternalInput").ap(),
        "ln2w": nc.dram_tensor("ln2w", [128, 8], F32, kind="ExternalInput").ap(),
        "identf_d": nc.dram_tensor("identf_d", [128, 128], F32, kind="ExternalInput").ap(),
        "oh8_d": nc.dram_tensor("oh8_d", [8, 8], F32, kind="ExternalInput").ap(),
        "h2T": nc.dram_tensor("h2T", [C, T], BF16, kind="Internal").ap(),
        "accT": nc.dram_tensor("accT", [C, T], F32, kind="Internal").ap(),
    }
    with P.es:
        P.sem_es = P.es
        gl = {
            "epsc": P.sbuf("g_eps", [1, 1], F32),
            "onecf": P.sbuf("g_one", [128, 1], F32),
            "onesr": P.sbuf("g_onesr", [1, 128], F32),
        }
        P.gl = gl
        mrow_g = P.sbuf("g_mrow", [1, T], F32)
        rrow_g = P.sbuf("g_rrow", [1, T], F32)
        lnwc = P.sbuf("g_lnw", [128, 8], F32)
        identf = P.sbuf("g_identf", [128, 128], F32)
        oh8 = P.sbuf("g_oh8", [8, 8], BF16)
        g8all = P.sbuf("g_g8all", [8, T], BF16)
        S = Script(P)
        sLD0 = S.sem()
        S.step("vector", lambda e: e.memset(gl["epsc"][:], EPS_LN))
        S.step("vector", lambda e: e.memset(gl["onecf"][:], 1.0))
        S.step("vector", lambda e: e.memset(gl["onesr"][:], 1.0))
        sLD1 = S.sem()
        S.step("sync", lambda e: e.dma_start(out=lnwc[:], in_=io["ln2w"]), incs=[(sLD0, 16)])
        S.step("sync", lambda e: e.dma_start(out=identf[:], in_=io["identf_d"]), incs=[(sLD0, 16)])
        S.step("gpsimd", lambda e: e.dma_start(out=oh8[:], in_=io["oh8_d"]), incs=[(sLD1, 16)])
        block = P.es.enter_context(nc.Block())
        io2 = dict(io)
        io2["identf"] = identf; io2["oh8"] = oh8; io2["g8all"] = g8all
        with ExitStack() as sc0:
            P_es_old = P.es; P.es = sc0
            rowbufs = (mrow_g, rrow_g, P.sbuf("g_trow", [1, T // 2], F32))
            rowsem = emit_rows(P, S, "b2r", io["x2T"], T, rms=True, rowbufs=rowbufs)
            P.es = P_es_old
        with ExitStack() as sc1:
            P_es_old = P.es; P.es = sc1
            io3 = dict(io2)
            io3["yT"] = io["x2T"]; io3["h1T"] = io["h2T"]; io3["lnw_col"] = lnwc
            sh2, nh2 = emit_b1_h1(P, S, io3, gl, T, rowbufs, rowsem, lnwsem=(sLD0, 32))
            P.es = P_es_old
        with ExitStack() as sc2:
            P_es_old = P.es; P.es = sc2
            gatesem = emit_b2_gates(P, S, io2, gl, T, waits0=[(sh2, nh2), (sLD0, 32), (sLD1, 16)])
            P.es = P_es_old
        with ExitStack() as sc3:
            P_es_old = P.es; P.es = sc3
            sae, nae = emit_b2_experts(P, S, io2, gl, T, gatesem, waits0=[gatesem])
            P.es = P_es_old
        with ExitStack() as sc4:
            P_es_old = P.es; P.es = sc4
            rowbufs2 = (mrow_g, rrow_g, P.sbuf("g_trow2", [1, T // 2], F32))
            rowsem2 = emit_rows(P, S, "b2p", io["accT"], T, rms=False, rowbufs=rowbufs2,
                                waits0=[(sae, nae)])
            emit_normalize(P, S, "b2n", io["accT"], io["xBT"], T, rowbufs2, rowsem2,
                           extra_src_waits=[(sae, nae)])
            P.es = P_es_old
        S.replay(block)
    return nc



def emit_c(P, S, io, gl, T, waits0=()):
    NT = T // NCK
    xBTd = io["xBT"]; keyd = io["key_t"]; vpd = io["vp_t"]; sCd = io["sCT"]
    onesr = gl["onesr"]
    keys = P.sbuf("c_keys", [128, 32 * KT * 128], BF16)
    vpr = P.sbuf("c_vpr", [128, 2 * KT * 128], BF16)
    onesb = P.sbuf("c_onesb", [128, 1], BF16)
    xbr = P.sbuf("c_xbr", [128, KT * NCK], BF16)
    xfr = P.sbuf("c_xfr", [128, KT * NCK], F32)
    gck = P.sbuf("c_gck", [128, 32 * NCK], BF16)
    gsq = P.sbuf("c_gsq", [128, NCK], BF16)
    srow = P.sbuf("c_srow", [1, NCK], F32)
    sclb = P.sbuf("c_sclb", [128, NCK], F32)
    s5s = P.sbuf("c_s5s", [128, KT * NCK], F32)
    c64 = P.sbuf("c_c64", [1, 1], F32)
    PP = [P.psum(f"c_pp{i}", [128, NCK]) for i in range(8)]
    sWL = S.sem(); sVP = [S.sem(), S.sem()]; sX = S.sem(); sXF = S.sem()
    sMM = S.sem(); sGE = S.sem(); sSQ = S.sem()
    sNR = S.sem(); sNRE = S.sem(); sAA = S.sem(); sBC = S.sem(); sBCE = S.sem()
    sOM = S.sem(); sOE = S.sem(); sST = S.sem(); sVV = S.sem()
    S.step("vector", lambda e: e.memset(onesb[:], 1.0), incs=[(sVV, 1)])
    S.step("vector", lambda e: e.memset(c64[:], 4.1588830833596715), incs=[(sVV, 1)])
    S.step("gpsimd", lambda e: e.dma_start(
               out=keys[:].rearrange("p (sg kt j) -> p sg kt j", kt=KT, j=128),
               in_=keyd.rearrange("sg p kt j -> p sg kt j")),
           waits=list(waits0), incs=[(sWL, 16)])
    nvp = [0, 0]
    for lc in range(NT):
        S.step("gpsimd",
               lambda e, lc=lc: e.dma_start(
                   out=xbr[:].rearrange("p (kt n) -> p kt n", n=NCK),
                   in_=xBTd[:, lc * NCK:(lc + 1) * NCK].rearrange("(kt p) n -> p kt n", p=128)),
               waits=([(sMM, 32 * lc)] if lc >= 1 else []) + (list(waits0) if lc == 0 else []),
               incs=[(sX, 16)])
        S.step("sync",
               lambda e, lc=lc: e.dma_start(
                   out=xfr[:].rearrange("p (cg n) -> p cg n", n=NCK),
                   in_=xBTd[:, lc * NCK:(lc + 1) * NCK].rearrange("(cg p) n -> p cg n", p=128)),
               waits=([(sVV, 2 + NCG * lc)] if lc >= 1 else []),
               incs=[(sXF, 16)])
        for sg in range(32):
            i = lc * 32 + sg
            sb = PP[i % 2]
            for kt in range(KT):
                S.step("tensor",
                       lambda e, sg=sg, kt=kt, sb=sb: e.matmul(
                           sb[:, :], keys[:, (sg * KT + kt) * 128:(sg * KT + kt + 1) * 128],
                           xbr[:, kt * NCK:(kt + 1) * NCK],
                           start=(kt == 0), stop=(kt == KT - 1)),
                       waits=([(sWL, 16), (sX, 16 * (lc + 1))] if (sg == 0 and kt == 0) else []) +
                             ([(sOE, NCG * lc)] if (sg <= 1 and kt == 0 and lc >= 1) else []) +
                             ([(sGE, i - 1)] if (kt == 0 and i >= 2) else []),
                       incs=([(sMM, 1)] if kt == KT - 1 else []))
            S.step("scalar",
                   lambda e, sg=sg, sb=sb: e.activation(gck[:, sg * NCK:(sg + 1) * NCK], sb[:, :], AF.Gelu),
                   waits=[(sMM, i + 1)],
                   incs=[(sGE, 1)])
            S.step("vector",
                   lambda e, sg=sg: e.tensor_mul(gsq[:, :], gck[:, sg * NCK:(sg + 1) * NCK],
                                                 gck[:, sg * NCK:(sg + 1) * NCK]),
                   waits=[(sGE, None)], incs=[(sSQ, 1)])
            S.step("tensor",
                   lambda e, sg=sg, lc=lc: e.matmul(PP[2][0:1, :], onesb[:], gsq[:, :],
                                                    start=(sg == 0), stop=(sg == 31)),
                   waits=[(sSQ, i + 1)] + ([(sVV, 1)] if (sg == 0 and lc == 0) else []) +
                         ([(sNRE, lc)] if (sg == 0 and lc >= 1) else []) +
                         ([(sOE, NCG * lc)] if (sg == 0 and lc >= 1) else []),
                   incs=([(sNR, 1)] if sg == 31 else []))
        S.step("vector", lambda e: e.tensor_copy(srow[0:1, :], PP[2][0:1, :]),
               waits=[(sNR, lc + 1)], incs=[(sNRE, 1)])
        S.step("scalar", lambda e: e.activation(srow[0:1, :], srow[0:1, :], AF.Ln),
               waits=[(sNRE, None)], incs=[(sAA, 1)])
        S.step("scalar", lambda e: e.activation(srow[0:1, :], srow[0:1, :], AF.Exp, scale=-0.5,
                                                bias=c64[0:1, 0:1]),
               waits=[(sAA, None)], incs=[(sAA, 1)])
        S.step("tensor",
               lambda e: e.matmul(PP[3][:, :], onesr[:], srow[0:1, :], start=True, stop=True),
               waits=[(sAA, None), (sVV, 2)] + ([(sBCE, lc), (sOE, NCG * lc)] if lc >= 1 else []),
               incs=[(sBC, 1)])
        S.step("vector", lambda e: e.tensor_copy(sclb[:, :], PP[3][:, :]),
               waits=[(sBC, None)] + ([(sVV, 2 + NCG * lc)] if lc >= 1 else []),
               incs=[(sBCE, 1)])
        for sg in range(32):
            slot = sg % 2
            S.step("gpsimd",
                   lambda e, sg=sg, slot=slot, lc=lc: e.dma_start(
                       out=vpr[:, slot * KT * 128:(slot + 1) * KT * 128].rearrange(
                           "p (vg j) -> p vg j", j=128),
                       in_=vpd[sg].rearrange("p vg j -> p vg j")),
                   waits=([(sOM, 32 * lc + sg - 1)] if (lc * 32 + sg) >= 2 else []) +
                         (list(waits0) if (lc == 0 and sg == 0) else []),
                   incs=[(sVP[slot], 16)])
            nvp[slot] += 16
            for vg in range(NCG):
                S.step("tensor",
                       lambda e, sg=sg, vg=vg, slot=slot: e.matmul(
                           PP[vg][:, :], vpr[:, (slot * KT + vg) * 128:(slot * KT + vg + 1) * 128],
                           gck[:, sg * NCK:(sg + 1) * NCK],
                           start=(sg == 0), stop=(sg == 31)),
                       waits=([(sVP[slot], nvp[slot])] if vg == 0 else []) +
                             ([(sGE, 32 * (lc + 1)), (sBCE, lc + 1)] if (sg == 0 and vg == 0) else []) +
                             ([(sOE, NCG * (lc - 1) + vg + 1)] if (sg == 0 and vg >= 2 and lc >= 1) else []),
                       incs=([(sOM, 1)] if vg == NCG - 1 else []))
        for vg in range(NCG):
            sl = slice(vg * NCK, (vg + 1) * NCK)
            S.step("vector", lambda e, sl=sl, vg=vg: e.tensor_mul(s5s[:, sl], PP[vg][:, :], sclb[:, :]),
                   waits=[(sOM, 32 * (lc + 1)), (sXF, 16 * (lc + 1))] +
                         ([(sST, 16 * lc)] if (vg == 0 and lc >= 1) else []),
                   incs=[(sOE, 1)])
            S.step("vector", lambda e, sl=sl: e.tensor_add(s5s[:, sl], s5s[:, sl], xfr[:, sl]),
                   waits=[(sOE, None)], incs=[(sVV, 1)])
        S.step("sync",
               lambda e, lc=lc: e.dma_start(
                   out=sCd[:, lc * NCK:(lc + 1) * NCK].rearrange("(cg p) n -> p cg n", p=128),
                   in_=s5s[:].rearrange("p (cg n) -> p cg n", n=NCK)),
               waits=[(sVV, 2 + NCG * (lc + 1))],
               incs=[(sST, 16)])
    S.step("sync", lambda e: None, waits=[(sST, 16 * NT)])
    return sST, 16 * NT


def build_c(T):
    P = Prog("c")
    P.T = T
    nc = P.nc
    io = {
        "xBT": nc.dram_tensor("xBT", [C, T], F32, kind="ExternalInput").ap(),
        "xCT": nc.dram_tensor("xCT", [C, T], F32, kind="ExternalOutput").ap(),
        "key_t": nc.dram_tensor("key_t", [32, 128, 8, 128], F32, kind="ExternalInput").ap(),
        "vp_t": nc.dram_tensor("vp_t", [32, 128, 8, 128], F32, kind="ExternalInput").ap(),
        "sCT": nc.dram_tensor("sCT", [C, T], F32, kind="Internal").ap(),
    }
    with P.es:
        P.sem_es = P.es
        gl = {
            "epsc": P.sbuf("g_eps", [1, 1], F32),
            "onecf": P.sbuf("g_one", [128, 1], F32),
            "onesr": P.sbuf("g_onesr", [1, 128], F32),
        }
        P.gl = gl
        mrow_g = P.sbuf("g_mrow", [1, T], F32)
        rrow_g = P.sbuf("g_rrow", [1, T], F32)
        S = Script(P)
        S.step("vector", lambda e: e.memset(gl["epsc"][:], EPS_LN))
        S.step("vector", lambda e: e.memset(gl["onecf"][:], 1.0))
        S.step("vector", lambda e: e.memset(gl["onesr"][:], 1.0))
        block = P.es.enter_context(nc.Block())
        with ExitStack() as sc0:
            P_es_old = P.es; P.es = sc0
            sc_, nc_ = emit_c(P, S, io, gl, T)
            P.es = P_es_old
        with ExitStack() as sc1:
            P_es_old = P.es; P.es = sc1
            rowbufs = (mrow_g, rrow_g, P.sbuf("g_trow", [1, T // 2], F32))
            rowsem = emit_rows(P, S, "cp", io["sCT"], T, rms=False, rowbufs=rowbufs,
                               waits0=[(sc_, nc_)])
            emit_normalize(P, S, "cn", io["sCT"], io["xCT"], T, rowbufs, rowsem,
                           extra_src_waits=[(sc_, nc_)])
            P.es = P_es_old
        S.replay(block)
    return nc


def prep_c(inputs, T):
    g = {}
    g["key_t"] = _tile_colblocks(np.ascontiguousarray(np.asarray(inputs["L1_attn_key_param_tokens"]).T))
    vp = np.asarray(inputs["L1_attn_value_param_tokens"])
    g["vp_t"] = np.ascontiguousarray(vp.reshape(32, 128, 8, 128))
    return g



def emit_d(P, S, io, gl, T, waits0=()):
    NT = T // NCK
    xCTd = io["xCT"]; wd = io["l1w_t"]; s6d = io["s6T"]
    alphad = io["alpha_d"]; biasd = io["bias_col"]
    onesr = gl["onesr"]
    wfs = P.sbuf("d_wfs", [128, KT * KT * 128], F32)
    wts = P.sbuf("d_wts", [128, KT * KT * 128], BF16)
    wab = P.sbuf("d_wab", [128, 128], F32)
    onesf = P.sbuf("d_onesf", [128, 1], F32)
    scl = P.sbuf("d_scl", [128, 8], F32)
    srow = P.sbuf("d_srow", [1, 8], F32)
    acol = P.sbuf("d_acol", [128, 1], F32)
    bcol = P.sbuf("d_bcol", [128, 8], F32)
    alr = P.sbuf("d_alr", [1, 1], F32)
    tmp1 = P.sbuf("d_tmp1", [128, 128], F32)
    tmp2 = P.sbuf("d_tmp2", [128, 128], F32)
    xbr = P.sbuf("d_xbr", [128, KT * NCK], BF16)
    xfr = P.sbuf("d_xfr", [128, KT * NCK], F32)
    s6s = P.sbuf("d_s6s", [128, KT * NCK], F32)
    pA = P.psum("d_pA", [1, 128])
    pB = P.psum("d_pB", [128, NCK])
    pM = [P.psum(f"d_pM{i}", [128, NCK]) for i in range(2)]
    sWL = S.sem(); sAL = S.sem(); sAB = S.sem(); sABE = S.sem(); sRD = S.sem(); sAA = S.sem()
    sBC = S.sem(); sBCE = S.sem(); sTN = S.sem(); sX = S.sem(); sXF = S.sem()
    sMM = S.sem(); sOE = S.sem(); sST = S.sem(); sVV = S.sem()
    S.step("vector", lambda e: e.memset(onesf[:], 1.0), incs=[(sVV, 1)])
    S.step("sync", lambda e: e.dma_start(out=alr[:], in_=alphad), waits=list(waits0), incs=[(sAL, 16)])
    S.step("sync", lambda e: e.dma_start(out=bcol[:], in_=biasd), incs=[(sAL, 16)])
    S.step("gpsimd", lambda e: e.dma_start(
               out=wfs[:].rearrange("p (cg kt j) -> p cg kt j", kt=KT, j=128),
               in_=wd.rearrange("cg p kt j -> p cg kt j")),
           waits=list(waits0), incs=[(sWL, 16)])
    nab = 0
    for i in range(64):
        S.step("scalar",
               lambda e, i=i: e.activation(wab[:, :], wfs[:, i * 128:(i + 1) * 128], AF.Abs),
               waits=([(sWL, 16)] if i == 0 else []) + ([(sAB, i)] if i >= 1 else []),
               incs=[(sAA, 1)])
        S.step("tensor",
               lambda e, i=i: e.matmul(pA[0:1, :], onesf[:], wab[:, :],
                                       start=(i == 0), stop=(i == 63)),
               waits=[(sAA, None)] + ([(sVV, 1)] if i == 0 else []),
               incs=[(sAB, 1)])
    S.step("vector", lambda e: e.tensor_copy(scl[0:1, 0:8], pA[0:1, 0:8]), waits=[(sAB, 64)])
    S.step("vector", lambda e: e.tensor_reduce(srow[0:1, 0:1], pA[0:1, :], axis=AX.X, op=ALU.add),
           waits=[(sAB, 64)], incs=[(sVV, 1)])
    S.step("vector", lambda e: e.tensor_scalar(srow[0:1, 1:2], srow[0:1, 0:1],
                                               0.7 / (1024.0 * 1024.0), None, ALU.mult),
           waits=[(sVV, None)], incs=[(sVV, 1)])
    S.step("vector", lambda e: e.tensor_scalar(srow[0:1, 2:3], srow[0:1, 1:2], -1.0, None, ALU.mult),
           waits=[(sVV, None)], incs=[(sVV, 1)])
    S.step("tensor", lambda e: e.matmul(pB[:, 0:2], onesr[:], srow[0:1, 1:3],
                                        start=True, stop=False, skip_group_check=True),
           waits=[(sVV, None)])
    S.step("tensor", lambda e: e.matmul(pB[:, 2:3], onesr[:], alr[0:1, 0:1],
                                        start=False, stop=True, skip_group_check=True),
           waits=[(sAL, 32)], incs=[(sBC, 1)])
    S.step("vector", lambda e: e.tensor_copy(scl[:, 0:3], pB[:, 0:3]),
           waits=[(sBC, 1)], incs=[(sBCE, 1)])
    for i in range(64):
        S.step("vector",
               lambda e, i=i: e.tensor_scalar(tmp1[:, :], wfs[:, i * 128:(i + 1) * 128],
                                              scl[:, 0:1], None, ALU.is_gt),
               waits=[(sBCE, 1), (sVV, None), (sTN, None)], incs=[(sVV, 1)])
        S.step("vector",
               lambda e, i=i: e.tensor_scalar(tmp2[:, :], wfs[:, i * 128:(i + 1) * 128],
                                              scl[:, 1:2], None, ALU.is_lt),
               waits=[(sVV, None)], incs=[(sVV, 1)])
        S.step("vector",
               lambda e, i=i: e.tensor_sub(wts[:, i * 128:(i + 1) * 128], tmp1[:, :], tmp2[:, :]),
               waits=[(sVV, None)], incs=[(sTN, 1)])
    v0 = S.n(sVV)
    for tc in range(NT):
        S.step("gpsimd",
               lambda e, tc=tc: e.dma_start(
                   out=xbr[:].rearrange("p (kt n) -> p kt n", n=NCK),
                   in_=xCTd[:, tc * NCK:(tc + 1) * NCK].rearrange("(kt p) n -> p kt n", p=128)),
               waits=([(sMM, NCG * tc)] if tc >= 1 else []),
               incs=[(sX, 16)])
        S.step("sync",
               lambda e, tc=tc: e.dma_start(
                   out=xfr[:].rearrange("p (cg n) -> p cg n", n=NCK),
                   in_=xCTd[:, tc * NCK:(tc + 1) * NCK].rearrange("(cg p) n -> p cg n", p=128)),
               waits=([(sVV, v0 + NCG * tc)] if tc >= 1 else []),
               incs=[(sXF, 16)])
        for cg in range(NCG):
            j = tc * NCG + cg
            b = pM[j % 2]
            for kt in range(KT):
                S.step("tensor",
                       lambda e, cg=cg, kt=kt, b=b: e.matmul(
                           b[:, :], wts[:, (cg * KT + kt) * 128:(cg * KT + kt + 1) * 128],
                           xbr[:, kt * NCK:(kt + 1) * NCK],
                           start=(kt == 0), stop=(kt == KT - 1)),
                       waits=([(sTN, 64), (sX, 16 * (tc + 1))] if (cg == 0 and kt == 0) else []) +
                             ([(sOE, j - 1)] if (kt == 0 and j >= 2) else []),
                       incs=([(sMM, 1)] if kt == KT - 1 else []))
            sl = slice(cg * NCK, (cg + 1) * NCK)
            S.step("vector",
                   lambda e, sl=sl, b=b, cg=cg: e.tensor_scalar(s6s[:, sl], b[:, :], scl[:, 2:3],
                                                                bcol[:, cg:cg + 1], ALU.mult, ALU.add),
                   waits=[(sMM, j + 1), (sXF, 16 * (tc + 1))] +
                         ([(sST, 16 * tc)] if (cg == 0 and tc >= 1) else []),
                   incs=[(sOE, 1)])
            S.step("vector", lambda e, sl=sl: e.tensor_add(s6s[:, sl], s6s[:, sl], xfr[:, sl]),
                   waits=[(sOE, None)], incs=[(sVV, 1)])
        S.step("sync",
               lambda e, tc=tc: e.dma_start(
                   out=s6d[:, tc * NCK:(tc + 1) * NCK].rearrange("(cg p) n -> p cg n", p=128),
                   in_=s6s[:].rearrange("p (cg n) -> p cg n", n=NCK)),
               waits=[(sVV, v0 + NCG * (tc + 1))],
               incs=[(sST, 16)])
    S.step("sync", lambda e: None, waits=[(sST, 16 * NT)])
    return sST, 16 * NT


def build_d(T):
    P = Prog("d")
    P.T = T
    nc = P.nc
    io = {
        "xCT": nc.dram_tensor("xCT", [C, T], F32, kind="ExternalInput").ap(),
        "outT": nc.dram_tensor("outT", [C, T], F32, kind="ExternalOutput").ap(),
        "l1w_t": nc.dram_tensor("l1w_t", [8, 128, 8, 128], F32, kind="ExternalInput").ap(),
        "alpha_d": nc.dram_tensor("alpha_d", [1, 1], F32, kind="ExternalInput").ap(),
        "bias_col": nc.dram_tensor("bias_col", [128, 8], F32, kind="ExternalInput").ap(),
        "s6T": nc.dram_tensor("s6T", [C, T], F32, kind="Internal").ap(),
    }
    with P.es:
        P.sem_es = P.es
        gl = {
            "epsc": P.sbuf("g_eps", [1, 1], F32),
            "onecf": P.sbuf("g_one", [128, 1], F32),
            "onesr": P.sbuf("g_onesr", [1, 128], F32),
        }
        P.gl = gl
        mrow_g = P.sbuf("g_mrow", [1, T], F32)
        rrow_g = P.sbuf("g_rrow", [1, T], F32)
        S = Script(P)
        S.step("vector", lambda e: e.memset(gl["epsc"][:], EPS_LN))
        S.step("vector", lambda e: e.memset(gl["onecf"][:], 1.0))
        S.step("vector", lambda e: e.memset(gl["onesr"][:], 1.0))
        block = P.es.enter_context(nc.Block())
        with ExitStack() as sc0:
            P_es_old = P.es; P.es = sc0
            sd_, nd_ = emit_d(P, S, io, gl, T)
            P.es = P_es_old
        with ExitStack() as sc1:
            P_es_old = P.es; P.es = sc1
            rowbufs = (mrow_g, rrow_g, P.sbuf("g_trow", [1, T // 2], F32))
            rowsem = emit_rows(P, S, "dp", io["s6T"], T, rms=False, rowbufs=rowbufs,
                               waits0=[(sd_, nd_)])
            emit_normalize(P, S, "dn", io["s6T"], io["outT"], T, rowbufs, rowsem,
                           extra_src_waits=[(sd_, nd_)])
            P.es = P_es_old
        S.replay(block)
    return nc


def prep_d(inputs, T):
    g = {}
    g["l1w_t"] = _tile_colblocks(np.ascontiguousarray(np.asarray(inputs["L1_ffn_weight"]).T))
    g["alpha_d"] = np.asarray(inputs["L1_ffn_alpha"], dtype=np.float32).reshape(1, 1)
    g["bias_col"] = _colmajor_vec(inputs["L1_ffn_bias"])
    return g


# version 04
# speedup vs baseline: 1.0166x; 1.0166x over previous
import numpy as np
import ml_dtypes
from contextlib import ExitStack
import concourse.bass as bass
import concourse.mybir as mybir
from concourse.bass_utils import run_bass_kernel_spmd

F32 = mybir.dt.float32
BF16 = mybir.dt.bfloat16
AF = mybir.ActivationFunctionType
ALU = mybir.AluOpType
AX = mybir.AxisListType

C = 1024
NCG = 8
KT = 8
NCK = 512
EPS_WKV = 1e-8
EPS_LN = 1e-5


class Prog:
    def __init__(self, name="prog"):
        self.nc = bass.Bass("TRN2", target_bir_lowering=False)
        self.es = ExitStack()
        self.name = name
        self._semn = 0

    def sbuf(self, name, shape, dt=F32):
        return self.es.enter_context(self.nc.sbuf_tensor(name, list(shape), dt))

    def psum(self, name, shape, dt=F32):
        return self.es.enter_context(self.nc.psum_tensor(name, list(shape), dt))

    def sem(self, name=None):
        self._semn += 1
        es = getattr(self, "sem_es", None) or self.es
        return es.enter_context(self.nc.semaphore(name or f"s{self._semn}"))


class Script:
    ENGINES = ("sync", "gpsimd", "vector", "scalar", "tensor")

    def __init__(self, P):
        self.P = P
        self.steps = []
        self.count = {}

    POOL_MAX = {"c": 50, "d": 20, "g": 20}

    class LSem:
        __slots__ = ("h", "base")

        def __init__(self, h, base):
            self.h = h
            self.base = base

    def sem(self, name=None, kind="c"):
        pools = getattr(self, "pools", None)
        if pools is None:
            self.pools = pools = {"c": [], "d": [], "g": []}
            self.pidx = {"c": 0, "d": 0, "g": 0}
            self.ledger = {}
        pool = pools[kind]
        if len(pool) < self.POOL_MAX[kind]:
            h = self.P.sem(name)
            self.ledger[id(h)] = 0
            pool.append(h)
        else:
            h = pool[self.pidx[kind] % len(pool)]
            self.pidx[kind] += 1
        ls = Script.LSem(h, self.ledger[id(h)])
        self.count[id(ls)] = [ls, 0]
        return ls

    def n(self, h):
        return self.count[id(h)][1]

    def step(self, engine, emit, waits=(), incs=()):
        assert engine in self.ENGINES
        assert len(incs) <= 1, "hardware: one semaphore update per instruction"
        rw = []
        for w in waits:
            if isinstance(w, tuple):
                ls, v = w
                v = self.n(ls) if v is None else v
            else:
                ls, v = w, self.n(w)
            assert v <= self.n(ls), f"wait target {v} not yet emitted (sem at {self.n(ls)})"
            if v > 0:
                rw.append((ls.h, v + ls.base))
        for ls, amt in incs:
            self.count[id(ls)][1] += amt
            self.ledger[id(ls.h)] += amt
        self.steps.append((engine, rw, emit, [(ls.h, amt) for ls, amt in incs]))

    def replay(self, block):
        for eng_name in self.ENGINES:
            mysteps = [s for s in self.steps if s[0] == eng_name]
            if not mysteps:
                continue
            sec = getattr(block, eng_name)

            def _body(eng, mysteps=mysteps):
                for _, waits, emit, incs in mysteps:
                    for h, v in waits:
                        eng.wait_ge(h, v)
                    instr = emit(eng)
                    for h, amt in incs:
                        assert instr is not None, "inc on step with no instruction"
                        instr.then_inc(h, amt)
            sec(_body)



def _colmajor_vec(v):
    return np.ascontiguousarray(np.asarray(v, dtype=np.float32).reshape(8, 128).T)

def _tile_colblocks(wT):
    cin, m = wT.shape
    ncg = m // 128
    a = np.asarray(wT).reshape(cin // 128, 128, ncg, 128)
    return np.ascontiguousarray(a.transpose(2, 1, 0, 3))

def prep_host(inputs):
    g = {}
    x = np.asarray(inputs["L0_attn_x"])
    B, T, _ = x.shape
    g["B"], g["T"] = B, T
    g["xT"] = [np.ascontiguousarray(x[b].T) for b in range(B)]
    g["wkt"] = _tile_colblocks(np.ascontiguousarray(np.asarray(inputs["L0_attn_Wk_t"]).T))
    g["wvt"] = _tile_colblocks(np.ascontiguousarray(np.asarray(inputs["L0_attn_Wv_t"]).T))
    g["wrt"] = _tile_colblocks(np.ascontiguousarray(np.asarray(inputs["L0_attn_Wr_t"]).T))
    pv = np.concatenate([
        _colmajor_vec(inputs["L0_attn_ln1_g"]),
        _colmajor_vec(inputs["L0_attn_ln1_b"]),
        _colmajor_vec(inputs["L0_attn_mix_k"]),
        _colmajor_vec(inputs["L0_attn_mix_v"]),
        _colmajor_vec(inputs["L0_attn_mix_r"]),
        _colmajor_vec(inputs["L0_attn_time_first"]),
        _colmajor_vec(inputs["L0_attn_ln2_g"]),
        _colmajor_vec(inputs["L0_attn_ln2_b"]),
        _colmajor_vec(inputs["L0_attn_cmix_k"]),
        _colmajor_vec(inputs["L0_attn_cmix_r"]),
    ], axis=1)
    g["pv0"] = np.ascontiguousarray(pv.astype(np.float32))
    return g


def emit_a1(P, S, io, gl, T):
    NT = T // NCK
    HALF = T // 2
    NHC = HALF // NCK

    epsc, onecf = gl["epsc"], gl["onecf"]

    xh = P.sbuf("a1_xh", [128, NCG * (T + 1)], BF16)

    def xg(i):
        return i * (T + 1) + 1
    pvt = P.sbuf("a1_pv", [128, 80], F32)
    pvd = P.sbuf("a1_pvd", [128, 40], F32)
    onesc = P.sbuf("a1_ones1", [128, 1], BF16)
    onesr = P.sbuf("a1_onesr", [1, 128], F32)
    sqt = P.sbuf("a1_sqt", [128, 2 * NCK], BF16)
    mrow = P.sbuf("a1_mrow", [1, T], F32)
    rrow = P.sbuf("a1_rrow", [1, T], F32)
    hb1 = P.sbuf("a1_hb1", [128, NCK], F32)
    hb2 = P.sbuf("a1_hb2", [128, NCK], F32)
    wring = P.sbuf("a1_wring", [128, 2 * 3 * C], BF16)
    stage = P.sbuf("a1_stage", [128, 6 * KT * 128], BF16)
    kH = P.sbuf("a1_kH", [128, HALF], F32)
    vH = P.sbuf("a1_vH", [128, HALF], F32)
    zH = P.sbuf("a1_zH", [128, HALF], F32)
    aH = P.sbuf("a1_aH", [128, HALF], F32)
    u1 = P.sbuf("a1_u1", [128, HALF], F32)
    u2 = P.sbuf("a1_u2", [128, HALF], F32)
    wH = P.sbuf("a1_wH", [128, HALF], F32)
    xr = P.sbuf("a1_xr", [128, HALF], F32)
    x1s = P.sbuf("a1_x1s", [128, HALF], F32)
    rg1 = P.sbuf("a1_rg1", [128, NCK], F32)
    rg2 = P.sbuf("a1_rg2", [128, NCK], F32)
    scr = P.sbuf("a1_scr", [128, 16], F32)

    CM, CNM, CMP, CDM, CEDM, CZC, CAC, CZR = range(8)

    pstat = [P.psum(f"a1_ps{i}", [128, NCK]) for i in range(4)]
    pbc = [P.psum(f"a1_pb{i}", [128, NCK]) for i in range(4)]
    pkb = pstat[0:2]
    pvb = pstat[2:4]
    prb = pbc[0]

    sLD0 = S.sem(kind="d"); sX = S.sem(kind="g"); sDRV = S.sem(); sSQ = S.sem(); sSQU = S.sem()
    sSTC = S.sem(); sRW = S.sem(); sBC = S.sem(); sBCC = S.sem()
    sWL = [S.sem(kind="g"), S.sem(kind="g")]; sPR = S.sem(); sKV = S.sem(); sKVE = S.sem()
    sVS = S.sem(); sSV = S.sem(); sRC = S.sem(); sRCC = S.sem()
    sXR = S.sem(kind="d"); sXRC = S.sem(); sCMB = S.sem(); sST = S.sem(kind="d"); sAA = S.sem()
    sVV = S.sem(); sSS = S.sem()

    xTd = io["xT"]; x1Td = io["x1T"]
    wktd = io["wkt"]; wvtd = io["wvt"]; wrtd = io["wrt"]; pv0d = io["pv0"]

    S.step("sync", lambda e: e.dma_start(out=pvt[:, 0:80], in_=pv0d), incs=[(sLD0, 16)])
    S.step("vector", lambda e: e.memset(onesc[:], 1.0))
    for _g in range(NCG):
        S.step("vector", lambda e, _g=_g: e.memset(xh[:, _g * (T + 1):_g * (T + 1) + 1], 0.0))
    S.step("vector", lambda e: e.memset(onesr[:], 1.0))
    S.step("vector", lambda e: e.tensor_scalar(pvd[:, 0:8], pvt[:, 16:24], -1.0, 1.0, ALU.mult, ALU.add),
           waits=[sLD0])
    S.step("vector", lambda e: e.tensor_scalar(pvd[:, 8:16], pvt[:, 24:32], -1.0, 1.0, ALU.mult, ALU.add))
    S.step("vector", lambda e: e.tensor_scalar(pvd[:, 16:24], pvt[:, 32:40], -1.0, 1.0, ALU.mult, ALU.add),
           incs=[(sDRV, 1)])
    S.step("scalar", lambda e: e.activation(pvd[:, 24:32], pvt[:, 40:48], AF.Exp), waits=[sLD0],
           incs=[(sDRV, 1)])

    for cg in range(NCG):
        S.step("gpsimd",
               lambda e, cg=cg: e.dma_start(out=xh[:, xg(cg):xg(cg) + T],
                                            in_=xTd[cg * 128:(cg + 1) * 128, :]),
               incs=[(sX, 16)])

    for ck in range(NT):
        bs = ck % 2
        for cg in range(NCG):
            i = ck * NCG + cg
            sl = (i % 2) * NCK
            S.step("scalar",
                   lambda e, cg=cg, ck=ck, sl=sl: e.activation(
                       sqt[:, sl:sl + NCK], xh[:, xg(cg) + ck * NCK: xg(cg) + (ck + 1) * NCK], AF.Square),
                   waits=[(sX, 16 * NCG)] + ([(sSQU, i - 1)] if i >= 2 else []),
                   incs=[(sSQ, 1)])
            S.step("tensor",
                   lambda e, cg=cg, ck=ck, bs=bs: e.matmul(
                       pstat[bs][0:1, :], onesc[:], xh[:, xg(cg) + ck * NCK: xg(cg) + (ck + 1) * NCK],
                       start=(cg == 0), stop=(cg == NCG - 1)),
                   waits=([(sX, 16 * NCG), (sDRV, 2)] if (ck == 0 and cg == 0) else []) +
                         ([(sSTC, 2 * (ck - 1))] if (cg == 0 and ck >= 2) else []))
            S.step("tensor",
                   lambda e, bs=bs, sl=sl, cg=cg: e.matmul(
                       pstat[2 + bs][0:1, :], onesc[:], sqt[:, sl:sl + NCK],
                       start=(cg == 0), stop=(cg == NCG - 1)),
                   waits=[(sSQ, i + 1)],
                   incs=[(sSQU, 1)])
        S.step("vector",
               lambda e, ck=ck, bs=bs: e.tensor_copy(mrow[0:1, ck * NCK:(ck + 1) * NCK], pstat[bs][0:1, :]),
               waits=[(sSQU, NCG * (ck + 1))])
        S.step("vector",
               lambda e, ck=ck, bs=bs: e.tensor_copy(rrow[0:1, ck * NCK:(ck + 1) * NCK], pstat[2 + bs][0:1, :]),
               incs=[(sSTC, 2)])

    nrw = 0
    for pi in range(2):
        ro = slice(pi * HALF, (pi + 1) * HALF)
        S.step("vector", lambda e, ro=ro: e.tensor_scalar(mrow[0:1, ro], mrow[0:1, ro], 1.0 / C, None, ALU.mult),
               waits=([(sSTC, 2 * NT)] if pi == 0 else []))
        S.step("vector", lambda e, ro=ro: e.tensor_scalar(rrow[0:1, ro], rrow[0:1, ro], 1.0 / C, None, ALU.mult),
               incs=[(sRW, 1)]); nrw += 1
        S.step("scalar", lambda e, ro=ro: e.activation(x1s[0:1, 0:HALF], mrow[0:1, ro], AF.Square),
               waits=[(sRW, nrw)], incs=[(sRW, 1)]); nrw += 1
        S.step("vector", lambda e, ro=ro: e.tensor_sub(rrow[0:1, ro], rrow[0:1, ro], x1s[0:1, 0:HALF]),
               waits=[(sRW, nrw)], incs=[(sRW, 1)]); nrw += 1
        S.step("scalar", lambda e, ro=ro: e.activation(x1s[0:1, 0:HALF], rrow[0:1, ro], AF.Ln, bias=epsc[0:1, 0:1]),
               waits=[(sRW, nrw)], incs=[(sAA, 1)])
        S.step("scalar", lambda e, ro=ro: e.activation(rrow[0:1, ro], x1s[0:1, 0:HALF], AF.Exp, scale=-0.5),
               waits=[sAA], incs=[(sRW, 1)]); nrw += 1

    for ck in range(NT):
        bs = ck % 2
        S.step("tensor",
               lambda e, ck=ck, bs=bs: e.matmul(pbc[bs][:, :], onesr[:], mrow[0:1, ck * NCK:(ck + 1) * NCK],
                                                start=True, stop=True),
               waits=[(sRW, 8)] + ([(sBCC, ck - 1)] if ck >= 2 else []))
        S.step("tensor",
               lambda e, ck=ck, bs=bs: e.matmul(pbc[2 + bs][:, :], onesr[:], rrow[0:1, ck * NCK:(ck + 1) * NCK],
                                                start=True, stop=True),
               incs=[(sBC, 1)])
        for cg in range(NCG):
            lo = xg(cg) + ck * NCK
            S.step("vector",
                   lambda e, lo=lo, bs=bs: e.tensor_sub(hb1[:], xh[:, lo:lo + NCK], pbc[bs][:, :]),
                   waits=([(sBC, ck + 1)] if cg == 0 else []), incs=[(sVV, 1)])
            S.step("vector",
                   lambda e, bs=bs: e.tensor_mul(hb2[:], hb1[:], pbc[2 + bs][:, :]),
                   waits=[(sVV, None)], incs=[(sVV, 1)])
            S.step("vector",
                   lambda e, lo=lo, cg=cg: e.tensor_scalar(xh[:, lo:lo + NCK], hb2[:], pvt[:, cg:cg + 1],
                                                           pvt[:, 8 + cg:8 + cg + 1], ALU.mult, ALU.add),
                   waits=[(sVV, None)],
                   incs=([(sBCC, 1)] if cg == NCG - 1 else []))

    QK1, QK2, QV1, QV2, QR1, QR2 = range(6)
    mixcol = {QK1: (pvt, 16), QK2: (pvd, 0), QV1: (pvt, 24), QV2: (pvd, 8), QR1: (pvt, 32), QR2: (pvd, 16)}
    wofs = {QK1: 0, QK2: 0, QV1: C, QV2: C, QR1: 2 * C, QR2: 2 * C}

    def stg(q, kt):
        o = (q * KT + kt) * 128
        return stage[:, o:o + 128]

    for cg in range(NCG):
        wslot = (cg % 2) * 3 * C
        for wi, wd in enumerate((wktd, wvtd, wrtd)):
            S.step("gpsimd",
                   lambda e, wi=wi, wd=wd, cg=cg, wslot=wslot: e.dma_start(
                       out=wring[:, wslot + wi * C: wslot + (wi + 1) * C],
                       in_=wd[cg].rearrange("p kt j -> p (kt j)")),
                   waits=([(sPR, cg - 1)] if cg >= 2 else []),
                   incs=[(sWL[cg % 2], 16)])
        for q in range(6):
            for kt in range(KT):
                col_t, col_o = mixcol[q]
                S.step("vector",
                       lambda e, q=q, kt=kt, wslot=wslot, col_t=col_t, col_o=col_o: e.tensor_scalar(
                           stg(q, kt), wring[:, wslot + wofs[q] + kt * 128: wslot + wofs[q] + (kt + 1) * 128],
                           col_t[:, col_o + kt:col_o + kt + 1], None, ALU.mult),
                       waits=([(sWL[cg % 2], 48 * (cg // 2 + 1)), (sDRV, 2), (sBCC, NT)] if (q == 0 and kt == 0) else []),
                       incs=([(sPR, 1)] if (q == 5 and kt == KT - 1) else []))
        for hf in range(2):
            hbase = hf * HALF
            first = (hf == 0)
            for hc in range(NHC):
                ck = hf * NHC + hc
                gck = cg * NT + ck
                t0 = hbase + hc * NCK
                kb = pkb[ck % 2]; vb = pvb[ck % 2]
                for q, bank, q2 in ((QK1, kb, QK2), (QV1, vb, QV2)):
                    for kt in range(KT):
                        S.step("tensor",
                               lambda e, q=q, kt=kt, bank=bank, t0=t0: e.matmul(
                                   bank[:, :], stg(q, kt), xh[:, xg(kt) + t0: xg(kt) + t0 + NCK],
                                   start=(kt == 0), stop=False),
                               waits=([(sPR, cg + 1)] + ([(sKVE, gck - 1)] if gck >= 2 else [])
                                      if (q == QK1 and kt == 0) else []))
                    for kt in range(KT):
                        S.step("tensor",
                               lambda e, q2=q2, kt=kt, bank=bank, t0=t0: e.matmul(
                                   bank[:, :], stg(q2, kt), xh[:, xg(kt) + t0 - 1: xg(kt) + t0 + NCK - 1],
                                   start=False, stop=(kt == KT - 1)),
                               incs=([(sKV, 1)] if kt == KT - 1 else []))
                S.step("vector",
                       lambda e, hc=hc, kb=kb: e.tensor_copy(kH[:, hc * NCK:(hc + 1) * NCK], kb[:, :]),
                       waits=[(sKV, 2 * gck + 2)])
                S.step("vector",
                       lambda e, hc=hc, vb=vb: e.tensor_copy(vH[:, hc * NCK:(hc + 1) * NCK], vb[:, :]),
                       incs=[(sKVE, 1)])

            zs = zH[:, 0:HALF - 1]
            last_v = [None]
            last_s = [None]

            def vs(emit, waits=(), inc=False):
                w = list(waits) + ([last_v[0]] if last_v[0] else [])
                s = sVS if inc else sVV
                S.step("vector", emit, waits=w, incs=[(s, 1)])
                last_v[0] = (s, S.n(s))

            def ss(emit, waits=(), inc=False):
                w = list(waits) + ([last_s[0]] if last_s[0] else [])
                s = sSV if inc else sSS
                S.step("scalar", emit, waits=w, incs=[(s, 1)])
                last_s[0] = (s, S.n(s))

            vs(lambda e: e.reduce_max(scr[:, CM:CM + 1], kH[:], axis=AX.X),
               waits=[(sKVE, cg * NT + hf * NHC + NHC)])
            vs(lambda e: e.memset(u2[:], 1.0))
            if not first:
                vs(lambda e: e.tensor_sub(scr[:, CDM:CDM + 1], scr[:, CMP:CMP + 1], scr[:, CM:CM + 1]))
            vs(lambda e: e.tensor_scalar(scr[:, CNM:CNM + 1], scr[:, CM:CM + 1], -1.0, None, ALU.mult),
               inc=True)
            if not first:
                ss(lambda e: e.activation(scr[:, CEDM:CEDM + 1], scr[:, CDM:CDM + 1], AF.Exp),
                   waits=[(sVS, None)])
            ss(lambda e: e.activation(wH[:], kH[:], AF.Exp, bias=scr[:, CNM:CNM + 1]),
               waits=[(sVS, None)], inc=True)
            if not first:
                vs(lambda e: e.tensor_mul(scr[:, CZC:CZC + 1], scr[:, CZR:CZR + 1], scr[:, CEDM:CEDM + 1]),
                   waits=[(sSV, None)])
            init = 0.0 if first else scr[:, CZC:CZC + 1]
            vs(lambda e, init=init: e.tensor_tensor_scan(zH[:], u2[:], wH[:], init, ALU.mult, ALU.add),
               waits=[(sSV, None)])
            if first:
                vs(lambda e: e.memset(u1[:, 0:1], 0.0))
            else:
                vs(lambda e: e.tensor_tensor(u1[:, 0:1], scr[:, CZC:CZC + 1], wH[:, 0:1], ALU.is_ge))
            vs(lambda e: e.tensor_tensor(u1[:, 1:HALF], zs, wH[:, 1:HALF], ALU.is_ge))
            vs(lambda e: e.tensor_scalar(u2[:], u1[:], EPS_WKV, 1.0, ALU.mult, ALU.add))
            vs(lambda e: e.tensor_scalar(u1[:], u1[:], -EPS_WKV, 1.0 + EPS_WKV, ALU.mult, ALU.add))
            vs(lambda e: e.tensor_mul(u1[:], u1[:], wH[:]))
            vs(lambda e, init=init: e.tensor_tensor_scan(zH[:], u2[:], u1[:], init, ALU.mult, ALU.add))
            if first:
                vs(lambda e: e.tensor_copy(u2[:, 0:1], wH[:, 0:1]))
            else:
                vs(lambda e: e.tensor_tensor(u2[:, 0:1], scr[:, CZC:CZC + 1], wH[:, 0:1], ALU.max))
            vs(lambda e: e.tensor_tensor(u2[:, 1:HALF], zs, wH[:, 1:HALF], ALU.max), inc=True)
            ss(lambda e: e.activation(u1[:], u2[:], AF.Ln), waits=[(sVS, None)])
            ss(lambda e: e.activation(u2[:], u1[:], AF.Exp, scale=-1.0), inc=True)
            if first:
                vs(lambda e: e.memset(u1[:, 0:1], 0.0), waits=[(sSV, None)])
            else:
                vs(lambda e: e.tensor_tensor(u1[:, 0:1], scr[:, CZC:CZC + 1], wH[:, 0:1], ALU.is_ge),
                   waits=[(sSV, None)])
            vs(lambda e: e.tensor_tensor(u1[:, 1:HALF], zs, wH[:, 1:HALF], ALU.is_ge))
            if first:
                vs(lambda e: e.memset(kH[:, 0:1], 0.0))
            else:
                vs(lambda e: e.tensor_mul(kH[:, 0:1], scr[:, CZC:CZC + 1], u2[:, 0:1]))
            vs(lambda e: e.tensor_mul(kH[:, 1:HALF], zs, u2[:, 1:HALF]))
            vs(lambda e: e.tensor_scalar(kH[:], kH[:], 1.0, None, ALU.min))
            vs(lambda e: e.tensor_tensor(kH[:], kH[:], u1[:], ALU.max))
            vs(lambda e: e.tensor_mul(u2[:], u2[:], wH[:]))
            vs(lambda e: e.tensor_scalar(u2[:], u2[:], 1.0, None, ALU.min))
            vs(lambda e: e.tensor_mul(u2[:], u2[:], vH[:]))
            ainit = 0.0 if first else scr[:, CAC:CAC + 1]
            vs(lambda e, ainit=ainit: e.tensor_tensor_scan(aH[:], kH[:], u2[:], ainit, ALU.mult, ALU.add))
            vs(lambda e, cg=cg: e.tensor_scalar(u2[:], wH[:], pvd[:, 24 + cg:24 + cg + 1], None, ALU.mult))
            if first:
                vs(lambda e: e.tensor_copy(wH[:, 0:1], u2[:, 0:1]))
            else:
                vs(lambda e: e.tensor_tensor(wH[:, 0:1], scr[:, CZC:CZC + 1], u2[:, 0:1], ALU.max))
            vs(lambda e: e.tensor_tensor(wH[:, 1:HALF], zs, u2[:, 1:HALF], ALU.max))
            vs(lambda e: e.scalar_tensor_tensor(wH[:], wH[:], EPS_WKV, u2[:], ALU.mult, ALU.add))
            vs(lambda e: e.tensor_add(wH[:, 1:HALF], wH[:, 1:HALF], zs))
            if not first:
                vs(lambda e: e.tensor_add(wH[:, 0:1], wH[:, 0:1], scr[:, CZC:CZC + 1]), inc=True)
            else:
                vs(lambda e: e.tensor_copy(scr[:, CZR:CZR + 1], zH[:, HALF - 1:HALF]), inc=True)
            ss(lambda e: e.activation(u1[:], wH[:], AF.Ln), waits=[(sVS, None)])
            ss(lambda e: e.activation(wH[:], u1[:], AF.Exp, scale=-1.0), inc=True)
            if first:
                vs(lambda e: e.memset(u1[:, 0:1], 0.0), waits=[(sSV, None)])
            else:
                vs(lambda e: e.tensor_mul(u1[:, 0:1], scr[:, CZC:CZC + 1], scr[:, CAC:CAC + 1]),
                   waits=[(sSV, None)])
            vs(lambda e: e.tensor_mul(u1[:, 1:HALF], zs, aH[:, 0:HALF - 1]))
            vs(lambda e: e.tensor_mul(u2[:], u2[:], vH[:]))
            vs(lambda e: e.tensor_add(u1[:], u1[:], u2[:]))
            vs(lambda e: e.tensor_mul(u1[:], u1[:], wH[:]))
            if hf == 0:
                vs(lambda e: e.tensor_copy(scr[:, CAC:CAC + 1], aH[:, HALF - 1:HALF]))
                vs(lambda e: e.tensor_copy(scr[:, CMP:CMP + 1], scr[:, CM:CM + 1]))
            wkv_tok = last_v[0]

            ih = 2 * cg + hf
            S.step("sync",
                   lambda e, cg=cg, hbase=hbase: e.dma_start(
                       out=xr[:], in_=xTd[cg * 128:(cg + 1) * 128, hbase:hbase + HALF]),
                   waits=([(sXRC, ih)] if ih >= 1 else []),
                   incs=[(sXR, 16)])

            for hc in range(NHC):
                t0 = hbase + hc * NCK
                j = cg * NT + hf * NHC + hc
                for kt in range(KT):
                    S.step("tensor",
                           lambda e, kt=kt, t0=t0: e.matmul(prb[:, :], stg(QR1, kt),
                                                            xh[:, xg(kt) + t0: xg(kt) + t0 + NCK],
                                                            start=(kt == 0), stop=False),
                           waits=([(sRCC, j)] if (kt == 0 and j >= 1) else []))
                for kt in range(KT):
                    S.step("tensor",
                           lambda e, kt=kt, t0=t0: e.matmul(prb[:, :], stg(QR2, kt),
                                                            xh[:, xg(kt) + t0 - 1: xg(kt) + t0 + NCK - 1],
                                                            start=False, stop=(kt == KT - 1)),
                           incs=([(sRC, 1)] if kt == KT - 1 else []))
                S.step("scalar", lambda e: e.activation(rg1[:], prb[:, :], AF.Exp, scale=-1.0),
                       waits=[sRC] + ([(sCMB, j)] if j >= 1 else []), incs=[(sAA, 1)])
                S.step("scalar", lambda e: e.activation(rg2[:], rg1[:], AF.Ln, bias=onecf[:, 0:1]),
                       waits=[sAA], incs=[(sAA, 1)])
                S.step("scalar", lambda e: e.activation(rg1[:], rg2[:], AF.Exp, scale=-1.0),
                       waits=[sAA], incs=[(sRCC, 1)])
                sl = slice(hc * NCK, (hc + 1) * NCK)
                S.step("vector", lambda e, sl=sl: e.tensor_mul(hb1[:], rg1[:], u1[:, sl]),
                       waits=[sRCC, (sXR, 16 * (ih + 1)), wkv_tok] +
                             ([(sST, 16 * ih)] if (hc == 0 and ih >= 1) else []) +
                             ([prev_add_tok] if hc >= 1 else []),
                       incs=[(sCMB, 1)])
                S.step("vector", lambda e, sl=sl: e.tensor_add(x1s[:, sl], hb1[:], xr[:, sl]),
                       waits=[(sCMB, None)],
                       incs=([(sXRC, 1)] if hc == NHC - 1 else [(sVV, 1)]))
                prev_add_tok = (sVV, S.n(sVV))
            S.step("sync",
                   lambda e, cg=cg, hbase=hbase: e.dma_start(
                       out=x1Td[cg * 128:(cg + 1) * 128, hbase:hbase + HALF], in_=x1s[:]),
                   waits=[(sXRC, ih + 1)],
                   incs=[(sST, 16)])
    S.step("sync", lambda e: None, waits=[(sST, 16 * NCG * 2)])
    return (sST, 16 * NCG * 2)


def build_a1(T):
    P = Prog("a1")
    nc = P.nc
    io = {
        "xT": nc.dram_tensor("xT", [C, T], F32, kind="ExternalInput").ap(),
        "x1T": nc.dram_tensor("x1T", [C, T], F32, kind="ExternalOutput").ap(),
        "wkt": nc.dram_tensor("wkt", [8, 128, 8, 128], F32, kind="ExternalInput").ap(),
        "wvt": nc.dram_tensor("wvt", [8, 128, 8, 128], F32, kind="ExternalInput").ap(),
        "wrt": nc.dram_tensor("wrt", [8, 128, 8, 128], F32, kind="ExternalInput").ap(),
        "pv0": nc.dram_tensor("pv0", [128, 80], F32, kind="ExternalInput").ap(),
    }
    with P.es:
        gl = {
            "epsc": P.sbuf("g_eps", [1, 1], F32),
            "onecf": P.sbuf("g_one", [128, 1], F32),
        }
        S = Script(P)
        S.step("vector", lambda e: e.memset(gl["epsc"][:], EPS_LN))
        S.step("vector", lambda e: e.memset(gl["onecf"][:], 1.0))
        block = P.es.enter_context(nc.Block())
        emit_a1(P, S, io, gl, T)
        S.replay(block)
    return nc


def _prep_a2(inputs):
    return {
        "wkc": _tile_colblocks(np.ascontiguousarray(np.asarray(inputs["L0_attn_Wk_c"]).T)),
        "wvc": _tile_colblocks(np.ascontiguousarray(np.asarray(inputs["L0_attn_Wv_c"]).T)),
        "wrc": _tile_colblocks(np.ascontiguousarray(np.asarray(inputs["L0_attn_Wr_c"]).T)),
    }


def kernel_multi(**inputs):
    T = int(np.asarray(inputs["L0_attn_x"]).shape[1])
    B = int(np.asarray(inputs["L0_attn_x"]).shape[0])
    cores = list(range(B))
    g1 = prep_host(inputs)
    ga2 = _prep_a2(inputs)
    gb1 = prep_b1(inputs, T)
    gb2 = prep_b2(inputs, T)
    gc = prep_c(inputs, T)
    gd = prep_d(inputs, T)

    nc_a1 = build_a1(T)
    maps = [dict(xT=g1["xT"][b], wkt=g1["wkt"], wvt=g1["wvt"], wrt=g1["wrt"], pv0=g1["pv0"])
            for b in range(B)]
    r1 = run_bass_kernel_spmd(nc_a1, maps, core_ids=cores)
    x1 = [r1.results[b]["x1T"] for b in range(B)]

    nc_a2 = build_a2(T)
    maps = [dict(xT=g1["xT"][b], x1T=x1[b], wkc=ga2["wkc"], wvc=ga2["wvc"], wrc=ga2["wrc"],
                 pv0=g1["pv0"]) for b in range(B)]
    r2 = run_bass_kernel_spmd(nc_a2, maps, core_ids=cores)
    y = [r2.results[b]["yT"] for b in range(B)]

    nc_b1 = build_b1(T)
    maps = [dict(yT=y[b], wqkv_t=gb1["wqkv_t"], wo_t=gb1["wo_t"], lnw=gb1["lnw"],
                 cosb_d=gb1["cosb_d"], sinb_d=gb1["sinb_d"], masks_d=gb1["masks_d"],
                 ident_d=gb1["ident_d"]) for b in range(B)]
    r3 = run_bass_kernel_spmd(nc_b1, maps, core_ids=cores)
    x2 = [r3.results[b]["x2T"] for b in range(B)]

    nc_b2 = build_b2(T)
    maps = [dict(yT=y[b], x2T=x2[b], gw_t=gb2["gw_t"], w13_t=gb2["w13_t"], w2_t=gb2["w2_t"],
                 ln2w=gb2["ln2w"], identf_d=gb2["identf_d"], oh8_d=gb2["oh8_d"]) for b in range(B)]
    r4 = run_bass_kernel_spmd(nc_b2, maps, core_ids=cores)
    xB = [r4.results[b]["xBT"] for b in range(B)]

    nc_c = build_c(T)
    maps = [dict(xBT=xB[b], key_t=gc["key_t"], vp_t=gc["vp_t"]) for b in range(B)]
    r5 = run_bass_kernel_spmd(nc_c, maps, core_ids=cores)
    xC = [r5.results[b]["xCT"] for b in range(B)]

    nc_d = build_d(T)
    maps = [dict(xCT=xC[b], l1w_t=gd["l1w_t"], alpha_d=gd["alpha_d"], bias_col=gd["bias_col"])
            for b in range(B)]
    r6 = run_bass_kernel_spmd(nc_d, maps, core_ids=cores)
    out = np.stack([np.ascontiguousarray(r6.results[b]["outT"].T) for b in range(B)], axis=0)
    return out.astype(np.float32)



def emit_rows(P, S, pref, src, T, rms, rowbufs, waits0=()):
    NT = T // NCK
    mrow, rrow, tmprow = rowbufs
    xall = P.sbuf(pref + "_xall", [128, NCG * T], BF16)
    sqt = P.sbuf(pref + "_sqt", [128, 2 * NCK], BF16)
    onesc = P.sbuf(pref + "_ones1", [128, 1], BF16)
    pst = [P.psum(pref + f"_ps{i}", [128, NCK]) for i in range(4)]
    sX = S.sem(kind="g"); sSQ = S.sem(); sSQU = S.sem(); sSTC = S.sem(); sRW = S.sem(); sAA = S.sem()
    S.step("vector", lambda e: e.memset(onesc[:], 1.0), waits=list(waits0), incs=[(sRW, 1)])
    for cg in range(NCG):
        S.step("gpsimd",
               lambda e, cg=cg: e.dma_start(out=xall[:, cg * T:(cg + 1) * T],
                                            in_=src[cg * 128:(cg + 1) * 128, :]),
               waits=list(waits0), incs=[(sX, 16)])
    for ck in range(NT):
        bs = ck % 2
        for cg in range(NCG):
            i = ck * NCG + cg
            sl = (i % 2) * NCK
            S.step("scalar",
                   lambda e, cg=cg, ck=ck, sl=sl: e.activation(
                       sqt[:, sl:sl + NCK], xall[:, cg * T + ck * NCK: cg * T + (ck + 1) * NCK], AF.Square),
                   waits=[(sX, 16 * NCG)] + ([(sSQU, i - 1)] if i >= 2 else []),
                   incs=[(sSQ, 1)])
            if not rms:
                S.step("tensor",
                       lambda e, cg=cg, ck=ck, bs=bs: e.matmul(
                           pst[bs][0:1, :], onesc[:], xall[:, cg * T + ck * NCK: cg * T + (ck + 1) * NCK],
                           start=(cg == 0), stop=(cg == NCG - 1)),
                       waits=([(sX, 16 * NCG), (sRW, 1)] if (ck == 0 and cg == 0) else []) +
                             ([(sSTC, 2 * (ck - 1))] if (cg == 0 and ck >= 2) else []))
            S.step("tensor",
                   lambda e, bs=bs, sl=sl, cg=cg: e.matmul(
                       pst[2 + bs][0:1, :], onesc[:], sqt[:, sl:sl + NCK],
                       start=(cg == 0), stop=(cg == NCG - 1)),
                   waits=[(sSQ, i + 1)] +
                         ([(sRW, 1)] if (rms and ck == 0 and cg == 0) else []) +
                         ([(sSTC, 2 * (ck - 1))] if (rms and cg == 0 and ck >= 2) else []),
                   incs=[(sSQU, 1)])
        if not rms:
            S.step("vector",
                   lambda e, ck=ck, bs=bs: e.tensor_copy(mrow[0:1, ck * NCK:(ck + 1) * NCK], pst[bs][0:1, :]),
                   waits=[(sSQU, NCG * (ck + 1))])
            S.step("vector",
                   lambda e, ck=ck, bs=bs: e.tensor_copy(rrow[0:1, ck * NCK:(ck + 1) * NCK], pst[2 + bs][0:1, :]),
                   incs=[(sSTC, 2)])
        else:
            S.step("vector",
                   lambda e, ck=ck, bs=bs: e.tensor_copy(rrow[0:1, ck * NCK:(ck + 1) * NCK], pst[2 + bs][0:1, :]),
                   waits=[(sSQU, NCG * (ck + 1))], incs=[(sSTC, 2)])
    nrw = 1
    HALF = T // 2
    epsc = P.gl["epsc"]
    for pi in range(2):
        ro = slice(pi * HALF, (pi + 1) * HALF)
        if not rms:
            S.step("vector", lambda e, ro=ro: e.tensor_scalar(mrow[0:1, ro], mrow[0:1, ro], 1.0 / C, None, ALU.mult),
                   waits=([(sSTC, 2 * NT)] if pi == 0 else []))
        S.step("vector", lambda e, ro=ro: e.tensor_scalar(rrow[0:1, ro], rrow[0:1, ro], 1.0 / C, None, ALU.mult),
               waits=([(sSTC, 2 * NT)] if (pi == 0 and rms) else []), incs=[(sRW, 1)]); nrw += 1
        if not rms:
            S.step("scalar", lambda e, ro=ro: e.activation(tmprow[0:1, 0:HALF], mrow[0:1, ro], AF.Square),
                   waits=[(sRW, nrw)], incs=[(sRW, 1)]); nrw += 1
            S.step("vector", lambda e, ro=ro: e.tensor_sub(rrow[0:1, ro], rrow[0:1, ro], tmprow[0:1, 0:HALF]),
                   waits=[(sRW, nrw)], incs=[(sRW, 1)]); nrw += 1
        S.step("scalar", lambda e, ro=ro: e.activation(tmprow[0:1, 0:HALF], rrow[0:1, ro], AF.Ln, bias=epsc[0:1, 0:1]),
               waits=[(sRW, nrw)], incs=[(sAA, 1)])
        S.step("scalar", lambda e, ro=ro: e.activation(rrow[0:1, ro], tmprow[0:1, 0:HALF], AF.Exp, scale=-0.5),
               waits=[sAA], incs=[(sRW, 1)]); nrw += 1
    return sRW, nrw


def emit_normalize(P, S, pref, src, dst, T, rowbufs, rowsem, rms=False, extra_src_waits=()):
    HALF = T // 2
    NHC = HALF // NCK
    mrow, rrow, _ = rowbufs
    semv, semn = rowsem
    onesr = P.sbuf(pref + "_onesr", [1, 128], F32)
    xr = P.sbuf(pref + "_xr", [128, HALF], F32)
    ys = P.sbuf(pref + "_ys", [128, HALF], F32)
    pbc = [P.psum(pref + f"_pb{i}", [128, NCK]) for i in range(4)]
    sXR = S.sem(kind="d"); sBC = S.sem(); sBCC = S.sem(); sST = S.sem(kind="d"); sVV = S.sem()
    S.step("vector", lambda e: e.memset(onesr[:], 1.0), incs=[(sVV, 1)])
    nb = 0
    nx = 0
    for cg in range(NCG):
        for hf in range(2):
            ih = 2 * cg + hf
            hbase = hf * HALF
            S.step("sync",
                   lambda e, cg=cg, hbase=hbase: e.dma_start(
                       out=xr[:], in_=src[cg * 128:(cg + 1) * 128, hbase:hbase + HALF]),
                   waits=([(sBCC, NHC * ih)] if ih >= 1 else []) + list(extra_src_waits),
                   incs=[(sXR, 16)])
            nx += 16
            for hc in range(NHC):
                ck = hf * NHC + hc
                bs = nb % 2
                if not rms:
                    S.step("tensor",
                           lambda e, ck=ck, bs=bs: e.matmul(pbc[bs][:, :], onesr[:],
                                                            mrow[0:1, ck * NCK:(ck + 1) * NCK],
                                                            start=True, stop=True),
                           waits=([(semv, semn), (sVV, 1)] if nb == 0 else []) +
                                 ([(sBCC, nb - 1)] if nb >= 2 else []))
                S.step("tensor",
                       lambda e, ck=ck, bs=bs: e.matmul(pbc[2 + bs][:, :], onesr[:],
                                                        rrow[0:1, ck * NCK:(ck + 1) * NCK],
                                                        start=True, stop=True),
                       waits=([(semv, semn), (sVV, 1)] if (rms and nb == 0) else []) +
                             ([(sBCC, nb - 1)] if (rms and nb >= 2) else []),
                       incs=[(sBC, 1)])
                nb += 1
                sl = slice(hc * NCK, (hc + 1) * NCK)
                if not rms:
                    S.step("vector",
                           lambda e, sl=sl, bs=bs: e.tensor_sub(ys[:, sl], xr[:, sl], pbc[bs][:, :]),
                           waits=[(sBC, nb), (sXR, nx)] + ([(sST, 16 * ih)] if (hc == 0 and ih >= 1) else []),
                           incs=[(sVV, 1)])
                    S.step("vector",
                           lambda e, sl=sl, bs=bs: e.tensor_mul(ys[:, sl], ys[:, sl], pbc[2 + bs][:, :]),
                           waits=[(sVV, None)], incs=[(sBCC, 1)])
                else:
                    S.step("vector",
                           lambda e, sl=sl, bs=bs: e.tensor_mul(ys[:, sl], xr[:, sl], pbc[2 + bs][:, :]),
                           waits=[(sBC, nb), (sXR, nx)] + ([(sST, 16 * ih)] if (hc == 0 and ih >= 1) else []),
                           incs=[(sBCC, 1)])
            S.step("sync",
                   lambda e, cg=cg, hbase=hbase: e.dma_start(
                       out=dst[cg * 128:(cg + 1) * 128, hbase:hbase + HALF], in_=ys[:]),
                   waits=[(sBCC, NHC * (ih + 1))],
                   incs=[(sST, 16)])
    S.step("sync", lambda e: None, waits=[(sST, 16 * NCG * 2)])
    return sST, 16 * NCG * 2



def emit_a2_tsbuild(P, S, pref, x1d, tsbuf, mixc_off, rowbufs, rowsem, pvt, sST_prev=None):
    T = P.T
    mrow, rrow, _ = rowbufs
    semv, semn = rowsem
    onesr = P.gl["onesr"]
    x1b = P.sbuf(pref + "_x1b", [128, P.T], BF16)
    h2t = P.sbuf(pref + "_h2t", [128, NCK], BF16)
    hb1 = P.sbuf(pref + "_hb1", [128, NCK], F32)
    hb2 = P.sbuf(pref + "_hb2", [128, NCK], F32)
    pbc = [P.psum(pref + f"_pb{i}", [128, NCK]) for i in range(4)]
    NT = T // NCK
    carry = P.sbuf(pref + "_carry", [128, 1], F32)
    sX1 = [S.sem(kind="g") for _ in range(2)]
    sBC = S.sem(); sBCC = S.sem(); sVV = S.sem(); sCGD = S.sem()

    def xg(i):
        return i * (T + 1) + 1
    for _g in range(NCG):
        S.step("vector", lambda e, _g=_g: e.memset(tsbuf[:, _g * (T + 1):_g * (T + 1) + 1], 0.0),
               waits=([(rowsem[0], rowsem[1])] + ([sST_prev] if sST_prev is not None else [])
                      if _g == 0 else []),
               incs=[(sVV, 1)])
    nb = 0
    for cg in range(NCG):
        S.step("gpsimd",
               lambda e, cg=cg: e.dma_start(out=x1b[:, :], in_=x1d[cg * 128:(cg + 1) * 128, :]),
               waits=([(sCGD, cg)] if cg >= 1 else []) + ([] if sST_prev is None or cg > 0 else [sST_prev]),
               incs=[(sX1[cg % 2], 16)])
        for ck in range(NT):
            bs = nb % 2
            S.step("tensor",
                   lambda e, ck=ck, bs=bs: e.matmul(pbc[bs][:, :], onesr[:], mrow[0:1, ck * NCK:(ck + 1) * NCK],
                                                    start=True, stop=True),
                   waits=([(semv, semn)] if nb == 0 else []) + ([(sBCC, nb - 1)] if nb >= 2 else []))
            S.step("tensor",
                   lambda e, ck=ck, bs=bs: e.matmul(pbc[2 + bs][:, :], onesr[:], rrow[0:1, ck * NCK:(ck + 1) * NCK],
                                                    start=True, stop=True),
                   incs=[(sBC, 1)])
            nb += 1
            S.step("vector",
                   lambda e, ck=ck, bs=bs: e.tensor_sub(hb1[:], x1b[:, ck * NCK:(ck + 1) * NCK], pbc[bs][:, :]),
                   waits=[(sX1[cg % 2], 16 * (cg // 2 + 1)), (sBC, nb), (sVV, None)], incs=[(sVV, 1)])
            S.step("vector",
                   lambda e, bs=bs: e.tensor_mul(hb2[:], hb1[:], pbc[2 + bs][:, :]),
                   waits=[(sVV, None)], incs=[(sVV, 1)])
            S.step("vector",
                   lambda e, cg=cg: e.tensor_scalar(h2t[:], hb2[:], pvt[:, 48 + cg:48 + cg + 1],
                                                    pvt[:, 56 + cg:56 + cg + 1], ALU.mult, ALU.add),
                   waits=[(sVV, None)], incs=[(sBCC, 1)])
            S.step("vector",
                   lambda e, cg=cg, ck=ck: e.tensor_scalar(
                       hb1[:, 0:NCK], h2t[:], pvt[:, mixc_off + cg:mixc_off + cg + 1], None, ALU.mult),
                   waits=[(sBCC, None)], incs=[(sVV, 1)])
            S.step("vector",
                   lambda e, cg=cg, ck=ck: e.scalar_tensor_tensor(
                       tsbuf[:, xg(cg) + ck * NCK + 1: xg(cg) + (ck + 1) * NCK],
                       h2t[:, 0:NCK - 1], pvd2_g(P, mixc_off, cg), hb1[:, 1:NCK], ALU.mult, ALU.add),
                   waits=[(sVV, None)], incs=[(sVV, 1)])
            if ck == 0:
                S.step("vector",
                       lambda e, cg=cg, ck=ck: e.tensor_copy(
                           tsbuf[:, xg(cg) + ck * NCK: xg(cg) + ck * NCK + 1], hb1[:, 0:1]),
                       waits=[(sVV, None)], incs=[(sVV, 1)])
            else:
                S.step("vector",
                       lambda e, cg=cg, ck=ck: e.scalar_tensor_tensor(
                           tsbuf[:, xg(cg) + ck * NCK: xg(cg) + ck * NCK + 1],
                           carry[:, 0:1], pvd2_g(P, mixc_off, cg), hb1[:, 0:1], ALU.mult, ALU.add),
                       waits=[(sVV, None)], incs=[(sVV, 1)])
            S.step("vector",
                   lambda e: e.tensor_copy(carry[:, 0:1], h2t[:, NCK - 1:NCK]),
                   waits=[(sVV, None)],
                   incs=([(sCGD, 1)] if ck == NT - 1 else [(sVV, 1)]))
    return sCGD


def pvd2_g(P, mixc_off, cg):
    base = 0 if mixc_off == 64 else 8
    return P.gl["pvd2"][:, base + cg:base + cg + 1]


def emit_a2_core(P, S, io, gl, T, waits0=()):
    NT = T // NCK
    HALF = T // 2
    NHC = HALF // NCK

    def xg(i):
        return i * (T + 1) + 1

    tsck = io["tsck"]
    wkcd = io["wkc"]; wvcd = io["wvc"]
    vvT = io["vvT"]

    sqC = P.sbuf("a2_sqC", [128, 32 * NCK], BF16)
    rel1 = P.sbuf("a2_rel1", [128, 4 * NCK], BF16)
    wkcr = P.sbuf("a2_wkcr", [128, 4 * KT * 128], BF16)
    wvcr = P.sbuf("a2_wvcr", [128, 2 * 32 * 128], BF16)
    vvs = P.sbuf("a2_vvs", [128, 2 * NCK], F32)
    pk = [P.psum(f"a2_pk{i}", [128, NCK]) for i in range(2)]
    pv_ = [P.psum(f"a2_pv{i}", [128, NCK]) for i in range(2)]
    sWK = [S.sem(kind="g") for _ in range(4)]
    sWV = [S.sem(kind="g") for _ in range(2)]
    sKC = S.sem(); sSQ = S.sem(); sVV_ = S.sem(); sVVE = S.sem(); sVST = S.sem(kind="d")
    sAA = S.sem()

    nwk = [0] * 4
    nwv = [0] * 2
    for tc in range(NT):
        for fg in range(32):
            j = tc * 32 + fg
            slot = j % 4
            S.step("gpsimd",
                   lambda e, fg=fg, slot=slot: e.dma_start(
                       out=wkcr[:, slot * KT * 128:(slot + 1) * KT * 128],
                       in_=wkcd[fg].rearrange("p kt j -> p (kt j)")),
                   waits=([(sKC, j - 3)] if j >= 4 else []) + (list(waits0) if j == 0 else []),
                   incs=[(sWK[slot], 16)])
            nwk[slot] += 16
            bk = pk[j % 2]
            for kt in range(KT):
                S.step("tensor",
                       lambda e, kt=kt, fg=fg, tc=tc, bk=bk, slot=slot: e.matmul(
                           bk[:, :], wkcr[:, (slot * KT + kt) * 128:(slot * KT + kt + 1) * 128],
                           tsck[:, xg(kt) + tc * NCK: xg(kt) + (tc + 1) * NCK],
                           start=(kt == 0), stop=(kt == KT - 1)),
                       waits=([(sWK[slot], nwk[slot])] if kt == 0 else []) +
                             ([(sSQ, j - 1)] if (kt == 0 and j >= 2) else []),
                       incs=([(sKC, 1)] if kt == KT - 1 else []))
            S.step("scalar",
                   lambda e, j=j, bk=bk: e.activation(rel1[:, (j % 4) * NCK:(j % 4 + 1) * NCK], bk[:, :], AF.Relu),
                   waits=[(sKC, j + 1)] + ([(sSQ, j - 3)] if j >= 4 else []),
                   incs=[(sAA, 1)])
            S.step("vector",
                   lambda e, j=j, fg=fg: e.tensor_mul(
                       sqC[:, fg * NCK:(fg + 1) * NCK],
                       rel1[:, (j % 4) * NCK:(j % 4 + 1) * NCK], rel1[:, (j % 4) * NCK:(j % 4 + 1) * NCK]),
                   waits=[(sAA, None)] + ([(sVV_, tc * NCG)] if (fg == 0 and tc >= 1) else []),
                   incs=[(sSQ, 1)])
        for cg in range(NCG):
            jj = tc * NCG + cg
            slot = jj % 2
            S.step("gpsimd",
                   lambda e, cg=cg, slot=slot: e.dma_start(
                       out=wvcr[:, slot * 32 * 128:(slot + 1) * 32 * 128],
                       in_=wvcd[cg].rearrange("p ft j -> p (ft j)")),
                   waits=([(sVV_, jj - 1)] if jj >= 2 else []),
                   incs=[(sWV[slot], 16)])
            nwv[slot] += 16
            bv = pv_[jj % 2]
            for ft in range(32):
                S.step("tensor",
                       lambda e, ft=ft, bv=bv, slot=slot: e.matmul(
                           bv[:, :], wvcr[:, (slot * 32 + ft) * 128:(slot * 32 + ft + 1) * 128],
                           sqC[:, ft * NCK:(ft + 1) * NCK],
                           start=(ft == 0), stop=(ft == 31)),
                       waits=([(sWV[slot], nwv[slot]), (sSQ, (tc + 1) * 32)] if ft == 0 else []) +
                             ([(sVVE, jj - 1)] if (ft == 0 and jj >= 2) else []),
                       incs=([(sVV_, 1)] if ft == 31 else []))
            S.step("vector",
                   lambda e, jj=jj, bv=bv: e.tensor_copy(vvs[:, (jj % 2) * NCK:(jj % 2 + 1) * NCK], bv[:, :]),
                   waits=[(sVV_, jj + 1)] + ([(sVST, 16 * jj)] if jj >= 2 else []),
                   incs=[(sVVE, 1)])
            S.step("sync",
                   lambda e, tc=tc, cg=cg, jj=jj: e.dma_start(
                       out=vvT[cg * 128:(cg + 1) * 128, tc * NCK:(tc + 1) * NCK],
                       in_=vvs[:, (jj % 2) * NCK:(jj % 2 + 1) * NCK]),
                   waits=[(sVVE, jj + 1)],
                   incs=[(sVST, 16)])
    S.step("sync", lambda e: None, waits=[(sVST, 16 * NT * NCG)])
    return sVST, 16 * NT * NCG


def emit_a2_combine(P, S, io, gl, T, waits0=()):
    NT = T // NCK
    HALF = T // 2
    NHC = HALF // NCK

    def xg(i):
        return i * (T + 1) + 1
    tscr = io["tscr"]; wrcd = io["wrc"]
    vvT = io["vvT"]; xTd = io["xT"]; x1Td = io["x1T"]; s2Td = io["s2T"]
    onecf = gl["onecf"]

    wrr = P.sbuf("a2c_wrr", [128, 2 * KT * 128], BF16)
    rg1 = P.sbuf("a2c_rg1", [128, NCK], F32)
    rg2 = P.sbuf("a2c_rg2", [128, NCK], F32)
    xr0 = P.sbuf("a2c_xr0", [128, HALF], F32)
    xr1 = P.sbuf("a2c_xr1", [128, HALF], F32)
    vvr = P.sbuf("a2c_vvr", [128, HALF], F32)
    s2s = P.sbuf("a2c_s2s", [128, HALF], F32)
    pr = [P.psum(f"a2c_pr{i}", [128, NCK]) for i in range(2)]
    sWR = [S.sem(kind="g") for _ in range(2)]
    sRC = S.sem(); sRCC = S.sem(); sXR = S.sem(kind="d"); sST = S.sem(kind="d"); sAA = S.sem(); sVV = S.sem()
    nwr = [0] * 2
    nxr = 0
    for cg in range(NCG):
        slot = cg % 2
        S.step("gpsimd",
               lambda e, cg=cg, slot=slot: e.dma_start(
                   out=wrr[:, slot * KT * 128:(slot + 1) * KT * 128],
                   in_=wrcd[cg].rearrange("p kt j -> p (kt j)")),
               waits=([(sRCC, (cg - 1) * NT)] if cg >= 2 else []) + (list(waits0) if cg == 0 else []),
               incs=[(sWR[slot], 16)])
        nwr[slot] += 16
        for hf in range(2):
            ih = 2 * cg + hf
            hbase = hf * HALF
            S.step("sync",
                   lambda e, cg=cg, hbase=hbase: e.dma_start(
                       out=xr0[:], in_=xTd[cg * 128:(cg + 1) * 128, hbase:hbase + HALF]),
                   waits=([(sVV, 3 * NHC * ih)] if ih >= 1 else []) + (list(waits0) if ih == 0 else []),
                   incs=[(sXR, 16)])
            S.step("sync",
                   lambda e, cg=cg, hbase=hbase: e.dma_start(
                       out=xr1[:], in_=x1Td[cg * 128:(cg + 1) * 128, hbase:hbase + HALF]),
                   incs=[(sXR, 16)])
            S.step("sync",
                   lambda e, cg=cg, hbase=hbase: e.dma_start(
                       out=vvr[:], in_=vvT[cg * 128:(cg + 1) * 128, hbase:hbase + HALF]),
                   incs=[(sXR, 16)])
            nxr += 48
            for hc in range(NHC):
                tc = hf * NHC + hc
                j = cg * NT + tc
                bk = pr[j % 2]
                for kt in range(KT):
                    S.step("tensor",
                           lambda e, kt=kt, tc=tc, bk=bk, slot=slot: e.matmul(
                               bk[:, :], wrr[:, (slot * KT + kt) * 128:(slot * KT + kt + 1) * 128],
                               tscr[:, xg(kt) + tc * NCK: xg(kt) + (tc + 1) * NCK],
                               start=(kt == 0), stop=(kt == KT - 1)),
                           waits=([(sWR[slot], nwr[slot])] if kt == 0 else []) +
                                 ([(sRCC, j - 1)] if (kt == 0 and j >= 2) else []),
                           incs=([(sRC, 1)] if kt == KT - 1 else []))
                S.step("scalar", lambda e, bk=bk, j=j: e.activation(rg1[:], bk[:, :], AF.Exp, scale=-1.0),
                       waits=[(sRC, j + 1)] + ([(sVV, 3 * j - 2)] if j >= 1 else []), incs=[(sAA, 1)])
                S.step("scalar", lambda e: e.activation(rg2[:], rg1[:], AF.Ln, bias=onecf[:, 0:1]),
                       waits=[(sAA, None)], incs=[(sAA, 1)])
                S.step("scalar", lambda e: e.activation(rg1[:], rg2[:], AF.Exp, scale=-1.0),
                       waits=[(sAA, None)], incs=[(sRCC, 1)])
                sl = slice(hc * NCK, (hc + 1) * NCK)
                S.step("vector", lambda e, sl=sl: e.tensor_mul(s2s[:, sl], rg1[:], vvr[:, sl]),
                       waits=[(sRCC, None), (sXR, nxr)] + ([(sST, 16 * ih)] if (hc == 0 and ih >= 1) else []),
                       incs=[(sVV, 1)])
                S.step("vector", lambda e, sl=sl: e.tensor_add(s2s[:, sl], s2s[:, sl], xr0[:, sl]),
                       waits=[(sVV, None)], incs=[(sVV, 1)])
                S.step("vector", lambda e, sl=sl: e.tensor_add(s2s[:, sl], s2s[:, sl], xr1[:, sl]),
                       waits=[(sVV, None)], incs=[(sVV, 1)])
            S.step("sync",
                   lambda e, cg=cg, hbase=hbase: e.dma_start(
                       out=s2Td[cg * 128:(cg + 1) * 128, hbase:hbase + HALF], in_=s2s[:]),
                   waits=[(sVV, None)],
                   incs=[(sST, 16)])
    S.step("sync", lambda e: None, waits=[(sST, 16 * NCG * 2)])
    return sST, 16 * NCG * 2


def build_a2(T):
    P = Prog("a2")
    P.T = T
    nc = P.nc
    io = {
        "xT": nc.dram_tensor("xT", [C, T], F32, kind="ExternalInput").ap(),
        "x1T": nc.dram_tensor("x1T", [C, T], F32, kind="ExternalInput").ap(),
        "yT": nc.dram_tensor("yT", [C, T], F32, kind="ExternalOutput").ap(),
        "wkc": nc.dram_tensor("wkc", [32, 128, 8, 128], F32, kind="ExternalInput").ap(),
        "wvc": nc.dram_tensor("wvc", [8, 128, 32, 128], F32, kind="ExternalInput").ap(),
        "wrc": nc.dram_tensor("wrc", [8, 128, 8, 128], F32, kind="ExternalInput").ap(),
        "pv0": nc.dram_tensor("pv0", [128, 80], F32, kind="ExternalInput").ap(),
        "vvT": nc.dram_tensor("vvT", [C, T], F32, kind="Internal").ap(),
        "s2T": nc.dram_tensor("s2T", [C, T], F32, kind="Internal").ap(),
    }
    with P.es:
        P.sem_es = P.es
        gl = {
            "epsc": P.sbuf("g_eps", [1, 1], F32),
            "onecf": P.sbuf("g_one", [128, 1], F32),
            "onesr": P.sbuf("g_onesr", [1, 128], F32),
            "pvd2": P.sbuf("g_pvd2", [128, 16], F32),
        }
        P.gl = gl
        pvt = P.sbuf("g_pvt", [128, 80], F32)
        rowbufs = (P.sbuf("g_mrow", [1, T], F32), P.sbuf("g_rrow", [1, T], F32), P.sbuf("g_trow", [1, T // 2], F32))
        S = Script(P)
        sLD0 = S.sem()
        S.step("vector", lambda e: e.memset(gl["epsc"][:], EPS_LN))
        S.step("vector", lambda e: e.memset(gl["onecf"][:], 1.0))
        S.step("vector", lambda e: e.memset(gl["onesr"][:], 1.0))
        S.step("sync", lambda e: e.dma_start(out=pvt[:, 0:80], in_=io["pv0"]), incs=[(sLD0, 16)])
        S.step("vector", lambda e: e.tensor_scalar(gl["pvd2"][:, 0:8], pvt[:, 64:72], -1.0, 1.0, ALU.mult, ALU.add),
               waits=[sLD0])
        S.step("vector", lambda e: e.tensor_scalar(gl["pvd2"][:, 8:16], pvt[:, 72:80], -1.0, 1.0, ALU.mult, ALU.add))
        block = P.es.enter_context(nc.Block())
        with ExitStack() as sc0:
            P_es_old = P.es; P.es = sc0
            rowsem = emit_rows(P, S, "a2r", io["x1T"], T, rms=False, rowbufs=rowbufs)
            P.es = P_es_old
        io2 = dict(io)
        with ExitStack() as sc1:
            P_es_old = P.es; P.es = sc1
            tsck = P.sbuf("a2_tsck", [128, NCG * (T + 1)], BF16)
            sCk = emit_a2_tsbuild(P, S, "a2tk", io["x1T"], tsck, 64, rowbufs, rowsem, pvt,
                                  sST_prev=rowsem)
            io2["tsck"] = tsck
            svst, nvst = emit_a2_core(P, S, io2, gl, T, waits0=[(sCk, NCG)])
            P.es = P_es_old
        with ExitStack() as sc2:
            P_es_old = P.es; P.es = sc2
            tscr = P.sbuf("a2_tscr", [128, NCG * (T + 1)], BF16)
            sCr = emit_a2_tsbuild(P, S, "a2tr", io["x1T"], tscr, 72, rowbufs, rowsem, pvt,
                                  sST_prev=(svst, nvst))
            io2["tscr"] = tscr
            sstc, nstc = emit_a2_combine(P, S, io2, gl, T, waits0=[(sCr, NCG)])
            P.es = P_es_old
        with ExitStack() as sc4:
            P_es_old = P.es; P.es = sc4
            rowsem2 = emit_rows(P, S, "a2p", io["s2T"], T, rms=False, rowbufs=rowbufs,
                                waits0=[(sstc, nstc)])
            emit_normalize(P, S, "a2n", io["s2T"], io["yT"], T, rowbufs, rowsem2,
                           extra_src_waits=[(sstc, nstc)])
            P.es = P_es_old
        S.replay(block)
    return nc



def emit_b1_h1(P, S, io, gl, T, rowbufs, rowsem, lnwsem=None, pref="b1h"):
    NT = T // NCK
    yTd = io["yT"]; h1Td = io["h1T"]; lw = io["lnw_col"]
    onesr = gl["onesr"]
    mrow, rrow, _ = rowbufs
    semv, semn = rowsem
    yb = P.sbuf(pref + "_yb", [128, 2 * P.T], BF16)
    h1t = P.sbuf(pref + "_h1", [128, P.T], BF16)
    pbc = [P.psum(pref + f"_pb{i}", [128, NCK]) for i in range(2)]
    sY = [S.sem(kind="g"), S.sem(kind="g")]; sBC = S.sem(); sBCC = S.sem(); sVV = S.sem(); sST = S.sem(kind="d")
    nb = 0
    for cg in range(NCG):
        S.step("gpsimd",
               lambda e, cg=cg: e.dma_start(out=yb[:, (cg % 2) * T:(cg % 2 + 1) * T],
                                            in_=yTd[cg * 128:(cg + 1) * 128, :]),
               waits=([rowsem] if cg == 0 else []) + ([(sST, 16 * (cg - 1))] if cg >= 2 else []),
               incs=[(sY[cg % 2], 16)])
        for ck in range(NT):
            bs = nb % 2
            S.step("tensor",
                   lambda e, ck=ck, bs=bs: e.matmul(pbc[bs][:, :], onesr[:], rrow[0:1, ck * NCK:(ck + 1) * NCK],
                                                    start=True, stop=True),
                   waits=([(semv, semn)] if nb == 0 else []) + ([(sBCC, nb - 1)] if nb >= 2 else []),
                   incs=[(sBC, 1)])
            nb += 1
            S.step("vector",
                   lambda e, ck=ck, bs=bs, cg=cg: e.tensor_mul(h1t[:, ck * NCK:(ck + 1) * NCK],
                                                               yb[:, (cg % 2) * T + ck * NCK:(cg % 2) * T + (ck + 1) * NCK],
                                                               pbc[bs][:, :]),
                   waits=[(sY[cg % 2], 16 * (cg // 2 + 1)), (sBC, nb)] +
                         ([(sST, 16 * cg)] if (ck == 0 and cg >= 1) else []),
                   incs=[(sVV, 1)])
            S.step("vector",
                   lambda e, ck=ck, cg=cg: e.tensor_scalar(h1t[:, ck * NCK:(ck + 1) * NCK],
                                                           h1t[:, ck * NCK:(ck + 1) * NCK],
                                                           lw[:, cg:cg + 1], None, ALU.mult),
                   waits=[(sVV, None)] + ([lnwsem] if (lnwsem and cg == 0 and ck == 0) else []),
                   incs=[(sBCC, 1)])
        S.step("sync",
               lambda e, cg=cg: e.dma_start(out=h1Td[cg * 128:(cg + 1) * 128, :], in_=h1t[:, :]),
               waits=[(sBCC, nb)],
               incs=[(sST, 16)])
    S.step("sync", lambda e: None, waits=[(sST, 16 * NCG)])
    return sST, 16 * NCG


def emit_b1_qkv(P, S, io, gl, T, waits0=()):
    NT = T // NCK
    h1Td = io["h1T"]; wq = io["wqkv_t"]
    Qr, Kr, Vb, VT = io["Qr"], io["Kr"], io["Vb"], io["VT"]
    cosb, sinb = io["cosb"], io["sinb"]
    ident = io["identb"]
    hr = P.sbuf("b1q_hr", [128, KT * NCK], BF16)
    wqs = P.sbuf("b1q_wqs", [128, 12 * KT * 128], BF16)
    wrs = P.sbuf("b1q_wrs", [128, 10 * KT * 128], BF16)
    t1 = P.sbuf("b1q_t1", [128, NCK], F32)
    t2 = P.sbuf("b1q_t2", [128, NCK], F32)
    pA = [P.psum(f"b1q_pA{i}", [128, NCK]) for i in range(2)]
    pB = [P.psum(f"b1q_pB{i}", [128, NCK]) for i in range(2)]
    pT = P.psum("b1q_pT", [128, 64], BF16)
    sWQ = S.sem(kind="g"); sH = [S.sem(kind="d"), S.sem(kind="d")]; sMM = S.sem(); sRP = S.sem(); sVV = S.sem()
    sTR = S.sem(); sTRE = S.sem(); sAA = S.sem()
    S.step("gpsimd", lambda e: e.dma_start(
               out=wqs[:].rearrange("p (hq kt j) -> p hq kt j", kt=KT, j=128),
               in_=wq.rearrange("hq p kt j -> p hq kt j")),
           waits=list(waits0), incs=[(sWQ, 16)])
    S.step("vector", lambda e: e.tensor_scalar(wqs[:, 0:8 * KT * 128], wqs[:, 0:8 * KT * 128], 0.125, None, ALU.mult),
           waits=[(sWQ, 16)], incs=[(sVV, 1)])
    for hq in range(10):
        for kt in range(KT):
            o = (hq * KT + kt) * 128
            S.step("vector", lambda e, o=o: e.tensor_scalar(wrs[:, o:o + 32], wqs[:, o + 32:o + 64], -1.0, None, ALU.mult),
                   waits=[(sVV, None)])
            S.step("vector", lambda e, o=o: e.tensor_copy(wrs[:, o + 32:o + 64], wqs[:, o:o + 32]))
            S.step("vector", lambda e, o=o: e.tensor_scalar(wrs[:, o + 64:o + 96], wqs[:, o + 96:o + 128], -1.0, None, ALU.mult))
            if not (hq == 9 and kt == KT - 1):
                S.step("vector", lambda e, o=o: e.tensor_copy(wrs[:, o + 96:o + 128], wqs[:, o + 64:o + 96]))
    o9 = (9 * KT + (KT - 1)) * 128
    S.step("vector", lambda e, o9=o9: e.tensor_copy(wrs[:, o9 + 96:o9 + 128], wqs[:, o9 + 64:o9 + 96]),
           incs=[(sRP, 1)])
    nmm = 0
    for tc in range(NT):
        hslot = 0
        S.step("sync",
               lambda e, tc=tc: e.dma_start(
                   out=hr[:, 0:KT * NCK].rearrange("p (kt n) -> p kt n", n=NCK),
                   in_=h1Td[:, tc * NCK:(tc + 1) * NCK].rearrange("(kt p) n -> p kt n", p=128)),
               waits=([(sMM, tc * 12)] if tc >= 1 else []),
               incs=[(sH[0], 16)])
        for hq in range(12):
            ia = (tc * 12 + hq)
            bA = pA[ia % 2]; bB = pB[ia % 2]
            rope = hq < 10
            for kt in range(KT):
                S.step("tensor",
                       lambda e, hq=hq, kt=kt, bA=bA, hslot=hslot: e.matmul(
                           bA[:, :], wqs[:, (hq * KT + kt) * 128:(hq * KT + kt + 1) * 128],
                           hr[:, hslot * KT * NCK + kt * NCK: hslot * KT * NCK + (kt + 1) * NCK],
                           start=(kt == 0), stop=(kt == KT - 1)),
                       waits=([(sH[0], 16 * (tc + 1)), (sRP, 1)] if kt == 0 else []) +
                             ([(sVV, 1 + 3 * (ia - 1))] if (kt == 0 and ia >= 2) else []))
            if rope:
                for kt in range(KT):
                    S.step("tensor",
                           lambda e, hq=hq, kt=kt, bB=bB, hslot=hslot: e.matmul(
                               bB[:, :], wrs[:, (hq * KT + kt) * 128:(hq * KT + kt + 1) * 128],
                               hr[:, hslot * KT * NCK + kt * NCK: hslot * KT * NCK + (kt + 1) * NCK],
                               start=(kt == 0), stop=(kt == KT - 1)),
                           incs=([(sMM, 1)] if kt == KT - 1 else []))
            else:
                S.step("tensor", lambda e, bB=bB: e.matmul(bB[:, 0:1], wqs[:, 0:128], wqs[:, 0:1],
                                                           start=True, stop=True),
                       incs=[(sMM, 1)])
            sl = slice(tc * NCK, (tc + 1) * NCK)
            if rope:
                dst = Qr if hq < 8 else Kr
                do = hq if hq < 8 else (hq - 8)
                S.step("vector",
                       lambda e, bA=bA, sl=sl: e.tensor_mul(t1[:], bA[:, :], cosb[:, sl]),
                       waits=[(sMM, ia + 1), (sVV, None)], incs=[(sVV, 1)])
                S.step("vector",
                       lambda e, bB=bB, sl=sl: e.tensor_mul(t2[:], bB[:, :], sinb[:, sl]),
                       waits=[(sVV, None)], incs=[(sVV, 1)])
                S.step("vector",
                       lambda e, dst=dst, do=do, tc=tc: e.tensor_add(
                           dst[:, do * T + tc * NCK: do * T + (tc + 1) * NCK], t1[:], t2[:]),
                       waits=[(sVV, None)], incs=[(sVV, 1)])
            else:
                do = hq - 10
                S.step("vector",
                       lambda e, bA=bA, do=do, tc=tc: e.tensor_copy(
                           Vb[:, do * T + tc * NCK: do * T + (tc + 1) * NCK], bA[:, :]),
                       waits=[(sMM, ia + 1)], incs=[(sVV, 3)])
            nmm += 1
    S.step("vector", lambda e: e.memset(VT[:], 1.0), waits=[(sVV, None)], incs=[(sVV, 1)])
    ntr = 0
    for kvh in range(4):
        vt = kvh // 2
        pbase = (kvh % 2) * 64
        for kt in range(T // 128):
            S.step("tensor",
                   lambda e, vt=vt, pbase=pbase, kt=kt: e.transpose(
                       pT[:, :], Vb[pbase:pbase + 64, vt * T + kt * 128: vt * T + (kt + 1) * 128],
                       ident[pbase:pbase + 64, 0:64]),
                   waits=[(sVV, None)] + ([(sTRE, ntr)] if ntr >= 1 else []),
                   incs=[(sTR, 1)])
            S.step("vector",
                   lambda e, kvh=kvh, kt=kt: e.tensor_copy(
                       VT[:, (kvh * (T // 128) + kt) * 65:(kvh * (T // 128) + kt) * 65 + 64], pT[:, :]),
                   waits=[(sTR, ntr + 1)], incs=[(sTRE, 1)])
            ntr += 1
    S.step("vector", lambda e: e.tensor_copy(t1[:, 0:1], t1[:, 0:1]), waits=[(sTRE, ntr)], incs=[(sRP, 1)])
    return sRP, 2


def emit_b1_attn(P, S, io, gl, T, qkvsem):
    NT = T // NCK
    NKT = T // 128
    Qr, Kr, VT = io["Qr"], io["Kr"], io["VT"]
    attTd = io["attT"]
    ATTs = P.sbuf("b1a_ATTs", [128, P.T], BF16)
    sATS = S.sem(kind="d")
    masks = io["masksb"]
    onesr = gl["onesr"]
    onesbc = io["onesbc"]
    onesf = io["onesf"]
    sqq = P.sbuf("b1a_sqq", [128, P.T], BF16)
    hsc = P.sbuf("b1a_hsc", [1, 64], F32)
    nmc = P.sbuf("b1a_nmc", [128, 16], F32)
    Pt = P.sbuf("b1a_pt", [128, 4 * NCK], BF16)
    dnr = P.sbuf("b1a_dnr", [128, NCK], F32)
    bcr = P.sbuf("b1a_bcr", [128, NCK], F32)
    pS = [P.psum(f"b1a_pS{i}", [128, NCK]) for i in range(3)]
    pO = [P.psum(f"b1a_pO{i}", [65, NCK]) for i in range(2)]
    pR = P.psum("b1a_pR", [128, NCK])
    pN = P.psum("b1a_pN", [1, NCK])
    sNB = S.sem(); sNBE = S.sem(); sSB = S.sem(); sSE = S.sem(); sPVk = S.sem()
    sDN = S.sem(); sAA = S.sem(); sVV = S.sem(); sOE = S.sem(); sBC2 = S.sem(); sBCE = S.sem()

    nq = 0
    for src_, nh, base in ((Qr, 16, 0), (Kr, 4, 16)):
        npairs = nh // 2
        for hp in range(npairs):
            S.step("vector",
                   lambda e, src_=src_, hp=hp: e.tensor_mul(sqq[:], src_[:, hp * T:(hp + 1) * T],
                                                            src_[:, hp * T:(hp + 1) * T]),
                   waits=([qkvsem] if (hp == 0 and base == 0) else []) + [(sVV, None)], incs=[(sVV, 1)])
            for sub in range(2):
                h = 2 * hp + sub
                for ck in range(NT):
                    S.step("tensor",
                           lambda e, sub=sub, ck=ck: e.matmul(
                               pN[0:1, :], onesbc[sub * 64:sub * 64 + 64, 0:1],
                               sqq[sub * 64:sub * 64 + 64, ck * NCK:(ck + 1) * NCK],
                               start=True, stop=True),
                           waits=[(sVV, None)] + ([(sNBE, nq)] if nq >= 1 else []),
                           incs=[(sNB, 1)])
                    nq += 1
                    S.step("vector",
                           lambda e, h=h, ck=ck, base=base: e.tensor_reduce(
                               hsc[0:1, (base + h):(base + h + 1)] if ck == 0 else hsc[0:1, 48:49],
                               pN[0:1, :], axis=AX.X, op=ALU.max),
                           waits=[(sNB, nq)], incs=[(sNBE, 1)])
                    if ck > 0:
                        S.step("vector",
                               lambda e, h=h, base=base: e.tensor_tensor(
                                   hsc[0:1, (base + h):(base + h + 1)], hsc[0:1, (base + h):(base + h + 1)],
                                   hsc[0:1, 48:49], ALU.max),
                               waits=[(sNBE, None)], incs=[(sVV, 1)])
    S.step("vector", lambda e: e.tensor_copy(hsc[0:1, 48:49], hsc[0:1, 48:49]),
           waits=[(sNBE, nq), (sVV, None)], incs=[(sVV, 1)])
    S.step("scalar", lambda e: e.activation(hsc[0:1, 20:40], hsc[0:1, 0:20], AF.Ln),
           waits=[(sVV, None)], incs=[(sAA, 1)])
    S.step("scalar", lambda e: e.activation(hsc[0:1, 0:20], hsc[0:1, 20:40], AF.Exp, scale=0.5),
           waits=[sAA], incs=[(sDN, 1)])
    for h in range(16):
        S.step("vector",
               lambda e, h=h: e.tensor_mul(hsc[0:1, 32 + h:33 + h], hsc[0:1, h:h + 1],
                                           hsc[0:1, 16 + h // 4:17 + h // 4]),
               waits=[(sDN, 1)] if h == 0 else [], incs=([(sVV, 1)] if h == 15 else []))
    S.step("vector", lambda e: e.tensor_scalar(hsc[0:1, 32:48], hsc[0:1, 32:48], -1.0, None, ALU.mult),
           waits=[(sVV, None)], incs=[(sVV, 1)])
    for h in range(16):
        S.step("tensor",
               lambda e, h=h: e.matmul(pR[:, 0:1], onesr[:], hsc[0:1, 32 + h:33 + h], start=True, stop=True),
               waits=[(sVV, None)] + ([(sBCE, h)] if h >= 1 else []),
               incs=[(sNB, 1)])
        S.step("vector", lambda e, h=h: e.tensor_copy(nmc[:, h:h + 1], pR[:, 0:1]),
               waits=[(sNB, None)], incs=[(sBCE, 1)])

    nse = 0
    npvk = 0
    nob = 0
    for qh in range(16):
        kvh = qh // 4
        vt = kvh // 2
        kbase = (kvh % 2) * 64
        qt = (qh % 4) + 4 * (qh // 8)
        qbase = ((qh // 4) % 2) * 64
        for tc in range(NT):
            nkt = 4 * (tc + 1)
            iob = qh * NT + tc
            ob = pO[iob % 2]
            for kt in range(nkt):
                sb = pS[nse % 3]
                S.step("tensor",
                       lambda e, kt=kt, tc=tc, sb=sb, qt=qt, qbase=qbase, vt=vt, kbase=kbase: e.matmul(
                           sb[:, :], Kr[kbase:kbase + 64, vt * T + kt * 128: vt * T + (kt + 1) * 128],
                           Qr[qbase:qbase + 64, qt * T + tc * NCK: qt * T + (tc + 1) * NCK],
                           start=True, stop=True),
                       waits=([(sBCE, 16)] if nse == 0 else []) + ([(sSE, nse - 2)] if nse >= 3 else []),
                       incs=[(sSB, 1)])
                if kt >= 4 * tc:
                    S.step("vector",
                           lambda e, sb=sb, kt=kt, tc=tc: e.tensor_add(
                               sb[:, :], sb[:, :], masks[:, (kt - 4 * tc) * NCK:(kt - 4 * tc + 1) * NCK]),
                           waits=[(sSB, nse + 1)], incs=[(sVV, 1)])
                    ewaits = [(sVV, None)]
                else:
                    ewaits = [(sSB, nse + 1)]
                S.step("scalar",
                       lambda e, sb=sb, qh=qh, i=nse: e.activation(
                           Pt[:, (i % 4) * NCK:(i % 4 + 1) * NCK], sb[:, :], AF.Exp, bias=nmc[:, qh:qh + 1]),
                       waits=ewaits + ([(sPVk, npvk - 3)] if npvk >= 4 else []),
                       incs=[(sSE, 1)])
                S.step("tensor",
                       lambda e, ob=ob, kvh=kvh, kt=kt, i=nse, nkt=nkt: e.matmul(
                           ob[:, :], VT[:, (kvh * NKT + kt) * 65:(kvh * NKT + kt + 1) * 65],
                           Pt[:, (i % 4) * NCK:(i % 4 + 1) * NCK],
                           start=(kt == 0), stop=(kt == nkt - 1)),
                       waits=[(sSE, nse + 1)] + ([(sOE, iob - 1)] if (kt == 0 and iob >= 2) else []),
                       incs=[(sPVk, 1)])
                nse += 1
                npvk += 1
            S.step("scalar", lambda e, ob=ob: e.activation(dnr[64:65, :], ob[64:65, :], AF.Ln),
                   waits=[(sPVk, npvk)], incs=[(sAA, 1)])
            S.step("scalar", lambda e: e.activation(bcr[64:65, :], dnr[64:65, :], AF.Exp, scale=-1.0),
                   waits=[sAA], incs=[(sDN, 1)])
            S.step("tensor",
                   lambda e: e.matmul(pR[0:64, :], onesf[64:65, 0:64], bcr[64:65, :], start=True, stop=True),
                   waits=[(sDN, None)] + ([(sBC2, iob)] if iob >= 1 else []),
                   incs=[(sNB, 1)])
            S.step("vector", lambda e: e.tensor_copy(bcr[0:64, :], pR[0:64, :]),
                   waits=[(sNB, None)], incs=[(sBC2, 1)])
            S.step("vector",
                   lambda e, ob=ob, tc=tc: e.tensor_mul(
                       ATTs[0:64, tc * NCK:(tc + 1) * NCK], ob[0:64, :], bcr[0:64, :]),
                   waits=[(sBC2, None)] + ([(sATS, 16 * qh)] if (tc == 0 and qh >= 1) else []),
                   incs=[(sOE, 1)])
            nob += 1
        S.step("sync",
               lambda e, qt=qt, qbase=qbase: e.dma_start(
                   out=attTd[qt * 128 + qbase: qt * 128 + qbase + 64, :], in_=ATTs[0:64, :]),
               waits=[(sOE, nob)], incs=[(sATS, 16)])
    S.step("sync", lambda e: None, waits=[(sATS, 16 * 16)])
    return sATS, 16 * 16


def emit_b1_wo(P, S, io, gl, T, attsem):
    NT = T // NCK
    attTd = io["attT"]; wod = io["wo_t"]; yTd = io["yT"]; x2Td = io["x2T"]
    wos = P.sbuf("b1w_wos", [128, KT * KT * 128], BF16)
    atr = P.sbuf("b1w_atr", [128, KT * NCK], BF16)
    yr = P.sbuf("b1w_yr", [128, KT * NCK], F32)
    x2s = P.sbuf("b1w_x2s", [128, KT * NCK], F32)
    pW = [P.psum(f"b1w_p{i}", [128, NCK]) for i in range(2)]
    sWL = S.sem(kind="g"); sAT = S.sem(kind="d"); sY = S.sem(kind="d"); sMM = S.sem(); sVV = S.sem(); sST = S.sem(kind="d")
    S.step("gpsimd", lambda e: e.dma_start(
               out=wos[:].rearrange("p (cg kt j) -> p cg kt j", kt=KT, j=128),
               in_=wod.rearrange("cg p kt j -> p cg kt j")),
           waits=[attsem], incs=[(sWL, 16)])
    for tc in range(NT):
        S.step("sync",
               lambda e, tc=tc: e.dma_start(
                   out=atr[:].rearrange("p (kt n) -> p kt n", n=NCK),
                   in_=attTd[:, tc * NCK:(tc + 1) * NCK].rearrange("(kt p) n -> p kt n", p=128)),
               waits=([(sMM, tc * NCG)] if tc >= 1 else []) + ([attsem] if tc == 0 else []),
               incs=[(sAT, 16)])
        S.step("sync",
               lambda e, tc=tc: e.dma_start(
                   out=yr[:].rearrange("p (cg n) -> p cg n", n=NCK),
                   in_=yTd[:, tc * NCK:(tc + 1) * NCK].rearrange("(cg p) n -> p cg n", p=128)),
               waits=([(sVV, tc * NCG)] if tc >= 1 else []),
               incs=[(sY, 16)])
        for cg in range(NCG):
            j = tc * NCG + cg
            b = pW[j % 2]
            for kt in range(KT):
                S.step("tensor",
                       lambda e, cg=cg, kt=kt, b=b: e.matmul(
                           b[:, :], wos[:, (cg * KT + kt) * 128:(cg * KT + kt + 1) * 128],
                           atr[:, kt * NCK:(kt + 1) * NCK],
                           start=(kt == 0), stop=(kt == KT - 1)),
                       waits=([(sWL, 16), (sAT, 16 * (tc + 1))] if kt == 0 and cg == 0 else []) +
                             ([(sVV, j - 1)] if (kt == 0 and j >= 2) else []),
                       incs=([(sMM, 1)] if kt == KT - 1 else []))
            S.step("vector", lambda e, cg=cg, b=b: e.tensor_add(
                       x2s[:, cg * NCK:(cg + 1) * NCK], b[:, :], yr[:, cg * NCK:(cg + 1) * NCK]),
                   waits=[(sMM, j + 1), (sY, 16 * (tc + 1))] +
                         ([(sST, 16 * tc)] if (cg == 0 and tc >= 1) else []),
                   incs=[(sVV, 1)])
        S.step("sync",
               lambda e, tc=tc: e.dma_start(
                   out=x2Td[:, tc * NCK:(tc + 1) * NCK].rearrange("(cg p) n -> p cg n", p=128),
                   in_=x2s[:].rearrange("p (cg n) -> p cg n", n=NCK)),
               waits=[(sVV, (tc + 1) * NCG)],
               incs=[(sST, 16)])
    S.step("sync", lambda e: None, waits=[(sST, 16 * NT)])
    return sST, 16 * NT


def build_b1(T):
    P = Prog("b1")
    P.T = T
    nc = P.nc
    NKT = T // 128
    io = {
        "yT": nc.dram_tensor("yT", [C, T], F32, kind="ExternalInput").ap(),
        "x2T": nc.dram_tensor("x2T", [C, T], F32, kind="ExternalOutput").ap(),
        "wqkv_t": nc.dram_tensor("wqkv_t", [12, 128, 8, 128], F32, kind="ExternalInput").ap(),
        "wo_t": nc.dram_tensor("wo_t", [8, 128, 8, 128], F32, kind="ExternalInput").ap(),
        "lnw": nc.dram_tensor("lnw", [128, 8], F32, kind="ExternalInput").ap(),
        "cosb_d": nc.dram_tensor("cosb_d", [128, T], F32, kind="ExternalInput").ap(),
        "sinb_d": nc.dram_tensor("sinb_d", [128, T], F32, kind="ExternalInput").ap(),
        "masks_d": nc.dram_tensor("masks_d", [128, 4 * NCK], F32, kind="ExternalInput").ap(),
        "ident_d": nc.dram_tensor("ident_d", [128, 128], F32, kind="ExternalInput").ap(),
        "h1T": nc.dram_tensor("h1T", [C, T], BF16, kind="Internal").ap(),
        "attT": nc.dram_tensor("attT", [C, T], BF16, kind="Internal").ap(),
    }
    with P.es:
        P.sem_es = P.es
        gl = {
            "epsc": P.sbuf("g_eps", [1, 1], F32),
            "onecf": P.sbuf("g_one", [128, 1], F32),
            "onesr": P.sbuf("g_onesr", [1, 128], F32),
        }
        P.gl = gl
        mrow_d = P.sbuf("g_mrow", [1, 1], F32)
        rrow_g = P.sbuf("g_rrow", [1, T], F32)
        lnwc = P.sbuf("g_lnw", [128, 8], F32)
        S = Script(P)
        sLD0 = S.sem()
        S.step("vector", lambda e: e.memset(gl["epsc"][:], EPS_LN))
        S.step("vector", lambda e: e.memset(gl["onecf"][:], 1.0))
        S.step("vector", lambda e: e.memset(gl["onesr"][:], 1.0))
        S.step("sync", lambda e: e.dma_start(out=lnwc[:], in_=io["lnw"]), incs=[(sLD0, 16)])
        block = P.es.enter_context(nc.Block())
        with ExitStack() as sc0:
            P_es_old = P.es; P.es = sc0
            rowbufs = (mrow_d, rrow_g, P.sbuf("g_trow", [1, T // 2], F32))
            rowsem = emit_rows(P, S, "b1r", io["yT"], T, rms=True, rowbufs=rowbufs)
            P.es = P_es_old
        io2 = dict(io)
        io2["lnw_col"] = lnwc
        with ExitStack() as sc1:
            P_es_old = P.es; P.es = sc1
            sh1, nh1 = emit_b1_h1(P, S, io2, gl, T, rowbufs, rowsem, lnwsem=(sLD0, 16))
            P.es = P_es_old
        with ExitStack() as scQKA:
            P_es_qka = P.es; P.es = scQKA
            Qr = P.sbuf("b1_Qr", [128, 8 * T], BF16)
            Kr = P.sbuf("b1_Kr", [128, 2 * T], BF16)
            io2["Qr"] = Qr; io2["Kr"] = Kr
            VT = P.sbuf("b1_VT", [128, 4 * NKT * 65], BF16)
            io2["VT"] = VT
            with ExitStack() as sc2:
                P_es_old = P.es; P.es = sc2
                cosb = P.sbuf("b1_cosb", [128, T], BF16)
                sinb = P.sbuf("b1_sinb", [128, T], BF16)
                identb = P.sbuf("b1_identb", [128, 128], BF16)
                io2["cosb"] = cosb; io2["sinb"] = sinb; io2["identb"] = identb
                sC = S.sem()
                S.step("gpsimd", lambda e: e.dma_start(out=cosb[:], in_=io["cosb_d"]),
                       waits=[(sh1, nh1)], incs=[(sC, 16)])
                S.step("gpsimd", lambda e: e.dma_start(out=sinb[:], in_=io["sinb_d"]), incs=[(sC, 16)])
                S.step("gpsimd", lambda e: e.dma_start(out=identb[:], in_=io["ident_d"]), incs=[(sC, 16)])
                Vb = P.sbuf("b1_Vb", [128, 2 * T], BF16)
                io2["Vb"] = Vb
                qkvsem = emit_b1_qkv(P, S, io2, gl, T, waits0=[(sh1, nh1), (sC, 48)])
                P.es = P_es_old
            with ExitStack() as sc3:
                P_es_old = P.es; P.es = sc3
                masksb = P.sbuf("b1_masks", [128, 4 * NCK], F32)
                onesbc = P.sbuf("b1_onesbc", [128, 1], BF16)
                onesf = P.sbuf("b1_onesf", [128, 64], F32)
                io2["masksb"] = masksb
                io2["onesbc"] = onesbc
                io2["onesf"] = onesf
                sM = S.sem()
                S.step("sync", lambda e: e.dma_start(out=masksb[:], in_=io["masks_d"]), waits=[qkvsem],
                       incs=[(sM, 16)])
                S.step("vector", lambda e: e.memset(onesbc[:], 1.0), waits=[(sM, 16), qkvsem])
                S.step("vector", lambda e: e.memset(onesf[:], 1.0))
                attsem = emit_b1_attn(P, S, io2, gl, T, qkvsem)
                P.es = P_es_old
            P.es = P_es_qka
        with ExitStack() as sc4:
            P_es_old = P.es; P.es = sc4
            emit_b1_wo(P, S, io2, gl, T, attsem)
            P.es = P_es_old
        S.replay(block)
    return nc


_HEADLIST = [0, 4, 1, 5, 2, 6, 3, 7, 8, 12, 9, 13, 10, 14, 11, 15]

def prep_b1(inputs, T):
    g = {}
    wqkv = np.asarray(inputs["L0_ffn_wqkv"])
    wo = np.asarray(inputs["L0_ffn_wo"])
    qcols = np.concatenate([np.arange(h * 64, (h + 1) * 64) for h in _HEADLIST] +
                           [np.arange(1024, 1536)])
    wqkvT = np.ascontiguousarray(wqkv.T[:, qcols])
    g["wqkv_t"] = _tile_colblocks(wqkvT)
    orows = np.concatenate([np.arange(h * 64, (h + 1) * 64) for h in _HEADLIST])
    woT = np.ascontiguousarray(wo.T[orows, :])
    g["wo_t"] = _tile_colblocks(woT)
    g["lnw"] = _colmajor_vec(inputs["L0_ffn_ln1_w"])
    hd = 64
    inv = 1.0 / (10000.0 ** (np.arange(0, hd, 2) / hd))
    t = np.arange(T)[:, None]
    fr = t * inv[None, :]
    emb = np.concatenate([fr, fr], axis=1)
    cos = np.cos(emb).T.astype(np.float32)
    sin = np.sin(emb).T.astype(np.float32)
    g["cosb_d"] = np.ascontiguousarray(np.vstack([cos, cos]))
    g["sinb_d"] = np.ascontiguousarray(np.vstack([sin, sin]))
    i_ = np.arange(128)[:, None]
    j_ = np.arange(NCK)[None, :]
    masks = np.concatenate([np.where(128 * o + i_ <= j_, 0.0, -1e30) for o in range(4)],
                           axis=1).astype(np.float32)
    g["masks_d"] = np.ascontiguousarray(masks)
    g["ident_d"] = np.ascontiguousarray(
        np.hstack([np.tile(np.eye(64, dtype=np.float32), (2, 1)), np.zeros((128, 64), np.float32)]))
    return g



def emit_b2_gates(P, S, io, gl, T, waits0=()):
    NT = T // NCK
    h2Td = io["h2T"]; gwd = io["gw_t"]; identf = io["identf"]
    g8all = io["g8all"]
    onesr = gl["onesr"]
    gws = P.sbuf("b2g_gws", [128, 64], BF16)
    hr = P.sbuf("b2g_hr", [128, KT * NCK], BF16)
    lg8 = P.sbuf("b2g_lg8", [8, NCK], F32)
    ltt = P.sbuf("b2g_ltt", [128, 8], F32)
    gtt = P.sbuf("b2g_gtt", [128, 8], F32)
    tf1 = P.sbuf("b2g_tf1", [128, 8], F32)
    c0 = P.sbuf("b2g_c0", [128, 4], F32)
    pL = P.psum("b2g_pL", [8, NCK])
    pT1 = P.psum("b2g_pT1", [128, 8])
    pT2 = P.psum("b2g_pT2", [8, 128])
    sW = S.sem(kind="g"); sH = S.sem(kind="d"); sMM = S.sem(); sLE = S.sem(); sT1 = S.sem(); sT1E = S.sem()
    sT2 = S.sem(); sT2E = S.sem(); sVV = S.sem(); sAA = S.sem()
    S.step("gpsimd", lambda e: e.dma_start(out=gws[:], in_=gwd), waits=list(waits0), incs=[(sW, 16)])
    nt1 = 0
    for tc in range(NT):
        S.step("sync",
               lambda e, tc=tc: e.dma_start(
                   out=hr[:].rearrange("p (kt n) -> p kt n", n=NCK),
                   in_=h2Td[:, tc * NCK:(tc + 1) * NCK].rearrange("(kt p) n -> p kt n", p=128)),
               waits=([(sMM, tc)] if tc >= 1 else []) + (list(waits0) if tc == 0 else []),
               incs=[(sH, 16)])
        for kt in range(KT):
            S.step("tensor",
                   lambda e, kt=kt: e.matmul(pL[:, :], gws[:, kt * 8:(kt + 1) * 8],
                                             hr[:, kt * NCK:(kt + 1) * NCK],
                                             start=(kt == 0), stop=(kt == KT - 1)),
                   waits=([(sW, 16), (sH, 16 * (tc + 1))] if kt == 0 else []) +
                         ([(sLE, tc)] if (kt == 0 and tc >= 1) else []),
                   incs=([(sMM, 1)] if kt == KT - 1 else []))
        S.step("vector", lambda e: e.tensor_copy(lg8[:, :], pL[:, :]),
               waits=[(sMM, tc + 1), (sVV, None)], incs=[(sLE, 1)])
        for tt in range(NCK // 128):
            gt = tc * (NCK // 128) + tt
            S.step("tensor",
                   lambda e, tt=tt: e.transpose(pT1[:, :], lg8[:, tt * 128:(tt + 1) * 128], identf[0:8, 0:8]),
                   waits=[(sLE, tc + 1)] + ([(sT1E, nt1)] if nt1 >= 1 else []),
                   incs=[(sT1, 1)])
            nt1 += 1
            S.step("vector", lambda e: e.tensor_copy(ltt[:, :], pT1[:, :]),
                   waits=[(sT1, nt1)], incs=[(sT1E, 1)])
            S.step("vector", lambda e: e.tensor_reduce(c0[:, 0:1], ltt[:, :], axis=AX.X, op=ALU.max),
                   waits=[(sT1E, None)], incs=[(sVV, 1)])
            S.step("vector", lambda e: e.tensor_scalar(tf1[:, :], ltt[:, :], c0[:, 0:1], None, ALU.is_ge),
                   waits=[(sVV, None)], incs=[(sVV, 1)])
            S.step("vector", lambda e: e.scalar_tensor_tensor(tf1[:, :], tf1[:, :], -1e30, ltt[:, :],
                                                              ALU.mult, ALU.add),
                   waits=[(sVV, None)], incs=[(sVV, 1)])
            S.step("vector", lambda e: e.tensor_reduce(c0[:, 1:2], tf1[:, :], axis=AX.X, op=ALU.max),
                   waits=[(sVV, None)], incs=[(sVV, 1)])
            S.step("vector", lambda e: e.tensor_scalar(c0[:, 2:3], c0[:, 0:1], -1.0, None, ALU.mult),
                   waits=[(sVV, None)], incs=[(sVV, 1)])
            S.step("scalar", lambda e: e.activation(gtt[:, :], ltt[:, :], AF.Exp, bias=c0[:, 2:3]),
                   waits=[(sVV, None)], incs=[(sAA, 1)])
            S.step("vector", lambda e: e.tensor_scalar(tf1[:, :], ltt[:, :], c0[:, 1:2], None, ALU.is_ge),
                   waits=[(sAA, None)], incs=[(sVV, 1)])
            S.step("vector", lambda e: e.tensor_mul(gtt[:, :], gtt[:, :], tf1[:, :]),
                   waits=[(sVV, None)], incs=[(sVV, 1)])
            S.step("vector", lambda e: e.tensor_reduce(c0[:, 3:4], gtt[:, :], axis=AX.X, op=ALU.add),
                   waits=[(sVV, None)], incs=[(sVV, 1)])
            S.step("vector", lambda e: e.reciprocal(c0[:, 2:3], c0[:, 3:4]),
                   waits=[(sVV, None)], incs=[(sVV, 1)])
            S.step("vector", lambda e: e.tensor_scalar(gtt[:, :], gtt[:, :], c0[:, 2:3], None, ALU.mult),
                   waits=[(sVV, None)], incs=[(sVV, 1)])
            S.step("tensor",
                   lambda e: e.transpose(pT2[:, :], gtt[:, :], identf[:, :]),
                   waits=[(sVV, None)] + ([(sT2E, S.n(sT2E))] if S.n(sT2) >= 1 else []),
                   incs=[(sT2, 1)])
            S.step("vector", lambda e, gt=gt: e.tensor_copy(g8all[0:8, gt * 128:(gt + 1) * 128], pT2[:, :]),
                   waits=[(sT2, None)], incs=[(sT2E, 1)])
    S.step("vector", lambda e: e.tensor_copy(c0[:, 0:1], c0[:, 0:1]), waits=[(sT2E, None)],
           incs=[(sVV, 1)])
    return sVV, S.n(sVV)


def emit_b2_experts(P, S, io, gl, T, gatesem, waits0=()):
    NT = T // NCK
    h2Td = io["h2T"]; w13d = io["w13_t"]; w2d = io["w2_t"]
    yTd = io["yT"]; x2Td = io["x2T"]; accTd = io["accT"]
    g8all = io["g8all"]; oh8 = io["oh8"]
    onesr = gl["onesr"]
    w13s = P.sbuf("b2e_w13", [128, KT * 32 * 128], BF16)
    w2s = P.sbuf("b2e_w2", [128, 16 * KT * 128], BF16)
    hr = P.sbuf("b2e_hr", [128, KT * NCK], BF16)
    gg = P.sbuf("b2e_gg", [128, 16 * NCK], BF16)
    sgt = P.sbuf("b2e_sgt", [128, NCK], F32)
    grs = P.sbuf("b2e_grs", [1, NCK], F32)
    gbs = P.sbuf("b2e_gbs", [128, NCK], F32)
    accs = P.sbuf("b2e_accs", [128, KT * NCK], F32)
    xtr = P.sbuf("b2e_xtr", [128, KT * NCK], F32)
    pG = [P.psum(f"b2e_pG{i}", [128, NCK]) for i in range(4)]
    pD = [P.psum(f"b2e_pD{i}", [128, NCK]) for i in range(2)]
    pE = P.psum("b2e_pE", [1, NCK])
    pB = P.psum("b2e_pB", [128, NCK])
    sW13 = S.sem(kind="g"); sW2 = S.sem(kind="g"); sH = S.sem(kind="d"); sGE = S.sem(); sGB = S.sem(); sGBE = S.sem()
    sG1 = S.sem(); sSI = S.sem(); sGG = S.sem(); sDN = S.sem(); sAC = S.sem(kind="d"); sACC = S.sem()
    sST = S.sem(kind="d"); sAA = S.sem(); sVV = S.sem(); sPDE = S.sem()
    nh = 0; ngg = 0; ndn = 0; nac = 0; nst = 0
    for e in range(NCG):
        S.step("gpsimd",
               lambda e_, e=e: e_.dma_start(
                   out=w13s[:].rearrange("p (kt ft j) -> p kt ft j", ft=32, j=128),
                   in_=w13d[e].rearrange("p kt ft j -> p kt ft j")),
               waits=(list(waits0) if e == 0 else []) + ([(sDN, NCG * NT * e)] if e >= 1 else []),
               incs=[(sW13, 16)])
        S.step("gpsimd",
               lambda e_, e=e: e_.dma_start(
                   out=w2s[:].rearrange("p (ft cg j) -> p ft cg j", cg=KT, j=128),
                   in_=w2d[e].rearrange("p ft cg j -> p ft cg j")),
               waits=([(sDN, NCG * NT * e)] if e >= 1 else []),
               incs=[(sW2, 16)])
        for tc in range(NT):
            it = e * NT + tc
            S.step("sync",
                   lambda e_, tc=tc: e_.dma_start(
                       out=hr[:].rearrange("p (kt n) -> p kt n", n=NCK),
                       in_=h2Td[:, tc * NCK:(tc + 1) * NCK].rearrange("(kt p) n -> p kt n", p=128)),
                   waits=([(sG1, 32 * it)] if it >= 1 else []) + ([gatesem] if it == 0 else []),
                   incs=[(sH, 16)])
            nh += 16
            S.step("tensor",
                   lambda e_, e=e, tc=tc: e_.matmul(pE[0:1, :], oh8[0:8, e:e + 1],
                                                    g8all[0:8, tc * NCK:(tc + 1) * NCK],
                                                    start=True, stop=True),
                   waits=[gatesem] + ([(sGE, it)] if it >= 1 else []),
                   incs=[(sGB, 1)])
            S.step("vector", lambda e_: e_.tensor_copy(grs[0:1, :], pE[0:1, :]),
                   waits=[(sGB, None)], incs=[(sGE, 1)])
            S.step("tensor",
                   lambda e_: e_.matmul(pB[:, :], onesr[:], grs[0:1, :], start=True, stop=True),
                   waits=[(sGE, None)] + ([(sGBE, it)] if it >= 1 else []),
                   incs=[(sGB, 1)])
            S.step("vector", lambda e_: e_.tensor_copy(gbs[:, :], pB[:, :]),
                   waits=[(sGB, None)] + ([(sACC, NCG * it)] if it >= 1 else []), incs=[(sGBE, 1)])
            for ft in range(16):
                b1 = pG[(2 * ft) % 4]; b3 = pG[(2 * ft + 1) % 4]
                for kt in range(KT):
                    S.step("tensor",
                           lambda e_, kt=kt, ft=ft, b1=b1: e_.matmul(
                               b1[:, :], w13s[:, (kt * 32 + ft) * 128:(kt * 32 + ft + 1) * 128],
                               hr[:, kt * NCK:(kt + 1) * NCK],
                               start=(kt == 0), stop=(kt == KT - 1)),
                           waits=([(sW13, 16 * (e + 1)), (sH, nh)] if (ft == 0 and kt == 0) else []) +
                                 ([(sSI, 2 * (it * 16 + ft) - 2)] if (kt == 0 and (it * 16 + ft) >= 2) else []),
                           incs=([(sG1, 1)] if kt == KT - 1 else []))
                for kt in range(KT):
                    S.step("tensor",
                           lambda e_, kt=kt, ft=ft, b3=b3: e_.matmul(
                               b3[:, :], w13s[:, (kt * 32 + ft + 16) * 128:(kt * 32 + ft + 16 + 1) * 128],
                               hr[:, kt * NCK:(kt + 1) * NCK],
                               start=(kt == 0), stop=(kt == KT - 1)),
                           incs=([(sG1, 1)] if kt == KT - 1 else []))
                S.step("scalar",
                       lambda e_, b1=b1: e_.activation(sgt[:, :], b1[:, :], AF.Silu),
                       waits=[(sG1, None), (sSI, None)] +
                             ([(sDN, NCG * it), (sACC, NCG * it)] if (ft == 0 and it >= 1) else []),
                       incs=[(sSI, 1)])
                S.step("vector",
                       lambda e_, ft=ft, b3=b3: e_.tensor_mul(gg[:, ft * NCK:(ft + 1) * NCK], sgt[:, :], b3[:, :]),
                       waits=[(sSI, None)], incs=[(sSI, 1)])
                ngg += 1
            S.step("sync",
                   lambda e_, tc=tc, e=e: e_.dma_start(
                       out=xtr[:].rearrange("p (cg n) -> p cg n", n=NCK),
                       in_=(yTd if e == 0 else accTd)[:, tc * NCK:(tc + 1) * NCK].rearrange(
                           "(cg p) n -> p cg n", p=128)),
                   waits=([(sACC, NCG * it)] if it >= 1 else []),
                   incs=[(sAC, 16)])
            nac += 16
            if e == 0:
                S.step("sync",
                       lambda e_, tc=tc: e_.dma_start(
                           out=accs[:].rearrange("p (cg n) -> p cg n", n=NCK),
                           in_=x2Td[:, tc * NCK:(tc + 1) * NCK].rearrange("(cg p) n -> p cg n", p=128)),
                       waits=([(sST, 16 * it)] if it >= 1 else []),
                       incs=[(sAC, 16)])
                nac += 16
            for cg in range(NCG):
                bd = pD[(it * NCG + cg) % 2]
                for ft in range(16):
                    S.step("tensor",
                           lambda e_, ft=ft, cg=cg, bd=bd: e_.matmul(
                               bd[:, :], w2s[:, (ft * KT + cg) * 128:(ft * KT + cg + 1) * 128],
                               gg[:, ft * NCK:(ft + 1) * NCK],
                               start=(ft == 0), stop=(ft == 15)),
                           waits=([(sW2, 16 * (e + 1)), (sSI, 32 * (it + 1))] if (cg == 0 and ft == 0) else []) +
                                 ([(sPDE, it * NCG + cg - 1)] if (ft == 0 and (it * NCG + cg) >= 2) else []),
                           incs=([(sDN, 1)] if ft == 15 else []))
                sl = slice(cg * NCK, (cg + 1) * NCK)
                S.step("vector", lambda e_, sl=sl, bd=bd: e_.tensor_mul(sgt[:, :], bd[:, :], gbs[:, :]),
                       waits=[(sDN, None), (sGBE, None), (sAC, nac), (sACC, None)] +
                             ([(sST, 16 * it)] if (cg == 0 and it >= 1) else []),
                       incs=[(sPDE, 1)])
                if e == 0:
                    S.step("vector", lambda e_, sl=sl: e_.tensor_add(accs[:, sl], accs[:, sl], xtr[:, sl]),
                           waits=[(sPDE, None)], incs=[(sVV, 1)])
                    S.step("vector", lambda e_, sl=sl: e_.tensor_add(accs[:, sl], accs[:, sl], sgt[:, :]),
                           waits=[(sVV, None)], incs=[(sACC, 1)])
                else:
                    S.step("vector", lambda e_, sl=sl: e_.tensor_add(accs[:, sl], xtr[:, sl], sgt[:, :]),
                           waits=[(sPDE, None)], incs=[(sACC, 1)])
            S.step("sync",
                   lambda e_, tc=tc: e_.dma_start(
                       out=accTd[:, tc * NCK:(tc + 1) * NCK].rearrange("(cg p) n -> p cg n", p=128),
                       in_=accs[:].rearrange("p (cg n) -> p cg n", n=NCK)),
                   waits=[(sACC, NCG * (it + 1))],
                   incs=[(sST, 16)])
            nst += 16
    S.step("sync", lambda e: None, waits=[(sST, nst)])
    return sST, nst


def prep_b2(inputs, T):
    g = {}
    gw = np.asarray(inputs["L0_ffn_gate_w"])
    gwT = gw.T.reshape(8, 128, 8).transpose(1, 0, 2).reshape(128, 64)
    g["gw_t"] = np.ascontiguousarray(gwT.astype(np.float32))
    w13 = np.asarray(inputs["L0_ffn_w13"])
    w13T = np.ascontiguousarray(np.transpose(w13, (0, 2, 1)))
    g["w13_t"] = np.ascontiguousarray(
        w13T.reshape(8, 8, 128, 32, 128).transpose(0, 2, 1, 3, 4))
    w2 = np.asarray(inputs["L0_ffn_w2"])
    w2T = np.ascontiguousarray(np.transpose(w2, (0, 2, 1)))
    g["w2_t"] = np.ascontiguousarray(
        w2T.reshape(8, 16, 128, 8, 128).transpose(0, 2, 1, 3, 4))
    g["ln2w"] = _colmajor_vec(inputs["L0_ffn_ln2_w"])
    g["identf_d"] = np.eye(128, dtype=np.float32)
    g["oh8_d"] = np.ascontiguousarray(np.eye(8, dtype=np.float32)[:, :])
    return g


def build_b2(T):
    P = Prog("b2")
    P.T = T
    nc = P.nc
    io = {
        "yT": nc.dram_tensor("yT", [C, T], F32, kind="ExternalInput").ap(),
        "x2T": nc.dram_tensor("x2T", [C, T], F32, kind="ExternalInput").ap(),
        "xBT": nc.dram_tensor("xBT", [C, T], F32, kind="ExternalOutput").ap(),
        "gw_t": nc.dram_tensor("gw_t", [128, 64], F32, kind="ExternalInput").ap(),
        "w13_t": nc.dram_tensor("w13_t", [8, 128, 8, 32, 128], F32, kind="ExternalInput").ap(),
        "w2_t": nc.dram_tensor("w2_t", [8, 128, 16, 8, 128], F32, kind="ExternalInput").ap(),
        "ln2w": nc.dram_tensor("ln2w", [128, 8], F32, kind="ExternalInput").ap(),
        "identf_d": nc.dram_tensor("identf_d", [128, 128], F32, kind="ExternalInput").ap(),
        "oh8_d": nc.dram_tensor("oh8_d", [8, 8], F32, kind="ExternalInput").ap(),
        "h2T": nc.dram_tensor("h2T", [C, T], BF16, kind="Internal").ap(),
        "accT": nc.dram_tensor("accT", [C, T], F32, kind="Internal").ap(),
    }
    with P.es:
        P.sem_es = P.es
        gl = {
            "epsc": P.sbuf("g_eps", [1, 1], F32),
            "onecf": P.sbuf("g_one", [128, 1], F32),
            "onesr": P.sbuf("g_onesr", [1, 128], F32),
        }
        P.gl = gl
        mrow_g = P.sbuf("g_mrow", [1, T], F32)
        rrow_g = P.sbuf("g_rrow", [1, T], F32)
        lnwc = P.sbuf("g_lnw", [128, 8], F32)
        identf = P.sbuf("g_identf", [128, 128], F32)
        oh8 = P.sbuf("g_oh8", [8, 8], BF16)
        g8all = P.sbuf("g_g8all", [8, T], BF16)
        S = Script(P)
        sLD0 = S.sem()
        S.step("vector", lambda e: e.memset(gl["epsc"][:], EPS_LN))
        S.step("vector", lambda e: e.memset(gl["onecf"][:], 1.0))
        S.step("vector", lambda e: e.memset(gl["onesr"][:], 1.0))
        sLD1 = S.sem()
        S.step("sync", lambda e: e.dma_start(out=lnwc[:], in_=io["ln2w"]), incs=[(sLD0, 16)])
        S.step("sync", lambda e: e.dma_start(out=identf[:], in_=io["identf_d"]), incs=[(sLD0, 16)])
        S.step("gpsimd", lambda e: e.dma_start(out=oh8[:], in_=io["oh8_d"]), incs=[(sLD1, 16)])
        block = P.es.enter_context(nc.Block())
        io2 = dict(io)
        io2["identf"] = identf; io2["oh8"] = oh8; io2["g8all"] = g8all
        with ExitStack() as sc0:
            P_es_old = P.es; P.es = sc0
            rowbufs = (mrow_g, rrow_g, P.sbuf("g_trow", [1, T // 2], F32))
            rowsem = emit_rows(P, S, "b2r", io["x2T"], T, rms=True, rowbufs=rowbufs)
            P.es = P_es_old
        with ExitStack() as sc1:
            P_es_old = P.es; P.es = sc1
            io3 = dict(io2)
            io3["yT"] = io["x2T"]; io3["h1T"] = io["h2T"]; io3["lnw_col"] = lnwc
            sh2, nh2 = emit_b1_h1(P, S, io3, gl, T, rowbufs, rowsem, lnwsem=(sLD0, 32), pref="b2h")
            P.es = P_es_old
        with ExitStack() as sc2:
            P_es_old = P.es; P.es = sc2
            gatesem = emit_b2_gates(P, S, io2, gl, T, waits0=[(sh2, nh2), (sLD0, 32), (sLD1, 16)])
            P.es = P_es_old
        with ExitStack() as sc3:
            P_es_old = P.es; P.es = sc3
            sae, nae = emit_b2_experts(P, S, io2, gl, T, gatesem, waits0=[gatesem])
            P.es = P_es_old
        with ExitStack() as sc4:
            P_es_old = P.es; P.es = sc4
            rowbufs2 = (mrow_g, rrow_g, P.sbuf("g_trow2", [1, T // 2], F32))
            rowsem2 = emit_rows(P, S, "b2p", io["accT"], T, rms=False, rowbufs=rowbufs2,
                                waits0=[(sae, nae)])
            emit_normalize(P, S, "b2n", io["accT"], io["xBT"], T, rowbufs2, rowsem2,
                           extra_src_waits=[(sae, nae)])
            P.es = P_es_old
        S.replay(block)
    return nc



def emit_c(P, S, io, gl, T, waits0=()):
    NT = T // NCK
    xBTd = io["xBT"]; keyd = io["key_t"]; vpd = io["vp_t"]; sCd = io["sCT"]
    onesr = gl["onesr"]
    keys = P.sbuf("c_keys", [128, 32 * KT * 128], BF16)
    vpr = P.sbuf("c_vpr", [128, 2 * KT * 128], BF16)
    onesb = P.sbuf("c_onesb", [128, 1], BF16)
    xbr = P.sbuf("c_xbr", [128, KT * NCK], BF16)
    xfr = P.sbuf("c_xfr", [128, KT * NCK], F32)
    gck = P.sbuf("c_gck", [128, 32 * NCK], BF16)
    gsq = P.sbuf("c_gsq", [128, NCK], BF16)
    srow = P.sbuf("c_srow", [1, NCK], F32)
    sclb = P.sbuf("c_sclb", [128, NCK], F32)
    s5s = P.sbuf("c_s5s", [128, KT * NCK], F32)
    c64 = P.sbuf("c_c64", [1, 1], F32)
    PP = [P.psum(f"c_pp{i}", [128, NCK]) for i in range(8)]
    sWL = S.sem(kind="g"); sVP = [S.sem(kind="g"), S.sem(kind="g")]; sX = S.sem(kind="g"); sXF = S.sem(kind="d")
    sMM = S.sem(); sGE = S.sem(); sSQ = S.sem()
    sNR = S.sem(); sNRE = S.sem(); sAA = S.sem(); sBC = S.sem(); sBCE = S.sem()
    sOM = S.sem(); sOE = S.sem(); sST = S.sem(kind="d"); sVV = S.sem()
    S.step("vector", lambda e: e.memset(onesb[:], 1.0), incs=[(sVV, 1)])
    S.step("vector", lambda e: e.memset(c64[:], 4.1588830833596715), incs=[(sVV, 1)])
    S.step("gpsimd", lambda e: e.dma_start(
               out=keys[:].rearrange("p (sg kt j) -> p sg kt j", kt=KT, j=128),
               in_=keyd.rearrange("sg p kt j -> p sg kt j")),
           waits=list(waits0), incs=[(sWL, 16)])
    nvp = [0, 0]
    for lc in range(NT):
        S.step("gpsimd",
               lambda e, lc=lc: e.dma_start(
                   out=xbr[:].rearrange("p (kt n) -> p kt n", n=NCK),
                   in_=xBTd[:, lc * NCK:(lc + 1) * NCK].rearrange("(kt p) n -> p kt n", p=128)),
               waits=([(sMM, 32 * lc)] if lc >= 1 else []) + (list(waits0) if lc == 0 else []),
               incs=[(sX, 16)])
        S.step("sync",
               lambda e, lc=lc: e.dma_start(
                   out=xfr[:].rearrange("p (cg n) -> p cg n", n=NCK),
                   in_=xBTd[:, lc * NCK:(lc + 1) * NCK].rearrange("(cg p) n -> p cg n", p=128)),
               waits=([(sVV, 2 + NCG * lc)] if lc >= 1 else []),
               incs=[(sXF, 16)])
        for sg in range(32):
            i = lc * 32 + sg
            sb = PP[i % 2]
            for kt in range(KT):
                S.step("tensor",
                       lambda e, sg=sg, kt=kt, sb=sb: e.matmul(
                           sb[:, :], keys[:, (sg * KT + kt) * 128:(sg * KT + kt + 1) * 128],
                           xbr[:, kt * NCK:(kt + 1) * NCK],
                           start=(kt == 0), stop=(kt == KT - 1)),
                       waits=([(sWL, 16), (sX, 16 * (lc + 1))] if (sg == 0 and kt == 0) else []) +
                             ([(sOE, NCG * lc)] if (sg <= 1 and kt == 0 and lc >= 1) else []) +
                             ([(sGE, i - 1)] if (kt == 0 and i >= 2) else []),
                       incs=([(sMM, 1)] if kt == KT - 1 else []))
            S.step("scalar",
                   lambda e, sg=sg, sb=sb: e.activation(gck[:, sg * NCK:(sg + 1) * NCK], sb[:, :], AF.Gelu),
                   waits=[(sMM, i + 1)],
                   incs=[(sGE, 1)])
            S.step("vector",
                   lambda e, sg=sg: e.tensor_mul(gsq[:, :], gck[:, sg * NCK:(sg + 1) * NCK],
                                                 gck[:, sg * NCK:(sg + 1) * NCK]),
                   waits=[(sGE, None)], incs=[(sSQ, 1)])
            S.step("tensor",
                   lambda e, sg=sg, lc=lc: e.matmul(PP[2][0:1, :], onesb[:], gsq[:, :],
                                                    start=(sg == 0), stop=(sg == 31)),
                   waits=[(sSQ, i + 1)] + ([(sVV, 1)] if (sg == 0 and lc == 0) else []) +
                         ([(sNRE, lc)] if (sg == 0 and lc >= 1) else []) +
                         ([(sOE, NCG * lc)] if (sg == 0 and lc >= 1) else []),
                   incs=([(sNR, 1)] if sg == 31 else []))
        S.step("vector", lambda e: e.tensor_copy(srow[0:1, :], PP[2][0:1, :]),
               waits=[(sNR, lc + 1)], incs=[(sNRE, 1)])
        S.step("scalar", lambda e: e.activation(srow[0:1, :], srow[0:1, :], AF.Ln),
               waits=[(sNRE, None)], incs=[(sAA, 1)])
        S.step("scalar", lambda e: e.activation(srow[0:1, :], srow[0:1, :], AF.Exp, scale=-0.5,
                                                bias=c64[0:1, 0:1]),
               waits=[(sAA, None)], incs=[(sAA, 1)])
        S.step("tensor",
               lambda e: e.matmul(PP[3][:, :], onesr[:], srow[0:1, :], start=True, stop=True),
               waits=[(sAA, None), (sVV, 2)] + ([(sBCE, lc), (sOE, NCG * lc)] if lc >= 1 else []),
               incs=[(sBC, 1)])
        S.step("vector", lambda e: e.tensor_copy(sclb[:, :], PP[3][:, :]),
               waits=[(sBC, None)] + ([(sVV, 2 + NCG * lc)] if lc >= 1 else []),
               incs=[(sBCE, 1)])
        for sg in range(32):
            slot = sg % 2
            S.step("gpsimd",
                   lambda e, sg=sg, slot=slot, lc=lc: e.dma_start(
                       out=vpr[:, slot * KT * 128:(slot + 1) * KT * 128].rearrange(
                           "p (vg j) -> p vg j", j=128),
                       in_=vpd[sg].rearrange("p vg j -> p vg j")),
                   waits=([(sOM, 32 * lc + sg - 1)] if (lc * 32 + sg) >= 2 else []) +
                         (list(waits0) if (lc == 0 and sg == 0) else []),
                   incs=[(sVP[slot], 16)])
            nvp[slot] += 16
            for vg in range(NCG):
                S.step("tensor",
                       lambda e, sg=sg, vg=vg, slot=slot: e.matmul(
                           PP[vg][:, :], vpr[:, (slot * KT + vg) * 128:(slot * KT + vg + 1) * 128],
                           gck[:, sg * NCK:(sg + 1) * NCK],
                           start=(sg == 0), stop=(sg == 31)),
                       waits=([(sVP[slot], nvp[slot])] if vg == 0 else []) +
                             ([(sGE, 32 * (lc + 1)), (sBCE, lc + 1)] if (sg == 0 and vg == 0) else []) +
                             ([(sOE, NCG * (lc - 1) + vg + 1)] if (sg == 0 and vg >= 2 and lc >= 1) else []),
                       incs=([(sOM, 1)] if vg == NCG - 1 else []))
        for vg in range(NCG):
            sl = slice(vg * NCK, (vg + 1) * NCK)
            S.step("vector", lambda e, sl=sl, vg=vg: e.tensor_mul(s5s[:, sl], PP[vg][:, :], sclb[:, :]),
                   waits=[(sOM, 32 * (lc + 1)), (sXF, 16 * (lc + 1))] +
                         ([(sST, 16 * lc)] if (vg == 0 and lc >= 1) else []),
                   incs=[(sOE, 1)])
            S.step("vector", lambda e, sl=sl: e.tensor_add(s5s[:, sl], s5s[:, sl], xfr[:, sl]),
                   waits=[(sOE, None)], incs=[(sVV, 1)])
        S.step("sync",
               lambda e, lc=lc: e.dma_start(
                   out=sCd[:, lc * NCK:(lc + 1) * NCK].rearrange("(cg p) n -> p cg n", p=128),
                   in_=s5s[:].rearrange("p (cg n) -> p cg n", n=NCK)),
               waits=[(sVV, 2 + NCG * (lc + 1))],
               incs=[(sST, 16)])
    S.step("sync", lambda e: None, waits=[(sST, 16 * NT)])
    return sST, 16 * NT


def build_c(T):
    P = Prog("c")
    P.T = T
    nc = P.nc
    io = {
        "xBT": nc.dram_tensor("xBT", [C, T], F32, kind="ExternalInput").ap(),
        "xCT": nc.dram_tensor("xCT", [C, T], F32, kind="ExternalOutput").ap(),
        "key_t": nc.dram_tensor("key_t", [32, 128, 8, 128], F32, kind="ExternalInput").ap(),
        "vp_t": nc.dram_tensor("vp_t", [32, 128, 8, 128], F32, kind="ExternalInput").ap(),
        "sCT": nc.dram_tensor("sCT", [C, T], F32, kind="Internal").ap(),
    }
    with P.es:
        P.sem_es = P.es
        gl = {
            "epsc": P.sbuf("g_eps", [1, 1], F32),
            "onecf": P.sbuf("g_one", [128, 1], F32),
            "onesr": P.sbuf("g_onesr", [1, 128], F32),
        }
        P.gl = gl
        mrow_g = P.sbuf("g_mrow", [1, T], F32)
        rrow_g = P.sbuf("g_rrow", [1, T], F32)
        S = Script(P)
        S.step("vector", lambda e: e.memset(gl["epsc"][:], EPS_LN))
        S.step("vector", lambda e: e.memset(gl["onecf"][:], 1.0))
        S.step("vector", lambda e: e.memset(gl["onesr"][:], 1.0))
        block = P.es.enter_context(nc.Block())
        with ExitStack() as sc0:
            P_es_old = P.es; P.es = sc0
            sc_, nc_ = emit_c(P, S, io, gl, T)
            P.es = P_es_old
        with ExitStack() as sc1:
            P_es_old = P.es; P.es = sc1
            rowbufs = (mrow_g, rrow_g, P.sbuf("g_trow", [1, T // 2], F32))
            rowsem = emit_rows(P, S, "cp", io["sCT"], T, rms=False, rowbufs=rowbufs,
                               waits0=[(sc_, nc_)])
            emit_normalize(P, S, "cn", io["sCT"], io["xCT"], T, rowbufs, rowsem,
                           extra_src_waits=[(sc_, nc_)])
            P.es = P_es_old
        S.replay(block)
    return nc


def prep_c(inputs, T):
    g = {}
    g["key_t"] = _tile_colblocks(np.ascontiguousarray(np.asarray(inputs["L1_attn_key_param_tokens"]).T))
    vp = np.asarray(inputs["L1_attn_value_param_tokens"])
    g["vp_t"] = np.ascontiguousarray(vp.reshape(32, 128, 8, 128))
    return g



def emit_d(P, S, io, gl, T, waits0=()):
    NT = T // NCK
    xCTd = io["xCT"]; wd = io["l1w_t"]; s6d = io["s6T"]
    alphad = io["alpha_d"]; biasd = io["bias_col"]
    onesr = gl["onesr"]
    wfs = P.sbuf("d_wfs", [128, KT * KT * 128], F32)
    wts = P.sbuf("d_wts", [128, KT * KT * 128], BF16)
    wab = P.sbuf("d_wab", [128, 128], F32)
    onesf = P.sbuf("d_onesf", [128, 1], F32)
    scl = P.sbuf("d_scl", [128, 8], F32)
    srow = P.sbuf("d_srow", [1, 8], F32)
    acol = P.sbuf("d_acol", [128, 1], F32)
    bcol = P.sbuf("d_bcol", [128, 8], F32)
    alr = P.sbuf("d_alr", [1, 1], F32)
    tmp1 = P.sbuf("d_tmp1", [128, 128], F32)
    tmp2 = P.sbuf("d_tmp2", [128, 128], F32)
    xbr = P.sbuf("d_xbr", [128, KT * NCK], BF16)
    xfr = P.sbuf("d_xfr", [128, KT * NCK], F32)
    s6s = P.sbuf("d_s6s", [128, KT * NCK], F32)
    pA = P.psum("d_pA", [1, 128])
    pB = P.psum("d_pB", [128, NCK])
    pM = [P.psum(f"d_pM{i}", [128, NCK]) for i in range(2)]
    sWL = S.sem(kind="g"); sAL = S.sem(kind="d"); sAB = S.sem(); sABE = S.sem(); sRD = S.sem(); sAA = S.sem()
    sBC = S.sem(); sBCE = S.sem(); sTN = S.sem(); sX = S.sem(kind="g"); sXF = S.sem(kind="d")
    sMM = S.sem(); sOE = S.sem(); sST = S.sem(kind="d"); sVV = S.sem()
    S.step("vector", lambda e: e.memset(onesf[:], 1.0), incs=[(sVV, 1)])
    S.step("sync", lambda e: e.dma_start(out=alr[:], in_=alphad), waits=list(waits0), incs=[(sAL, 16)])
    S.step("sync", lambda e: e.dma_start(out=bcol[:], in_=biasd), incs=[(sAL, 16)])
    S.step("gpsimd", lambda e: e.dma_start(
               out=wfs[:].rearrange("p (cg kt j) -> p cg kt j", kt=KT, j=128),
               in_=wd.rearrange("cg p kt j -> p cg kt j")),
           waits=list(waits0), incs=[(sWL, 16)])
    nab = 0
    for i in range(64):
        S.step("scalar",
               lambda e, i=i: e.activation(wab[:, :], wfs[:, i * 128:(i + 1) * 128], AF.Abs),
               waits=([(sWL, 16)] if i == 0 else []) + ([(sAB, i)] if i >= 1 else []),
               incs=[(sAA, 1)])
        S.step("tensor",
               lambda e, i=i: e.matmul(pA[0:1, :], onesf[:], wab[:, :],
                                       start=(i == 0), stop=(i == 63)),
               waits=[(sAA, None)] + ([(sVV, 1)] if i == 0 else []),
               incs=[(sAB, 1)])
    S.step("vector", lambda e: e.tensor_copy(scl[0:1, 0:8], pA[0:1, 0:8]), waits=[(sAB, 64)])
    S.step("vector", lambda e: e.tensor_reduce(srow[0:1, 0:1], pA[0:1, :], axis=AX.X, op=ALU.add),
           waits=[(sAB, 64)], incs=[(sVV, 1)])
    S.step("vector", lambda e: e.tensor_scalar(srow[0:1, 1:2], srow[0:1, 0:1],
                                               0.7 / (1024.0 * 1024.0), None, ALU.mult),
           waits=[(sVV, None)], incs=[(sVV, 1)])
    S.step("vector", lambda e: e.tensor_scalar(srow[0:1, 2:3], srow[0:1, 1:2], -1.0, None, ALU.mult),
           waits=[(sVV, None)], incs=[(sVV, 1)])
    S.step("tensor", lambda e: e.matmul(pB[:, 0:2], onesr[:], srow[0:1, 1:3],
                                        start=True, stop=False, skip_group_check=True),
           waits=[(sVV, None)])
    S.step("tensor", lambda e: e.matmul(pB[:, 2:3], onesr[:], alr[0:1, 0:1],
                                        start=False, stop=True, skip_group_check=True),
           waits=[(sAL, 32)], incs=[(sBC, 1)])
    S.step("vector", lambda e: e.tensor_copy(scl[:, 0:3], pB[:, 0:3]),
           waits=[(sBC, 1)], incs=[(sBCE, 1)])
    for i in range(64):
        S.step("vector",
               lambda e, i=i: e.tensor_scalar(tmp1[:, :], wfs[:, i * 128:(i + 1) * 128],
                                              scl[:, 0:1], None, ALU.is_gt),
               waits=[(sBCE, 1), (sVV, None), (sTN, None)], incs=[(sVV, 1)])
        S.step("vector",
               lambda e, i=i: e.tensor_scalar(tmp2[:, :], wfs[:, i * 128:(i + 1) * 128],
                                              scl[:, 1:2], None, ALU.is_lt),
               waits=[(sVV, None)], incs=[(sVV, 1)])
        S.step("vector",
               lambda e, i=i: e.tensor_sub(wts[:, i * 128:(i + 1) * 128], tmp1[:, :], tmp2[:, :]),
               waits=[(sVV, None)], incs=[(sTN, 1)])
    v0 = S.n(sVV)
    for tc in range(NT):
        S.step("gpsimd",
               lambda e, tc=tc: e.dma_start(
                   out=xbr[:].rearrange("p (kt n) -> p kt n", n=NCK),
                   in_=xCTd[:, tc * NCK:(tc + 1) * NCK].rearrange("(kt p) n -> p kt n", p=128)),
               waits=([(sMM, NCG * tc)] if tc >= 1 else []),
               incs=[(sX, 16)])
        S.step("sync",
               lambda e, tc=tc: e.dma_start(
                   out=xfr[:].rearrange("p (cg n) -> p cg n", n=NCK),
                   in_=xCTd[:, tc * NCK:(tc + 1) * NCK].rearrange("(cg p) n -> p cg n", p=128)),
               waits=([(sVV, v0 + NCG * tc)] if tc >= 1 else []),
               incs=[(sXF, 16)])
        for cg in range(NCG):
            j = tc * NCG + cg
            b = pM[j % 2]
            for kt in range(KT):
                S.step("tensor",
                       lambda e, cg=cg, kt=kt, b=b: e.matmul(
                           b[:, :], wts[:, (cg * KT + kt) * 128:(cg * KT + kt + 1) * 128],
                           xbr[:, kt * NCK:(kt + 1) * NCK],
                           start=(kt == 0), stop=(kt == KT - 1)),
                       waits=([(sTN, 64), (sX, 16 * (tc + 1))] if (cg == 0 and kt == 0) else []) +
                             ([(sOE, j - 1)] if (kt == 0 and j >= 2) else []),
                       incs=([(sMM, 1)] if kt == KT - 1 else []))
            sl = slice(cg * NCK, (cg + 1) * NCK)
            S.step("vector",
                   lambda e, sl=sl, b=b, cg=cg: e.tensor_scalar(s6s[:, sl], b[:, :], scl[:, 2:3],
                                                                bcol[:, cg:cg + 1], ALU.mult, ALU.add),
                   waits=[(sMM, j + 1), (sXF, 16 * (tc + 1))] +
                         ([(sST, 16 * tc)] if (cg == 0 and tc >= 1) else []),
                   incs=[(sOE, 1)])
            S.step("vector", lambda e, sl=sl: e.tensor_add(s6s[:, sl], s6s[:, sl], xfr[:, sl]),
                   waits=[(sOE, None)], incs=[(sVV, 1)])
        S.step("sync",
               lambda e, tc=tc: e.dma_start(
                   out=s6d[:, tc * NCK:(tc + 1) * NCK].rearrange("(cg p) n -> p cg n", p=128),
                   in_=s6s[:].rearrange("p (cg n) -> p cg n", n=NCK)),
               waits=[(sVV, v0 + NCG * (tc + 1))],
               incs=[(sST, 16)])
    S.step("sync", lambda e: None, waits=[(sST, 16 * NT)])
    return sST, 16 * NT


def build_d(T):
    P = Prog("d")
    P.T = T
    nc = P.nc
    io = {
        "xCT": nc.dram_tensor("xCT", [C, T], F32, kind="ExternalInput").ap(),
        "outT": nc.dram_tensor("outT", [C, T], F32, kind="ExternalOutput").ap(),
        "l1w_t": nc.dram_tensor("l1w_t", [8, 128, 8, 128], F32, kind="ExternalInput").ap(),
        "alpha_d": nc.dram_tensor("alpha_d", [1, 1], F32, kind="ExternalInput").ap(),
        "bias_col": nc.dram_tensor("bias_col", [128, 8], F32, kind="ExternalInput").ap(),
        "s6T": nc.dram_tensor("s6T", [C, T], F32, kind="Internal").ap(),
    }
    with P.es:
        P.sem_es = P.es
        gl = {
            "epsc": P.sbuf("g_eps", [1, 1], F32),
            "onecf": P.sbuf("g_one", [128, 1], F32),
            "onesr": P.sbuf("g_onesr", [1, 128], F32),
        }
        P.gl = gl
        mrow_g = P.sbuf("g_mrow", [1, T], F32)
        rrow_g = P.sbuf("g_rrow", [1, T], F32)
        S = Script(P)
        S.step("vector", lambda e: e.memset(gl["epsc"][:], EPS_LN))
        S.step("vector", lambda e: e.memset(gl["onecf"][:], 1.0))
        S.step("vector", lambda e: e.memset(gl["onesr"][:], 1.0))
        block = P.es.enter_context(nc.Block())
        with ExitStack() as sc0:
            P_es_old = P.es; P.es = sc0
            sd_, nd_ = emit_d(P, S, io, gl, T)
            P.es = P_es_old
        with ExitStack() as sc1:
            P_es_old = P.es; P.es = sc1
            rowbufs = (mrow_g, rrow_g, P.sbuf("g_trow", [1, T // 2], F32))
            rowsem = emit_rows(P, S, "dp", io["s6T"], T, rms=False, rowbufs=rowbufs,
                               waits0=[(sd_, nd_)])
            emit_normalize(P, S, "dn", io["s6T"], io["outT"], T, rowbufs, rowsem,
                           extra_src_waits=[(sd_, nd_)])
            P.es = P_es_old
        S.replay(block)
    return nc


def prep_d(inputs, T):
    g = {}
    g["l1w_t"] = _tile_colblocks(np.ascontiguousarray(np.asarray(inputs["L1_ffn_weight"]).T))
    g["alpha_d"] = np.asarray(inputs["L1_ffn_alpha"], dtype=np.float32).reshape(1, 1)
    g["bias_col"] = _colmajor_vec(inputs["L1_ffn_bias"])
    return g



def build_fused(T):
    P = Prog("mk")
    P.T = T
    nc = P.nc
    NKT = T // 128
    d = {}
    def din(name, shape, dt=F32):
        d[name] = nc.dram_tensor(name, list(shape), dt, kind="ExternalInput").ap()
    def dint(name, shape, dt=F32):
        d[name] = nc.dram_tensor(name, list(shape), dt, kind="Internal").ap()
    din("xT", [C, T]); din("wkt", [8, 128, 8, 128]); din("wvt", [8, 128, 8, 128])
    din("wrt", [8, 128, 8, 128]); din("pv0", [128, 80])
    din("wkc", [32, 128, 8, 128]); din("wvc", [8, 128, 32, 128]); din("wrc", [8, 128, 8, 128])
    din("wqkv_t", [12, 128, 8, 128]); din("wo_t", [8, 128, 8, 128]); din("lnw", [128, 8])
    din("cosb_d", [128, T]); din("sinb_d", [128, T]); din("masks_d", [128, 4 * NCK])
    din("ident_d", [128, 128])
    din("gw_t", [128, 64]); din("w13_t", [8, 128, 8, 32, 128]); din("w2_t", [8, 128, 16, 8, 128])
    din("ln2w", [128, 8]); din("identf_d", [128, 128]); din("oh8_d", [8, 8])
    din("key_t", [32, 128, 8, 128]); din("vp_t", [32, 128, 8, 128])
    din("l1w_t", [8, 128, 8, 128]); din("alpha_d", [1, 1]); din("bias_col", [128, 8])
    d["outT"] = nc.dram_tensor("outT", [C, T], F32, kind="ExternalOutput").ap()
    for n in ["x1T", "vvT", "s2T", "yT", "x2T", "accT", "xBT", "sCT", "xCT", "s6T"]:
        dint(n, [C, T])
    dint("h1T", [C, T], BF16); dint("attT", [C, T], BF16); dint("h2T", [C, T], BF16)

    with P.es:
        P.sem_es = P.es
        gl = {
            "epsc": P.sbuf("g_eps", [1, 1], F32),
            "onecf": P.sbuf("g_one", [128, 1], F32),
            "onesr": P.sbuf("g_onesr", [1, 128], F32),
            "pvd2": P.sbuf("g_pvd2", [128, 16], F32),
        }
        P.gl = gl
        pvt = P.sbuf("g_pvt", [128, 80], F32)
        lnwc = P.sbuf("g_lnw", [128, 8], F32)
        lnw2c = P.sbuf("g_lnw2", [128, 8], F32)
        S = Script(P)
        sLD0 = S.sem(kind="d")
        S.step("vector", lambda e: e.memset(gl["epsc"][:], EPS_LN))
        S.step("vector", lambda e: e.memset(gl["onecf"][:], 1.0))
        S.step("vector", lambda e: e.memset(gl["onesr"][:], 1.0))
        S.step("sync", lambda e: e.dma_start(out=pvt[:, 0:80], in_=d["pv0"]), incs=[(sLD0, 16)])
        S.step("sync", lambda e: e.dma_start(out=lnwc[:], in_=d["lnw"]), incs=[(sLD0, 16)])
        S.step("sync", lambda e: e.dma_start(out=lnw2c[:], in_=d["ln2w"]), incs=[(sLD0, 16)])
        S.step("vector", lambda e: e.tensor_scalar(gl["pvd2"][:, 0:8], pvt[:, 64:72], -1.0, 1.0,
                                                   ALU.mult, ALU.add), waits=[(sLD0, 48)])
        S.step("vector", lambda e: e.tensor_scalar(gl["pvd2"][:, 8:16], pvt[:, 72:80], -1.0, 1.0,
                                                   ALU.mult, ALU.add))
        block = P.es.enter_context(nc.Block())

        def scoped(fn):
            with ExitStack() as sc:
                old = P.es; P.es = sc
                r = fn()
                P.es = old
            return r

        ioa1 = dict(xT=d["xT"], x1T=d["x1T"], wkt=d["wkt"], wvt=d["wvt"], wrt=d["wrt"], pv0=d["pv0"])
        sa1 = scoped(lambda: emit_a1(P, S, ioa1, gl, T))
        ioa2 = dict(xT=d["xT"], x1T=d["x1T"], yT=d["yT"], wkc=d["wkc"], wvc=d["wvc"], wrc=d["wrc"],
                    vvT=d["vvT"], s2T=d["s2T"])
        def _a2():
            mrow_g = P.sbuf("a2_mrow", [1, T], F32)
            rrow_g = P.sbuf("a2_rrow", [1, T], F32)
            rowsem = scoped(lambda: emit_rows(P, S, "a2r", d["x1T"], T, rms=False,
                                              rowbufs=(mrow_g, rrow_g, P.sbuf("a2_tr0", [1, T // 2], F32)),
                                              waits0=[sa1]))
            def _core():
                tsck = P.sbuf("a2_tsck", [128, NCG * (T + 1)], BF16)
                sCk = emit_a2_tsbuild(P, S, "a2tk", d["x1T"], tsck, 64,
                                      (mrow_g, rrow_g, None), rowsem, pvt, sST_prev=rowsem)
                io2 = dict(ioa2); io2["tsck"] = tsck
                return emit_a2_core(P, S, io2, gl, T, waits0=[(sCk, NCG)])
            svst, nvst = scoped(_core)
            def _comb():
                tscr = P.sbuf("a2_tscr", [128, NCG * (T + 1)], BF16)
                sCr = emit_a2_tsbuild(P, S, "a2tr", d["x1T"], tscr, 72,
                                      (mrow_g, rrow_g, None), rowsem, pvt, sST_prev=(svst, nvst))
                io2 = dict(ioa2); io2["tscr"] = tscr
                return emit_a2_combine(P, S, io2, gl, T, waits0=[(sCr, NCG)])
            sstc = scoped(_comb)
            def _pl():
                rb = (mrow_g, rrow_g, P.sbuf("a2_tr1", [1, T // 2], F32))
                rowsem2 = emit_rows(P, S, "a2p", d["s2T"], T, rms=False, rowbufs=rb, waits0=[sstc])
                return emit_normalize(P, S, "a2n", d["s2T"], d["yT"], T, rb, rowsem2,
                                      extra_src_waits=[sstc])
            return scoped(_pl)
        sa2 = scoped(_a2)
        iob1 = dict(yT=d["yT"], x2T=d["x2T"], wqkv_t=d["wqkv_t"], wo_t=d["wo_t"],
                    cosb_d=d["cosb_d"], sinb_d=d["sinb_d"], masks_d=d["masks_d"],
                    ident_d=d["ident_d"], h1T=d["h1T"], attT=d["attT"], lnw_col=lnwc)
        def _b1():
            mrow_g = P.sbuf("b1_mrow", [1, 1], F32)
            rrow_g = P.sbuf("b1_rrow", [1, T], F32)
            def _r():
                rb = (mrow_g, rrow_g, P.sbuf("b1_tr0", [1, T // 2], F32))
                return emit_rows(P, S, "b1r", d["yT"], T, rms=True, rowbufs=rb, waits0=[sa2]), rb
            (rowsem, rb) = scoped(_r)
            sh1 = scoped(lambda: emit_b1_h1(P, S, iob1, gl, T, rb, rowsem, lnwsem=(sLD0, 48)))
            def _qka():
                Qr = P.sbuf("b1_Qr", [128, 8 * T], BF16)
                Kr = P.sbuf("b1_Kr", [128, 2 * T], BF16)
                VT = P.sbuf("b1_VT", [128, 4 * NKT * 65], BF16)
                io2 = dict(iob1); io2["Qr"] = Qr; io2["Kr"] = Kr; io2["VT"] = VT
                def _qkv():
                    cosb = P.sbuf("b1_cosb", [128, T], BF16)
                    sinb = P.sbuf("b1_sinb", [128, T], BF16)
                    identb = P.sbuf("b1_identb", [128, 128], BF16)
                    io2["cosb"] = cosb; io2["sinb"] = sinb; io2["identb"] = identb
                    sC = S.sem(kind="g")
                    S.step("gpsimd", lambda e: e.dma_start(out=cosb[:], in_=d["cosb_d"]),
                           waits=[sh1], incs=[(sC, 16)])
                    S.step("gpsimd", lambda e: e.dma_start(out=sinb[:], in_=d["sinb_d"]), incs=[(sC, 16)])
                    S.step("gpsimd", lambda e: e.dma_start(out=identb[:], in_=d["ident_d"]), incs=[(sC, 16)])
                    io2["Vb"] = P.sbuf("b1_Vb", [128, 2 * T], BF16)
                    return emit_b1_qkv(P, S, io2, gl, T, waits0=[sh1, (sC, 48)])
                qkvsem = scoped(_qkv)
                def _attn():
                    masksb = P.sbuf("b1_masks", [128, 4 * NCK], F32)
                    onesbc = P.sbuf("b1_onesbc", [128, 1], BF16)
                    onesf = P.sbuf("b1_onesf", [128, 64], F32)
                    io2["masksb"] = masksb; io2["onesbc"] = onesbc; io2["onesf"] = onesf
                    sM = S.sem(kind="d")
                    S.step("sync", lambda e: e.dma_start(out=masksb[:], in_=d["masks_d"]),
                           waits=[qkvsem], incs=[(sM, 16)])
                    S.step("vector", lambda e: e.memset(onesbc[:], 1.0), waits=[(sM, 16), qkvsem])
                    S.step("vector", lambda e: e.memset(onesf[:], 1.0))
                    return emit_b1_attn(P, S, io2, gl, T, qkvsem)
                return scoped(_attn), io2
            attsem, io2 = scoped(_qka)
            return scoped(lambda: emit_b1_wo(P, S, io2, gl, T, attsem))
        sb1 = scoped(_b1)
        iob2 = dict(yT=d["yT"], x2T=d["x2T"], xBT=d["xBT"], gw_t=d["gw_t"], w13_t=d["w13_t"],
                    w2_t=d["w2_t"], identf_d=d["identf_d"], oh8_d=d["oh8_d"], h2T=d["h2T"],
                    accT=d["accT"])
        def _b2():
            mrow_g = P.sbuf("b2_mrow", [1, T], F32)
            rrow_g = P.sbuf("b2_rrow", [1, T], F32)
            identf = P.sbuf("b2_identf", [128, 128], F32)
            oh8 = P.sbuf("b2_oh8", [8, 8], BF16)
            g8all = P.sbuf("b2_g8all", [8, T], BF16)
            sLD2 = S.sem(kind="d"); sLD3 = S.sem(kind="g")
            S.step("sync", lambda e: e.dma_start(out=identf[:], in_=d["identf_d"]),
                   waits=[sb1], incs=[(sLD2, 16)])
            S.step("gpsimd", lambda e: e.dma_start(out=oh8[:], in_=d["oh8_d"]),
                   waits=[sb1], incs=[(sLD3, 16)])
            io2 = dict(iob2); io2["identf"] = identf; io2["oh8"] = oh8; io2["g8all"] = g8all
            def _r():
                rb = (mrow_g, rrow_g, P.sbuf("b2_tr0", [1, T // 2], F32))
                return emit_rows(P, S, "b2r", d["x2T"], T, rms=True, rowbufs=rb, waits0=[sb1]), rb
            (rowsem, rb) = scoped(_r)
            def _h2():
                io3 = dict(io2); io3["yT"] = d["x2T"]; io3["h1T"] = d["h2T"]; io3["lnw_col"] = lnw2c
                return emit_b1_h1(P, S, io3, gl, T, rb, rowsem, lnwsem=(sLD0, 48), pref="b2h")
            sh2 = scoped(_h2)
            gatesem = scoped(lambda: emit_b2_gates(P, S, io2, gl, T,
                                                   waits0=[sh2, (sLD2, 16), (sLD3, 16)]))
            sae = scoped(lambda: emit_b2_experts(P, S, io2, gl, T, gatesem, waits0=[gatesem]))
            def _pl():
                rb2 = (mrow_g, rrow_g, P.sbuf("b2_tr1", [1, T // 2], F32))
                rowsem2 = emit_rows(P, S, "b2p", d["accT"], T, rms=False, rowbufs=rb2, waits0=[sae])
                return emit_normalize(P, S, "b2n", d["accT"], d["xBT"], T, rb2, rowsem2,
                                      extra_src_waits=[sae])
            return scoped(_pl)
        sb2 = scoped(_b2)
        ioc = dict(xBT=d["xBT"], xCT=d["xCT"], key_t=d["key_t"], vp_t=d["vp_t"], sCT=d["sCT"])
        def _c():
            mrow_g = P.sbuf("c_mrow", [1, T], F32)
            rrow_g = P.sbuf("c_rrow", [1, T], F32)
            sc_ = scoped(lambda: emit_c(P, S, ioc, gl, T, waits0=[sb2]))
            def _pl():
                rb = (mrow_g, rrow_g, P.sbuf("c_tr0", [1, T // 2], F32))
                rowsem = emit_rows(P, S, "cp", d["sCT"], T, rms=False, rowbufs=rb, waits0=[sc_])
                return emit_normalize(P, S, "cn", d["sCT"], d["xCT"], T, rb, rowsem,
                                      extra_src_waits=[sc_])
            return scoped(_pl)
        sc2_ = scoped(_c)
        iod = dict(xCT=d["xCT"], outT=d["outT"], l1w_t=d["l1w_t"], alpha_d=d["alpha_d"],
                   bias_col=d["bias_col"], s6T=d["s6T"])
        def _d():
            mrow_g = P.sbuf("d_mrow", [1, T], F32)
            rrow_g = P.sbuf("d_rrow", [1, T], F32)
            sd_ = scoped(lambda: emit_d(P, S, iod, gl, T, waits0=[sc2_]))
            def _pl():
                rb = (mrow_g, rrow_g, P.sbuf("d_tr0", [1, T // 2], F32))
                rowsem = emit_rows(P, S, "dp", d["s6T"], T, rms=False, rowbufs=rb, waits0=[sd_])
                return emit_normalize(P, S, "dn", d["s6T"], d["outT"], T, rb, rowsem,
                                      extra_src_waits=[sd_])
            return scoped(_pl)
        scoped(_d)
        S.replay(block)
    return nc


def kernel(**inputs):
    x = np.asarray(inputs["L0_attn_x"])
    B, T, _ = x.shape
    _ = inputs["L0_attn_time_decay"]
    g1 = prep_host(inputs)
    ga2 = _prep_a2(inputs)
    gb1 = prep_b1(inputs, T)
    gb2 = prep_b2(inputs, T)
    gc = prep_c(inputs, T)
    gd = prep_d(inputs, T)
    nc = build_fused(T)
    maps = [dict(xT=g1["xT"][b], wkt=g1["wkt"], wvt=g1["wvt"], wrt=g1["wrt"], pv0=g1["pv0"],
                 wkc=ga2["wkc"], wvc=ga2["wvc"], wrc=ga2["wrc"],
                 wqkv_t=gb1["wqkv_t"], wo_t=gb1["wo_t"], lnw=gb1["lnw"], cosb_d=gb1["cosb_d"],
                 sinb_d=gb1["sinb_d"], masks_d=gb1["masks_d"], ident_d=gb1["ident_d"],
                 gw_t=gb2["gw_t"], w13_t=gb2["w13_t"], w2_t=gb2["w2_t"], ln2w=gb2["ln2w"],
                 identf_d=gb2["identf_d"], oh8_d=gb2["oh8_d"],
                 key_t=gc["key_t"], vp_t=gc["vp_t"],
                 l1w_t=gd["l1w_t"], alpha_d=gd["alpha_d"], bias_col=gd["bias_col"])
            for b in range(B)]
    res = run_bass_kernel_spmd(nc, maps, core_ids=list(range(B)))
    out = np.stack([np.ascontiguousarray(res.results[b]["outT"].T) for b in range(B)], axis=0)
    return out.astype(np.float32)


# version 07
# speedup vs baseline: 1.2574x; 1.2369x over previous
import numpy as np
import ml_dtypes
from contextlib import ExitStack
import concourse.bass as bass
import concourse.mybir as mybir
from concourse.bass_utils import run_bass_kernel_spmd

F32 = mybir.dt.float32
BF16 = mybir.dt.bfloat16
AF = mybir.ActivationFunctionType
ALU = mybir.AluOpType
AX = mybir.AxisListType

C = 1024
NCG = 8
KT = 8
NCK = 512
EPS_WKV = 1e-8
EPS_LN = 1e-5


class Prog:
    def __init__(self, name="prog"):
        self.nc = bass.Bass("TRN2", target_bir_lowering=False)
        self.es = ExitStack()
        self.name = name
        self._semn = 0

    def sbuf(self, name, shape, dt=F32):
        return self.es.enter_context(self.nc.sbuf_tensor(name, list(shape), dt))

    def psum(self, name, shape, dt=F32):
        return self.es.enter_context(self.nc.psum_tensor(name, list(shape), dt))

    def sem(self, name=None):
        self._semn += 1
        es = getattr(self, "sem_es", None) or self.es
        return es.enter_context(self.nc.semaphore(name or f"s{self._semn}"))


class Script:
    ENGINES = ("sync", "gpsimd", "vector", "scalar", "tensor")

    def __init__(self, P):
        self.P = P
        self.steps = []
        self.count = {}

    POOL_MAX = {"c": 50, "d": 20, "g": 20}

    class LSem:
        __slots__ = ("h", "base")

        def __init__(self, h, base):
            self.h = h
            self.base = base

    def sem(self, name=None, kind="c"):
        pools = getattr(self, "pools", None)
        if pools is None:
            self.pools = pools = {"c": [], "d": [], "g": []}
            self.pidx = {"c": 0, "d": 0, "g": 0}
            self.ledger = {}
        pool = pools[kind]
        if len(pool) < self.POOL_MAX[kind]:
            h = self.P.sem(name)
            self.ledger[id(h)] = 0
            pool.append(h)
        else:
            h = pool[self.pidx[kind] % len(pool)]
            self.pidx[kind] += 1
        ls = Script.LSem(h, self.ledger[id(h)])
        self.count[id(ls)] = [ls, 0]
        return ls

    def n(self, h):
        return self.count[id(h)][1]

    def step(self, engine, emit, waits=(), incs=()):
        assert engine in self.ENGINES
        assert len(incs) <= 1, "hardware: one semaphore update per instruction"
        rw = []
        for w in waits:
            if isinstance(w, tuple):
                ls, v = w
                v = self.n(ls) if v is None else v
            else:
                ls, v = w, self.n(w)
            assert v <= self.n(ls), f"wait target {v} not yet emitted (sem at {self.n(ls)})"
            if v > 0:
                rw.append((ls.h, v + ls.base))
        for ls, amt in incs:
            self.count[id(ls)][1] += amt
            self.ledger[id(ls.h)] += amt
        self.steps.append((engine, rw, emit, [(ls.h, amt) for ls, amt in incs]))

    def replay(self, block):
        for eng_name in self.ENGINES:
            mysteps = [s for s in self.steps if s[0] == eng_name]
            if not mysteps:
                continue
            sec = getattr(block, eng_name)

            def _body(eng, mysteps=mysteps):
                for _, waits, emit, incs in mysteps:
                    for h, v in waits:
                        eng.wait_ge(h, v)
                    instr = emit(eng)
                    for h, amt in incs:
                        assert instr is not None, "inc on step with no instruction"
                        instr.then_inc(h, amt)
            sec(_body)



def _colmajor_vec(v):
    return np.ascontiguousarray(np.asarray(v, dtype=np.float32).reshape(8, 128).T)

def _tile_colblocks(wT):
    cin, m = wT.shape
    ncg = m // 128
    a = np.asarray(wT).reshape(cin // 128, 128, ncg, 128)
    return np.ascontiguousarray(a.transpose(2, 1, 0, 3))

def prep_host(inputs):
    g = {}
    x = np.asarray(inputs["L0_attn_x"])
    B, T, _ = x.shape
    g["B"], g["T"] = B, T
    g["xT"] = [np.ascontiguousarray(x[b].T) for b in range(B)]
    g["wkt"] = _tile_colblocks(np.ascontiguousarray(np.asarray(inputs["L0_attn_Wk_t"]).T))
    g["wvt"] = _tile_colblocks(np.ascontiguousarray(np.asarray(inputs["L0_attn_Wv_t"]).T))
    g["wrt"] = _tile_colblocks(np.ascontiguousarray(np.asarray(inputs["L0_attn_Wr_t"]).T))
    pv = np.concatenate([
        _colmajor_vec(inputs["L0_attn_ln1_g"]),
        _colmajor_vec(inputs["L0_attn_ln1_b"]),
        _colmajor_vec(inputs["L0_attn_mix_k"]),
        _colmajor_vec(inputs["L0_attn_mix_v"]),
        _colmajor_vec(inputs["L0_attn_mix_r"]),
        _colmajor_vec(inputs["L0_attn_time_first"]),
        _colmajor_vec(inputs["L0_attn_ln2_g"]),
        _colmajor_vec(inputs["L0_attn_ln2_b"]),
        _colmajor_vec(inputs["L0_attn_cmix_k"]),
        _colmajor_vec(inputs["L0_attn_cmix_r"]),
    ], axis=1)
    g["pv0"] = np.ascontiguousarray(pv.astype(np.float32))
    return g


def emit_a1(P, S, io, gl, T):
    NT = T // NCK
    HALF = T // 2
    NHC = HALF // NCK

    epsc, onecf = gl["epsc"], gl["onecf"]

    xh = P.sbuf("a1_xh", [128, NCG * (T + 1)], BF16)

    def xg(i):
        return i * (T + 1) + 1
    pvt = P.sbuf("a1_pv", [128, 80], F32)
    pvd = P.sbuf("a1_pvd", [128, 40], F32)
    onesc = P.sbuf("a1_ones1", [128, 1], BF16)
    onesr = P.sbuf("a1_onesr", [1, 128], F32)
    sqt = P.sbuf("a1_sqt", [128, 2 * NCK], BF16)
    mrow = P.sbuf("a1_mrow", [1, T], F32)
    rrow = P.sbuf("a1_rrow", [1, T], F32)
    hb1 = P.sbuf("a1_hb1", [128, NCK], F32)
    hb2 = P.sbuf("a1_hb2", [128, NCK], F32)
    wring = P.sbuf("a1_wring", [128, 2 * 3 * C], BF16)
    stage = P.sbuf("a1_stage", [128, 6 * KT * 128], BF16)
    kH = P.sbuf("a1_kH", [128, HALF], F32)
    vH = P.sbuf("a1_vH", [128, HALF], F32)
    zH = P.sbuf("a1_zH", [128, HALF], F32)
    aH = P.sbuf("a1_aH", [128, HALF], F32)
    u1 = P.sbuf("a1_u1", [128, HALF], F32)
    u2 = P.sbuf("a1_u2", [128, HALF], F32)
    wH = P.sbuf("a1_wH", [128, HALF], F32)
    xr = P.sbuf("a1_xr", [128, HALF], F32)
    x1s = P.sbuf("a1_x1s", [128, HALF], F32)
    rg1 = P.sbuf("a1_rg1", [128, NCK], F32)
    rg2 = P.sbuf("a1_rg2", [128, NCK], F32)
    scr = P.sbuf("a1_scr", [128, 16], F32)

    CM, CNM, CMP, CDM, CEDM, CZC, CAC, CZR = range(8)

    pstat = [P.psum(f"a1_ps{i}", [128, NCK]) for i in range(4)]
    pbc = [P.psum(f"a1_pb{i}", [128, NCK]) for i in range(4)]
    pkb = pstat[0:2]
    pvb = pstat[2:4]
    prb = pbc[0]

    sLD0 = S.sem(kind="d"); sX = S.sem(kind="g"); sDRV = S.sem(); sSQ = S.sem(); sSQU = S.sem()
    sSTC = S.sem(); sRW = S.sem(); sBC = S.sem(); sBCC = S.sem()
    sWL = [S.sem(kind="g"), S.sem(kind="g")]; sPR = S.sem(); sKV = S.sem(); sKVE = S.sem()
    sVS = S.sem(); sSV = S.sem(); sRC = S.sem(); sRCC = S.sem()
    sXR = S.sem(kind="d"); sXRC = S.sem(); sCMB = S.sem(); sST = S.sem(kind="d"); sAA = S.sem()
    sVV = S.sem(); sSS = S.sem()

    xTd = io["xT"]; x1Td = io["x1T"]
    wktd = io["wkt"]; wvtd = io["wvt"]; wrtd = io["wrt"]; pv0d = io["pv0"]

    S.step("sync", lambda e: e.dma_start(out=pvt[:, 0:80], in_=pv0d), incs=[(sLD0, 16)])
    S.step("vector", lambda e: e.memset(onesc[:], 1.0))
    for _g in range(NCG):
        S.step("vector", lambda e, _g=_g: e.memset(xh[:, _g * (T + 1):_g * (T + 1) + 1], 0.0))
    S.step("vector", lambda e: e.memset(onesr[:], 1.0))
    S.step("vector", lambda e: e.tensor_scalar(pvd[:, 0:8], pvt[:, 16:24], -1.0, 1.0, ALU.mult, ALU.add),
           waits=[sLD0])
    S.step("vector", lambda e: e.tensor_scalar(pvd[:, 8:16], pvt[:, 24:32], -1.0, 1.0, ALU.mult, ALU.add))
    S.step("vector", lambda e: e.tensor_scalar(pvd[:, 16:24], pvt[:, 32:40], -1.0, 1.0, ALU.mult, ALU.add),
           incs=[(sDRV, 1)])
    S.step("scalar", lambda e: e.activation(pvd[:, 24:32], pvt[:, 40:48], AF.Exp), waits=[sLD0],
           incs=[(sDRV, 1)])

    for cg in range(NCG):
        S.step("gpsimd",
               lambda e, cg=cg: e.dma_start(out=xh[:, xg(cg):xg(cg) + T],
                                            in_=xTd[cg * 128:(cg + 1) * 128, :]),
               incs=[(sX, 16)])

    for ck in range(NT):
        bs = ck % 2
        for cg in range(NCG):
            i = ck * NCG + cg
            sl = (i % 2) * NCK
            S.step("scalar",
                   lambda e, cg=cg, ck=ck, sl=sl: e.activation(
                       sqt[:, sl:sl + NCK], xh[:, xg(cg) + ck * NCK: xg(cg) + (ck + 1) * NCK], AF.Square),
                   waits=[(sX, 16 * NCG)] + ([(sSQU, i - 1)] if i >= 2 else []),
                   incs=[(sSQ, 1)])
            S.step("tensor",
                   lambda e, cg=cg, ck=ck, bs=bs: e.matmul(
                       pstat[bs][0:1, :], onesc[:], xh[:, xg(cg) + ck * NCK: xg(cg) + (ck + 1) * NCK],
                       start=(cg == 0), stop=(cg == NCG - 1)),
                   waits=([(sX, 16 * NCG), (sDRV, 2)] if (ck == 0 and cg == 0) else []) +
                         ([(sSTC, 2 * (ck - 1))] if (cg == 0 and ck >= 2) else []))
            S.step("tensor",
                   lambda e, bs=bs, sl=sl, cg=cg: e.matmul(
                       pstat[2 + bs][0:1, :], onesc[:], sqt[:, sl:sl + NCK],
                       start=(cg == 0), stop=(cg == NCG - 1)),
                   waits=[(sSQ, i + 1)],
                   incs=[(sSQU, 1)])
        S.step("vector",
               lambda e, ck=ck, bs=bs: e.tensor_copy(mrow[0:1, ck * NCK:(ck + 1) * NCK], pstat[bs][0:1, :]),
               waits=[(sSQU, NCG * (ck + 1))])
        S.step("vector",
               lambda e, ck=ck, bs=bs: e.tensor_copy(rrow[0:1, ck * NCK:(ck + 1) * NCK], pstat[2 + bs][0:1, :]),
               incs=[(sSTC, 2)])

    nrw = 0
    for pi in range(2):
        ro = slice(pi * HALF, (pi + 1) * HALF)
        S.step("vector", lambda e, ro=ro: e.tensor_scalar(mrow[0:1, ro], mrow[0:1, ro], 1.0 / C, None, ALU.mult),
               waits=([(sSTC, 2 * NT)] if pi == 0 else []))
        S.step("vector", lambda e, ro=ro: e.tensor_scalar(rrow[0:1, ro], rrow[0:1, ro], 1.0 / C, None, ALU.mult),
               incs=[(sRW, 1)]); nrw += 1
        S.step("scalar", lambda e, ro=ro: e.activation(x1s[0:1, 0:HALF], mrow[0:1, ro], AF.Square),
               waits=[(sRW, nrw)], incs=[(sRW, 1)]); nrw += 1
        S.step("vector", lambda e, ro=ro: e.tensor_sub(rrow[0:1, ro], rrow[0:1, ro], x1s[0:1, 0:HALF]),
               waits=[(sRW, nrw)], incs=[(sRW, 1)]); nrw += 1
        S.step("scalar", lambda e, ro=ro: e.activation(x1s[0:1, 0:HALF], rrow[0:1, ro], AF.Ln, bias=epsc[0:1, 0:1]),
               waits=[(sRW, nrw)], incs=[(sAA, 1)])
        S.step("scalar", lambda e, ro=ro: e.activation(rrow[0:1, ro], x1s[0:1, 0:HALF], AF.Exp, scale=-0.5),
               waits=[sAA], incs=[(sRW, 1)]); nrw += 1

    for ck in range(NT):
        bs = ck % 2
        S.step("tensor",
               lambda e, ck=ck, bs=bs: e.matmul(pbc[bs][:, :], onesr[:], mrow[0:1, ck * NCK:(ck + 1) * NCK],
                                                start=True, stop=True),
               waits=[(sRW, 8)] + ([(sBCC, ck - 1)] if ck >= 2 else []))
        S.step("tensor",
               lambda e, ck=ck, bs=bs: e.matmul(pbc[2 + bs][:, :], onesr[:], rrow[0:1, ck * NCK:(ck + 1) * NCK],
                                                start=True, stop=True),
               incs=[(sBC, 1)])
        for cg in range(NCG):
            lo = xg(cg) + ck * NCK
            S.step("vector",
                   lambda e, lo=lo, bs=bs: e.tensor_sub(hb1[:], xh[:, lo:lo + NCK], pbc[bs][:, :]),
                   waits=([(sBC, ck + 1)] if cg == 0 else []), incs=[(sVV, 1)])
            S.step("vector",
                   lambda e, bs=bs: e.tensor_mul(hb2[:], hb1[:], pbc[2 + bs][:, :]),
                   waits=[(sVV, None)], incs=[(sVV, 1)])
            S.step("vector",
                   lambda e, lo=lo, cg=cg: e.tensor_scalar(xh[:, lo:lo + NCK], hb2[:], pvt[:, cg:cg + 1],
                                                           pvt[:, 8 + cg:8 + cg + 1], ALU.mult, ALU.add),
                   waits=[(sVV, None)],
                   incs=([(sBCC, 1)] if cg == NCG - 1 else []))

    QK1, QK2, QV1, QV2, QR1, QR2 = range(6)
    mixcol = {QK1: (pvt, 16), QK2: (pvd, 0), QV1: (pvt, 24), QV2: (pvd, 8), QR1: (pvt, 32), QR2: (pvd, 16)}
    wofs = {QK1: 0, QK2: 0, QV1: C, QV2: C, QR1: 2 * C, QR2: 2 * C}

    def stg(q, kt):
        o = (q * KT + kt) * 128
        return stage[:, o:o + 128]

    for cg in range(NCG):
        wslot = (cg % 2) * 3 * C
        for wi, wd in enumerate((wktd, wvtd, wrtd)):
            S.step("gpsimd",
                   lambda e, wi=wi, wd=wd, cg=cg, wslot=wslot: e.dma_start(
                       out=wring[:, wslot + wi * C: wslot + (wi + 1) * C],
                       in_=wd[cg].rearrange("p kt j -> p (kt j)")),
                   waits=([(sPR, cg - 1)] if cg >= 2 else []),
                   incs=[(sWL[cg % 2], 16)])
        for q in range(6):
            for kt in range(KT):
                col_t, col_o = mixcol[q]
                S.step("vector",
                       lambda e, q=q, kt=kt, wslot=wslot, col_t=col_t, col_o=col_o: e.tensor_scalar(
                           stg(q, kt), wring[:, wslot + wofs[q] + kt * 128: wslot + wofs[q] + (kt + 1) * 128],
                           col_t[:, col_o + kt:col_o + kt + 1], None, ALU.mult),
                       waits=([(sWL[cg % 2], 48 * (cg // 2 + 1)), (sDRV, 2), (sBCC, NT)] if (q == 0 and kt == 0) else []),
                       incs=([(sPR, 1)] if (q == 5 and kt == KT - 1) else []))
        for hf in range(2):
            hbase = hf * HALF
            first = (hf == 0)
            for hc in range(NHC):
                ck = hf * NHC + hc
                gck = cg * NT + ck
                t0 = hbase + hc * NCK
                kb = pkb[ck % 2]; vb = pvb[ck % 2]
                for q, bank, q2 in ((QK1, kb, QK2), (QV1, vb, QV2)):
                    for kt in range(KT):
                        S.step("tensor",
                               lambda e, q=q, kt=kt, bank=bank, t0=t0: e.matmul(
                                   bank[:, :], stg(q, kt), xh[:, xg(kt) + t0: xg(kt) + t0 + NCK],
                                   start=(kt == 0), stop=False),
                               waits=([(sPR, cg + 1)] + ([(sKVE, gck - 1)] if gck >= 2 else [])
                                      if (q == QK1 and kt == 0) else []))
                    for kt in range(KT):
                        S.step("tensor",
                               lambda e, q2=q2, kt=kt, bank=bank, t0=t0: e.matmul(
                                   bank[:, :], stg(q2, kt), xh[:, xg(kt) + t0 - 1: xg(kt) + t0 + NCK - 1],
                                   start=False, stop=(kt == KT - 1)),
                               incs=([(sKV, 1)] if kt == KT - 1 else []))
                S.step("vector",
                       lambda e, hc=hc, kb=kb: e.tensor_copy(kH[:, hc * NCK:(hc + 1) * NCK], kb[:, :]),
                       waits=[(sKV, 2 * gck + 2)])
                S.step("vector",
                       lambda e, hc=hc, vb=vb: e.tensor_copy(vH[:, hc * NCK:(hc + 1) * NCK], vb[:, :]),
                       incs=[(sKVE, 1)])

            zs = zH[:, 0:HALF - 1]
            last_v = [None]
            last_s = [None]

            def vs(emit, waits=(), inc=False):
                w = list(waits) + ([last_v[0]] if last_v[0] else [])
                s = sVS if inc else sVV
                S.step("vector", emit, waits=w, incs=[(s, 1)])
                last_v[0] = (s, S.n(s))

            def ss(emit, waits=(), inc=False):
                w = list(waits) + ([last_s[0]] if last_s[0] else [])
                s = sSV if inc else sSS
                S.step("scalar", emit, waits=w, incs=[(s, 1)])
                last_s[0] = (s, S.n(s))

            vs(lambda e: e.reduce_max(scr[:, CM:CM + 1], kH[:], axis=AX.X),
               waits=[(sKVE, cg * NT + hf * NHC + NHC)])
            vs(lambda e: e.memset(u2[:], 1.0))
            if not first:
                vs(lambda e: e.tensor_sub(scr[:, CDM:CDM + 1], scr[:, CMP:CMP + 1], scr[:, CM:CM + 1]))
            vs(lambda e: e.tensor_scalar(scr[:, CNM:CNM + 1], scr[:, CM:CM + 1], -1.0, None, ALU.mult),
               inc=True)
            if not first:
                ss(lambda e: e.activation(scr[:, CEDM:CEDM + 1], scr[:, CDM:CDM + 1], AF.Exp),
                   waits=[(sVS, None)])
            ss(lambda e: e.activation(wH[:], kH[:], AF.Exp, bias=scr[:, CNM:CNM + 1]),
               waits=[(sVS, None)], inc=True)
            if not first:
                vs(lambda e: e.tensor_mul(scr[:, CZC:CZC + 1], scr[:, CZR:CZR + 1], scr[:, CEDM:CEDM + 1]),
                   waits=[(sSV, None)])
            init = 0.0 if first else scr[:, CZC:CZC + 1]
            vs(lambda e, init=init: e.tensor_tensor_scan(zH[:], u2[:], wH[:], init, ALU.mult, ALU.add),
               waits=[(sSV, None)])
            if first:
                vs(lambda e: e.memset(u1[:, 0:1], 0.0))
            else:
                vs(lambda e: e.tensor_tensor(u1[:, 0:1], scr[:, CZC:CZC + 1], wH[:, 0:1], ALU.is_ge))
            vs(lambda e: e.tensor_tensor(u1[:, 1:HALF], zs, wH[:, 1:HALF], ALU.is_ge))
            vs(lambda e: e.tensor_scalar(u2[:], u1[:], EPS_WKV, 1.0, ALU.mult, ALU.add))
            vs(lambda e: e.tensor_scalar(u1[:], u1[:], -EPS_WKV, 1.0 + EPS_WKV, ALU.mult, ALU.add))
            vs(lambda e: e.tensor_mul(u1[:], u1[:], wH[:]))
            vs(lambda e, init=init: e.tensor_tensor_scan(zH[:], u2[:], u1[:], init, ALU.mult, ALU.add))
            if first:
                vs(lambda e: e.tensor_copy(u2[:, 0:1], wH[:, 0:1]))
            else:
                vs(lambda e: e.tensor_tensor(u2[:, 0:1], scr[:, CZC:CZC + 1], wH[:, 0:1], ALU.max))
            vs(lambda e: e.tensor_tensor(u2[:, 1:HALF], zs, wH[:, 1:HALF], ALU.max), inc=True)
            ss(lambda e: e.activation(u1[:], u2[:], AF.Ln), waits=[(sVS, None)])
            ss(lambda e: e.activation(u2[:], u1[:], AF.Exp, scale=-1.0), inc=True)
            if first:
                vs(lambda e: e.memset(u1[:, 0:1], 0.0), waits=[(sSV, None)])
            else:
                vs(lambda e: e.tensor_tensor(u1[:, 0:1], scr[:, CZC:CZC + 1], wH[:, 0:1], ALU.is_ge),
                   waits=[(sSV, None)])
            vs(lambda e: e.tensor_tensor(u1[:, 1:HALF], zs, wH[:, 1:HALF], ALU.is_ge))
            if first:
                vs(lambda e: e.memset(kH[:, 0:1], 0.0))
            else:
                vs(lambda e: e.tensor_mul(kH[:, 0:1], scr[:, CZC:CZC + 1], u2[:, 0:1]))
            vs(lambda e: e.tensor_mul(kH[:, 1:HALF], zs, u2[:, 1:HALF]))
            vs(lambda e: e.tensor_scalar(kH[:], kH[:], 1.0, None, ALU.min))
            vs(lambda e: e.tensor_tensor(kH[:], kH[:], u1[:], ALU.max))
            vs(lambda e: e.tensor_mul(u2[:], u2[:], wH[:]))
            vs(lambda e: e.tensor_scalar(u2[:], u2[:], 1.0, None, ALU.min))
            vs(lambda e: e.tensor_mul(u2[:], u2[:], vH[:]))
            ainit = 0.0 if first else scr[:, CAC:CAC + 1]
            vs(lambda e, ainit=ainit: e.tensor_tensor_scan(aH[:], kH[:], u2[:], ainit, ALU.mult, ALU.add))
            vs(lambda e, cg=cg: e.tensor_scalar(u2[:], wH[:], pvd[:, 24 + cg:24 + cg + 1], None, ALU.mult))
            if first:
                vs(lambda e: e.tensor_copy(wH[:, 0:1], u2[:, 0:1]))
            else:
                vs(lambda e: e.tensor_tensor(wH[:, 0:1], scr[:, CZC:CZC + 1], u2[:, 0:1], ALU.max))
            vs(lambda e: e.tensor_tensor(wH[:, 1:HALF], zs, u2[:, 1:HALF], ALU.max))
            vs(lambda e: e.scalar_tensor_tensor(wH[:], wH[:], EPS_WKV, u2[:], ALU.mult, ALU.add))
            vs(lambda e: e.tensor_add(wH[:, 1:HALF], wH[:, 1:HALF], zs))
            if not first:
                vs(lambda e: e.tensor_add(wH[:, 0:1], wH[:, 0:1], scr[:, CZC:CZC + 1]), inc=True)
            else:
                vs(lambda e: e.tensor_copy(scr[:, CZR:CZR + 1], zH[:, HALF - 1:HALF]), inc=True)
            ss(lambda e: e.activation(u1[:], wH[:], AF.Ln), waits=[(sVS, None)])
            ss(lambda e: e.activation(wH[:], u1[:], AF.Exp, scale=-1.0), inc=True)
            if first:
                vs(lambda e: e.memset(u1[:, 0:1], 0.0), waits=[(sSV, None)])
            else:
                vs(lambda e: e.tensor_mul(u1[:, 0:1], scr[:, CZC:CZC + 1], scr[:, CAC:CAC + 1]),
                   waits=[(sSV, None)])
            vs(lambda e: e.tensor_mul(u1[:, 1:HALF], zs, aH[:, 0:HALF - 1]))
            vs(lambda e: e.tensor_mul(u2[:], u2[:], vH[:]))
            vs(lambda e: e.tensor_add(u1[:], u1[:], u2[:]))
            vs(lambda e: e.tensor_mul(u1[:], u1[:], wH[:]))
            if hf == 0:
                vs(lambda e: e.tensor_copy(scr[:, CAC:CAC + 1], aH[:, HALF - 1:HALF]))
                vs(lambda e: e.tensor_copy(scr[:, CMP:CMP + 1], scr[:, CM:CM + 1]))
            wkv_tok = last_v[0]

            ih = 2 * cg + hf
            S.step("sync",
                   lambda e, cg=cg, hbase=hbase: e.dma_start(
                       out=xr[:], in_=xTd[cg * 128:(cg + 1) * 128, hbase:hbase + HALF]),
                   waits=([(sXRC, ih)] if ih >= 1 else []),
                   incs=[(sXR, 16)])

            for hc in range(NHC):
                t0 = hbase + hc * NCK
                j = cg * NT + hf * NHC + hc
                for kt in range(KT):
                    S.step("tensor",
                           lambda e, kt=kt, t0=t0: e.matmul(prb[:, :], stg(QR1, kt),
                                                            xh[:, xg(kt) + t0: xg(kt) + t0 + NCK],
                                                            start=(kt == 0), stop=False),
                           waits=([(sRCC, j)] if (kt == 0 and j >= 1) else []))
                for kt in range(KT):
                    S.step("tensor",
                           lambda e, kt=kt, t0=t0: e.matmul(prb[:, :], stg(QR2, kt),
                                                            xh[:, xg(kt) + t0 - 1: xg(kt) + t0 + NCK - 1],
                                                            start=False, stop=(kt == KT - 1)),
                           incs=([(sRC, 1)] if kt == KT - 1 else []))
                S.step("scalar", lambda e: e.activation(rg1[:], prb[:, :], AF.Exp, scale=-1.0),
                       waits=[sRC] + ([(sCMB, j)] if j >= 1 else []), incs=[(sAA, 1)])
                S.step("scalar", lambda e: e.activation(rg2[:], rg1[:], AF.Ln, bias=onecf[:, 0:1]),
                       waits=[sAA], incs=[(sAA, 1)])
                S.step("scalar", lambda e: e.activation(rg1[:], rg2[:], AF.Exp, scale=-1.0),
                       waits=[sAA], incs=[(sRCC, 1)])
                sl = slice(hc * NCK, (hc + 1) * NCK)
                S.step("vector", lambda e, sl=sl: e.tensor_mul(hb1[:], rg1[:], u1[:, sl]),
                       waits=[sRCC, (sXR, 16 * (ih + 1)), wkv_tok] +
                             ([(sST, 16 * ih)] if (hc == 0 and ih >= 1) else []) +
                             ([prev_add_tok] if hc >= 1 else []),
                       incs=[(sCMB, 1)])
                S.step("vector", lambda e, sl=sl: e.tensor_add(x1s[:, sl], hb1[:], xr[:, sl]),
                       waits=[(sCMB, None)],
                       incs=([(sXRC, 1)] if hc == NHC - 1 else [(sVV, 1)]))
                prev_add_tok = (sVV, S.n(sVV))
            S.step("sync",
                   lambda e, cg=cg, hbase=hbase: e.dma_start(
                       out=x1Td[cg * 128:(cg + 1) * 128, hbase:hbase + HALF], in_=x1s[:]),
                   waits=[(sXRC, ih + 1)],
                   incs=[(sST, 16)])
    S.step("sync", lambda e: None, waits=[(sST, 16 * NCG * 2)])
    return (sST, 16 * NCG * 2)


def build_a1(T):
    P = Prog("a1")
    nc = P.nc
    io = {
        "xT": nc.dram_tensor("xT", [C, T], F32, kind="ExternalInput").ap(),
        "x1T": nc.dram_tensor("x1T", [C, T], F32, kind="ExternalOutput").ap(),
        "wkt": nc.dram_tensor("wkt", [8, 128, 8, 128], F32, kind="ExternalInput").ap(),
        "wvt": nc.dram_tensor("wvt", [8, 128, 8, 128], F32, kind="ExternalInput").ap(),
        "wrt": nc.dram_tensor("wrt", [8, 128, 8, 128], F32, kind="ExternalInput").ap(),
        "pv0": nc.dram_tensor("pv0", [128, 80], F32, kind="ExternalInput").ap(),
    }
    with P.es:
        gl = {
            "epsc": P.sbuf("g_eps", [1, 1], F32),
            "onecf": P.sbuf("g_one", [128, 1], F32),
        }
        S = Script(P)
        S.step("vector", lambda e: e.memset(gl["epsc"][:], EPS_LN))
        S.step("vector", lambda e: e.memset(gl["onecf"][:], 1.0))
        block = P.es.enter_context(nc.Block())
        emit_a1(P, S, io, gl, T)
        S.replay(block)
    return nc


def _prep_a2(inputs):
    return {
        "wkc": _tile_colblocks(np.ascontiguousarray(np.asarray(inputs["L0_attn_Wk_c"]).T)),
        "wvc": _tile_colblocks(np.ascontiguousarray(np.asarray(inputs["L0_attn_Wv_c"]).T)),
        "wrc": _tile_colblocks(np.ascontiguousarray(np.asarray(inputs["L0_attn_Wr_c"]).T)),
    }


def kernel_multi(**inputs):
    T = int(np.asarray(inputs["L0_attn_x"]).shape[1])
    B = int(np.asarray(inputs["L0_attn_x"]).shape[0])
    cores = list(range(B))
    g1 = prep_host(inputs)
    ga2 = _prep_a2(inputs)
    gb1 = prep_b1(inputs, T)
    gb2 = prep_b2(inputs, T)
    gc = prep_c(inputs, T)
    gd = prep_d(inputs, T)

    nc_a1 = build_a1(T)
    maps = [dict(xT=g1["xT"][b], wkt=g1["wkt"], wvt=g1["wvt"], wrt=g1["wrt"], pv0=g1["pv0"])
            for b in range(B)]
    r1 = run_bass_kernel_spmd(nc_a1, maps, core_ids=cores)
    x1 = [r1.results[b]["x1T"] for b in range(B)]

    nc_a2 = build_a2(T)
    maps = [dict(xT=g1["xT"][b], x1T=x1[b], wkc=ga2["wkc"], wvc=ga2["wvc"], wrc=ga2["wrc"],
                 pv0=g1["pv0"]) for b in range(B)]
    r2 = run_bass_kernel_spmd(nc_a2, maps, core_ids=cores)
    y = [r2.results[b]["yT"] for b in range(B)]

    nc_b1 = build_b1(T)
    maps = [dict(yT=y[b], wqkv_t=gb1["wqkv_t"], wo_t=gb1["wo_t"], lnw=gb1["lnw"],
                 cosb_d=gb1["cosb_d"], sinb_d=gb1["sinb_d"], masks_d=gb1["masks_d"],
                 ident_d=gb1["ident_d"]) for b in range(B)]
    r3 = run_bass_kernel_spmd(nc_b1, maps, core_ids=cores)
    x2 = [r3.results[b]["x2T"] for b in range(B)]

    nc_b2 = build_b2(T)
    maps = [dict(yT=y[b], x2T=x2[b], gw_t=gb2["gw_t"], w13_t=gb2["w13_t"], w2_t=gb2["w2_t"],
                 ln2w=gb2["ln2w"], identf_d=gb2["identf_d"], oh8_d=gb2["oh8_d"]) for b in range(B)]
    r4 = run_bass_kernel_spmd(nc_b2, maps, core_ids=cores)
    xB = [r4.results[b]["xBT"] for b in range(B)]

    nc_c = build_c(T)
    maps = [dict(xBT=xB[b], key_t=gc["key_t"], vp_t=gc["vp_t"]) for b in range(B)]
    r5 = run_bass_kernel_spmd(nc_c, maps, core_ids=cores)
    xC = [r5.results[b]["xCT"] for b in range(B)]

    nc_d = build_d(T)
    maps = [dict(xCT=xC[b], l1w_t=gd["l1w_t"], alpha_d=gd["alpha_d"], bias_col=gd["bias_col"])
            for b in range(B)]
    r6 = run_bass_kernel_spmd(nc_d, maps, core_ids=cores)
    out = np.stack([np.ascontiguousarray(r6.results[b]["outT"].T) for b in range(B)], axis=0)
    return out.astype(np.float32)



def emit_rows(P, S, pref, src, T, rms, rowbufs, waits0=()):
    NT = T // NCK
    mrow, rrow, tmprow = rowbufs
    xall = P.sbuf(pref + "_xall", [128, NCG * T], BF16)
    sqt = P.sbuf(pref + "_sqt", [128, 2 * NCK], BF16)
    onesc = P.sbuf(pref + "_ones1", [128, 1], BF16)
    pst = [P.psum(pref + f"_ps{i}", [128, NCK]) for i in range(4)]
    sX = S.sem(kind="g"); sSQ = S.sem(); sSQU = S.sem(); sSTC = S.sem(); sRW = S.sem(); sAA = S.sem()
    S.step("vector", lambda e: e.memset(onesc[:], 1.0), waits=list(waits0), incs=[(sRW, 1)])
    for cg in range(NCG):
        S.step("gpsimd",
               lambda e, cg=cg: e.dma_start(out=xall[:, cg * T:(cg + 1) * T],
                                            in_=src[cg * 128:(cg + 1) * 128, :]),
               waits=list(waits0), incs=[(sX, 16)])
    for ck in range(NT):
        bs = ck % 2
        for cg in range(NCG):
            i = ck * NCG + cg
            sl = (i % 2) * NCK
            S.step("scalar",
                   lambda e, cg=cg, ck=ck, sl=sl: e.activation(
                       sqt[:, sl:sl + NCK], xall[:, cg * T + ck * NCK: cg * T + (ck + 1) * NCK], AF.Square),
                   waits=[(sX, 16 * NCG)] + ([(sSQU, i - 1)] if i >= 2 else []),
                   incs=[(sSQ, 1)])
            if not rms:
                S.step("tensor",
                       lambda e, cg=cg, ck=ck, bs=bs: e.matmul(
                           pst[bs][0:1, :], onesc[:], xall[:, cg * T + ck * NCK: cg * T + (ck + 1) * NCK],
                           start=(cg == 0), stop=(cg == NCG - 1)),
                       waits=([(sX, 16 * NCG), (sRW, 1)] if (ck == 0 and cg == 0) else []) +
                             ([(sSTC, 2 * (ck - 1))] if (cg == 0 and ck >= 2) else []))
            S.step("tensor",
                   lambda e, bs=bs, sl=sl, cg=cg: e.matmul(
                       pst[2 + bs][0:1, :], onesc[:], sqt[:, sl:sl + NCK],
                       start=(cg == 0), stop=(cg == NCG - 1)),
                   waits=[(sSQ, i + 1)] +
                         ([(sRW, 1)] if (rms and ck == 0 and cg == 0) else []) +
                         ([(sSTC, 2 * (ck - 1))] if (rms and cg == 0 and ck >= 2) else []),
                   incs=[(sSQU, 1)])
        if not rms:
            S.step("vector",
                   lambda e, ck=ck, bs=bs: e.tensor_copy(mrow[0:1, ck * NCK:(ck + 1) * NCK], pst[bs][0:1, :]),
                   waits=[(sSQU, NCG * (ck + 1))])
            S.step("vector",
                   lambda e, ck=ck, bs=bs: e.tensor_copy(rrow[0:1, ck * NCK:(ck + 1) * NCK], pst[2 + bs][0:1, :]),
                   incs=[(sSTC, 2)])
        else:
            S.step("vector",
                   lambda e, ck=ck, bs=bs: e.tensor_copy(rrow[0:1, ck * NCK:(ck + 1) * NCK], pst[2 + bs][0:1, :]),
                   waits=[(sSQU, NCG * (ck + 1))], incs=[(sSTC, 2)])
    nrw = 1
    HALF = T // 2
    epsc = P.gl["epsc"]
    for pi in range(2):
        ro = slice(pi * HALF, (pi + 1) * HALF)
        if not rms:
            S.step("vector", lambda e, ro=ro: e.tensor_scalar(mrow[0:1, ro], mrow[0:1, ro], 1.0 / C, None, ALU.mult),
                   waits=([(sSTC, 2 * NT)] if pi == 0 else []))
        S.step("vector", lambda e, ro=ro: e.tensor_scalar(rrow[0:1, ro], rrow[0:1, ro], 1.0 / C, None, ALU.mult),
               waits=([(sSTC, 2 * NT)] if (pi == 0 and rms) else []), incs=[(sRW, 1)]); nrw += 1
        if not rms:
            S.step("scalar", lambda e, ro=ro: e.activation(tmprow[0:1, 0:HALF], mrow[0:1, ro], AF.Square),
                   waits=[(sRW, nrw)], incs=[(sRW, 1)]); nrw += 1
            S.step("vector", lambda e, ro=ro: e.tensor_sub(rrow[0:1, ro], rrow[0:1, ro], tmprow[0:1, 0:HALF]),
                   waits=[(sRW, nrw)], incs=[(sRW, 1)]); nrw += 1
        S.step("scalar", lambda e, ro=ro: e.activation(tmprow[0:1, 0:HALF], rrow[0:1, ro], AF.Ln, bias=epsc[0:1, 0:1]),
               waits=[(sRW, nrw)], incs=[(sAA, 1)])
        S.step("scalar", lambda e, ro=ro: e.activation(rrow[0:1, ro], tmprow[0:1, 0:HALF], AF.Exp, scale=-0.5),
               waits=[sAA], incs=[(sRW, 1)]); nrw += 1
    return sRW, nrw


def emit_normalize(P, S, pref, src, dst, T, rowbufs, rowsem, rms=False, extra_src_waits=()):
    HALF = T // 2
    NHC = HALF // NCK
    mrow, rrow, _ = rowbufs
    semv, semn = rowsem
    onesr = P.sbuf(pref + "_onesr", [1, 128], F32)
    xr = P.sbuf(pref + "_xr", [128, HALF], F32)
    ys = P.sbuf(pref + "_ys", [128, HALF], F32)
    pbc = [P.psum(pref + f"_pb{i}", [128, NCK]) for i in range(4)]
    sXR = S.sem(kind="d"); sBC = S.sem(); sBCC = S.sem(); sST = S.sem(kind="d"); sVV = S.sem()
    S.step("vector", lambda e: e.memset(onesr[:], 1.0), incs=[(sVV, 1)])
    nb = 0
    nx = 0
    for cg in range(NCG):
        for hf in range(2):
            ih = 2 * cg + hf
            hbase = hf * HALF
            S.step("sync",
                   lambda e, cg=cg, hbase=hbase: e.dma_start(
                       out=xr[:], in_=src[cg * 128:(cg + 1) * 128, hbase:hbase + HALF]),
                   waits=([(sBCC, NHC * ih)] if ih >= 1 else []) + list(extra_src_waits),
                   incs=[(sXR, 16)])
            nx += 16
            for hc in range(NHC):
                ck = hf * NHC + hc
                bs = nb % 2
                if not rms:
                    S.step("tensor",
                           lambda e, ck=ck, bs=bs: e.matmul(pbc[bs][:, :], onesr[:],
                                                            mrow[0:1, ck * NCK:(ck + 1) * NCK],
                                                            start=True, stop=True),
                           waits=([(semv, semn), (sVV, 1)] if nb == 0 else []) +
                                 ([(sBCC, nb - 1)] if nb >= 2 else []))
                S.step("tensor",
                       lambda e, ck=ck, bs=bs: e.matmul(pbc[2 + bs][:, :], onesr[:],
                                                        rrow[0:1, ck * NCK:(ck + 1) * NCK],
                                                        start=True, stop=True),
                       waits=([(semv, semn), (sVV, 1)] if (rms and nb == 0) else []) +
                             ([(sBCC, nb - 1)] if (rms and nb >= 2) else []),
                       incs=[(sBC, 1)])
                nb += 1
                sl = slice(hc * NCK, (hc + 1) * NCK)
                if not rms:
                    S.step("vector",
                           lambda e, sl=sl, bs=bs: e.tensor_sub(ys[:, sl], xr[:, sl], pbc[bs][:, :]),
                           waits=[(sBC, nb), (sXR, nx)] + ([(sST, 16 * ih)] if (hc == 0 and ih >= 1) else []),
                           incs=[(sVV, 1)])
                    S.step("vector",
                           lambda e, sl=sl, bs=bs: e.tensor_mul(ys[:, sl], ys[:, sl], pbc[2 + bs][:, :]),
                           waits=[(sVV, None)], incs=[(sBCC, 1)])
                else:
                    S.step("vector",
                           lambda e, sl=sl, bs=bs: e.tensor_mul(ys[:, sl], xr[:, sl], pbc[2 + bs][:, :]),
                           waits=[(sBC, nb), (sXR, nx)] + ([(sST, 16 * ih)] if (hc == 0 and ih >= 1) else []),
                           incs=[(sBCC, 1)])
            S.step("sync",
                   lambda e, cg=cg, hbase=hbase: e.dma_start(
                       out=dst[cg * 128:(cg + 1) * 128, hbase:hbase + HALF], in_=ys[:]),
                   waits=[(sBCC, NHC * (ih + 1))],
                   incs=[(sST, 16)])
    S.step("sync", lambda e: None, waits=[(sST, 16 * NCG * 2)])
    return sST, 16 * NCG * 2



def emit_a2_tsbuild(P, S, pref, x1d, tsbuf, mixc_off, rowbufs, rowsem, pvt, sST_prev=None):
    T = P.T
    mrow, rrow, _ = rowbufs
    semv, semn = rowsem
    onesr = P.gl["onesr"]
    x1b = P.sbuf(pref + "_x1b", [128, P.T], BF16)
    h2t = P.sbuf(pref + "_h2t", [128, NCK], BF16)
    hb1 = P.sbuf(pref + "_hb1", [128, NCK], F32)
    hb2 = P.sbuf(pref + "_hb2", [128, NCK], F32)
    pbc = [P.psum(pref + f"_pb{i}", [128, NCK]) for i in range(4)]
    NT = T // NCK
    carry = P.sbuf(pref + "_carry", [128, 1], F32)
    sX1 = [S.sem(kind="g") for _ in range(2)]
    sBC = S.sem(); sBCC = S.sem(); sVV = S.sem(); sCGD = S.sem()

    def xg(i):
        return i * (T + 1) + 1
    for _g in range(NCG):
        S.step("vector", lambda e, _g=_g: e.memset(tsbuf[:, _g * (T + 1):_g * (T + 1) + 1], 0.0),
               waits=([(rowsem[0], rowsem[1])] + ([sST_prev] if sST_prev is not None else [])
                      if _g == 0 else []),
               incs=[(sVV, 1)])
    nb = 0
    for cg in range(NCG):
        S.step("gpsimd",
               lambda e, cg=cg: e.dma_start(out=x1b[:, :], in_=x1d[cg * 128:(cg + 1) * 128, :]),
               waits=([(sCGD, cg)] if cg >= 1 else []) + ([] if sST_prev is None or cg > 0 else [sST_prev]),
               incs=[(sX1[cg % 2], 16)])
        for ck in range(NT):
            bs = nb % 2
            S.step("tensor",
                   lambda e, ck=ck, bs=bs: e.matmul(pbc[bs][:, :], onesr[:], mrow[0:1, ck * NCK:(ck + 1) * NCK],
                                                    start=True, stop=True),
                   waits=([(semv, semn)] if nb == 0 else []) + ([(sBCC, nb - 1)] if nb >= 2 else []))
            S.step("tensor",
                   lambda e, ck=ck, bs=bs: e.matmul(pbc[2 + bs][:, :], onesr[:], rrow[0:1, ck * NCK:(ck + 1) * NCK],
                                                    start=True, stop=True),
                   incs=[(sBC, 1)])
            nb += 1
            S.step("vector",
                   lambda e, ck=ck, bs=bs: e.tensor_sub(hb1[:], x1b[:, ck * NCK:(ck + 1) * NCK], pbc[bs][:, :]),
                   waits=[(sX1[cg % 2], 16 * (cg // 2 + 1)), (sBC, nb), (sVV, None)], incs=[(sVV, 1)])
            S.step("vector",
                   lambda e, bs=bs: e.tensor_mul(hb2[:], hb1[:], pbc[2 + bs][:, :]),
                   waits=[(sVV, None)], incs=[(sVV, 1)])
            S.step("vector",
                   lambda e, cg=cg: e.tensor_scalar(h2t[:], hb2[:], pvt[:, 48 + cg:48 + cg + 1],
                                                    pvt[:, 56 + cg:56 + cg + 1], ALU.mult, ALU.add),
                   waits=[(sVV, None)], incs=[(sBCC, 1)])
            S.step("vector",
                   lambda e, cg=cg, ck=ck: e.tensor_scalar(
                       hb1[:, 0:NCK], h2t[:], pvt[:, mixc_off + cg:mixc_off + cg + 1], None, ALU.mult),
                   waits=[(sBCC, None)], incs=[(sVV, 1)])
            S.step("vector",
                   lambda e, cg=cg, ck=ck: e.scalar_tensor_tensor(
                       tsbuf[:, xg(cg) + ck * NCK + 1: xg(cg) + (ck + 1) * NCK],
                       h2t[:, 0:NCK - 1], pvd2_g(P, mixc_off, cg), hb1[:, 1:NCK], ALU.mult, ALU.add),
                   waits=[(sVV, None)], incs=[(sVV, 1)])
            if ck == 0:
                S.step("vector",
                       lambda e, cg=cg, ck=ck: e.tensor_copy(
                           tsbuf[:, xg(cg) + ck * NCK: xg(cg) + ck * NCK + 1], hb1[:, 0:1]),
                       waits=[(sVV, None)], incs=[(sVV, 1)])
            else:
                S.step("vector",
                       lambda e, cg=cg, ck=ck: e.scalar_tensor_tensor(
                           tsbuf[:, xg(cg) + ck * NCK: xg(cg) + ck * NCK + 1],
                           carry[:, 0:1], pvd2_g(P, mixc_off, cg), hb1[:, 0:1], ALU.mult, ALU.add),
                       waits=[(sVV, None)], incs=[(sVV, 1)])
            S.step("vector",
                   lambda e: e.tensor_copy(carry[:, 0:1], h2t[:, NCK - 1:NCK]),
                   waits=[(sVV, None)],
                   incs=([(sCGD, 1)] if ck == NT - 1 else [(sVV, 1)]))
    return sCGD


def pvd2_g(P, mixc_off, cg):
    base = 0 if mixc_off == 64 else 8
    return P.gl["pvd2"][:, base + cg:base + cg + 1]


def emit_a2_core(P, S, io, gl, T, waits0=()):
    NT = T // NCK
    HALF = T // 2
    NHC = HALF // NCK

    def xg(i):
        return i * (T + 1) + 1

    tsck = io["tsck"]
    wkcd = io["wkc"]; wvcd = io["wvc"]
    vvT = io["vvT"]

    sqC = P.sbuf("a2_sqC", [128, 32 * NCK], BF16)
    rel1 = P.sbuf("a2_rel1", [128, 4 * NCK], BF16)
    wkcr = P.sbuf("a2_wkcr", [128, 4 * KT * 128], BF16)
    wvcr = P.sbuf("a2_wvcr", [128, 2 * 32 * 128], BF16)
    vvs = P.sbuf("a2_vvs", [128, 2 * NCK], F32)
    pk = [P.psum(f"a2_pk{i}", [128, NCK]) for i in range(2)]
    pv_ = [P.psum(f"a2_pv{i}", [128, NCK]) for i in range(2)]
    sWK = [S.sem(kind="g") for _ in range(4)]
    sWV = [S.sem(kind="g") for _ in range(2)]
    sKC = S.sem(); sSQ = S.sem(); sVV_ = S.sem(); sVVE = S.sem(); sVST = S.sem(kind="d")
    sAA = S.sem()

    nwk = [0] * 4
    nwv = [0] * 2
    for tc in range(NT):
        for fg in range(32):
            j = tc * 32 + fg
            slot = j % 4
            S.step("gpsimd",
                   lambda e, fg=fg, slot=slot: e.dma_start(
                       out=wkcr[:, slot * KT * 128:(slot + 1) * KT * 128],
                       in_=wkcd[fg].rearrange("p kt j -> p (kt j)")),
                   waits=([(sKC, j - 3)] if j >= 4 else []) + (list(waits0) if j == 0 else []),
                   incs=[(sWK[slot], 16)])
            nwk[slot] += 16
            bk = pk[j % 2]
            for kt in range(KT):
                S.step("tensor",
                       lambda e, kt=kt, fg=fg, tc=tc, bk=bk, slot=slot: e.matmul(
                           bk[:, :], wkcr[:, (slot * KT + kt) * 128:(slot * KT + kt + 1) * 128],
                           tsck[:, xg(kt) + tc * NCK: xg(kt) + (tc + 1) * NCK],
                           start=(kt == 0), stop=(kt == KT - 1)),
                       waits=([(sWK[slot], nwk[slot])] if kt == 0 else []) +
                             ([(sSQ, j - 1)] if (kt == 0 and j >= 2) else []),
                       incs=([(sKC, 1)] if kt == KT - 1 else []))
            S.step("scalar",
                   lambda e, j=j, bk=bk: e.activation(rel1[:, (j % 4) * NCK:(j % 4 + 1) * NCK], bk[:, :], AF.Relu),
                   waits=[(sKC, j + 1)] + ([(sSQ, j - 3)] if j >= 4 else []),
                   incs=[(sAA, 1)])
            S.step("vector",
                   lambda e, j=j, fg=fg: e.tensor_mul(
                       sqC[:, fg * NCK:(fg + 1) * NCK],
                       rel1[:, (j % 4) * NCK:(j % 4 + 1) * NCK], rel1[:, (j % 4) * NCK:(j % 4 + 1) * NCK]),
                   waits=[(sAA, None)] + ([(sVV_, tc * NCG)] if (fg == 0 and tc >= 1) else []),
                   incs=[(sSQ, 1)])
        for cg in range(NCG):
            jj = tc * NCG + cg
            slot = jj % 2
            S.step("gpsimd",
                   lambda e, cg=cg, slot=slot: e.dma_start(
                       out=wvcr[:, slot * 32 * 128:(slot + 1) * 32 * 128],
                       in_=wvcd[cg].rearrange("p ft j -> p (ft j)")),
                   waits=([(sVV_, jj - 1)] if jj >= 2 else []),
                   incs=[(sWV[slot], 16)])
            nwv[slot] += 16
            bv = pv_[jj % 2]
            for ft in range(32):
                S.step("tensor",
                       lambda e, ft=ft, bv=bv, slot=slot: e.matmul(
                           bv[:, :], wvcr[:, (slot * 32 + ft) * 128:(slot * 32 + ft + 1) * 128],
                           sqC[:, ft * NCK:(ft + 1) * NCK],
                           start=(ft == 0), stop=(ft == 31)),
                       waits=([(sWV[slot], nwv[slot]), (sSQ, (tc + 1) * 32)] if ft == 0 else []) +
                             ([(sVVE, jj - 1)] if (ft == 0 and jj >= 2) else []),
                       incs=([(sVV_, 1)] if ft == 31 else []))
            S.step("vector",
                   lambda e, jj=jj, bv=bv: e.tensor_copy(vvs[:, (jj % 2) * NCK:(jj % 2 + 1) * NCK], bv[:, :]),
                   waits=[(sVV_, jj + 1)] + ([(sVST, 16 * jj)] if jj >= 2 else []),
                   incs=[(sVVE, 1)])
            S.step("sync",
                   lambda e, tc=tc, cg=cg, jj=jj: e.dma_start(
                       out=vvT[cg * 128:(cg + 1) * 128, tc * NCK:(tc + 1) * NCK],
                       in_=vvs[:, (jj % 2) * NCK:(jj % 2 + 1) * NCK]),
                   waits=[(sVVE, jj + 1)],
                   incs=[(sVST, 16)])
    S.step("sync", lambda e: None, waits=[(sVST, 16 * NT * NCG)])
    return sVST, 16 * NT * NCG


def emit_a2_combine(P, S, io, gl, T, waits0=()):
    NT = T // NCK
    HALF = T // 2
    NHC = HALF // NCK

    def xg(i):
        return i * (T + 1) + 1
    tscr = io["tscr"]; wrcd = io["wrc"]
    vvT = io["vvT"]; xTd = io["xT"]; x1Td = io["x1T"]; s2Td = io["s2T"]
    onecf = gl["onecf"]

    wrr = P.sbuf("a2c_wrr", [128, 2 * KT * 128], BF16)
    rg1 = P.sbuf("a2c_rg1", [128, NCK], F32)
    rg2 = P.sbuf("a2c_rg2", [128, NCK], F32)
    xr0 = P.sbuf("a2c_xr0", [128, HALF], F32)
    xr1 = P.sbuf("a2c_xr1", [128, HALF], F32)
    vvr = P.sbuf("a2c_vvr", [128, HALF], F32)
    s2s = P.sbuf("a2c_s2s", [128, HALF], F32)
    pr = [P.psum(f"a2c_pr{i}", [128, NCK]) for i in range(2)]
    sWR = [S.sem(kind="g") for _ in range(2)]
    sRC = S.sem(); sRCC = S.sem(); sXR = S.sem(kind="d"); sST = S.sem(kind="d"); sAA = S.sem(); sVV = S.sem()
    nwr = [0] * 2
    nxr = 0
    for cg in range(NCG):
        slot = cg % 2
        S.step("gpsimd",
               lambda e, cg=cg, slot=slot: e.dma_start(
                   out=wrr[:, slot * KT * 128:(slot + 1) * KT * 128],
                   in_=wrcd[cg].rearrange("p kt j -> p (kt j)")),
               waits=([(sRCC, (cg - 1) * NT)] if cg >= 2 else []) + (list(waits0) if cg == 0 else []),
               incs=[(sWR[slot], 16)])
        nwr[slot] += 16
        for hf in range(2):
            ih = 2 * cg + hf
            hbase = hf * HALF
            S.step("sync",
                   lambda e, cg=cg, hbase=hbase: e.dma_start(
                       out=xr0[:], in_=xTd[cg * 128:(cg + 1) * 128, hbase:hbase + HALF]),
                   waits=([(sVV, 3 * NHC * ih)] if ih >= 1 else []) + (list(waits0) if ih == 0 else []),
                   incs=[(sXR, 16)])
            S.step("sync",
                   lambda e, cg=cg, hbase=hbase: e.dma_start(
                       out=xr1[:], in_=x1Td[cg * 128:(cg + 1) * 128, hbase:hbase + HALF]),
                   incs=[(sXR, 16)])
            S.step("sync",
                   lambda e, cg=cg, hbase=hbase: e.dma_start(
                       out=vvr[:], in_=vvT[cg * 128:(cg + 1) * 128, hbase:hbase + HALF]),
                   incs=[(sXR, 16)])
            nxr += 48
            for hc in range(NHC):
                tc = hf * NHC + hc
                j = cg * NT + tc
                bk = pr[j % 2]
                for kt in range(KT):
                    S.step("tensor",
                           lambda e, kt=kt, tc=tc, bk=bk, slot=slot: e.matmul(
                               bk[:, :], wrr[:, (slot * KT + kt) * 128:(slot * KT + kt + 1) * 128],
                               tscr[:, xg(kt) + tc * NCK: xg(kt) + (tc + 1) * NCK],
                               start=(kt == 0), stop=(kt == KT - 1)),
                           waits=([(sWR[slot], nwr[slot])] if kt == 0 else []) +
                                 ([(sRCC, j - 1)] if (kt == 0 and j >= 2) else []),
                           incs=([(sRC, 1)] if kt == KT - 1 else []))
                S.step("scalar", lambda e, bk=bk, j=j: e.activation(rg1[:], bk[:, :], AF.Exp, scale=-1.0),
                       waits=[(sRC, j + 1)] + ([(sVV, 3 * j - 2)] if j >= 1 else []), incs=[(sAA, 1)])
                S.step("scalar", lambda e: e.activation(rg2[:], rg1[:], AF.Ln, bias=onecf[:, 0:1]),
                       waits=[(sAA, None)], incs=[(sAA, 1)])
                S.step("scalar", lambda e: e.activation(rg1[:], rg2[:], AF.Exp, scale=-1.0),
                       waits=[(sAA, None)], incs=[(sRCC, 1)])
                sl = slice(hc * NCK, (hc + 1) * NCK)
                S.step("vector", lambda e, sl=sl: e.tensor_mul(s2s[:, sl], rg1[:], vvr[:, sl]),
                       waits=[(sRCC, None), (sXR, nxr)] + ([(sST, 16 * ih)] if (hc == 0 and ih >= 1) else []),
                       incs=[(sVV, 1)])
                S.step("vector", lambda e, sl=sl: e.tensor_add(s2s[:, sl], s2s[:, sl], xr0[:, sl]),
                       waits=[(sVV, None)], incs=[(sVV, 1)])
                S.step("vector", lambda e, sl=sl: e.tensor_add(s2s[:, sl], s2s[:, sl], xr1[:, sl]),
                       waits=[(sVV, None)], incs=[(sVV, 1)])
            S.step("sync",
                   lambda e, cg=cg, hbase=hbase: e.dma_start(
                       out=s2Td[cg * 128:(cg + 1) * 128, hbase:hbase + HALF], in_=s2s[:]),
                   waits=[(sVV, None)],
                   incs=[(sST, 16)])
    S.step("sync", lambda e: None, waits=[(sST, 16 * NCG * 2)])
    return sST, 16 * NCG * 2


def build_a2(T):
    P = Prog("a2")
    P.T = T
    nc = P.nc
    io = {
        "xT": nc.dram_tensor("xT", [C, T], F32, kind="ExternalInput").ap(),
        "x1T": nc.dram_tensor("x1T", [C, T], F32, kind="ExternalInput").ap(),
        "yT": nc.dram_tensor("yT", [C, T], F32, kind="ExternalOutput").ap(),
        "wkc": nc.dram_tensor("wkc", [32, 128, 8, 128], F32, kind="ExternalInput").ap(),
        "wvc": nc.dram_tensor("wvc", [8, 128, 32, 128], F32, kind="ExternalInput").ap(),
        "wrc": nc.dram_tensor("wrc", [8, 128, 8, 128], F32, kind="ExternalInput").ap(),
        "pv0": nc.dram_tensor("pv0", [128, 80], F32, kind="ExternalInput").ap(),
        "vvT": nc.dram_tensor("vvT", [C, T], F32, kind="Internal").ap(),
        "s2T": nc.dram_tensor("s2T", [C, T], F32, kind="Internal").ap(),
    }
    with P.es:
        P.sem_es = P.es
        gl = {
            "epsc": P.sbuf("g_eps", [1, 1], F32),
            "onecf": P.sbuf("g_one", [128, 1], F32),
            "onesr": P.sbuf("g_onesr", [1, 128], F32),
            "pvd2": P.sbuf("g_pvd2", [128, 16], F32),
        }
        P.gl = gl
        pvt = P.sbuf("g_pvt", [128, 80], F32)
        rowbufs = (P.sbuf("g_mrow", [1, T], F32), P.sbuf("g_rrow", [1, T], F32), P.sbuf("g_trow", [1, T // 2], F32))
        S = Script(P)
        sLD0 = S.sem()
        S.step("vector", lambda e: e.memset(gl["epsc"][:], EPS_LN))
        S.step("vector", lambda e: e.memset(gl["onecf"][:], 1.0))
        S.step("vector", lambda e: e.memset(gl["onesr"][:], 1.0))
        S.step("sync", lambda e: e.dma_start(out=pvt[:, 0:80], in_=io["pv0"]), incs=[(sLD0, 16)])
        S.step("vector", lambda e: e.tensor_scalar(gl["pvd2"][:, 0:8], pvt[:, 64:72], -1.0, 1.0, ALU.mult, ALU.add),
               waits=[sLD0])
        S.step("vector", lambda e: e.tensor_scalar(gl["pvd2"][:, 8:16], pvt[:, 72:80], -1.0, 1.0, ALU.mult, ALU.add))
        block = P.es.enter_context(nc.Block())
        with ExitStack() as sc0:
            P_es_old = P.es; P.es = sc0
            rowsem = emit_rows(P, S, "a2r", io["x1T"], T, rms=False, rowbufs=rowbufs)
            P.es = P_es_old
        io2 = dict(io)
        with ExitStack() as sc1:
            P_es_old = P.es; P.es = sc1
            tsck = P.sbuf("a2_tsck", [128, NCG * (T + 1)], BF16)
            sCk = emit_a2_tsbuild(P, S, "a2tk", io["x1T"], tsck, 64, rowbufs, rowsem, pvt,
                                  sST_prev=rowsem)
            io2["tsck"] = tsck
            svst, nvst = emit_a2_core(P, S, io2, gl, T, waits0=[(sCk, NCG)])
            P.es = P_es_old
        with ExitStack() as sc2:
            P_es_old = P.es; P.es = sc2
            tscr = P.sbuf("a2_tscr", [128, NCG * (T + 1)], BF16)
            sCr = emit_a2_tsbuild(P, S, "a2tr", io["x1T"], tscr, 72, rowbufs, rowsem, pvt,
                                  sST_prev=(svst, nvst))
            io2["tscr"] = tscr
            sstc, nstc = emit_a2_combine(P, S, io2, gl, T, waits0=[(sCr, NCG)])
            P.es = P_es_old
        with ExitStack() as sc4:
            P_es_old = P.es; P.es = sc4
            rowsem2 = emit_rows(P, S, "a2p", io["s2T"], T, rms=False, rowbufs=rowbufs,
                                waits0=[(sstc, nstc)])
            emit_normalize(P, S, "a2n", io["s2T"], io["yT"], T, rowbufs, rowsem2,
                           extra_src_waits=[(sstc, nstc)])
            P.es = P_es_old
        S.replay(block)
    return nc



def emit_b1_h1(P, S, io, gl, T, rowbufs, rowsem, lnwsem=None, pref="b1h"):
    NT = T // NCK
    yTd = io["yT"]; h1Td = io["h1T"]; lw = io["lnw_col"]
    onesr = gl["onesr"]
    mrow, rrow, _ = rowbufs
    semv, semn = rowsem
    yb = P.sbuf(pref + "_yb", [128, 2 * P.T], BF16)
    h1t = P.sbuf(pref + "_h1", [128, P.T], BF16)
    pbc = [P.psum(pref + f"_pb{i}", [128, NCK]) for i in range(2)]
    sY = [S.sem(kind="g"), S.sem(kind="g")]; sBC = S.sem(); sBCC = S.sem(); sVV = S.sem(); sST = S.sem(kind="d")
    nb = 0
    for cg in range(NCG):
        S.step("gpsimd",
               lambda e, cg=cg: e.dma_start(out=yb[:, (cg % 2) * T:(cg % 2 + 1) * T],
                                            in_=yTd[cg * 128:(cg + 1) * 128, :]),
               waits=([rowsem] if cg == 0 else []) + ([(sST, 16 * (cg - 1))] if cg >= 2 else []),
               incs=[(sY[cg % 2], 16)])
        for ck in range(NT):
            bs = nb % 2
            S.step("tensor",
                   lambda e, ck=ck, bs=bs: e.matmul(pbc[bs][:, :], onesr[:], rrow[0:1, ck * NCK:(ck + 1) * NCK],
                                                    start=True, stop=True),
                   waits=([(semv, semn)] if nb == 0 else []) + ([(sBCC, nb - 1)] if nb >= 2 else []),
                   incs=[(sBC, 1)])
            nb += 1
            S.step("vector",
                   lambda e, ck=ck, bs=bs, cg=cg: e.tensor_mul(h1t[:, ck * NCK:(ck + 1) * NCK],
                                                               yb[:, (cg % 2) * T + ck * NCK:(cg % 2) * T + (ck + 1) * NCK],
                                                               pbc[bs][:, :]),
                   waits=[(sY[cg % 2], 16 * (cg // 2 + 1)), (sBC, nb)] +
                         ([(sST, 16 * cg)] if (ck == 0 and cg >= 1) else []),
                   incs=[(sVV, 1)])
            S.step("vector",
                   lambda e, ck=ck, cg=cg: e.tensor_scalar(h1t[:, ck * NCK:(ck + 1) * NCK],
                                                           h1t[:, ck * NCK:(ck + 1) * NCK],
                                                           lw[:, cg:cg + 1], None, ALU.mult),
                   waits=[(sVV, None)] + ([lnwsem] if (lnwsem and cg == 0 and ck == 0) else []),
                   incs=[(sBCC, 1)])
        S.step("sync",
               lambda e, cg=cg: e.dma_start(out=h1Td[cg * 128:(cg + 1) * 128, :], in_=h1t[:, :]),
               waits=[(sBCC, nb)],
               incs=[(sST, 16)])
    S.step("sync", lambda e: None, waits=[(sST, 16 * NCG)])
    return sST, 16 * NCG


def emit_b1_qkv(P, S, io, gl, T, waits0=()):
    NT = T // NCK
    h1Td = io["h1T"]; wq = io["wqkv_t"]
    Qr, Kr, Vb, VT = io["Qr"], io["Kr"], io["Vb"], io["VT"]
    cosb, sinb = io["cosb"], io["sinb"]
    ident = io["identb"]
    hr = P.sbuf("b1q_hr", [128, KT * NCK], BF16)
    wqs = P.sbuf("b1q_wqs", [128, 12 * KT * 128], BF16)
    wrs = P.sbuf("b1q_wrs", [128, 10 * KT * 128], BF16)
    t1 = P.sbuf("b1q_t1", [128, NCK], F32)
    t2 = P.sbuf("b1q_t2", [128, NCK], F32)
    pA = [P.psum(f"b1q_pA{i}", [128, NCK]) for i in range(2)]
    pB = [P.psum(f"b1q_pB{i}", [128, NCK]) for i in range(2)]
    pT = P.psum("b1q_pT", [128, 64], BF16)
    sWQ = S.sem(kind="g"); sH = [S.sem(kind="d"), S.sem(kind="d")]; sMM = S.sem(); sRP = S.sem(); sVV = S.sem()
    sTR = S.sem(); sTRE = S.sem(); sAA = S.sem()
    S.step("gpsimd", lambda e: e.dma_start(
               out=wqs[:].rearrange("p (hq kt j) -> p hq kt j", kt=KT, j=128),
               in_=wq.rearrange("hq p kt j -> p hq kt j")),
           waits=list(waits0), incs=[(sWQ, 16)])
    S.step("vector", lambda e: e.tensor_scalar(wqs[:, 0:8 * KT * 128], wqs[:, 0:8 * KT * 128], 0.125, None, ALU.mult),
           waits=[(sWQ, 16)], incs=[(sVV, 1)])
    for hq in range(10):
        for kt in range(KT):
            o = (hq * KT + kt) * 128
            S.step("vector", lambda e, o=o: e.tensor_scalar(wrs[:, o:o + 32], wqs[:, o + 32:o + 64], -1.0, None, ALU.mult),
                   waits=[(sVV, None)])
            S.step("vector", lambda e, o=o: e.tensor_copy(wrs[:, o + 32:o + 64], wqs[:, o:o + 32]))
            S.step("vector", lambda e, o=o: e.tensor_scalar(wrs[:, o + 64:o + 96], wqs[:, o + 96:o + 128], -1.0, None, ALU.mult))
            if not (hq == 9 and kt == KT - 1):
                S.step("vector", lambda e, o=o: e.tensor_copy(wrs[:, o + 96:o + 128], wqs[:, o + 64:o + 96]))
    o9 = (9 * KT + (KT - 1)) * 128
    S.step("vector", lambda e, o9=o9: e.tensor_copy(wrs[:, o9 + 96:o9 + 128], wqs[:, o9 + 64:o9 + 96]),
           incs=[(sRP, 1)])
    nmm = 0
    for tc in range(NT):
        hslot = 0
        S.step("sync",
               lambda e, tc=tc: e.dma_start(
                   out=hr[:, 0:KT * NCK].rearrange("p (kt n) -> p kt n", n=NCK),
                   in_=h1Td[:, tc * NCK:(tc + 1) * NCK].rearrange("(kt p) n -> p kt n", p=128)),
               waits=([(sMM, tc * 12)] if tc >= 1 else []),
               incs=[(sH[0], 16)])
        for hq in range(12):
            ia = (tc * 12 + hq)
            bA = pA[ia % 2]; bB = pB[ia % 2]
            rope = hq < 10
            for kt in range(KT):
                S.step("tensor",
                       lambda e, hq=hq, kt=kt, bA=bA, hslot=hslot: e.matmul(
                           bA[:, :], wqs[:, (hq * KT + kt) * 128:(hq * KT + kt + 1) * 128],
                           hr[:, hslot * KT * NCK + kt * NCK: hslot * KT * NCK + (kt + 1) * NCK],
                           start=(kt == 0), stop=(kt == KT - 1)),
                       waits=([(sH[0], 16 * (tc + 1)), (sRP, 1)] if kt == 0 else []) +
                             ([(sVV, 1 + 3 * (ia - 1))] if (kt == 0 and ia >= 2) else []))
            if rope:
                for kt in range(KT):
                    S.step("tensor",
                           lambda e, hq=hq, kt=kt, bB=bB, hslot=hslot: e.matmul(
                               bB[:, :], wrs[:, (hq * KT + kt) * 128:(hq * KT + kt + 1) * 128],
                               hr[:, hslot * KT * NCK + kt * NCK: hslot * KT * NCK + (kt + 1) * NCK],
                               start=(kt == 0), stop=(kt == KT - 1)),
                           incs=([(sMM, 1)] if kt == KT - 1 else []))
            else:
                S.step("tensor", lambda e, bB=bB: e.matmul(bB[:, 0:1], wqs[:, 0:128], wqs[:, 0:1],
                                                           start=True, stop=True),
                       incs=[(sMM, 1)])
            sl = slice(tc * NCK, (tc + 1) * NCK)
            if rope:
                dst = Qr if hq < 8 else Kr
                do = hq if hq < 8 else (hq - 8)
                S.step("vector",
                       lambda e, bA=bA, sl=sl: e.tensor_mul(t1[:], bA[:, :], cosb[:, sl]),
                       waits=[(sMM, ia + 1), (sVV, None)], incs=[(sVV, 1)])
                S.step("vector",
                       lambda e, bB=bB, sl=sl: e.tensor_mul(t2[:], bB[:, :], sinb[:, sl]),
                       waits=[(sVV, None)], incs=[(sVV, 1)])
                S.step("vector",
                       lambda e, dst=dst, do=do, tc=tc: e.tensor_add(
                           dst[:, do * T + tc * NCK: do * T + (tc + 1) * NCK], t1[:], t2[:]),
                       waits=[(sVV, None)], incs=[(sVV, 1)])
            else:
                do = hq - 10
                S.step("vector",
                       lambda e, bA=bA, do=do, tc=tc: e.tensor_copy(
                           Vb[:, do * T + tc * NCK: do * T + (tc + 1) * NCK], bA[:, :]),
                       waits=[(sMM, ia + 1)], incs=[(sVV, 3)])
            nmm += 1
    S.step("vector", lambda e: e.memset(VT[:], 1.0), waits=[(sVV, None)], incs=[(sVV, 1)])
    ntr = 0
    for kvh in range(4):
        vt = kvh // 2
        pbase = (kvh % 2) * 64
        for kt in range(T // 128):
            S.step("tensor",
                   lambda e, vt=vt, pbase=pbase, kt=kt: e.transpose(
                       pT[:, :], Vb[pbase:pbase + 64, vt * T + kt * 128: vt * T + (kt + 1) * 128],
                       ident[pbase:pbase + 64, 0:64]),
                   waits=[(sVV, None)] + ([(sTRE, ntr)] if ntr >= 1 else []),
                   incs=[(sTR, 1)])
            S.step("vector",
                   lambda e, kvh=kvh, kt=kt: e.tensor_copy(
                       VT[:, (kvh * (T // 128) + kt) * 65:(kvh * (T // 128) + kt) * 65 + 64], pT[:, :]),
                   waits=[(sTR, ntr + 1)], incs=[(sTRE, 1)])
            ntr += 1
    S.step("vector", lambda e: e.tensor_copy(t1[:, 0:1], t1[:, 0:1]), waits=[(sTRE, ntr)], incs=[(sRP, 1)])
    return sRP, 2


def emit_b1_attn(P, S, io, gl, T, qkvsem):
    NT = T // NCK
    NKT = T // 128
    Qr, Kr, VT = io["Qr"], io["Kr"], io["VT"]
    attTd = io["attT"]
    ATTs = P.sbuf("b1a_ATTs", [128, P.T], BF16)
    sATS = S.sem(kind="d")
    masks = io["masksb"]
    onesr = gl["onesr"]
    onesbc = io["onesbc"]
    onesf = io["onesf"]
    sqq = P.sbuf("b1a_sqq", [128, P.T], BF16)
    hsc = P.sbuf("b1a_hsc", [1, 64], F32)
    nmc = P.sbuf("b1a_nmc", [128, 16], F32)
    Pt = P.sbuf("b1a_pt", [128, 8 * NCK], BF16)
    dnr = P.sbuf("b1a_dnr", [128, NCK], F32)
    bcr = P.sbuf("b1a_bcr", [128, NCK], F32)
    pS = [P.psum(f"b1a_pS{i}", [128, NCK]) for i in range(4)]
    pO = [P.psum(f"b1a_pO{i}", [65, NCK]) for i in range(2)]
    pR = P.psum("b1a_pR", [128, NCK])
    pN = P.psum("b1a_pN", [1, NCK])
    sNB = S.sem(); sNBE = S.sem(); sSB = S.sem(); sSE = S.sem(); sPVk = S.sem()
    sDN = S.sem(); sAA = S.sem(); sVV = S.sem(); sOE = S.sem(); sBC2 = S.sem(); sBCE = S.sem()

    nq = 0
    for src_, nh, base in ((Qr, 16, 0), (Kr, 4, 16)):
        npairs = nh // 2
        for hp in range(npairs):
            S.step("vector",
                   lambda e, src_=src_, hp=hp: e.tensor_mul(sqq[:], src_[:, hp * T:(hp + 1) * T],
                                                            src_[:, hp * T:(hp + 1) * T]),
                   waits=([qkvsem] if (hp == 0 and base == 0) else []) + [(sVV, None)], incs=[(sVV, 1)])
            for sub in range(2):
                h = 2 * hp + sub
                for ck in range(NT):
                    S.step("tensor",
                           lambda e, sub=sub, ck=ck: e.matmul(
                               pN[0:1, :], onesbc[sub * 64:sub * 64 + 64, 0:1],
                               sqq[sub * 64:sub * 64 + 64, ck * NCK:(ck + 1) * NCK],
                               start=True, stop=True),
                           waits=[(sVV, None)] + ([(sNBE, nq)] if nq >= 1 else []),
                           incs=[(sNB, 1)])
                    nq += 1
                    S.step("vector",
                           lambda e, h=h, ck=ck, base=base: e.tensor_reduce(
                               hsc[0:1, (base + h):(base + h + 1)] if ck == 0 else hsc[0:1, 48:49],
                               pN[0:1, :], axis=AX.X, op=ALU.max),
                           waits=[(sNB, nq)], incs=[(sNBE, 1)])
                    if ck > 0:
                        S.step("vector",
                               lambda e, h=h, base=base: e.tensor_tensor(
                                   hsc[0:1, (base + h):(base + h + 1)], hsc[0:1, (base + h):(base + h + 1)],
                                   hsc[0:1, 48:49], ALU.max),
                               waits=[(sNBE, None)], incs=[(sVV, 1)])
    S.step("vector", lambda e: e.tensor_copy(hsc[0:1, 48:49], hsc[0:1, 48:49]),
           waits=[(sNBE, nq), (sVV, None)], incs=[(sVV, 1)])
    S.step("scalar", lambda e: e.activation(hsc[0:1, 20:40], hsc[0:1, 0:20], AF.Ln),
           waits=[(sVV, None)], incs=[(sAA, 1)])
    S.step("scalar", lambda e: e.activation(hsc[0:1, 0:20], hsc[0:1, 20:40], AF.Exp, scale=0.5),
           waits=[sAA], incs=[(sDN, 1)])
    for h in range(16):
        S.step("vector",
               lambda e, h=h: e.tensor_mul(hsc[0:1, 32 + h:33 + h], hsc[0:1, h:h + 1],
                                           hsc[0:1, 16 + h // 4:17 + h // 4]),
               waits=[(sDN, 1)] if h == 0 else [], incs=([(sVV, 1)] if h == 15 else []))
    S.step("vector", lambda e: e.tensor_scalar(hsc[0:1, 32:48], hsc[0:1, 32:48], -1.0, None, ALU.mult),
           waits=[(sVV, None)], incs=[(sVV, 1)])
    for h in range(16):
        S.step("tensor",
               lambda e, h=h: e.matmul(pR[:, 0:1], onesr[:], hsc[0:1, 32 + h:33 + h], start=True, stop=True),
               waits=[(sVV, None)] + ([(sBCE, h)] if h >= 1 else []),
               incs=[(sNB, 1)])
        S.step("vector", lambda e, h=h: e.tensor_copy(nmc[:, h:h + 1], pR[:, 0:1]),
               waits=[(sNB, None)], incs=[(sBCE, 1)])

    nse = 0
    npvk = 0
    nob = 0
    GB = 4
    for qh in range(16):
        kvh = qh // 4
        vt = kvh // 2
        kbase = (kvh % 2) * 64
        qt = (qh % 4) + 4 * (qh // 8)
        qbase = ((qh // 4) % 2) * 64
        for tc in range(NT):
            nkt = 4 * (tc + 1)
            iob = qh * NT + tc
            ob = pO[iob % 2]
            for g0 in range(0, nkt, GB):
                kts = list(range(g0, min(g0 + GB, nkt)))
                base_nse = nse
                for kt in kts:
                    sb = pS[nse % 4]
                    S.step("tensor",
                           lambda e, kt=kt, tc=tc, sb=sb, qt=qt, qbase=qbase, vt=vt, kbase=kbase: e.matmul(
                               sb[:, :], Kr[kbase:kbase + 64, vt * T + kt * 128: vt * T + (kt + 1) * 128],
                               Qr[qbase:qbase + 64, qt * T + tc * NCK: qt * T + (tc + 1) * NCK],
                               start=True, stop=True),
                           waits=([(sBCE, 16)] if nse == 0 else []) + ([(sSE, nse - 3)] if nse >= 4 else []),
                           incs=[(sSB, 1)])
                    if kt >= 4 * tc:
                        S.step("vector",
                               lambda e, sb=sb, kt=kt, tc=tc: e.tensor_add(
                                   sb[:, :], sb[:, :], masks[:, (kt - 4 * tc) * NCK:(kt - 4 * tc + 1) * NCK]),
                               waits=[(sSB, nse + 1)], incs=[(sVV, 1)])
                        ewaits = [(sVV, None)]
                    else:
                        ewaits = [(sSB, nse + 1)]
                    S.step("scalar",
                           lambda e, sb=sb, qh=qh, i=nse: e.activation(
                               Pt[:, (i % 8) * NCK:(i % 8 + 1) * NCK], sb[:, :], AF.Exp, bias=nmc[:, qh:qh + 1]),
                           waits=ewaits + ([(sPVk, npvk - 7)] if npvk >= 8 else []),
                           incs=[(sSE, 1)])
                    nse += 1
                    npvk += 1
                for j, kt in enumerate(kts):
                    i = base_nse + j
                    S.step("tensor",
                           lambda e, ob=ob, kvh=kvh, kt=kt, i=i, nkt=nkt: e.matmul(
                               ob[:, :], VT[:, (kvh * NKT + kt) * 65:(kvh * NKT + kt + 1) * 65],
                               Pt[:, (i % 8) * NCK:(i % 8 + 1) * NCK],
                               start=(kt == 0), stop=(kt == nkt - 1)),
                           waits=[(sSE, i + 1)] + ([(sOE, iob - 1)] if (kt == 0 and iob >= 2) else []),
                           incs=[(sPVk, 1)])
            S.step("vector", lambda e, ob=ob: e.reciprocal(bcr[64:65, :], ob[64:65, :]),
                   waits=[(sPVk, npvk)], incs=[(sDN, 1)])
            S.step("tensor",
                   lambda e: e.matmul(pR[0:64, :], onesf[64:65, 0:64], bcr[64:65, :], start=True, stop=True),
                   waits=[(sDN, None)] + ([(sBC2, iob)] if iob >= 1 else []),
                   incs=[(sNB, 1)])
            S.step("vector", lambda e: e.tensor_copy(bcr[0:64, :], pR[0:64, :]),
                   waits=[(sNB, None)], incs=[(sBC2, 1)])
            S.step("vector",
                   lambda e, ob=ob, tc=tc: e.tensor_mul(
                       ATTs[0:64, tc * NCK:(tc + 1) * NCK], ob[0:64, :], bcr[0:64, :]),
                   waits=[(sBC2, None)] + ([(sATS, 16 * qh)] if (tc == 0 and qh >= 1) else []),
                   incs=[(sOE, 1)])
            nob += 1
        S.step("sync",
               lambda e, qt=qt, qbase=qbase: e.dma_start(
                   out=attTd[qt * 128 + qbase: qt * 128 + qbase + 64, :], in_=ATTs[0:64, :]),
               waits=[(sOE, nob)], incs=[(sATS, 16)])
    S.step("sync", lambda e: None, waits=[(sATS, 16 * 16)])
    return sATS, 16 * 16


def emit_b1_wo(P, S, io, gl, T, attsem):
    NT = T // NCK
    attTd = io["attT"]; wod = io["wo_t"]; yTd = io["yT"]; x2Td = io["x2T"]
    wos = P.sbuf("b1w_wos", [128, KT * KT * 128], BF16)
    atr = P.sbuf("b1w_atr", [128, KT * NCK], BF16)
    yr = P.sbuf("b1w_yr", [128, KT * NCK], F32)
    x2s = P.sbuf("b1w_x2s", [128, KT * NCK], F32)
    pW = [P.psum(f"b1w_p{i}", [128, NCK]) for i in range(2)]
    sWL = S.sem(kind="g"); sAT = S.sem(kind="d"); sY = S.sem(kind="d"); sMM = S.sem(); sVV = S.sem(); sST = S.sem(kind="d")
    S.step("gpsimd", lambda e: e.dma_start(
               out=wos[:].rearrange("p (cg kt j) -> p cg kt j", kt=KT, j=128),
               in_=wod.rearrange("cg p kt j -> p cg kt j")),
           waits=[attsem], incs=[(sWL, 16)])
    for tc in range(NT):
        S.step("sync",
               lambda e, tc=tc: e.dma_start(
                   out=atr[:].rearrange("p (kt n) -> p kt n", n=NCK),
                   in_=attTd[:, tc * NCK:(tc + 1) * NCK].rearrange("(kt p) n -> p kt n", p=128)),
               waits=([(sMM, tc * NCG)] if tc >= 1 else []) + ([attsem] if tc == 0 else []),
               incs=[(sAT, 16)])
        S.step("sync",
               lambda e, tc=tc: e.dma_start(
                   out=yr[:].rearrange("p (cg n) -> p cg n", n=NCK),
                   in_=yTd[:, tc * NCK:(tc + 1) * NCK].rearrange("(cg p) n -> p cg n", p=128)),
               waits=([(sVV, tc * NCG)] if tc >= 1 else []),
               incs=[(sY, 16)])
        for cg in range(NCG):
            j = tc * NCG + cg
            b = pW[j % 2]
            for kt in range(KT):
                S.step("tensor",
                       lambda e, cg=cg, kt=kt, b=b: e.matmul(
                           b[:, :], wos[:, (cg * KT + kt) * 128:(cg * KT + kt + 1) * 128],
                           atr[:, kt * NCK:(kt + 1) * NCK],
                           start=(kt == 0), stop=(kt == KT - 1)),
                       waits=([(sWL, 16), (sAT, 16 * (tc + 1))] if kt == 0 and cg == 0 else []) +
                             ([(sVV, j - 1)] if (kt == 0 and j >= 2) else []),
                       incs=([(sMM, 1)] if kt == KT - 1 else []))
            S.step("vector", lambda e, cg=cg, b=b: e.tensor_add(
                       x2s[:, cg * NCK:(cg + 1) * NCK], b[:, :], yr[:, cg * NCK:(cg + 1) * NCK]),
                   waits=[(sMM, j + 1), (sY, 16 * (tc + 1))] +
                         ([(sST, 16 * tc)] if (cg == 0 and tc >= 1) else []),
                   incs=[(sVV, 1)])
        S.step("sync",
               lambda e, tc=tc: e.dma_start(
                   out=x2Td[:, tc * NCK:(tc + 1) * NCK].rearrange("(cg p) n -> p cg n", p=128),
                   in_=x2s[:].rearrange("p (cg n) -> p cg n", n=NCK)),
               waits=[(sVV, (tc + 1) * NCG)],
               incs=[(sST, 16)])
    S.step("sync", lambda e: None, waits=[(sST, 16 * NT)])
    return sST, 16 * NT


def build_b1(T):
    P = Prog("b1")
    P.T = T
    nc = P.nc
    NKT = T // 128
    io = {
        "yT": nc.dram_tensor("yT", [C, T], F32, kind="ExternalInput").ap(),
        "x2T": nc.dram_tensor("x2T", [C, T], F32, kind="ExternalOutput").ap(),
        "wqkv_t": nc.dram_tensor("wqkv_t", [12, 128, 8, 128], F32, kind="ExternalInput").ap(),
        "wo_t": nc.dram_tensor("wo_t", [8, 128, 8, 128], F32, kind="ExternalInput").ap(),
        "lnw": nc.dram_tensor("lnw", [128, 8], F32, kind="ExternalInput").ap(),
        "cosb_d": nc.dram_tensor("cosb_d", [128, T], F32, kind="ExternalInput").ap(),
        "sinb_d": nc.dram_tensor("sinb_d", [128, T], F32, kind="ExternalInput").ap(),
        "masks_d": nc.dram_tensor("masks_d", [128, 4 * NCK], F32, kind="ExternalInput").ap(),
        "ident_d": nc.dram_tensor("ident_d", [128, 128], F32, kind="ExternalInput").ap(),
        "h1T": nc.dram_tensor("h1T", [C, T], BF16, kind="Internal").ap(),
        "attT": nc.dram_tensor("attT", [C, T], BF16, kind="Internal").ap(),
    }
    with P.es:
        P.sem_es = P.es
        gl = {
            "epsc": P.sbuf("g_eps", [1, 1], F32),
            "onecf": P.sbuf("g_one", [128, 1], F32),
            "onesr": P.sbuf("g_onesr", [1, 128], F32),
        }
        P.gl = gl
        mrow_d = P.sbuf("g_mrow", [1, 1], F32)
        rrow_g = P.sbuf("g_rrow", [1, T], F32)
        lnwc = P.sbuf("g_lnw", [128, 8], F32)
        S = Script(P)
        sLD0 = S.sem()
        S.step("vector", lambda e: e.memset(gl["epsc"][:], EPS_LN))
        S.step("vector", lambda e: e.memset(gl["onecf"][:], 1.0))
        S.step("vector", lambda e: e.memset(gl["onesr"][:], 1.0))
        S.step("sync", lambda e: e.dma_start(out=lnwc[:], in_=io["lnw"]), incs=[(sLD0, 16)])
        block = P.es.enter_context(nc.Block())
        with ExitStack() as sc0:
            P_es_old = P.es; P.es = sc0
            rowbufs = (mrow_d, rrow_g, P.sbuf("g_trow", [1, T // 2], F32))
            rowsem = emit_rows(P, S, "b1r", io["yT"], T, rms=True, rowbufs=rowbufs)
            P.es = P_es_old
        io2 = dict(io)
        io2["lnw_col"] = lnwc
        with ExitStack() as sc1:
            P_es_old = P.es; P.es = sc1
            sh1, nh1 = emit_b1_h1(P, S, io2, gl, T, rowbufs, rowsem, lnwsem=(sLD0, 16))
            P.es = P_es_old
        with ExitStack() as scQKA:
            P_es_qka = P.es; P.es = scQKA
            Qr = P.sbuf("b1_Qr", [128, 8 * T], BF16)
            Kr = P.sbuf("b1_Kr", [128, 2 * T], BF16)
            io2["Qr"] = Qr; io2["Kr"] = Kr
            VT = P.sbuf("b1_VT", [128, 4 * NKT * 65], BF16)
            io2["VT"] = VT
            with ExitStack() as sc2:
                P_es_old = P.es; P.es = sc2
                cosb = P.sbuf("b1_cosb", [128, T], BF16)
                sinb = P.sbuf("b1_sinb", [128, T], BF16)
                identb = P.sbuf("b1_identb", [128, 128], BF16)
                io2["cosb"] = cosb; io2["sinb"] = sinb; io2["identb"] = identb
                sC = S.sem()
                S.step("gpsimd", lambda e: e.dma_start(out=cosb[:], in_=io["cosb_d"]),
                       waits=[(sh1, nh1)], incs=[(sC, 16)])
                S.step("gpsimd", lambda e: e.dma_start(out=sinb[:], in_=io["sinb_d"]), incs=[(sC, 16)])
                S.step("gpsimd", lambda e: e.dma_start(out=identb[:], in_=io["ident_d"]), incs=[(sC, 16)])
                Vb = P.sbuf("b1_Vb", [128, 2 * T], BF16)
                io2["Vb"] = Vb
                qkvsem = emit_b1_qkv(P, S, io2, gl, T, waits0=[(sh1, nh1), (sC, 48)])
                P.es = P_es_old
            with ExitStack() as sc3:
                P_es_old = P.es; P.es = sc3
                masksb = P.sbuf("b1_masks", [128, 4 * NCK], F32)
                onesbc = P.sbuf("b1_onesbc", [128, 1], BF16)
                onesf = P.sbuf("b1_onesf", [128, 64], F32)
                io2["masksb"] = masksb
                io2["onesbc"] = onesbc
                io2["onesf"] = onesf
                sM = S.sem()
                S.step("sync", lambda e: e.dma_start(out=masksb[:], in_=io["masks_d"]), waits=[qkvsem],
                       incs=[(sM, 16)])
                S.step("vector", lambda e: e.memset(onesbc[:], 1.0), waits=[(sM, 16), qkvsem])
                S.step("vector", lambda e: e.memset(onesf[:], 1.0))
                attsem = emit_b1_attn(P, S, io2, gl, T, qkvsem)
                P.es = P_es_old
            P.es = P_es_qka
        with ExitStack() as sc4:
            P_es_old = P.es; P.es = sc4
            emit_b1_wo(P, S, io2, gl, T, attsem)
            P.es = P_es_old
        S.replay(block)
    return nc


_HEADLIST = [0, 4, 1, 5, 2, 6, 3, 7, 8, 12, 9, 13, 10, 14, 11, 15]

def prep_b1(inputs, T):
    g = {}
    wqkv = np.asarray(inputs["L0_ffn_wqkv"])
    wo = np.asarray(inputs["L0_ffn_wo"])
    qcols = np.concatenate([np.arange(h * 64, (h + 1) * 64) for h in _HEADLIST] +
                           [np.arange(1024, 1536)])
    wqkvT = np.ascontiguousarray(wqkv.T[:, qcols])
    g["wqkv_t"] = _tile_colblocks(wqkvT)
    orows = np.concatenate([np.arange(h * 64, (h + 1) * 64) for h in _HEADLIST])
    woT = np.ascontiguousarray(wo.T[orows, :])
    g["wo_t"] = _tile_colblocks(woT)
    g["lnw"] = _colmajor_vec(inputs["L0_ffn_ln1_w"])
    hd = 64
    inv = 1.0 / (10000.0 ** (np.arange(0, hd, 2) / hd))
    t = np.arange(T)[:, None]
    fr = t * inv[None, :]
    emb = np.concatenate([fr, fr], axis=1)
    cos = np.cos(emb).T.astype(np.float32)
    sin = np.sin(emb).T.astype(np.float32)
    g["cosb_d"] = np.ascontiguousarray(np.vstack([cos, cos]))
    g["sinb_d"] = np.ascontiguousarray(np.vstack([sin, sin]))
    i_ = np.arange(128)[:, None]
    j_ = np.arange(NCK)[None, :]
    masks = np.concatenate([np.where(128 * o + i_ <= j_, 0.0, -1e30) for o in range(4)],
                           axis=1).astype(np.float32)
    g["masks_d"] = np.ascontiguousarray(masks)
    g["ident_d"] = np.ascontiguousarray(
        np.hstack([np.tile(np.eye(64, dtype=np.float32), (2, 1)), np.zeros((128, 64), np.float32)]))
    return g



def emit_b2_gates(P, S, io, gl, T, waits0=()):
    NT = T // NCK
    h2Td = io["h2T"]; gwd = io["gw_t"]; identf = io["identf"]
    g8all = io["g8all"]
    onesr = gl["onesr"]
    gws = P.sbuf("b2g_gws", [128, 64], BF16)
    hr = P.sbuf("b2g_hr", [128, KT * NCK], BF16)
    lg8 = P.sbuf("b2g_lg8", [8, NCK], F32)
    ltt = P.sbuf("b2g_ltt", [128, 8], F32)
    gtt = P.sbuf("b2g_gtt", [128, 8], F32)
    tf1 = P.sbuf("b2g_tf1", [128, 8], F32)
    c0 = P.sbuf("b2g_c0", [128, 4], F32)
    pL = P.psum("b2g_pL", [8, NCK])
    pT1 = P.psum("b2g_pT1", [128, 8])
    pT2 = P.psum("b2g_pT2", [8, 128])
    sW = S.sem(kind="g"); sH = S.sem(kind="d"); sMM = S.sem(); sLE = S.sem(); sT1 = S.sem(); sT1E = S.sem()
    sT2 = S.sem(); sT2E = S.sem(); sVV = S.sem(); sAA = S.sem()
    S.step("gpsimd", lambda e: e.dma_start(out=gws[:], in_=gwd), waits=list(waits0), incs=[(sW, 16)])
    nt1 = 0
    for tc in range(NT):
        S.step("sync",
               lambda e, tc=tc: e.dma_start(
                   out=hr[:].rearrange("p (kt n) -> p kt n", n=NCK),
                   in_=h2Td[:, tc * NCK:(tc + 1) * NCK].rearrange("(kt p) n -> p kt n", p=128)),
               waits=([(sMM, tc)] if tc >= 1 else []) + (list(waits0) if tc == 0 else []),
               incs=[(sH, 16)])
        for kt in range(KT):
            S.step("tensor",
                   lambda e, kt=kt: e.matmul(pL[:, :], gws[:, kt * 8:(kt + 1) * 8],
                                             hr[:, kt * NCK:(kt + 1) * NCK],
                                             start=(kt == 0), stop=(kt == KT - 1)),
                   waits=([(sW, 16), (sH, 16 * (tc + 1))] if kt == 0 else []) +
                         ([(sLE, tc)] if (kt == 0 and tc >= 1) else []),
                   incs=([(sMM, 1)] if kt == KT - 1 else []))
        S.step("vector", lambda e: e.tensor_copy(lg8[:, :], pL[:, :]),
               waits=[(sMM, tc + 1), (sVV, None)], incs=[(sLE, 1)])
        for tt in range(NCK // 128):
            gt = tc * (NCK // 128) + tt
            S.step("tensor",
                   lambda e, tt=tt: e.transpose(pT1[:, :], lg8[:, tt * 128:(tt + 1) * 128], identf[0:8, 0:8]),
                   waits=[(sLE, tc + 1)] + ([(sT1E, nt1)] if nt1 >= 1 else []),
                   incs=[(sT1, 1)])
            nt1 += 1
            S.step("vector", lambda e: e.tensor_copy(ltt[:, :], pT1[:, :]),
                   waits=[(sT1, nt1)], incs=[(sT1E, 1)])
            S.step("vector", lambda e: e.tensor_reduce(c0[:, 0:1], ltt[:, :], axis=AX.X, op=ALU.max),
                   waits=[(sT1E, None)], incs=[(sVV, 1)])
            S.step("vector", lambda e: e.tensor_scalar(tf1[:, :], ltt[:, :], c0[:, 0:1], None, ALU.is_ge),
                   waits=[(sVV, None)], incs=[(sVV, 1)])
            S.step("vector", lambda e: e.scalar_tensor_tensor(tf1[:, :], tf1[:, :], -1e30, ltt[:, :],
                                                              ALU.mult, ALU.add),
                   waits=[(sVV, None)], incs=[(sVV, 1)])
            S.step("vector", lambda e: e.tensor_reduce(c0[:, 1:2], tf1[:, :], axis=AX.X, op=ALU.max),
                   waits=[(sVV, None)], incs=[(sVV, 1)])
            S.step("vector", lambda e: e.tensor_scalar(c0[:, 2:3], c0[:, 0:1], -1.0, None, ALU.mult),
                   waits=[(sVV, None)], incs=[(sVV, 1)])
            S.step("scalar", lambda e: e.activation(gtt[:, :], ltt[:, :], AF.Exp, bias=c0[:, 2:3]),
                   waits=[(sVV, None)], incs=[(sAA, 1)])
            S.step("vector", lambda e: e.tensor_scalar(tf1[:, :], ltt[:, :], c0[:, 1:2], None, ALU.is_ge),
                   waits=[(sAA, None)], incs=[(sVV, 1)])
            S.step("vector", lambda e: e.tensor_mul(gtt[:, :], gtt[:, :], tf1[:, :]),
                   waits=[(sVV, None)], incs=[(sVV, 1)])
            S.step("vector", lambda e: e.tensor_reduce(c0[:, 3:4], gtt[:, :], axis=AX.X, op=ALU.add),
                   waits=[(sVV, None)], incs=[(sVV, 1)])
            S.step("vector", lambda e: e.reciprocal(c0[:, 2:3], c0[:, 3:4]),
                   waits=[(sVV, None)], incs=[(sVV, 1)])
            S.step("vector", lambda e: e.tensor_scalar(gtt[:, :], gtt[:, :], c0[:, 2:3], None, ALU.mult),
                   waits=[(sVV, None)], incs=[(sVV, 1)])
            S.step("tensor",
                   lambda e: e.transpose(pT2[:, :], gtt[:, :], identf[:, :]),
                   waits=[(sVV, None)] + ([(sT2E, S.n(sT2E))] if S.n(sT2) >= 1 else []),
                   incs=[(sT2, 1)])
            S.step("vector", lambda e, gt=gt: e.tensor_copy(g8all[0:8, gt * 128:(gt + 1) * 128], pT2[:, :]),
                   waits=[(sT2, None)], incs=[(sT2E, 1)])
    S.step("vector", lambda e: e.tensor_copy(c0[:, 0:1], c0[:, 0:1]), waits=[(sT2E, None)],
           incs=[(sVV, 1)])
    return sVV, S.n(sVV)


def emit_b2_experts(P, S, io, gl, T, gatesem, waits0=()):
    NT = T // NCK
    h2Td = io["h2T"]; w13d = io["w13_t"]; w2d = io["w2_t"]
    yTd = io["yT"]; x2Td = io["x2T"]; accTd = io["accT"]
    g8all = io["g8all"]
    w13s = P.sbuf("b2e_w13", [128, KT * 32 * 128], BF16)
    w2s = P.sbuf("b2e_w2", [128, 16 * KT * 128], BF16)
    hr = P.sbuf("b2e_hr", [128, 2 * KT * NCK], BF16)
    gg = P.sbuf("b2e_gg", [128, 16 * NCK], BF16)
    sgt = P.sbuf("b2e_sgt", [128, NCK], F32)
    dmt = P.sbuf("b2e_dmt", [128, NCK], F32)
    grsr = P.sbuf("b2e_grsr", [1, T], BF16)
    onesrb = P.sbuf("b2e_onesrb", [1, 128], BF16)
    gbs = P.sbuf("b2e_gbs", [128, 2 * NCK], F32)
    accs = P.sbuf("b2e_accs", [128, KT * NCK], F32)
    xtr = P.sbuf("b2e_xtr", [128, KT * NCK], F32)
    pG = [P.psum(f"b2e_pG{i}", [128, NCK]) for i in range(4)]
    pD = [P.psum(f"b2e_pD{i}", [128, NCK]) for i in range(2)]
    pB = P.psum("b2e_pB", [128, NCK])
    sW13 = S.sem(kind="g"); sW2 = S.sem(kind="g"); sH = [S.sem(kind="d"), S.sem(kind="d")]; sGR = S.sem(kind="g")
    sGB = S.sem(); sGBE = S.sem()
    sG1 = S.sem(); sSI = S.sem(); sDN = S.sem(); sAC = S.sem(kind="d"); sACC = S.sem()
    sST = S.sem(kind="d"); sVV = S.sem(); sPDE = S.sem()
    nh = 0; nac = 0; nst = 0
    S.step("vector", lambda e_: e_.memset(onesrb[:], 1.0), waits=[gatesem], incs=[(sVV, 1)])
    S.step("sync",
           lambda e_: e_.dma_start(
               out=hr[:, 0:KT * NCK].rearrange("p (kt n) -> p kt n", n=NCK),
               in_=h2Td[:, 0:NCK].rearrange("(kt p) n -> p kt n", p=128)),
           waits=[gatesem] + list(waits0), incs=[(sH[0], 16)])
    nh += 16
    for e in range(NCG):
        S.step("gpsimd",
               lambda e_, e=e: e_.dma_start(out=grsr[0:1, :], in_=g8all[e:e + 1, :]),
               waits=([gatesem] if e == 0 else []) + ([(sGB, NT * e)] if e >= 1 else []),
               incs=[(sGR, 16)])
        S.step("gpsimd",
               lambda e_, e=e: e_.dma_start(
                   out=w13s[:].rearrange("p (kt ft j) -> p kt ft j", ft=32, j=128),
                   in_=w13d[e].rearrange("p kt ft j -> p kt ft j")),
               waits=(list(waits0) if e == 0 else []) + ([(sG1, 32 * NT * e)] if e >= 1 else []),
               incs=[(sW13, 16)])
        S.step("gpsimd",
               lambda e_, e=e: e_.dma_start(
                   out=w2s[:].rearrange("p (ft cg j) -> p ft cg j", cg=KT, j=128),
                   in_=w2d[e].rearrange("p ft cg j -> p ft cg j")),
               waits=(list(waits0) if e == 0 else []) + ([(sDN, NCG * NT * e)] if e >= 1 else []),
               incs=[(sW2, 16)])
        for tc in range(NT):
            it = e * NT + tc
            hs = (it % 2) * KT * NCK
            S.step("tensor",
                   lambda e_, e=e, tc=tc: e_.matmul(pB[:, :], onesrb[0:1, :],
                                                    grsr[0:1, tc * NCK:(tc + 1) * NCK],
                                                    start=True, stop=True),
                   waits=([(sGR, 16 * (e + 1)), (sVV, 1)] if tc == 0 else []) +
                         ([(sGBE, it)] if it >= 1 else []),
                   incs=[(sGB, 1)])
            S.step("vector", lambda e_, it=it: e_.tensor_copy(
                       gbs[:, (it % 2) * NCK:(it % 2 + 1) * NCK], pB[:, :]),
                   waits=[(sGB, it + 1)] + ([(sPDE, NCG * (it - 1))] if it >= 2 else []),
                   incs=[(sGBE, 1)])
            if it + 1 < NCG * NT:
                tcn = (tc + 1) % NT
                S.step("sync",
                       lambda e_, tcn=tcn, hs2=((it + 1) % 2) * KT * NCK: e_.dma_start(
                           out=hr[:, hs2:hs2 + KT * NCK].rearrange("p (kt n) -> p kt n", n=NCK),
                           in_=h2Td[:, tcn * NCK:(tcn + 1) * NCK].rearrange("(kt p) n -> p kt n", p=128)),
                       waits=([(sG1, 32 * it)] if it >= 1 else []),
                       incs=[(sH[(it + 1) % 2], 16)])
                nh += 16
            for ft in range(16):
                b1 = pG[(2 * ft) % 4]; b3 = pG[(2 * ft + 1) % 4]
                for kt in range(KT):
                    S.step("tensor",
                           lambda e_, kt=kt, ft=ft, b1=b1, hs=hs: e_.matmul(
                               b1[:, :], w13s[:, (kt * 32 + ft) * 128:(kt * 32 + ft + 1) * 128],
                               hr[:, hs + kt * NCK:hs + (kt + 1) * NCK],
                               start=(kt == 0), stop=(kt == KT - 1)),
                           waits=([(sW13, 16 * (e + 1)), (sH[it % 2], 16 * (it // 2 + 1))] if (ft == 0 and kt == 0) else []) +
                                 ([(sSI, 2 * (it * 16 + ft) - 2)] if (kt == 0 and (it * 16 + ft) >= 2) else []),
                           incs=([(sG1, 1)] if kt == KT - 1 else []))
                for kt in range(KT):
                    S.step("tensor",
                           lambda e_, kt=kt, ft=ft, b3=b3, hs=hs: e_.matmul(
                               b3[:, :], w13s[:, (kt * 32 + ft + 16) * 128:(kt * 32 + ft + 16 + 1) * 128],
                               hr[:, hs + kt * NCK:hs + (kt + 1) * NCK],
                               start=(kt == 0), stop=(kt == KT - 1)),
                           incs=([(sG1, 1)] if kt == KT - 1 else []))
                S.step("scalar",
                       lambda e_, b1=b1: e_.activation(sgt[:, :], b1[:, :], AF.Silu),
                       waits=[(sG1, None), (sSI, None)],
                       incs=[(sSI, 1)])
                S.step("vector",
                       lambda e_, ft=ft, b3=b3: e_.tensor_mul(gg[:, ft * NCK:(ft + 1) * NCK], sgt[:, :], b3[:, :]),
                       waits=[(sSI, None)], incs=[(sSI, 1)])
            S.step("sync",
                   lambda e_, tc=tc, e=e: e_.dma_start(
                       out=xtr[:].rearrange("p (cg n) -> p cg n", n=NCK),
                       in_=(yTd if e == 0 else accTd)[:, tc * NCK:(tc + 1) * NCK].rearrange(
                           "(cg p) n -> p cg n", p=128)),
                   waits=([(sACC, NCG * it)] if it >= 1 else []),
                   incs=[(sAC, 16)])
            nac += 16
            if e == 0:
                S.step("sync",
                       lambda e_, tc=tc: e_.dma_start(
                           out=accs[:].rearrange("p (cg n) -> p cg n", n=NCK),
                           in_=x2Td[:, tc * NCK:(tc + 1) * NCK].rearrange("(cg p) n -> p cg n", p=128)),
                       waits=([(sST, 16 * it)] if it >= 1 else []),
                       incs=[(sAC, 16)])
                nac += 16
            for cg in range(NCG):
                bd = pD[(it * NCG + cg) % 2]
                for ft in range(16):
                    S.step("tensor",
                           lambda e_, ft=ft, cg=cg, bd=bd: e_.matmul(
                               bd[:, :], w2s[:, (ft * KT + cg) * 128:(ft * KT + cg + 1) * 128],
                               gg[:, ft * NCK:(ft + 1) * NCK],
                               start=(ft == 0), stop=(ft == 15)),
                           waits=([(sW2, 16 * (e + 1))] if (cg == 0 and ft == 0) else []) +
                                 ([(sSI, 2 * (it * 16 + ft) + 2)] if cg == 0 else []) +
                                 ([(sPDE, it * NCG + cg - 1)] if (ft == 0 and (it * NCG + cg) >= 2) else []),
                           incs=([(sDN, 1)] if ft == 15 else []))
                sl = slice(cg * NCK, (cg + 1) * NCK)
                S.step("vector", lambda e_, sl=sl, bd=bd, it=it: e_.tensor_mul(
                           dmt[:, :], bd[:, :], gbs[:, (it % 2) * NCK:(it % 2 + 1) * NCK]),
                       waits=[(sDN, None), (sGBE, None), (sAC, nac), (sACC, None)] +
                             ([(sST, 16 * it)] if (cg == 0 and it >= 1) else []),
                       incs=[(sPDE, 1)])
                if e == 0:
                    S.step("vector", lambda e_, sl=sl: e_.tensor_add(accs[:, sl], accs[:, sl], xtr[:, sl]),
                           waits=[(sPDE, None)], incs=[(sVV, 1)])
                    S.step("vector", lambda e_, sl=sl: e_.tensor_add(accs[:, sl], accs[:, sl], dmt[:, :]),
                           waits=[(sVV, None)], incs=[(sACC, 1)])
                else:
                    S.step("vector", lambda e_, sl=sl: e_.tensor_add(accs[:, sl], xtr[:, sl], dmt[:, :]),
                           waits=[(sPDE, None)], incs=[(sACC, 1)])
            S.step("sync",
                   lambda e_, tc=tc: e_.dma_start(
                       out=accTd[:, tc * NCK:(tc + 1) * NCK].rearrange("(cg p) n -> p cg n", p=128),
                       in_=accs[:].rearrange("p (cg n) -> p cg n", n=NCK)),
                   waits=[(sACC, NCG * (it + 1))],
                   incs=[(sST, 16)])
            nst += 16
    S.step("sync", lambda e: None, waits=[(sST, nst)])
    return sST, nst


def prep_b2(inputs, T):
    g = {}
    gw = np.asarray(inputs["L0_ffn_gate_w"])
    gwT = gw.T.reshape(8, 128, 8).transpose(1, 0, 2).reshape(128, 64)
    g["gw_t"] = np.ascontiguousarray(gwT.astype(np.float32))
    w13 = np.asarray(inputs["L0_ffn_w13"])
    w13T = np.ascontiguousarray(np.transpose(w13, (0, 2, 1)))
    g["w13_t"] = np.ascontiguousarray(
        w13T.reshape(8, 8, 128, 32, 128).transpose(0, 2, 1, 3, 4))
    w2 = np.asarray(inputs["L0_ffn_w2"])
    w2T = np.ascontiguousarray(np.transpose(w2, (0, 2, 1)))
    g["w2_t"] = np.ascontiguousarray(
        w2T.reshape(8, 16, 128, 8, 128).transpose(0, 2, 1, 3, 4))
    g["ln2w"] = _colmajor_vec(inputs["L0_ffn_ln2_w"])
    g["identf_d"] = np.eye(128, dtype=np.float32)
    g["oh8_d"] = np.ascontiguousarray(np.eye(8, dtype=np.float32)[:, :])
    return g


def build_b2(T):
    P = Prog("b2")
    P.T = T
    nc = P.nc
    io = {
        "yT": nc.dram_tensor("yT", [C, T], F32, kind="ExternalInput").ap(),
        "x2T": nc.dram_tensor("x2T", [C, T], F32, kind="ExternalInput").ap(),
        "xBT": nc.dram_tensor("xBT", [C, T], F32, kind="ExternalOutput").ap(),
        "gw_t": nc.dram_tensor("gw_t", [128, 64], F32, kind="ExternalInput").ap(),
        "w13_t": nc.dram_tensor("w13_t", [8, 128, 8, 32, 128], F32, kind="ExternalInput").ap(),
        "w2_t": nc.dram_tensor("w2_t", [8, 128, 16, 8, 128], F32, kind="ExternalInput").ap(),
        "ln2w": nc.dram_tensor("ln2w", [128, 8], F32, kind="ExternalInput").ap(),
        "identf_d": nc.dram_tensor("identf_d", [128, 128], F32, kind="ExternalInput").ap(),
        "oh8_d": nc.dram_tensor("oh8_d", [8, 8], F32, kind="ExternalInput").ap(),
        "h2T": nc.dram_tensor("h2T", [C, T], BF16, kind="Internal").ap(),
        "accT": nc.dram_tensor("accT", [C, T], F32, kind="Internal").ap(),
    }
    with P.es:
        P.sem_es = P.es
        gl = {
            "epsc": P.sbuf("g_eps", [1, 1], F32),
            "onecf": P.sbuf("g_one", [128, 1], F32),
            "onesr": P.sbuf("g_onesr", [1, 128], F32),
        }
        P.gl = gl
        mrow_g = P.sbuf("g_mrow", [1, T], F32)
        rrow_g = P.sbuf("g_rrow", [1, T], F32)
        lnwc = P.sbuf("g_lnw", [128, 8], F32)
        identf = P.sbuf("g_identf", [128, 128], F32)
        oh8 = P.sbuf("g_oh8", [8, 8], BF16)
        g8all = P.sbuf("g_g8all", [8, T], BF16)
        S = Script(P)
        sLD0 = S.sem()
        S.step("vector", lambda e: e.memset(gl["epsc"][:], EPS_LN))
        S.step("vector", lambda e: e.memset(gl["onecf"][:], 1.0))
        S.step("vector", lambda e: e.memset(gl["onesr"][:], 1.0))
        sLD1 = S.sem()
        S.step("sync", lambda e: e.dma_start(out=lnwc[:], in_=io["ln2w"]), incs=[(sLD0, 16)])
        S.step("sync", lambda e: e.dma_start(out=identf[:], in_=io["identf_d"]), incs=[(sLD0, 16)])
        S.step("gpsimd", lambda e: e.dma_start(out=oh8[:], in_=io["oh8_d"]), incs=[(sLD1, 16)])
        block = P.es.enter_context(nc.Block())
        io2 = dict(io)
        io2["identf"] = identf; io2["oh8"] = oh8; io2["g8all"] = g8all
        with ExitStack() as sc0:
            P_es_old = P.es; P.es = sc0
            rowbufs = (mrow_g, rrow_g, P.sbuf("g_trow", [1, T // 2], F32))
            rowsem = emit_rows(P, S, "b2r", io["x2T"], T, rms=True, rowbufs=rowbufs)
            P.es = P_es_old
        with ExitStack() as sc1:
            P_es_old = P.es; P.es = sc1
            io3 = dict(io2)
            io3["yT"] = io["x2T"]; io3["h1T"] = io["h2T"]; io3["lnw_col"] = lnwc
            sh2, nh2 = emit_b1_h1(P, S, io3, gl, T, rowbufs, rowsem, lnwsem=(sLD0, 32), pref="b2h")
            P.es = P_es_old
        with ExitStack() as sc2:
            P_es_old = P.es; P.es = sc2
            gatesem = emit_b2_gates(P, S, io2, gl, T, waits0=[(sh2, nh2), (sLD0, 32), (sLD1, 16)])
            P.es = P_es_old
        with ExitStack() as sc3:
            P_es_old = P.es; P.es = sc3
            sae, nae = emit_b2_experts(P, S, io2, gl, T, gatesem, waits0=[gatesem])
            P.es = P_es_old
        with ExitStack() as sc4:
            P_es_old = P.es; P.es = sc4
            rowbufs2 = (mrow_g, rrow_g, P.sbuf("g_trow2", [1, T // 2], F32))
            rowsem2 = emit_rows(P, S, "b2p", io["accT"], T, rms=False, rowbufs=rowbufs2,
                                waits0=[(sae, nae)])
            emit_normalize(P, S, "b2n", io["accT"], io["xBT"], T, rowbufs2, rowsem2,
                           extra_src_waits=[(sae, nae)])
            P.es = P_es_old
        S.replay(block)
    return nc



def emit_c(P, S, io, gl, T, waits0=()):
    NT = T // NCK
    xBTd = io["xBT"]; keyd = io["key_t"]; vpd = io["vp_t"]; sCd = io["sCT"]
    onesr = gl["onesr"]
    keys = P.sbuf("c_keys", [128, 32 * KT * 128], BF16)
    vps = P.sbuf("c_vps", [128, 32 * KT * 128], BF16)
    onesb = P.sbuf("c_onesb", [128, 1], BF16)
    xbr = P.sbuf("c_xbr", [128, KT * NCK], BF16)
    xfr = P.sbuf("c_xfr", [128, KT * NCK], F32)
    gck = P.sbuf("c_gck", [128, 32 * NCK], BF16)
    gsq = P.sbuf("c_gsq", [128, NCK], BF16)
    srow = P.sbuf("c_srow", [1, NCK], F32)
    sclb = P.sbuf("c_sclb", [128, NCK], F32)
    s5s = P.sbuf("c_s5s", [128, KT * NCK], F32)
    c64 = P.sbuf("c_c64", [1, 1], F32)
    PP = [P.psum(f"c_pp{i}", [128, NCK]) for i in range(8)]
    sWL = S.sem(kind="g"); sVL = S.sem(kind="g"); sX = S.sem(kind="g"); sXF = S.sem(kind="d")
    sMM = S.sem(); sGE = S.sem(); sSQ = S.sem()
    sNR = S.sem(); sNRE = S.sem(); sAA = S.sem(); sBC = S.sem(); sBCE = S.sem()
    sOM = S.sem(); sOE = S.sem(); sST = S.sem(kind="d"); sVV = S.sem()
    S.step("vector", lambda e: e.memset(onesb[:], 1.0), incs=[(sVV, 1)])
    S.step("vector", lambda e: e.memset(c64[:], 4.1588830833596715), incs=[(sVV, 1)])
    S.step("gpsimd", lambda e: e.dma_start(
               out=keys[:].rearrange("p (sg kt j) -> p sg kt j", kt=KT, j=128),
               in_=keyd.rearrange("sg p kt j -> p sg kt j")),
           waits=list(waits0), incs=[(sWL, 16)])
    S.step("gpsimd", lambda e: e.dma_start(
               out=vps[:].rearrange("p (sg vg j) -> p sg vg j", vg=KT, j=128),
               in_=vpd.rearrange("sg p vg j -> p sg vg j")),
           incs=[(sVL, 16)])
    for lc in range(NT):
        S.step("gpsimd",
               lambda e, lc=lc: e.dma_start(
                   out=xbr[:].rearrange("p (kt n) -> p kt n", n=NCK),
                   in_=xBTd[:, lc * NCK:(lc + 1) * NCK].rearrange("(kt p) n -> p kt n", p=128)),
               waits=([(sMM, 32 * lc)] if lc >= 1 else []) + (list(waits0) if lc == 0 else []),
               incs=[(sX, 16)])
        S.step("sync",
               lambda e, lc=lc: e.dma_start(
                   out=xfr[:].rearrange("p (cg n) -> p cg n", n=NCK),
                   in_=xBTd[:, lc * NCK:(lc + 1) * NCK].rearrange("(cg p) n -> p cg n", p=128)),
               waits=([(sVV, 2 + NCG * lc)] if lc >= 1 else []),
               incs=[(sXF, 16)])
        for sg in range(32):
            i = lc * 32 + sg
            sb = PP[i % 2]
            for kt in range(KT):
                S.step("tensor",
                       lambda e, sg=sg, kt=kt, sb=sb: e.matmul(
                           sb[:, :], keys[:, (sg * KT + kt) * 128:(sg * KT + kt + 1) * 128],
                           xbr[:, kt * NCK:(kt + 1) * NCK],
                           start=(kt == 0), stop=(kt == KT - 1)),
                       waits=([(sWL, 16), (sX, 16 * (lc + 1))] if (sg == 0 and kt == 0) else []) +
                             ([(sOE, NCG * lc)] if (sg <= 1 and kt == 0 and lc >= 1) else []) +
                             ([(sGE, i - 1)] if (kt == 0 and i >= 2) else []),
                       incs=([(sMM, 1)] if kt == KT - 1 else []))
            S.step("scalar",
                   lambda e, sg=sg, sb=sb: e.activation(gck[:, sg * NCK:(sg + 1) * NCK], sb[:, :], AF.Gelu),
                   waits=[(sMM, i + 1)],
                   incs=[(sGE, 1)])
            S.step("vector",
                   lambda e, sg=sg: e.tensor_mul(gsq[:, :], gck[:, sg * NCK:(sg + 1) * NCK],
                                                 gck[:, sg * NCK:(sg + 1) * NCK]),
                   waits=[(sGE, None)], incs=[(sSQ, 1)])
            S.step("tensor",
                   lambda e, sg=sg, lc=lc: e.matmul(PP[2][0:1, :], onesb[:], gsq[:, :],
                                                    start=(sg == 0), stop=(sg == 31)),
                   waits=[(sSQ, i + 1)] + ([(sVV, 1)] if (sg == 0 and lc == 0) else []) +
                         ([(sNRE, lc)] if (sg == 0 and lc >= 1) else []) +
                         ([(sOE, NCG * lc)] if (sg == 0 and lc >= 1) else []),
                   incs=([(sNR, 1)] if sg == 31 else []))
        S.step("vector", lambda e: e.tensor_copy(srow[0:1, :], PP[2][0:1, :]),
               waits=[(sNR, lc + 1)], incs=[(sNRE, 1)])
        S.step("scalar", lambda e: e.activation(srow[0:1, :], srow[0:1, :], AF.Ln),
               waits=[(sNRE, None)], incs=[(sAA, 1)])
        S.step("scalar", lambda e: e.activation(srow[0:1, :], srow[0:1, :], AF.Exp, scale=-0.5,
                                                bias=c64[0:1, 0:1]),
               waits=[(sAA, None)], incs=[(sAA, 1)])
        S.step("tensor",
               lambda e: e.matmul(PP[3][:, :], onesr[:], srow[0:1, :], start=True, stop=True),
               waits=[(sAA, None), (sVV, 2)] + ([(sBCE, lc), (sOE, NCG * lc)] if lc >= 1 else []),
               incs=[(sBC, 1)])
        S.step("vector", lambda e: e.tensor_copy(sclb[:, :], PP[3][:, :]),
               waits=[(sBC, None)] + ([(sVV, 2 + NCG * lc)] if lc >= 1 else []),
               incs=[(sBCE, 1)])
        for sg in range(32):
            for vg in range(NCG):
                S.step("tensor",
                       lambda e, sg=sg, vg=vg: e.matmul(
                           PP[vg][:, :], vps[:, (sg * KT + vg) * 128:(sg * KT + vg + 1) * 128],
                           gck[:, sg * NCK:(sg + 1) * NCK],
                           start=(sg == 0), stop=(sg == 31)),
                       waits=([(sVL, 16)] if (lc == 0 and sg == 0 and vg == 0) else []) +
                             ([(sGE, 32 * (lc + 1)), (sBCE, lc + 1)] if (sg == 0 and vg == 0) else []) +
                             ([(sOE, NCG * (lc - 1) + vg + 1)] if (sg == 0 and vg >= 2 and lc >= 1) else []),
                       incs=([(sOM, 1)] if vg == NCG - 1 else []))
        for vg in range(NCG):
            sl = slice(vg * NCK, (vg + 1) * NCK)
            S.step("vector", lambda e, sl=sl, vg=vg: e.tensor_mul(s5s[:, sl], PP[vg][:, :], sclb[:, :]),
                   waits=[(sOM, 32 * (lc + 1)), (sXF, 16 * (lc + 1))] +
                         ([(sST, 16 * lc)] if (vg == 0 and lc >= 1) else []),
                   incs=[(sOE, 1)])
            S.step("vector", lambda e, sl=sl: e.tensor_add(s5s[:, sl], s5s[:, sl], xfr[:, sl]),
                   waits=[(sOE, None)], incs=[(sVV, 1)])
        S.step("sync",
               lambda e, lc=lc: e.dma_start(
                   out=sCd[:, lc * NCK:(lc + 1) * NCK].rearrange("(cg p) n -> p cg n", p=128),
                   in_=s5s[:].rearrange("p (cg n) -> p cg n", n=NCK)),
               waits=[(sVV, 2 + NCG * (lc + 1))],
               incs=[(sST, 16)])
    S.step("sync", lambda e: None, waits=[(sST, 16 * NT)])
    return sST, 16 * NT


def build_c(T):
    P = Prog("c")
    P.T = T
    nc = P.nc
    io = {
        "xBT": nc.dram_tensor("xBT", [C, T], F32, kind="ExternalInput").ap(),
        "xCT": nc.dram_tensor("xCT", [C, T], F32, kind="ExternalOutput").ap(),
        "key_t": nc.dram_tensor("key_t", [32, 128, 8, 128], F32, kind="ExternalInput").ap(),
        "vp_t": nc.dram_tensor("vp_t", [32, 128, 8, 128], F32, kind="ExternalInput").ap(),
        "sCT": nc.dram_tensor("sCT", [C, T], F32, kind="Internal").ap(),
    }
    with P.es:
        P.sem_es = P.es
        gl = {
            "epsc": P.sbuf("g_eps", [1, 1], F32),
            "onecf": P.sbuf("g_one", [128, 1], F32),
            "onesr": P.sbuf("g_onesr", [1, 128], F32),
        }
        P.gl = gl
        S = Script(P)
        S.step("vector", lambda e: e.memset(gl["epsc"][:], EPS_LN))
        S.step("vector", lambda e: e.memset(gl["onecf"][:], 1.0))
        S.step("vector", lambda e: e.memset(gl["onesr"][:], 1.0))
        block = P.es.enter_context(nc.Block())
        with ExitStack() as sc0:
            P_es_old = P.es; P.es = sc0
            sc_, nc_ = emit_c(P, S, io, gl, T)
            P.es = P_es_old
        with ExitStack() as sc1:
            P_es_old = P.es; P.es = sc1
            rowbufs = (P.sbuf("g_mrow", [1, T], F32), P.sbuf("g_rrow", [1, T], F32),
                       P.sbuf("g_trow", [1, T // 2], F32))
            rowsem = emit_rows(P, S, "cp", io["sCT"], T, rms=False, rowbufs=rowbufs,
                               waits0=[(sc_, nc_)])
            emit_normalize(P, S, "cn", io["sCT"], io["xCT"], T, rowbufs, rowsem,
                           extra_src_waits=[(sc_, nc_)])
            P.es = P_es_old
        S.replay(block)
    return nc


def prep_c(inputs, T):
    g = {}
    g["key_t"] = _tile_colblocks(np.ascontiguousarray(np.asarray(inputs["L1_attn_key_param_tokens"]).T))
    vp = np.asarray(inputs["L1_attn_value_param_tokens"])
    g["vp_t"] = np.ascontiguousarray(vp.reshape(32, 128, 8, 128))
    return g



def emit_d(P, S, io, gl, T, waits0=()):
    NT = T // NCK
    xCTd = io["xCT"]; wd = io["l1w_t"]; s6d = io["s6T"]
    alphad = io["alpha_d"]; biasd = io["bias_col"]
    onesr = gl["onesr"]
    wfs = P.sbuf("d_wfs", [128, KT * KT * 128], F32)
    wts = P.sbuf("d_wts", [128, KT * KT * 128], BF16)
    wab = P.sbuf("d_wab", [128, 128], F32)
    onesf = P.sbuf("d_onesf", [128, 1], F32)
    scl = P.sbuf("d_scl", [128, 8], F32)
    srow = P.sbuf("d_srow", [1, 8], F32)
    acol = P.sbuf("d_acol", [128, 1], F32)
    bcol = P.sbuf("d_bcol", [128, 8], F32)
    alr = P.sbuf("d_alr", [1, 1], F32)
    tmp1 = P.sbuf("d_tmp1", [128, 128], F32)
    tmp2 = P.sbuf("d_tmp2", [128, 128], F32)
    xbr = P.sbuf("d_xbr", [128, KT * NCK], BF16)
    xfr = P.sbuf("d_xfr", [128, KT * NCK], F32)
    s6s = P.sbuf("d_s6s", [128, KT * NCK], F32)
    pA = P.psum("d_pA", [1, 128])
    pB = P.psum("d_pB", [128, NCK])
    pM = [P.psum(f"d_pM{i}", [128, NCK]) for i in range(2)]
    sWL = S.sem(kind="g"); sAL = S.sem(kind="d"); sAB = S.sem(); sABE = S.sem(); sRD = S.sem(); sAA = S.sem()
    sBC = S.sem(); sBCE = S.sem(); sTN = S.sem(); sX = S.sem(kind="g"); sXF = S.sem(kind="d")
    sMM = S.sem(); sOE = S.sem(); sST = S.sem(kind="d"); sVV = S.sem()
    S.step("vector", lambda e: e.memset(onesf[:], 1.0), incs=[(sVV, 1)])
    S.step("sync", lambda e: e.dma_start(out=alr[:], in_=alphad), waits=list(waits0), incs=[(sAL, 16)])
    S.step("sync", lambda e: e.dma_start(out=bcol[:], in_=biasd), incs=[(sAL, 16)])
    S.step("gpsimd", lambda e: e.dma_start(
               out=wfs[:].rearrange("p (cg kt j) -> p cg kt j", kt=KT, j=128),
               in_=wd.rearrange("cg p kt j -> p cg kt j")),
           waits=list(waits0), incs=[(sWL, 16)])
    nab = 0
    for i in range(64):
        S.step("scalar",
               lambda e, i=i: e.activation(wab[:, :], wfs[:, i * 128:(i + 1) * 128], AF.Abs),
               waits=([(sWL, 16)] if i == 0 else []) + ([(sAB, i)] if i >= 1 else []),
               incs=[(sAA, 1)])
        S.step("tensor",
               lambda e, i=i: e.matmul(pA[0:1, :], onesf[:], wab[:, :],
                                       start=(i == 0), stop=(i == 63)),
               waits=[(sAA, None)] + ([(sVV, 1)] if i == 0 else []),
               incs=[(sAB, 1)])
    S.step("vector", lambda e: e.tensor_copy(scl[0:1, 0:8], pA[0:1, 0:8]), waits=[(sAB, 64)])
    S.step("vector", lambda e: e.tensor_reduce(srow[0:1, 0:1], pA[0:1, :], axis=AX.X, op=ALU.add),
           waits=[(sAB, 64)], incs=[(sVV, 1)])
    S.step("vector", lambda e: e.tensor_scalar(srow[0:1, 1:2], srow[0:1, 0:1],
                                               0.7 / (1024.0 * 1024.0), None, ALU.mult),
           waits=[(sVV, None)], incs=[(sVV, 1)])
    S.step("vector", lambda e: e.tensor_scalar(srow[0:1, 2:3], srow[0:1, 1:2], -1.0, None, ALU.mult),
           waits=[(sVV, None)], incs=[(sVV, 1)])
    S.step("tensor", lambda e: e.matmul(pB[:, 0:2], onesr[:], srow[0:1, 1:3],
                                        start=True, stop=False, skip_group_check=True),
           waits=[(sVV, None)])
    S.step("tensor", lambda e: e.matmul(pB[:, 2:3], onesr[:], alr[0:1, 0:1],
                                        start=False, stop=True, skip_group_check=True),
           waits=[(sAL, 32)], incs=[(sBC, 1)])
    S.step("vector", lambda e: e.tensor_copy(scl[:, 0:3], pB[:, 0:3]),
           waits=[(sBC, 1)], incs=[(sBCE, 1)])
    for i in range(64):
        S.step("vector",
               lambda e, i=i: e.tensor_scalar(tmp1[:, :], wfs[:, i * 128:(i + 1) * 128],
                                              scl[:, 0:1], None, ALU.is_gt),
               waits=[(sBCE, 1), (sVV, None), (sTN, None)], incs=[(sVV, 1)])
        S.step("vector",
               lambda e, i=i: e.tensor_scalar(tmp2[:, :], wfs[:, i * 128:(i + 1) * 128],
                                              scl[:, 1:2], None, ALU.is_lt),
               waits=[(sVV, None)], incs=[(sVV, 1)])
        S.step("vector",
               lambda e, i=i: e.tensor_sub(wts[:, i * 128:(i + 1) * 128], tmp1[:, :], tmp2[:, :]),
               waits=[(sVV, None)], incs=[(sTN, 1)])
    v0 = S.n(sVV)
    for tc in range(NT):
        S.step("gpsimd",
               lambda e, tc=tc: e.dma_start(
                   out=xbr[:].rearrange("p (kt n) -> p kt n", n=NCK),
                   in_=xCTd[:, tc * NCK:(tc + 1) * NCK].rearrange("(kt p) n -> p kt n", p=128)),
               waits=([(sMM, NCG * tc)] if tc >= 1 else []),
               incs=[(sX, 16)])
        S.step("sync",
               lambda e, tc=tc: e.dma_start(
                   out=xfr[:].rearrange("p (cg n) -> p cg n", n=NCK),
                   in_=xCTd[:, tc * NCK:(tc + 1) * NCK].rearrange("(cg p) n -> p cg n", p=128)),
               waits=([(sVV, v0 + NCG * tc)] if tc >= 1 else []),
               incs=[(sXF, 16)])
        for cg in range(NCG):
            j = tc * NCG + cg
            b = pM[j % 2]
            for kt in range(KT):
                S.step("tensor",
                       lambda e, cg=cg, kt=kt, b=b: e.matmul(
                           b[:, :], wts[:, (cg * KT + kt) * 128:(cg * KT + kt + 1) * 128],
                           xbr[:, kt * NCK:(kt + 1) * NCK],
                           start=(kt == 0), stop=(kt == KT - 1)),
                       waits=([(sTN, 64), (sX, 16 * (tc + 1))] if (cg == 0 and kt == 0) else []) +
                             ([(sOE, j - 1)] if (kt == 0 and j >= 2) else []),
                       incs=([(sMM, 1)] if kt == KT - 1 else []))
            sl = slice(cg * NCK, (cg + 1) * NCK)
            S.step("vector",
                   lambda e, sl=sl, b=b, cg=cg: e.tensor_scalar(s6s[:, sl], b[:, :], scl[:, 2:3],
                                                                bcol[:, cg:cg + 1], ALU.mult, ALU.add),
                   waits=[(sMM, j + 1), (sXF, 16 * (tc + 1))] +
                         ([(sST, 16 * tc)] if (cg == 0 and tc >= 1) else []),
                   incs=[(sOE, 1)])
            S.step("vector", lambda e, sl=sl: e.tensor_add(s6s[:, sl], s6s[:, sl], xfr[:, sl]),
                   waits=[(sOE, None)], incs=[(sVV, 1)])
        S.step("sync",
               lambda e, tc=tc: e.dma_start(
                   out=s6d[:, tc * NCK:(tc + 1) * NCK].rearrange("(cg p) n -> p cg n", p=128),
                   in_=s6s[:].rearrange("p (cg n) -> p cg n", n=NCK)),
               waits=[(sVV, v0 + NCG * (tc + 1))],
               incs=[(sST, 16)])
    S.step("sync", lambda e: None, waits=[(sST, 16 * NT)])
    return sST, 16 * NT


def build_d(T):
    P = Prog("d")
    P.T = T
    nc = P.nc
    io = {
        "xCT": nc.dram_tensor("xCT", [C, T], F32, kind="ExternalInput").ap(),
        "outT": nc.dram_tensor("outT", [C, T], F32, kind="ExternalOutput").ap(),
        "l1w_t": nc.dram_tensor("l1w_t", [8, 128, 8, 128], F32, kind="ExternalInput").ap(),
        "alpha_d": nc.dram_tensor("alpha_d", [1, 1], F32, kind="ExternalInput").ap(),
        "bias_col": nc.dram_tensor("bias_col", [128, 8], F32, kind="ExternalInput").ap(),
        "s6T": nc.dram_tensor("s6T", [C, T], F32, kind="Internal").ap(),
    }
    with P.es:
        P.sem_es = P.es
        gl = {
            "epsc": P.sbuf("g_eps", [1, 1], F32),
            "onecf": P.sbuf("g_one", [128, 1], F32),
            "onesr": P.sbuf("g_onesr", [1, 128], F32),
        }
        P.gl = gl
        mrow_g = P.sbuf("g_mrow", [1, T], F32)
        rrow_g = P.sbuf("g_rrow", [1, T], F32)
        S = Script(P)
        S.step("vector", lambda e: e.memset(gl["epsc"][:], EPS_LN))
        S.step("vector", lambda e: e.memset(gl["onecf"][:], 1.0))
        S.step("vector", lambda e: e.memset(gl["onesr"][:], 1.0))
        block = P.es.enter_context(nc.Block())
        with ExitStack() as sc0:
            P_es_old = P.es; P.es = sc0
            sd_, nd_ = emit_d(P, S, io, gl, T)
            P.es = P_es_old
        with ExitStack() as sc1:
            P_es_old = P.es; P.es = sc1
            rowbufs = (mrow_g, rrow_g, P.sbuf("g_trow", [1, T // 2], F32))
            rowsem = emit_rows(P, S, "dp", io["s6T"], T, rms=False, rowbufs=rowbufs,
                               waits0=[(sd_, nd_)])
            emit_normalize(P, S, "dn", io["s6T"], io["outT"], T, rowbufs, rowsem,
                           extra_src_waits=[(sd_, nd_)])
            P.es = P_es_old
        S.replay(block)
    return nc


def prep_d(inputs, T):
    g = {}
    g["l1w_t"] = _tile_colblocks(np.ascontiguousarray(np.asarray(inputs["L1_ffn_weight"]).T))
    g["alpha_d"] = np.asarray(inputs["L1_ffn_alpha"], dtype=np.float32).reshape(1, 1)
    g["bias_col"] = _colmajor_vec(inputs["L1_ffn_bias"])
    return g



def build_fused(T):
    P = Prog("mk")
    P.T = T
    nc = P.nc
    NKT = T // 128
    d = {}
    def din(name, shape, dt=F32):
        d[name] = nc.dram_tensor(name, list(shape), dt, kind="ExternalInput").ap()
    def dint(name, shape, dt=F32):
        d[name] = nc.dram_tensor(name, list(shape), dt, kind="Internal").ap()
    din("xT", [C, T]); din("wkt", [8, 128, 8, 128]); din("wvt", [8, 128, 8, 128])
    din("wrt", [8, 128, 8, 128]); din("pv0", [128, 80])
    din("wkc", [32, 128, 8, 128]); din("wvc", [8, 128, 32, 128]); din("wrc", [8, 128, 8, 128])
    din("wqkv_t", [12, 128, 8, 128]); din("wo_t", [8, 128, 8, 128]); din("lnw", [128, 8])
    din("cosb_d", [128, T]); din("sinb_d", [128, T]); din("masks_d", [128, 4 * NCK])
    din("ident_d", [128, 128])
    din("gw_t", [128, 64]); din("w13_t", [8, 128, 8, 32, 128]); din("w2_t", [8, 128, 16, 8, 128])
    din("ln2w", [128, 8]); din("identf_d", [128, 128]); din("oh8_d", [8, 8])
    din("key_t", [32, 128, 8, 128]); din("vp_t", [32, 128, 8, 128])
    din("l1w_t", [8, 128, 8, 128]); din("alpha_d", [1, 1]); din("bias_col", [128, 8])
    d["outT"] = nc.dram_tensor("outT", [C, T], F32, kind="ExternalOutput").ap()
    for n in ["x1T", "vvT", "s2T", "yT", "x2T", "accT", "xBT", "sCT", "xCT", "s6T"]:
        dint(n, [C, T])
    dint("h1T", [C, T], BF16); dint("attT", [C, T], BF16); dint("h2T", [C, T], BF16)

    with P.es:
        P.sem_es = P.es
        gl = {
            "epsc": P.sbuf("g_eps", [1, 1], F32),
            "onecf": P.sbuf("g_one", [128, 1], F32),
            "onesr": P.sbuf("g_onesr", [1, 128], F32),
            "pvd2": P.sbuf("g_pvd2", [128, 16], F32),
        }
        P.gl = gl
        pvt = P.sbuf("g_pvt", [128, 80], F32)
        lnwc = P.sbuf("g_lnw", [128, 8], F32)
        lnw2c = P.sbuf("g_lnw2", [128, 8], F32)
        S = Script(P)
        sLD0 = S.sem(kind="d")
        S.step("vector", lambda e: e.memset(gl["epsc"][:], EPS_LN))
        S.step("vector", lambda e: e.memset(gl["onecf"][:], 1.0))
        S.step("vector", lambda e: e.memset(gl["onesr"][:], 1.0))
        S.step("sync", lambda e: e.dma_start(out=pvt[:, 0:80], in_=d["pv0"]), incs=[(sLD0, 16)])
        S.step("sync", lambda e: e.dma_start(out=lnwc[:], in_=d["lnw"]), incs=[(sLD0, 16)])
        S.step("sync", lambda e: e.dma_start(out=lnw2c[:], in_=d["ln2w"]), incs=[(sLD0, 16)])
        S.step("vector", lambda e: e.tensor_scalar(gl["pvd2"][:, 0:8], pvt[:, 64:72], -1.0, 1.0,
                                                   ALU.mult, ALU.add), waits=[(sLD0, 48)])
        S.step("vector", lambda e: e.tensor_scalar(gl["pvd2"][:, 8:16], pvt[:, 72:80], -1.0, 1.0,
                                                   ALU.mult, ALU.add))
        block = P.es.enter_context(nc.Block())

        def scoped(fn):
            with ExitStack() as sc:
                old = P.es; P.es = sc
                r = fn()
                P.es = old
            return r

        ioa1 = dict(xT=d["xT"], x1T=d["x1T"], wkt=d["wkt"], wvt=d["wvt"], wrt=d["wrt"], pv0=d["pv0"])
        sa1 = scoped(lambda: emit_a1(P, S, ioa1, gl, T))
        ioa2 = dict(xT=d["xT"], x1T=d["x1T"], yT=d["yT"], wkc=d["wkc"], wvc=d["wvc"], wrc=d["wrc"],
                    vvT=d["vvT"], s2T=d["s2T"])
        def _a2():
            mrow_g = P.sbuf("a2_mrow", [1, T], F32)
            rrow_g = P.sbuf("a2_rrow", [1, T], F32)
            rowsem = scoped(lambda: emit_rows(P, S, "a2r", d["x1T"], T, rms=False,
                                              rowbufs=(mrow_g, rrow_g, P.sbuf("a2_tr0", [1, T // 2], F32)),
                                              waits0=[sa1]))
            def _core():
                tsck = P.sbuf("a2_tsck", [128, NCG * (T + 1)], BF16)
                sCk = emit_a2_tsbuild(P, S, "a2tk", d["x1T"], tsck, 64,
                                      (mrow_g, rrow_g, None), rowsem, pvt, sST_prev=rowsem)
                io2 = dict(ioa2); io2["tsck"] = tsck
                return emit_a2_core(P, S, io2, gl, T, waits0=[(sCk, NCG)])
            svst, nvst = scoped(_core)
            def _comb():
                tscr = P.sbuf("a2_tscr", [128, NCG * (T + 1)], BF16)
                sCr = emit_a2_tsbuild(P, S, "a2tr", d["x1T"], tscr, 72,
                                      (mrow_g, rrow_g, None), rowsem, pvt, sST_prev=(svst, nvst))
                io2 = dict(ioa2); io2["tscr"] = tscr
                return emit_a2_combine(P, S, io2, gl, T, waits0=[(sCr, NCG)])
            sstc = scoped(_comb)
            def _pl():
                rb = (mrow_g, rrow_g, P.sbuf("a2_tr1", [1, T // 2], F32))
                rowsem2 = emit_rows(P, S, "a2p", d["s2T"], T, rms=False, rowbufs=rb, waits0=[sstc])
                return emit_normalize(P, S, "a2n", d["s2T"], d["yT"], T, rb, rowsem2,
                                      extra_src_waits=[sstc])
            return scoped(_pl)
        sa2 = scoped(_a2)
        iob1 = dict(yT=d["yT"], x2T=d["x2T"], wqkv_t=d["wqkv_t"], wo_t=d["wo_t"],
                    cosb_d=d["cosb_d"], sinb_d=d["sinb_d"], masks_d=d["masks_d"],
                    ident_d=d["ident_d"], h1T=d["h1T"], attT=d["attT"], lnw_col=lnwc)
        def _b1():
            mrow_g = P.sbuf("b1_mrow", [1, 1], F32)
            rrow_g = P.sbuf("b1_rrow", [1, T], F32)
            def _r():
                rb = (mrow_g, rrow_g, P.sbuf("b1_tr0", [1, T // 2], F32))
                return emit_rows(P, S, "b1r", d["yT"], T, rms=True, rowbufs=rb, waits0=[sa2]), rb
            (rowsem, rb) = scoped(_r)
            sh1 = scoped(lambda: emit_b1_h1(P, S, iob1, gl, T, rb, rowsem, lnwsem=(sLD0, 48)))
            def _qka():
                Qr = P.sbuf("b1_Qr", [128, 8 * T], BF16)
                Kr = P.sbuf("b1_Kr", [128, 2 * T], BF16)
                VT = P.sbuf("b1_VT", [128, 4 * NKT * 65], BF16)
                io2 = dict(iob1); io2["Qr"] = Qr; io2["Kr"] = Kr; io2["VT"] = VT
                def _qkv():
                    cosb = P.sbuf("b1_cosb", [128, T], BF16)
                    sinb = P.sbuf("b1_sinb", [128, T], BF16)
                    identb = P.sbuf("b1_identb", [128, 128], BF16)
                    io2["cosb"] = cosb; io2["sinb"] = sinb; io2["identb"] = identb
                    sC = S.sem(kind="g")
                    S.step("gpsimd", lambda e: e.dma_start(out=cosb[:], in_=d["cosb_d"]),
                           waits=[sh1], incs=[(sC, 16)])
                    S.step("gpsimd", lambda e: e.dma_start(out=sinb[:], in_=d["sinb_d"]), incs=[(sC, 16)])
                    S.step("gpsimd", lambda e: e.dma_start(out=identb[:], in_=d["ident_d"]), incs=[(sC, 16)])
                    io2["Vb"] = P.sbuf("b1_Vb", [128, 2 * T], BF16)
                    return emit_b1_qkv(P, S, io2, gl, T, waits0=[sh1, (sC, 48)])
                qkvsem = scoped(_qkv)
                def _attn():
                    masksb = P.sbuf("b1_masks", [128, 4 * NCK], F32)
                    onesbc = P.sbuf("b1_onesbc", [128, 1], BF16)
                    onesf = P.sbuf("b1_onesf", [128, 64], F32)
                    io2["masksb"] = masksb; io2["onesbc"] = onesbc; io2["onesf"] = onesf
                    sM = S.sem(kind="d")
                    S.step("sync", lambda e: e.dma_start(out=masksb[:], in_=d["masks_d"]),
                           waits=[qkvsem], incs=[(sM, 16)])
                    S.step("vector", lambda e: e.memset(onesbc[:], 1.0), waits=[(sM, 16), qkvsem])
                    S.step("vector", lambda e: e.memset(onesf[:], 1.0))
                    return emit_b1_attn(P, S, io2, gl, T, qkvsem)
                return scoped(_attn), io2
            attsem, io2 = scoped(_qka)
            return scoped(lambda: emit_b1_wo(P, S, io2, gl, T, attsem))
        sb1 = scoped(_b1)
        iob2 = dict(yT=d["yT"], x2T=d["x2T"], xBT=d["xBT"], gw_t=d["gw_t"], w13_t=d["w13_t"],
                    w2_t=d["w2_t"], identf_d=d["identf_d"], oh8_d=d["oh8_d"], h2T=d["h2T"],
                    accT=d["accT"])
        def _b2():
            identf = P.sbuf("b2_identf", [128, 128], F32)
            oh8 = P.sbuf("b2_oh8", [8, 8], BF16)
            g8all = P.sbuf("b2_g8all", [8, T], BF16)
            sLD2 = S.sem(kind="d"); sLD3 = S.sem(kind="g")
            S.step("sync", lambda e: e.dma_start(out=identf[:], in_=d["identf_d"]),
                   waits=[sb1], incs=[(sLD2, 16)])
            S.step("gpsimd", lambda e: e.dma_start(out=oh8[:], in_=d["oh8_d"]),
                   waits=[sb1], incs=[(sLD3, 16)])
            io2 = dict(iob2); io2["identf"] = identf; io2["oh8"] = oh8; io2["g8all"] = g8all
            def _rh2():
                mrow_g = P.sbuf("b2_mrow", [1, T], F32)
                rrow_g = P.sbuf("b2_rrow", [1, T], F32)
                def _r():
                    rb = (mrow_g, rrow_g, P.sbuf("b2_tr0", [1, T // 2], F32))
                    return emit_rows(P, S, "b2r", d["x2T"], T, rms=True, rowbufs=rb, waits0=[sb1]), rb
                (rowsem, rb) = scoped(_r)
                def _h2():
                    io3 = dict(io2); io3["yT"] = d["x2T"]; io3["h1T"] = d["h2T"]; io3["lnw_col"] = lnw2c
                    return emit_b1_h1(P, S, io3, gl, T, rb, rowsem, lnwsem=(sLD0, 48), pref="b2h")
                return scoped(_h2)
            sh2 = scoped(_rh2)
            gatesem = scoped(lambda: emit_b2_gates(P, S, io2, gl, T,
                                                   waits0=[sh2, (sLD2, 16), (sLD3, 16)]))
            sae = scoped(lambda: emit_b2_experts(P, S, io2, gl, T, gatesem, waits0=[gatesem]))
            def _pl():
                rb2 = (P.sbuf("b2_mrow2", [1, T], F32), P.sbuf("b2_rrow2", [1, T], F32),
                       P.sbuf("b2_tr1", [1, T // 2], F32))
                rowsem2 = emit_rows(P, S, "b2p", d["accT"], T, rms=False, rowbufs=rb2, waits0=[sae])
                return emit_normalize(P, S, "b2n", d["accT"], d["xBT"], T, rb2, rowsem2,
                                      extra_src_waits=[sae])
            return scoped(_pl)
        sb2 = scoped(_b2)
        ioc = dict(xBT=d["xBT"], xCT=d["xCT"], key_t=d["key_t"], vp_t=d["vp_t"], sCT=d["sCT"])
        def _c():
            sc_ = scoped(lambda: emit_c(P, S, ioc, gl, T, waits0=[sb2]))
            def _pl():
                rb = (P.sbuf("c_mrow", [1, T], F32), P.sbuf("c_rrow", [1, T], F32),
                      P.sbuf("c_tr0", [1, T // 2], F32))
                rowsem = emit_rows(P, S, "cp", d["sCT"], T, rms=False, rowbufs=rb, waits0=[sc_])
                return emit_normalize(P, S, "cn", d["sCT"], d["xCT"], T, rb, rowsem,
                                      extra_src_waits=[sc_])
            return scoped(_pl)
        sc2_ = scoped(_c)
        iod = dict(xCT=d["xCT"], outT=d["outT"], l1w_t=d["l1w_t"], alpha_d=d["alpha_d"],
                   bias_col=d["bias_col"], s6T=d["s6T"])
        def _d():
            mrow_g = P.sbuf("d_mrow", [1, T], F32)
            rrow_g = P.sbuf("d_rrow", [1, T], F32)
            sd_ = scoped(lambda: emit_d(P, S, iod, gl, T, waits0=[sc2_]))
            def _pl():
                rb = (mrow_g, rrow_g, P.sbuf("d_tr0", [1, T // 2], F32))
                rowsem = emit_rows(P, S, "dp", d["s6T"], T, rms=False, rowbufs=rb, waits0=[sd_])
                return emit_normalize(P, S, "dn", d["s6T"], d["outT"], T, rb, rowsem,
                                      extra_src_waits=[sd_])
            return scoped(_pl)
        scoped(_d)
        S.replay(block)
    return nc


def kernel(**inputs):
    x = np.asarray(inputs["L0_attn_x"])
    B, T, _ = x.shape
    _ = inputs["L0_attn_time_decay"]
    g1 = prep_host(inputs)
    ga2 = _prep_a2(inputs)
    gb1 = prep_b1(inputs, T)
    gb2 = prep_b2(inputs, T)
    gc = prep_c(inputs, T)
    gd = prep_d(inputs, T)
    nc = build_fused(T)
    maps = [dict(xT=g1["xT"][b], wkt=g1["wkt"], wvt=g1["wvt"], wrt=g1["wrt"], pv0=g1["pv0"],
                 wkc=ga2["wkc"], wvc=ga2["wvc"], wrc=ga2["wrc"],
                 wqkv_t=gb1["wqkv_t"], wo_t=gb1["wo_t"], lnw=gb1["lnw"], cosb_d=gb1["cosb_d"],
                 sinb_d=gb1["sinb_d"], masks_d=gb1["masks_d"], ident_d=gb1["ident_d"],
                 gw_t=gb2["gw_t"], w13_t=gb2["w13_t"], w2_t=gb2["w2_t"], ln2w=gb2["ln2w"],
                 identf_d=gb2["identf_d"], oh8_d=gb2["oh8_d"],
                 key_t=gc["key_t"], vp_t=gc["vp_t"],
                 l1w_t=gd["l1w_t"], alpha_d=gd["alpha_d"], bias_col=gd["bias_col"])
            for b in range(B)]
    res = run_bass_kernel_spmd(nc, maps, core_ids=list(range(B)))
    out = np.stack([np.ascontiguousarray(res.results[b]["outT"].T) for b in range(B)], axis=0)
    return out.astype(np.float32)
